# version 05
# speedup vs baseline: 1.4997x; 1.1463x over previous
import numpy as np
import concourse.bass as bass
import concourse.mybir as mybir
from concourse.bass_utils import run_bass_kernel_spmd
from concourse.alu_op_type import AluOpType as ALU

F32 = mybir.dt.float32
BF16 = mybir.dt.bfloat16
AF = mybir.ActivationFunctionType

NCORES = 8
B, S, D = 4, 4096, 1024
HALF = S // 2
LN10K = float(np.log(10000.0))
LN2 = float(np.log(2.0))
TWO_PI = 2 * np.pi
TWO_PI_HI = 6.28125
TWO_PI_LO = TWO_PI - TWO_PI_HI
C_L0 = 24.0
C_L1 = 10.0
EPS = 1e-5

ENGINES = ("sync", "vector", "scalar", "tensor", "gpsimd")

QPAIRS = [(0, 4), (1, 5), (2, 6), (3, 7), (8, 12), (9, 13), (10, 14), (11, 15)]
QCHUNK, QBASE = {}, {}
for _c, (_a, _b) in enumerate(QPAIRS):
    QCHUNK[_a], QBASE[_a] = _c, 0
    QCHUNK[_b], QBASE[_b] = _c, 64


class H:
    __slots__ = ("eng", "kind", "idx")

    def __init__(self, eng, kind, idx):
        self.eng, self.kind, self.idx = eng, kind, idx


class Prog:

    def __init__(self):
        self.ops = {e: [] for e in ENGINES}
        self.ccount = {e: 0 for e in ENGINES}
        self.dcount = {e: 0 for e in ENGINES}
        self.floor = {}

    def _waits_for(self, deps):
        w = {}

        def add(d):
            if d is None:
                return
            if isinstance(d, (list, tuple)):
                for x in d:
                    add(x)
            else:
                key = (d.eng, d.kind)
                if w.get(key, -1) < d.idx:
                    w[key] = d.idx
        add(deps)
        for key, val in self.floor.items():
            if w.get(key, -1) < val:
                w[key] = val
        return w

    def op(self, eng, emit, deps=(), n=1):
        w = self._waits_for(deps)
        if eng == "tensor":
            c = self.ccount["tensor"]
            if c + n > 16:
                key = ("tensor", "c")
                need = c + n - 16
                if w.get(key, -1) < need:
                    w[key] = need
        self.ops[eng].append((emit, w, "c", n))
        self.ccount[eng] += n
        return H(eng, "c", self.ccount[eng])

    def dma(self, eng, emit, deps=()):
        w = self._waits_for(deps)
        self.ops[eng].append((emit, w, "d", 1))
        self.dcount[eng] += 1
        return H(eng, "d", 16 * self.dcount[eng])

    def set_floor(self, deps):
        w = {}
        for d in deps:
            key = (d.eng, d.kind)
            if w.get(key, -1) < d.idx:
                w[key] = d.idx
        self.floor = w

    def barrier(self):
        out = []
        for e in ENGINES:
            if self.ccount[e]:
                out.append(H(e, "c", self.ccount[e]))
            if self.dcount[e]:
                out.append(H(e, "d", 16 * self.dcount[e]))
        return out

    def build(self, nc):
        stack, sems = [], {}
        for e in ENGINES:
            for k in ("c", "d"):
                g = nc.semaphore(f"s_{k}_{e}")
                sems[(e, k)] = g.__enter__()
                stack.append(g)
        bg = nc.Block()
        block = bg.__enter__()

        def make_body(engname):
            oplist = self.ops[engname]

            def body(engobj):
                waited = {}
                for emit, w, kind, n in oplist:
                    for key, val in sorted(w.items()):
                        if waited.get(key, -1) >= val:
                            continue
                        engobj.wait_ge(sems[key], val)
                        waited[key] = val
                    last = emit(engobj)
                    if kind == "c":
                        last.then_inc(sems[(engname, "c")], n)
                    else:
                        last.then_inc(sems[(engname, "d")], 16)
            return body

        for e in ENGINES:
            if self.ops[e]:
                getattr(block, e)(make_body(e))
        bg.__exit__(None, None, None)
        for g in reversed(stack):
            g.__exit__(None, None, None)


class Arena:
    def __init__(self, t, ncols):
        self.t, self.ncols, self.off, self.marks = t, ncols, 0, []

    def alloc(self, p, w, align=32):
        o = (self.off + align - 1) // align * align
        assert o + w <= self.ncols, f"arena overflow {o}+{w}>{self.ncols}"
        self.off = o + w
        self.last_off = o
        return self.t[0:p, o:o + w]

    def mark(self):
        self.marks.append(self.off)

    def release(self):
        self.off = self.marks.pop()



def ts(out, in0, s1, op0, s2=None, op1=None, accum=None):
    kw = {}
    if op1 is not None:
        kw["op1"] = op1
    if accum is not None:
        kw["accum_out"] = accum
    return lambda e: e.tensor_scalar(out=out, in0=in0, scalar1=s1, scalar2=s2,
                                     op0=op0, **kw)


def tt(out, a, bb, op):
    return lambda e: e.tensor_tensor(out=out, in0=a, in1=bb, op=op)


def stt(out, in0, scalar, in1, op0, op1):
    return lambda e: e.scalar_tensor_tensor(out=out, in0=in0, scalar=scalar,
                                            in1=in1, op0=op0, op1=op1)


def act(out, in_, func, bias=0.0, scale=1.0, accum=None):
    kw = {"accum_out": accum} if accum is not None else {}
    return lambda e: e.activation(out=out, in_=in_, func=func, bias=bias,
                                  scale=scale, **kw)


def cp(out, in_):
    return lambda e: e.tensor_copy(out=out, in_=in_)


def mm(out, lhsT, rhs, start=True, stop=True):
    return lambda e: e.matmul(out, lhsT, rhs, start=start, stop=stop)


def frac_cascade(p, u, d, topbit, nbits, deps, eng="vector"):
    h = deps
    for k in range(topbit, topbit - nbits, -1):
        c = float(2.0 ** k)
        h1 = p.op(eng, ts(d, u, c, ALU.is_ge, -c, ALU.mult), deps=h)
        h = [p.op(eng, tt(u, u, d, ALU.add), deps=[h1])]
    return h[0]


DEBUG_NGROUPS = 99
SGROUPS = []
_j = 0
while _j < 32:
    _n = min(3, 32 - _j)
    SGROUPS.append((_j, _n))
    _j += _n



GNB16, GNF32 = 54784, 23296

def _standalone(emit, ios, upto="full"):
    nc = bass.Bass("TRN2", target_bir_lowering=False)
    t = {}
    for (name, shape, dt_, kind) in ios:
        t[name] = nc.dram_tensor(name, shape, dt_, kind=kind).ap()
    with (
        nc.sbuf_tensor("AB16", [128, GNB16], BF16) as bt,
        nc.sbuf_tensor("AF32", [128, GNF32], F32) as ft,
        nc.sbuf_tensor("AI32", [128, 2048], mybir.dt.int32) as it,
        nc.psum_tensor("PS", [128, 4096], F32) as PS,
    ):
        p = Prog()
        ctx = dict(nc=nc, p=p, bt=bt, ft=ft, PS=PS, t=t, it=it)
        emit(ctx, upto)
        p.op("gpsimd", lambda e: e.nop(), deps=p.barrier())
        p.build(nc)
    return nc

A_IOS = [("xbT", [D, S], F32, "ExternalInput"), ("xoT", [D, HALF], F32, "ExternalInput"),
         ("wq", [D, 1024], F32, "ExternalInput"), ("wk", [D, 256], F32, "ExternalInput"),
         ("wv", [D, 256], F32, "ExternalInput"), ("wo", [D, 1024], F32, "ExternalInput"),
         ("posr", [1, S], F32, "ExternalInput"), ("poso", [1, HALF], F32, "ExternalInput"),
         ("didx", [1, 32], F32, "ExternalInput"),
         ("onesf", [1, 128], F32, "ExternalInput"), ("jrep", [32, 128], F32, "ExternalInput"),
         ("jsgn", [32, 128], F32, "ExternalInput"), ("x1T", [D, HALF], F32, "ExternalOutput")]

B_IOS = [("x1T", [D, HALF], F32, "ExternalInput"), ("w1T", [D, 4096], F32, "ExternalInput"),
         ("w2T", [4096, D], F32, "ExternalInput"), ("g1c", [128, 8], F32, "ExternalInput"),
         ("b1lnc", [128, 8], F32, "ExternalInput"), ("b1c", [128, 32], F32, "ExternalInput"),
         ("g2c", [128, 32], F32, "ExternalInput"), ("b2lnc", [128, 32], F32, "ExternalInput"),
         ("b2c", [128, 8], F32, "ExternalInput"), ("onesf", [1, 128], F32, "ExternalInput"),
         ("i128", [128, 128], F32, "ExternalInput"), ("x2T", [D, HALF], F32, "ExternalOutput")]

C1_IOS = [("x2T", [D, S], F32, "ExternalInput"), ("x2oT", [D, HALF], F32, "ExternalInput"),
          ("wqkvT", [D, 3072], F32, "ExternalInput"), ("onesf", [1, 128], F32, "ExternalInput"),
          ("qd", [D, HALF], BF16, "ExternalOutput"), ("kd", [D, S], BF16, "ExternalOutput"),
          ("vd", [S, D], BF16, "ExternalOutput"), ("koT", [D, HALF], BF16, "ExternalOutput")]

C2_IOS = [("qd", [D, HALF], BF16, "ExternalInput"), ("kd", [D, S], BF16, "ExternalInput"),
          ("i128", [128, 128], F32, "ExternalInput"),
          ("mrowP", [128, 256], F32, "ExternalOutput")]

D_IOS = [("x2oT", [D, HALF], F32, "ExternalInput"), ("qd", [D, HALF], BF16, "ExternalInput"),
         ("kd", [D, S], BF16, "ExternalInput"), ("vd", [S, D], BF16, "ExternalInput"),
         ("diagm", [16, S], F32, "ExternalInput"), ("diagmo", [16, HALF], F32, "ExternalInput"),
         ("woutT", [D, D], F32, "ExternalInput"), ("boutc", [128, 8], F32, "ExternalInput"),
         ("e65g", [16, 1040], BF16, "ExternalInput"), ("i16b", [16, 16], BF16, "ExternalInput"),
         ("onesf", [1, 128], F32, "ExternalInput"), ("i128", [128, 128], F32, "ExternalInput"),
         ("x3T", [D, HALF], F32, "ExternalOutput"), ("x3lnT", [D, HALF], F32, "ExternalOutput"),
         ("amax", [1, 16], F32, "ExternalOutput")]

F_IOS = [("x3T", [D, HALF], F32, "ExternalInput"), ("x3lnT", [D, HALF], F32, "ExternalInput"),
         ("amax8", [1, 16], F32, "ExternalInput"), ("wT", [D, D], F32, "ExternalInput"),
         ("biasc", [128, 8], F32, "ExternalInput"), ("onesf", [1, 128], F32, "ExternalInput"),
         ("i128", [128, 128], F32, "ExternalInput"), ("outT", [D, HALF], F32, "ExternalOutput")]


def build_launch_A(upto="full"):
    return _standalone(emit_A, A_IOS, upto)

def build_launch_B(upto="full"):
    return _standalone(emit_B, B_IOS, upto)

def build_launch_C1(upto="full"):
    return _standalone(emit_C1, C1_IOS, upto)

def build_launch_C2(upto="full"):
    return _standalone(emit_C2, C2_IOS, upto)

def build_launch_D(upto="full"):
    return _standalone(emit_D, D_IOS, upto)

def build_launch_F(upto="full"):
    return _standalone(emit_F, F_IOS, upto)


PAIRS_RG = [[0, 1], [2, 3], [4, 5], [6, 7]]
ALL_RG = [[0, 1, 2, 3, 4, 5, 6, 7]]


def build_fused():
    nc = bass.Bass("TRN2", target_bir_lowering=False)
    t = {}
    ext_inputs = [
        ("xbT", [D, S], F32), ("xoT", [D, HALF], F32), ("wq", [D, 1024], F32),
        ("wk", [D, 256], F32), ("wv", [D, 256], F32), ("wo", [D, 1024], F32),
        ("posr", [1, S], F32), ("poso", [1, HALF], F32), ("didx", [1, 32], F32),
        ("onesf", [1, 128], F32), ("jrep", [32, 128], F32), ("jsgn", [32, 128], F32),
        ("w1T", [D, 4096], F32), ("w2T", [4096, D], F32),
        ("g1c", [128, 8], F32), ("b1lnc", [128, 8], F32), ("b1c", [128, 32], F32),
        ("g2c", [128, 32], F32), ("b2lnc", [128, 32], F32), ("b2c", [128, 8], F32),
        ("i128", [128, 128], F32), ("wqkvT", [D, 3072], F32),
        ("woutT", [D, D], F32), ("boutc", [128, 8], F32),
        ("e65g", [16, 1040], BF16), ("i16b", [16, 16], BF16),
        ("wT", [D, D], F32), ("biasc", [128, 8], F32),
    ]
    for (name, shape, dt_) in ext_inputs:
        t[name] = nc.dram_tensor(name, shape, dt_, kind="ExternalInput").ap()
    outT = nc.dram_tensor("outT", [D, HALF], F32, kind="ExternalOutput").ap()
    x1d = nc.dram_tensor("x1d", [D, HALF], F32, kind="Internal").ap()
    x2own = nc.dram_tensor("x2own", [1, D * HALF], F32, kind="Internal").ap()
    x2g = nc.dram_tensor("x2g", [2, D * HALF], F32, kind="Internal").ap()
    qdD = nc.dram_tensor("qdD", [D, HALF], BF16, kind="Internal").ap()
    kdD = nc.dram_tensor("kdD", [D, S], BF16, kind="Internal").ap()
    koD = nc.dram_tensor("koD", [D, HALF], BF16, kind="Internal").ap()
    vdD = nc.dram_tensor("vdD", [S, D], BF16, kind="Internal").ap()
    mrowD = nc.dram_tensor("mrowD", [1, 16 * HALF], F32, kind="Internal").ap()
    mrowG = nc.dram_tensor("mrowG", [2, 16 * HALF], F32, kind="Internal").ap()
    x3d = nc.dram_tensor("x3d", [D, HALF], F32, kind="Internal").ap()
    x3lnd = nc.dram_tensor("x3lnd", [D, HALF], F32, kind="Internal").ap()
    amaxD = nc.dram_tensor("amaxD", [1, 16], F32, kind="Internal").ap()
    amaxG = nc.dram_tensor("amaxG", [1, 16], F32, kind="Internal").ap()

    with (
        nc.sbuf_tensor("AB16", [128, GNB16], BF16) as bt,
        nc.sbuf_tensor("AF32", [128, GNF32], F32) as ft,
        nc.psum_tensor("PS", [128, 4096], F32) as PS,
    ):
        p = Prog()

        def ctx_for(names_map, extra=None):
            tt_ = dict(t)
            tt_.update(names_map)
            c = dict(nc=nc, p=p, bt=bt, ft=ft, PS=PS, t=tt_)
            if extra:
                c.update(extra)
            return c

        emit_A(ctx_for({"x1T": x1d}))
        p.set_floor(p.barrier())
        emit_B(ctx_for({"x1T": x1d,
                        "x2T": x2own.rearrange("o (a b) -> (o a) b", b=HALF)}))
        p.set_floor(p.barrier())
        hag = p.dma("gpsimd", lambda e: e.collective_compute(
            "AllGather", ALU.bypass, PAIRS_RG, [x2own], [x2g]))
        p.set_floor(p.barrier())
        x2g4 = x2g.rearrange("r (c p i) -> r p c i", p=128, i=HALF)
        x2chunks = [x2g4[ic8 // 4, :, :, (ic8 % 4) * 512:(ic8 % 4) * 512 + 512]
                    for ic8 in range(8)]
        emit_C1(ctx_for({"x2T": x2g.rearrange("r (c i) -> (r c) i", i=HALF)[0:D, :],
                         "x2oT": x2own.rearrange("o (a b) -> (o a) b", b=HALF),
                         "qd": qdD, "kd": kdD, "vd": vdD, "koT": koD},
                        extra={"x2chunks": x2chunks}))
        p.set_floor(p.barrier())
        emit_C2(ctx_for({"qd": qdD, "kd": kdD, "koT": koD,
                         "mrow": mrowD.rearrange("o (h i) -> (o h) i", i=HALF)}))
        p.set_floor(p.barrier())
        hagm = p.dma("gpsimd", lambda e: e.collective_compute(
            "AllGather", ALU.bypass, PAIRS_RG, [mrowD], [mrowG]))
        p.set_floor(p.barrier())
        mg = mrowG.rearrange("r (h i) -> r h i", i=HALF)
        emit_D(ctx_for({"x2oT": x2own.rearrange("o (a b) -> (o a) b", b=HALF),
                        "qd": qdD, "kd": kdD, "vd": vdD,
                        "diagm": mrowG.rearrange("r (h i) -> (r h) i", i=HALF)[0:16, :],
                        "diagmo": mrowD.rearrange("o (h i) -> (o h) i", i=HALF),
                        "x3T": x3d, "x3lnT": x3lnd, "amax": amaxD},
                       extra={"diag2": [mg[0], mg[1]]}))
        p.set_floor(p.barrier())
        har = p.dma("gpsimd", lambda e: e.collective_compute(
            "AllReduce", ALU.max, ALL_RG, [amaxD], [amaxG]))
        p.set_floor(p.barrier())
        emit_F(ctx_for({"x3T": x3d, "x3lnT": x3lnd, "amax8": amaxG,
                        "outT": outT}))
        p.op("gpsimd", lambda e: e.nop(), deps=p.barrier())
        p.build(nc)
    return nc


def make_fused_inputs(inputs):
    in_A = make_A_inputs(inputs)
    x_dummy = np.zeros((B, S, D), dtype=np.float32)
    common_B = make_B_inputs(inputs, x_dummy)[0]
    wqkvT = np.ascontiguousarray(inputs["L1_attn_w_qkv"].T.astype(np.float32))
    woutT = np.ascontiguousarray(inputs["L1_attn_w_out"].T.astype(np.float32))
    bout = np.asarray(inputs["L1_attn_b_out"], dtype=np.float32)
    wTt = np.ascontiguousarray(inputs["L1_ffn_weight"].T.astype(np.float32))
    biasv = np.asarray(inputs["L1_ffn_bias"], dtype=np.float32)
    import ml_dtypes
    in_maps = []
    for c in range(NCORES):
        m = dict(in_A[c])
        for k in ("w1T", "w2T", "g1c", "b1lnc", "b1c", "g2c", "b2lnc", "b2c",
                  "i128"):
            m[k] = common_B[k]
        m["wqkvT"] = wqkvT
        m["woutT"] = woutT
        m["boutc"] = np.ascontiguousarray(bout.reshape(-1, 128).T)
        m["e65g"] = _e65g()
        m["i16b"] = np.eye(16, dtype=np.float32).astype(ml_dtypes.bfloat16)
        m["wT"] = wTt
        m["biasc"] = np.ascontiguousarray(biasv.reshape(-1, 128).T)
        in_maps.append(m)
    return in_maps


def kernel_fused(**inputs):
    cids = list(range(NCORES))
    in_maps = make_fused_inputs(inputs)
    r = run_bass_kernel_spmd(build_fused(), in_maps, core_ids=cids)
    out = np.empty((B, S, D), dtype=np.float32)
    for i, c in enumerate(cids):
        b, half = c // 2, c % 2
        out[b, half * HALF:(half + 1) * HALF] = r.results[i]["outT"].T
    return out



def emit_A(ctx, upto="full"):
    nc, p, bt, ft, PS, t = ctx["nc"], ctx["p"], ctx["bt"], ctx["ft"], ctx["PS"], ctx["t"]
    xbT = t["xbT"]
    xoT = t["xoT"]
    wq = t["wq"]
    wk = t["wk"]
    wv = t["wv"]
    wo = t["wo"]
    posr = t["posr"]
    poso = t["poso"]
    onesf = t["onesf"]
    jrep = t["jrep"]
    jsgn = t["jsgn"]
    x1T = t["x1T"]
    didx = t["didx"]
    if True:
        ba = Arena(bt, GNB16)
        fa = Arena(ft, GNF32)

        onesb = ba.alloc(128, 32)
        qT = ba.alloc(128, 8 * HALF)
        o_qT = ba.last_off
        kT = ba.alloc(128, 2 * S)
        Vp = ba.alloc(128, 32 * 260)
        sqb = ba.alloc(128, 4096)
        o_sqb = ba.last_off
        ba.mark()
        wqb = ba.alloc(128, 8 * 1024)
        wkb = ba.alloc(128, 8 * 256)
        wvb = ba.alloc(128, 8 * 256)
        xlnc = ba.alloc(128, 4096)

        onesfT = fa.alloc(1, 128)
        onescol = fa.alloc(128, 32)
        rowsc = fa.alloc(128, 4096)
        scrA = fa.alloc(128, 4096)
        scrB = fa.alloc(128, 1024)
        scrC = fa.alloc(128, 1024)
        fa.mark()
        CCk = fa.alloc(128, S)
        SSk = fa.alloc(128, S)
        CCq = fa.alloc(128, HALF)
        SSq = fa.alloc(128, HALF)
        jrA = fa.alloc(64, 128)
        o_jr = fa.last_off
        jsA = fa.alloc(64, 128)
        o_js = fa.last_off
        invfr = fa.alloc(1, 64)
        didxr = fa.alloc(1, 64)
        jrepT = ft[32:64, o_jr:o_jr + 128]
        jsgnT = ft[32:64, o_js:o_js + 128]

        frq = scrA[0:32, :]
        u32 = rowsc[0:32, :]
        srow = scrA[32:64, :]

        ps_s = [PS[:, 0:1536], PS[:, 1536:3072]]
        ps_av = PS[:, 3072:3584]
        ps_m = PS[:, 3584:4096]
        ps_b0 = PS[:, 0:512]
        ps_b1 = PS[:, 512:1024]
        ps_b3 = PS[:, 1536:2048]
        ps_b4 = PS[:, 2048:2560]

        h_ones = p.op("vector", lambda e: e.memset(onesb[:, 0:1], 1.0))
        h_ob1 = p.op("vector", lambda e: e.memset(onesb[:, 1:2], 1.0 / 1024.0),
                     deps=[h_ones])
        h_ocol = p.op("vector", lambda e: e.memset(onescol[:, 0:1], 1.0 / 1024.0))
        h_negc = p.op("vector", lambda e: e.memset(onescol[:, 1:2], -C_L0))
        h_onesf = p.dma("sync", lambda e: e.dma_start(out=onesfT, in_=onesf))
        h_j1 = p.dma("sync", lambda e: e.dma_start(out=jrepT, in_=jrep))
        h_j2 = p.dma("sync", lambda e: e.dma_start(out=jsgnT, in_=jsgn))

        h_di = p.dma("sync", lambda e: e.dma_start(out=didxr[0:1, 0:32], in_=didx))
        h_invf = p.op("scalar", act(invfr[0:1, 0:32], didxr[0:1, 0:32], AF.Exp,
                                    scale=-LN10K / 32.0), deps=[h_di])

        def build_tables2(pos_ap, n, CXd, SXd, dep0):
            hp = p.dma("sync", lambda e: e.dma_start(out=rowsc[0:1, 0:n], in_=pos_ap),
                       deps=dep0)
            hf = [hp, h_invf]
            for c4 in range(n // 512):
                sl = slice(c4 * 512, (c4 + 1) * 512)
                hm = p.op("tensor", (lambda sl: mm(ps_m[0:32, 0:512], invfr[0:1, 0:32],
                                                   rowsc[0:1, sl]))(sl), deps=hf)
                hf = [p.op("scalar", (lambda sl: act(frq[:, sl], ps_m[0:32, 0:512],
                                                     AF.Identity))(sl), deps=[hm])]
            h_frq = hf[0]

            def sin_rows(shift, deps):
                OFF = 1024.0
                fr = frq[:, 0:n]
                uu = u32[:, 0:n]
                dk = CXd[0:32, 0:n]
                h1 = p.op("vector", ts(uu, fr, 1.0 / TWO_PI, ALU.mult,
                                       shift / TWO_PI + 0.5 + OFF, ALU.add), deps=deps)
                hfr = frac_cascade(p, uu, dk, 10, 11, [h1])
                h2 = p.op("vector", ts(dk, fr, 1.0 / TWO_PI, ALU.mult,
                                       shift / TWO_PI + 0.5 + OFF, ALU.add), deps=[hfr])
                h3 = p.op("vector", tt(dk, dk, uu, ALU.subtract), deps=[h2])
                h4 = p.op("vector", ts(dk, dk, -OFF, ALU.add), deps=[h3])
                h5 = p.op("vector", stt(uu, dk, -TWO_PI_HI, fr, ALU.mult, ALU.add),
                          deps=[h4])
                h6 = p.op("vector", ts(uu, uu, float(shift), ALU.add), deps=[h5])
                h7 = p.op("vector", stt(uu, dk, -TWO_PI_LO, uu, ALU.mult, ALU.add),
                          deps=[h6])
                h8 = p.op("vector", ts(uu, uu, float(np.pi), ALU.min,
                                       float(-np.pi), ALU.max), deps=[h7])
                return p.op("scalar", act(srow[:, 0:n], uu, AF.Sin), deps=[h8])

            hs = sin_rows(0.0, [h_frq])
            hlast = hs
            for c4 in range(n // 512):
                sl = slice(c4 * 512, (c4 + 1) * 512)
                h1 = p.op("tensor", (lambda sl: mm(ps_m[:, 0:512], jsgnT,
                                                   srow[:, sl]))(sl),
                          deps=[hs, h_j2] if c4 == 0 else [hlast, hs])
                hlast = p.op("scalar", (lambda sl: act(SXd[:, sl], ps_m[:, 0:512],
                                                       AF.Identity))(sl), deps=[h1])
            hc = sin_rows(np.pi / 2, [hlast])
            hlast = hc
            for c4 in range(n // 512):
                sl = slice(c4 * 512, (c4 + 1) * 512)
                h1 = p.op("tensor", (lambda sl: mm(ps_m[:, 0:512], jrepT,
                                                   srow[:, sl]))(sl),
                          deps=[hc, h_j1] if c4 == 0 else [hlast, hc])
                hlast = p.op("scalar", (lambda sl: act(CXd[:, sl], ps_m[:, 0:512],
                                                       AF.Identity))(sl), deps=[h1])
            return hlast

        h_t1 = build_tables2(posr, S, CCk, SSk, [h_di])
        h_t2 = build_tables2(poso, HALF, CCq, SSq, [h_t1])

        def load_weight(dst, src, ncols, dep0):
            dep = dep0
            out = None
            for c in range(8):
                hd = p.dma("sync", (lambda c: lambda e: e.dma_start(
                    out=scrB[:, 0:ncols], in_=src[c * 128:(c + 1) * 128, :]))(c),
                    deps=dep)
                out = p.op("vector", (lambda c: cp(dst[:, c * ncols:(c + 1) * ncols],
                                                   scrB[:, 0:ncols]))(c), deps=[hd])
                dep = [out]
            return out

        h_wq = load_weight(wqb, wq, 1024, [h_t2])
        h_wk = load_weight(wkb, wk, 256, [h_wq])
        h_wv = load_weight(wvb, wv, 256, [h_wk])
        bar_w = p.barrier()

        xr3 = xbT.rearrange("(c p) i -> p c i", p=128)
        xo3 = xoT.rearrange("(c p) i -> p c i", p=128)
        scrA3 = scrA.rearrange("p (c i) -> p c i", i=512)

        def ln_chunk(src3, tok, deps):
            hx = p.dma("sync", (lambda tok: lambda e: e.dma_start(
                out=scrA3, in_=src3[:, :, tok]))(tok), deps=deps)
            hsq = p.op("scalar", act(sqb, scrA, AF.Square), deps=[hx])
            hsum = None
            for c in range(8):
                blk = slice(c * 512, c * 512 + 512)
                hsum = p.op("tensor", (lambda blk, st, sp: mm(
                    ps_b0[0:1, :], onescol[:, 0:1], scrA[:, blk],
                    start=st, stop=sp))(blk, c == 0, c == 7),
                    deps=[hx, h_ocol] + deps if hsum is None else [hsum])
            hsqs = None
            for c in range(8):
                blk = slice(c * 512, c * 512 + 512)
                hsqs = p.op("tensor", (lambda blk, st, sp: mm(
                    ps_b1[0:1, :], onesb[:, 1:2], sqb[:, blk],
                    start=st, stop=sp))(blk, c == 0, c == 7),
                    deps=[hsq, h_ob1] + deps if hsqs is None else [hsqs])
            rm = rowsc[0:1, 0:512]
            rv = rowsc[0:1, 512:1024]
            rr = rowsc[0:1, 1024:1536]
            rmb = rowsc[0:1, 1536:2048]
            h1 = p.op("vector", cp(rm, ps_b0[0:1, :]), deps=[hsum] + deps)
            h2 = p.op("scalar", act(rv, rm, AF.Square), deps=[h1])
            h3 = p.op("vector", stt(rv, rv, -1.0, ps_b1[0:1, :], ALU.mult, ALU.add),
                      deps=[h2, hsqs])
            h3b = p.op("vector", ts(rv, rv, EPS, ALU.add), deps=[h3])
            h4 = p.op("scalar", act(rr, rv, AF.Ln), deps=[h3b])
            h5 = p.op("scalar", act(rr, rr, AF.Exp, scale=-0.5), deps=[h4])
            h6 = p.op("vector", stt(rmb, rm, -1.0, rr, ALU.mult, ALU.mult), deps=[h5])
            hb1 = p.op("tensor", mm(ps_b3[:, :], onesfT, rr), deps=[h5, h_onesf] + deps)
            hb2 = p.op("tensor", mm(ps_b4[:, :], onesfT, rmb), deps=[h6])
            happ = []
            hprev = None
            for c in range(8):
                blk = slice(c * 512, c * 512 + 512)
                ha = p.op("vector", (lambda blk: tt(scrB[:, 0:512], scrA[:, blk],
                                                    ps_b3[:, :], ALU.mult))(blk),
                          deps=[hb1, hx] + deps + ([hprev] if hprev else []))
                hb_ = p.op("vector", (lambda blk: tt(xlnc[:, blk], scrB[:, 0:512],
                                                     ps_b4[:, :], ALU.add))(blk),
                          deps=[ha, hb2])
                hprev = hb_
                happ.append(hb_)
            return happ, [h1, h3, h6, hb1, hb2]

        def rope_evac(dst_ap, tokslice, CX, SX, deps):
            hsw1 = p.op("vector", cp(scrC[0:64, 0:512], ps_m[64:128, :]), deps=deps)
            hsw2 = p.op("vector", cp(scrC[64:128, 0:512], ps_m[0:64, :]), deps=deps)
            ht1 = p.op("vector", (lambda: tt(scrB[:, 0:512], ps_m[:, :],
                                             CX[:, tokslice], ALU.mult))(),
                       deps=[hsw1, hsw2])
            ht2 = p.op("vector", (lambda: tt(scrC[:, 512:1024], scrC[:, 0:512],
                                             SX[:, tokslice], ALU.mult))(),
                       deps=[ht1])
            he = p.op("vector", tt(dst_ap, scrB[:, 0:512], scrC[:, 512:1024], ALU.add),
                      deps=[ht2])
            return he

        prev_tail = bar_w
        h_psm_free = bar_w
        h_psav_free = bar_w
        for ic8 in range(8):
            tok = slice(ic8 * 512, ic8 * 512 + 512)
            happ, hstats = ln_chunk(xr3, tok, [prev_tail] if not isinstance(prev_tail, list) else prev_tail)
            evacs = []
            for kc in range(2):
                hacc = None
                for c in range(8):
                    lhs = wkb[:, c * 256 + kc * 128: c * 256 + kc * 128 + 128]
                    rhs = xlnc[:, c * 512:(c + 1) * 512]
                    hacc = p.op("tensor", (lambda lhs, rhs, st, sp: mm(
                        ps_m[:, :], lhs, rhs, start=st, stop=sp))(
                            lhs, rhs, c == 0, c == 7),
                        deps=(happ + [h_psm_free, bar_w] if hacc is None
                              else [hacc]))
                dstc = kc * S + ic8 * 512
                he = rope_evac(kT[:, dstc:dstc + 512], tok, CCk, SSk, [hacc])
                h_psm_free = he
                evacs.append(he)
            for jq in range(4):
                jc = ic8 * 4 + jq
                hacc = None
                for c in range(8):
                    lhs = xlnc[:, c * 512 + jq * 128: c * 512 + jq * 128 + 128]
                    rhs = wvb[:, c * 256:(c + 1) * 256]
                    hacc = p.op("tensor", (lambda lhs, rhs, st, sp: mm(
                        ps_av[:, 0:256], lhs, rhs, start=st, stop=sp))(
                            lhs, rhs, c == 0, c == 7),
                        deps=(happ + [h_psav_free, bar_w] if hacc is None
                              else [hacc]))
                vdst = Vp[:, jc * 260: jc * 260 + 260].rearrange(
                    "p (h e) -> p h e", e=65)
                hvc = p.op("vector", (lambda vdst: cp(
                    vdst[:, :, 0:64],
                    ps_av[:, 0:256].rearrange("p (h e) -> p h e", e=64)))(vdst),
                    deps=[hacc])
                hvo = p.op("vector", (lambda vdst: lambda e: e.memset(
                    vdst[:, :, 64:65], 1.0))(vdst), deps=[hvc])
                h_psav_free = hvc
            prev_tail = happ + hstats + evacs + [h_psm_free, h_psav_free]

        for oc in range(4):
            otok = slice(oc * 512, oc * 512 + 512)
            happ, hstats = ln_chunk(xo3, otok, prev_tail)
            evacs = []
            for dqc in range(8):
                hacc = None
                for c in range(8):
                    lhs = wqb[:, c * 1024 + dqc * 128: c * 1024 + dqc * 128 + 128]
                    rhs = xlnc[:, c * 512:(c + 1) * 512]
                    hacc = p.op("tensor", (lambda lhs, rhs, st, sp: mm(
                        ps_m[:, :], lhs, rhs, start=st, stop=sp))(
                            lhs, rhs, c == 0, c == 7),
                        deps=(happ + [h_psm_free, bar_w] if hacc is None
                              else [hacc]))
                dstc = dqc * HALF + oc * 512
                he = rope_evac(qT[:, dstc:dstc + 512], otok, CCq, SSq, [hacc])
                h_psm_free = he
                evacs.append(he)
            prev_tail = happ + hstats + evacs + [h_psm_free]

        bar_proj = p.barrier()

        h_rep = bar_proj
        for (tile, width, nch) in ((qT, HALF, 8), (kT, S, 2)):
            for c in range(nch):
                sl = slice(c * width, (c + 1) * width)
                w_ = width
                h1 = p.op("vector", (lambda tile, sl, w_: cp(
                    sqb[0:32, 0:w_], tile[32:64, sl]))(tile, sl, w_), deps=h_rep)
                h2 = p.op("vector", (lambda tile, sl: cp(
                    tile[32:64, sl], tile[64:96, sl]))(tile, sl), deps=[h1])
                h3 = p.op("vector", (lambda tile, sl, w_: cp(
                    tile[64:96, sl], sqb[0:32, 0:w_]))(tile, sl, w_), deps=[h2])
                h_rep = [h3]
        bar_rep = p.barrier()

        PTb = [bt[0:128, o_sqb:o_sqb + 1024], bt[0:128, o_sqb + 1024:o_sqb + 2048]]
        ba.release()
        numT = ba.alloc(128, 8 * HALF)

        ps_pair = [PS[:, 0:1024], PS[:, 1024:2048]]
        ps_avA = PS[:, 2048:2560]
        ps_avB = PS[:, 2560:3072]
        ps_nrm = PS[:, 3072:3584]

        h_expbuf = [bar_rep, bar_rep]
        h_avbuf = [bar_rep, bar_rep]
        h_avA_free = [bar_rep]
        h_avB_free = [bar_rep]
        h_nrmA_prev = bar_rep
        h_nrmB_prev = bar_rep
        h_den_readers = [bar_rep]
        denrow = rowsc[0:1, 0:4096]
        attn_limit = 9999
        skip_norm = False
        skip_av = False
        skip_den = False
        if upto.startswith("attn") and len(upto) > 4:
            tail = upto[4:]
            while tail and tail[-1] in "nad":
                if tail[-1] == "n":
                    skip_norm = True
                elif tail[-1] == "a":
                    skip_av = True
                    skip_den = True
                    skip_norm = True
                elif tail[-1] == "d":
                    skip_den = True
                    skip_norm = True
                tail = tail[:-1]
            attn_limit = int(tail)
        attn_iter = 0
        for pr, (hA, hB) in enumerate(QPAIRS):
            qc = pr
            kvA, kvB = hA // 4, hB // 4
            kcA, kbA = kvA // 2, (kvA % 2) * 64
            kcB, kbB = kvB // 2, (kvB % 2) * 64
            assert kbA == 0 and kbB == 64
            pair_dens = []
            pair_copies = {}
            for ic in range(4):
                if attn_iter >= attn_limit:
                    continue
                attn_iter += 1
                isl = slice(qc * HALF + ic * 512, qc * HALF + ic * 512 + 512)
                h_avA = h_avB = None
                pend = None
                for jc in range(32):
                    buf = jc % 2

                    def emit_spair(buf, jc, kcA, kcB, isl):
                        def f(e):
                            e.matmul(ps_pair[buf][:, 0:512],
                                     kT[0:64, kcA * S + jc * 128: kcA * S + jc * 128 + 128],
                                     qT[0:64, isl], start=True, stop=True)
                            return e.matmul(
                                ps_pair[buf][:, 512:1024],
                                kT[64:128, kcB * S + jc * 128: kcB * S + jc * 128 + 128],
                                qT[64:128, isl], start=True, stop=True)
                        return f
                    hs = p.op("tensor", emit_spair(buf, jc, kcA, kcB, isl),
                              deps=[h_expbuf[buf], bar_rep], n=2)

                    if pend is not None and not skip_av:
                        pjc, pbuf, phex = pend

                        def emit_avs(pjc, pbuf, side):
                            kv = kvA if side == 0 else kvB
                            def f(e):
                                return e.matmul(
                                    (ps_avA if side == 0 else ps_avB)[0:65, :],
                                    Vp[:, pjc * 260 + kv * 65: pjc * 260 + kv * 65 + 65],
                                    PTb[pbuf][:, side * 512: side * 512 + 512],
                                    start=(pjc == 0), stop=(pjc == 31))
                            return f
                        h_avA = p.op("tensor", emit_avs(pjc, pbuf, 0),
                                     deps=([phex] + (h_avA_free if pjc == 0
                                                     else [h_avA])))
                        h_avB = p.op("tensor", emit_avs(pjc, pbuf, 1),
                                     deps=([phex] + (h_avB_free if pjc == 0
                                                     else [h_avB])))
                        h_avbuf[pbuf] = h_avB

                    hex_ = p.op("scalar", (lambda buf: act(
                        PTb[buf][:, 0:1024], ps_pair[buf][:, 0:1024],
                        AF.Exp, bias=onescol[:, 1:2], scale=0.125))(buf),
                        deps=[hs, h_avbuf[buf], h_negc])
                    h_expbuf[buf] = hex_
                    pend = (jc, buf, hex_)

                if pend is not None and not skip_av:
                    pjc, pbuf, phex = pend

                    def emit_avs(pjc, pbuf, side):
                        kv = kvA if side == 0 else kvB
                        def f(e):
                            return e.matmul(
                                (ps_avA if side == 0 else ps_avB)[0:65, :],
                                Vp[:, pjc * 260 + kv * 65: pjc * 260 + kv * 65 + 65],
                                PTb[pbuf][:, side * 512: side * 512 + 512],
                                start=(pjc == 0), stop=(pjc == 31))
                        return f
                    h_avA = p.op("tensor", emit_avs(pjc, pbuf, 0),
                                 deps=[phex, h_avA])
                    h_avB = p.op("tensor", emit_avs(pjc, pbuf, 1),
                                 deps=[phex, h_avB])
                    h_avbuf[pbuf] = h_avB
                if skip_av:
                    continue

                for side, (qb, ps_av_s, h_av_s) in enumerate(
                        ((0, ps_avA, h_avA), (64, ps_avB, h_avB))):
                    frees = []
                    if not skip_den:
                        hd1 = p.op("vector", (lambda side, ic, ps_av_s: cp(
                            denrow[0:1, side * 2048 + ic * 512:
                                   side * 2048 + ic * 512 + 512],
                            ps_av_s[64:65, :]))(side, ic, ps_av_s),
                            deps=[h_av_s] + h_den_readers)
                        pair_dens.append(hd1)
                        frees.append(hd1)
                    hcopy = p.op("vector", (lambda qb, ps_av_s: cp(
                        numT[qb:qb + 64, isl], ps_av_s[0:64, :]))(qb, ps_av_s),
                        deps=[h_av_s])
                    pair_copies[(side, ic)] = hcopy
                    frees.append(hcopy)
                    if side == 0:
                        h_avA_free = frees
                    else:
                        h_avB_free = frees
            if not (skip_av or skip_den or skip_norm) and pair_dens:
                hrec = p.op("vector",
                            lambda e: e.reciprocal(out=denrow, in_=denrow),
                            deps=pair_dens)
                readers = []
                for side, qb in ((0, 0), (1, 64)):
                    hn_prev = h_nrmA_prev if side == 0 else h_nrmB_prev
                    for ic in range(4):
                        if (side, ic) not in pair_copies:
                            continue
                        isl2 = slice(qc * HALF + ic * 512, qc * HALF + ic * 512 + 512)
                        rsl = slice(side * 2048 + ic * 512,
                                    side * 2048 + ic * 512 + 512)
                        hf = p.op("tensor", (lambda qb, rsl: mm(
                            ps_nrm[qb:qb + 64, :], onesfT[0:1, 0:64],
                            denrow[0:1, rsl]))(qb, rsl),
                            deps=[hrec, hn_prev])
                        hn = p.op("vector", (lambda qb, isl2: tt(
                            numT[qb:qb + 64, isl2], numT[qb:qb + 64, isl2],
                            ps_nrm[qb:qb + 64, :], ALU.mult))(qb, isl2),
                            deps=[hf, pair_copies.get((side, ic), bar_rep)])
                        readers.append(hf)
                        hn_prev = hn
                    if side == 0:
                        h_nrmA_prev = hn_prev
                    else:
                        h_nrmB_prev = hn_prev
                h_den_readers = readers
        bar_attn = p.barrier()

        wob = bt[0:128, o_qT:o_qT + 8 * 1024]
        dep = bar_attn
        h_wo = None
        for c in range(8):
            hd = p.dma("sync", (lambda c: lambda e: e.dma_start(
                out=scrB[:, 0:1024], in_=wo[c * 128:(c + 1) * 128, :]))(c), deps=dep)
            h_wo = p.op("vector", (lambda c: cp(wob[:, c * 1024:(c + 1) * 1024],
                                                scrB[:, 0:1024]))(c), deps=[hd])
            dep = [h_wo]
        x1r3 = x1T.rearrange("(c p) i -> p c i", p=128)
        scrB3 = scrB.rearrange("p (c i) -> p c i", i=512)
        hout = []
        hra = [h_wo]
        h_psm_free = bar_attn
        for ic in range(4):
            otok = slice(ic * 512, ic * 512 + 512)
            hx = p.dma("sync", (lambda otok: lambda e: e.dma_start(
                out=scrA3, in_=xo3[:, :, otok]))(otok), deps=hra)
            hnew = []
            for doc in range(8):
                hacc = None
                for c in range(8):
                    lhs = wob[:, c * 1024 + doc * 128: c * 1024 + doc * 128 + 128]
                    rhs = numT[:, c * HALF + ic * 512: c * HALF + ic * 512 + 512]
                    hacc = p.op("tensor", (lambda lhs, rhs, st, sp: mm(
                        ps_m[:, :], lhs, rhs, start=st, stop=sp))(
                            lhs, rhs, c == 0, c == 7),
                        deps=([h_wo, bar_attn, h_psm_free] if hacc is None else [hacc]))
                blk = slice(doc * 512, doc * 512 + 512)
                hr = p.op("vector", (lambda blk: tt(scrA[:, blk], ps_m[:, :],
                                                    scrA[:, blk], ALU.add))(blk),
                          deps=[hacc, hx])
                h_psm_free = hr
                hnew.append(hr)
            ho = p.dma("gpsimd", (lambda ic: lambda e: e.dma_start(
                out=x1r3[:, :, slice(ic * 512, ic * 512 + 512)], in_=scrA3))(ic),
                deps=hnew)
            hout.append(ho)
            hra = hnew + [ho]
        p.op("gpsimd", lambda e: e.nop(), deps=hout)

    return p.barrier()





def emit_B(ctx, upto="full"):
    nc, p, bt, ft, PS, t = ctx["nc"], ctx["p"], ctx["bt"], ctx["ft"], ctx["PS"], ctx["t"]
    x1T = t["x1T"]
    w1T = t["w1T"]
    w2T = t["w2T"]
    g1c = t["g1c"]
    b1lnc = t["b1lnc"]
    b1c = t["b1c"]
    g2c = t["g2c"]
    b2lnc = t["b2lnc"]
    b2c = t["b2c"]
    onesf = t["onesf"]
    i128 = t["i128"]
    x2T = t["x2T"]
    it = ctx["it"]
    if True:
        ba = Arena(bt, GNB16)
        fa = Arena(ft, GNF32)

        dqw1 = ba.alloc(128, 8 * 4096)
        h1q = ba.alloc(128, 32 * 512)
        o_h1q = ba.last_off
        xq = ba.alloc(128, 8 * 512)
        w2q = ba.alloc(128, 1024)
        onesb = ba.alloc(128, 32)
        bsc = bt[0:128, o_h1q:o_h1q + 4096]

        onesfT = fa.alloc(1, 128)
        i128T = fa.alloc(128, 128)
        fcol = fa.alloc(128, 32)
        cols = fa.alloc(128, 160)
        rowsc = fa.alloc(128, 4096)
        scrA = fa.alloc(128, 4096)
        scrT = fa.alloc(128, 4096)
        scrU = fa.alloc(128, 4096)
        scrV = fa.alloc(128, 512)
        scrW = fa.alloc(128, 512)
        scrD = fa.alloc(128, 512)
        scrM = fa.alloc(128, 512)
        invs1B = fa.alloc(128, 512)
        invs2B = fa.alloc(128, 512)

        PSB = [PS[:, k * 512:(k + 1) * 512] for k in range(8)]

        h_of = p.dma("sync", lambda e: e.dma_start(out=onesfT, in_=onesf))
        h_i128 = p.dma("sync", lambda e: e.dma_start(out=i128T, in_=i128))
        for (dst0, w_, srcw) in ((0, 8, g1c), (8, 8, b1lnc), (16, 32, b1c),
                                 (48, 32, g2c), (80, 32, b2lnc), (112, 8, b2c)):
            p.dma("sync", (lambda dst0, w_, srcw: lambda e: e.dma_start(
                out=cols[:, dst0:dst0 + w_], in_=srcw))(dst0, w_, srcw))
        h_ones = p.op("vector", lambda e: e.memset(onesb[:, 0:1], 1.0))
        h_o1 = p.op("vector", lambda e: e.memset(onesb[:, 1:2], 1.0 / 1024.0),
                    deps=[h_ones])
        h_o2 = p.op("vector", lambda e: e.memset(onesb[:, 2:3], 1.0 / 4096.0),
                    deps=[h_o1])
        h_fc = p.op("vector", lambda e: e.memset(fcol[:, 0:1], 1.0 / 1024.0))
        bar0 = p.barrier()

        def wmean(srcw, nchunks, ncols, out_idx, dep0):
            ncc = ncols // 512
            assert ncc <= 8
            hacc = [None] * ncc
            dep = list(dep0)
            habs = None
            for c in range(nchunks):
                hd = p.dma("sync", (lambda c: lambda e: e.dma_start(
                    out=scrU[:, 0:ncols], in_=srcw[c * 128:(c + 1) * 128, :]))(c),
                    deps=dep + ([habs] if habs else []))
                habs = p.op("scalar", (lambda ncols: act(bsc[:, 0:ncols],
                                                         scrU[:, 0:ncols], AF.Abs))(ncols),
                            deps=[hd])
                dep = []
                for cc in range(ncc):
                    hacc[cc] = p.op("tensor", (lambda cc, c: mm(
                        PSB[cc % 4][0:1, (cc // 4) * 512 if False else 0:512][0:1, 0:512]
                        if False else PSB[cc % 4][0:1, :], onesb[:, 0:1],
                        bsc[:, cc * 512:cc * 512 + 512],
                        start=(c == 0), stop=(c == nchunks - 1)))(cc, c),
                        deps=[habs, h_ones] + ([hacc[cc]] if hacc[cc] else []))
            hred = None
            for cc in range(ncc):
                h1 = p.op("vector", (lambda cc: cp(rowsc[0:1, 0:512],
                                                   PSB[cc % 4][0:1, :]))(cc),
                          deps=[hacc[cc]] + ([hred] if hred else []))
                h2 = p.op("vector", (lambda cc: lambda e: e.reduce_sum(
                    out=rowsc[0:1, 1024 + cc:1025 + cc], in_=rowsc[0:1, 0:512],
                    axis=mybir.AxisListType.X))(cc), deps=[h1])
                hred = h2
            h3 = p.op("vector", (lambda ncc, out_idx: lambda e: e.reduce_sum(
                out=rowsc[0:1, 1040 + out_idx:1041 + out_idx],
                in_=rowsc[0:1, 1024:1024 + ncc],
                axis=mybir.AxisListType.X))(ncc, out_idx), deps=[hred])
            return h3

        def wmean_cols(srcw, nchunks, ncols, out_idx, dep0):
            if ncols <= 2048:
                return wmean(srcw, nchunks, ncols, out_idx, dep0)
            h1 = wmean_sub(srcw, nchunks, ncols, 0, ncols // 2, 14, dep0)
            h2 = wmean_sub(srcw, nchunks, ncols, ncols // 2, ncols // 2, 15, [h1])
            h3 = p.op("vector", (lambda out_idx: lambda e: e.reduce_sum(
                out=rowsc[0:1, 1040 + out_idx:1041 + out_idx],
                in_=rowsc[0:1, 1038:1040],
                axis=mybir.AxisListType.X))(out_idx), deps=[h2])
            return h3

        def wmean_sub(srcw, nchunks, ncols, c0, cw, tmp_idx, dep0):
            ncc = cw // 512
            hacc = [None] * ncc
            dep = list(dep0)
            habs = None
            for c in range(nchunks):
                hd = p.dma("sync", (lambda c, c0, cw: lambda e: e.dma_start(
                    out=scrU[:, 0:cw], in_=srcw[c * 128:(c + 1) * 128, c0:c0 + cw]))(c, c0, cw),
                    deps=dep + ([habs] if habs else []))
                habs = p.op("scalar", (lambda cw: act(bsc[:, 0:cw],
                                                      scrU[:, 0:cw], AF.Abs))(cw),
                            deps=[hd])
                dep = []
                for cc in range(ncc):
                    hacc[cc] = p.op("tensor", (lambda cc, c: mm(
                        PSB[cc][0:1, :], onesb[:, 0:1],
                        bsc[:, cc * 512:cc * 512 + 512],
                        start=(c == 0), stop=(c == nchunks - 1)))(cc, c),
                        deps=[habs, h_ones] + ([hacc[cc]] if hacc[cc] else []))
            hred = None
            for cc in range(ncc):
                h1 = p.op("vector", (lambda cc: cp(rowsc[0:1, 0:512],
                                                   PSB[cc][0:1, :]))(cc),
                          deps=[hacc[cc]] + ([hred] if hred else []))
                h2 = p.op("vector", (lambda cc: lambda e: e.reduce_sum(
                    out=rowsc[0:1, 1024 + cc:1025 + cc], in_=rowsc[0:1, 0:512],
                    axis=mybir.AxisListType.X))(cc), deps=[h1])
                hred = h2
            h3 = p.op("vector", (lambda ncc, tmp_idx: lambda e: e.reduce_sum(
                out=rowsc[0:1, 1024 + tmp_idx:1025 + tmp_idx],
                in_=rowsc[0:1, 1024:1024 + ncc],
                axis=mybir.AxisListType.X))(ncc, tmp_idx), deps=[hred])
            return h3

        h_w1m = wmean_cols(w1T, 8, 4096, 0, bar0)
        h_w2m = wmean(w2T, 32, 1024, 1, [h_w1m])
        h5 = p.op("vector", ts(rowsc[0:1, 1044:1045], rowsc[0:1, 1040:1041],
                               1.0 / (1024 * 4096), ALU.mult, 1e-5, ALU.max),
                  deps=[h_w2m])
        h6 = p.op("vector", ts(rowsc[0:1, 1045:1046], rowsc[0:1, 1041:1042],
                               1.0 / (4096 * 1024), ALU.mult, 1e-5, ALU.max),
                  deps=[h5])
        hb = p.op("tensor", mm(PSB[7][:, 0:2], onesfT, rowsc[0:1, 1044:1046]),
                  deps=[h6, h_of])
        hc1 = p.op("vector", cp(cols[:, 124:126], PSB[7][:, 0:2]), deps=[hb])
        h_t1 = p.op("vector", ts(cols[:, 120:121], cols[:, 124:125], 0.5, ALU.mult),
                    deps=[hc1])
        h_t1n = p.op("vector", ts(cols[:, 121:122], cols[:, 120:121], -1.0, ALU.mult),
                     deps=[h_t1])
        h_t2 = p.op("vector", ts(cols[:, 122:123], cols[:, 125:126], 0.5, ALU.mult),
                    deps=[h_t1n])
        h_t2n = p.op("vector", ts(cols[:, 123:124], cols[:, 122:123], -1.0, ALU.mult),
                     deps=[h_t2])
        bar_mean = p.barrier()

        dep = bar_mean
        for c in range(8):
            hd = p.dma("sync", (lambda c: lambda e: e.dma_start(
                out=scrU, in_=w1T[c * 128:(c + 1) * 128, :]))(c), deps=dep)
            hp_ = p.op("vector", ts(scrT, scrU, cols[:, 120:121], ALU.is_gt),
                       deps=[hd, h_t1])
            hn_ = p.op("vector", ts(rowsc, scrU, cols[:, 121:122], ALU.is_lt),
                       deps=[hd, h_t1n])
            hq = p.op("vector", (lambda c: tt(dqw1[:, c * 4096:(c + 1) * 4096],
                                              scrT, rowsc, ALU.subtract))(c),
                      deps=[hp_, hn_])
            dep = [hq, hp_, hn_]
        bar_w1 = p.barrier()

        prev_stats = [bar_w1]

        def row_stats(hsum, hsqs):
            nonlocal_box = prev_stats
            rm = rowsc[0:1, 0:512]
            rv = rowsc[0:1, 512:1024]
            rr = rowsc[0:1, 2048:2560]
            rmb = rowsc[0:1, 2560:3072]
            h1 = p.op("vector", cp(rm, PSB[0][0:1, :]),
                      deps=[hsum] + list(nonlocal_box))
            h2 = p.op("scalar", act(rv, rm, AF.Square), deps=[h1])
            h3 = p.op("vector", stt(rv, rv, -1.0, PSB[1][0:1, :], ALU.mult, ALU.add),
                      deps=[h2, hsqs])
            h3b = p.op("vector", ts(rv, rv, EPS, ALU.add), deps=[h3])
            h4 = p.op("scalar", act(rr, rv, AF.Ln), deps=[h3b])
            h5_ = p.op("scalar", act(rr, rr, AF.Exp, scale=-0.5), deps=[h4])
            h6_ = p.op("vector", stt(rmb, rm, -1.0, rr, ALU.mult, ALU.mult), deps=[h5_])
            hb1 = p.op("tensor", mm(PSB[3][:, :], onesfT, rr), deps=[h5_, h_of])
            hb2 = p.op("tensor", mm(PSB[4][:, :], onesfT, rmb), deps=[h6_])
            nonlocal_box.clear()
            nonlocal_box.extend([hb1, hb2, h6_])
            return [hb1, hb2]

        def token_scale(mean_col, invs_dstB, habs_last):
            hc1 = p.op("vector", cp(scrV[0:64, :], scrM[64:128, :]), deps=[habs_last])
            hm1 = p.op("vector", tt(scrM[0:64, :], scrM[0:64, :], scrV[0:64, :],
                                    ALU.max), deps=[hc1])
            hc2 = p.op("vector", cp(scrV[0:32, :], scrM[32:64, :]), deps=[hm1])
            hm2 = p.op("vector", tt(scrM[0:32, :], scrM[0:32, :], scrV[0:32, :],
                                    ALU.max), deps=[hc2])
            htr = []
            for g in range(4):
                h1 = p.op("tensor", (lambda g: lambda e: e.transpose(
                    PSB[5][0:128, g * 32:(g + 1) * 32],
                    scrM[0:32, g * 128:(g + 1) * 128],
                    i128T[0:32, 0:32]))(g),
                    deps=[hm2, h_i128] + ([htr[-1]] if htr else []))
                h2 = p.op("vector", (lambda g: lambda e: e.reduce_max(
                    out=scrD[:, g:g + 1], in_=PSB[5][0:128, g * 32:(g + 1) * 32],
                    axis=mybir.AxisListType.X))(g), deps=[h1])
                htr.append(h2)
            hcl = p.op("vector", ts(scrD[:, 0:4], scrD[:, 0:4], 1e-5, ALU.max),
                       deps=htr)
            hs = p.op("vector", lambda e: e.reciprocal(out=scrD[:, 4:8],
                                                       in_=scrD[:, 0:4]), deps=[hcl])
            hs2 = p.op("vector", ts(scrD[:, 4:8], scrD[:, 4:8], 127.0, ALU.mult),
                       deps=[hs])
            hi = p.op("vector", ts(scrD[:, 8:12], scrD[:, 0:4],
                                   cols[:, mean_col:mean_col + 1],
                                   ALU.mult, 1.0 / 127.0, ALU.mult), deps=[hcl])
            hrow = []
            for gi, srcc in enumerate((4, 5, 6, 7, 8, 9, 10, 11)):
                dstc = gi * 128
                h1 = p.op("tensor", (lambda srcc: lambda e: e.transpose(
                    PSB[5][0:1, 0:128], scrD[:, srcc:srcc + 1], i128T))(srcc),
                    deps=[hs2, hi, h_i128] + ([hrow[-1]] if hrow else []))
                h2 = p.op("vector", (lambda dstc: cp(
                    rowsc[0:1, 1024 + dstc:1024 + dstc + 128],
                    PSB[5][0:1, 0:128]))(dstc), deps=[h1])
                hrow.append(h2)
            hbs = p.op("tensor", mm(PSB[2][:, :], onesfT, rowsc[0:1, 1024:1536]),
                       deps=[hrow[-1], h_of])
            hbi = p.op("tensor", mm(PSB[5][0:128, :], onesfT, rowsc[0:1, 1536:2048]),
                       deps=[hrow[-1], hbs])
            hbc = p.op("scalar", (lambda: act(invs_dstB, PSB[5][0:128, :],
                                              AF.Identity))(), deps=[hbi])
            return [hbs, hbc]

        def quant_wide(src_ap, dst_ap, w, deps):
            r = w // 512
            fsc = scrT[:, 0:w]
            f3 = fsc.rearrange("p (r t) -> p r t", t=512)
            s3 = src_ap.rearrange("p (r t) -> p r t", t=512)
            sc3 = PSB[2][:, :].unsqueeze(1).broadcast_to((128, r, 512))
            h1 = p.op("vector", (lambda f3, s3, sc3: lambda e: e.tensor_tensor(
                out=f3, in0=s3, in1=sc3, op=ALU.mult))(f3, s3, sc3), deps=deps)
            h2 = p.op("vector", ts(fsc, fsc, 127.4999, ALU.min,
                                   -128.4999, ALU.max), deps=[h1])
            h3 = p.op("vector", (lambda fsc, w: lambda e: e.tensor_scalar_add(
                out=it[:, 0:w], in0=fsc, scalar1=0.0))(fsc, w), deps=[h2])
            h4 = p.op("vector", (lambda fsc, w: cp(fsc, it[:, 0:w]))(fsc, w),
                      deps=[h3])
            h5_ = p.op("vector", cp(dst_ap, fsc), deps=[h4])
            return h5_

        x1r3 = x1T.rearrange("(c p) i -> p c i", p=128)
        x2r3 = x2T.rearrange("(c p) i -> p c i", p=128)
        scrA3 = scrA.rearrange("p (c i) -> p c i", i=512)

        h_prev = bar_w1
        for Q in range(4):
            qtok = slice(Q * 512, Q * 512 + 512)
            hx = p.dma("sync", (lambda qtok: lambda e: e.dma_start(
                out=scrA3, in_=x1r3[:, :, qtok]))(qtok), deps=h_prev)
            hsum = None
            for c in range(8):
                blk = slice(c * 512, c * 512 + 512)
                hsum = p.op("tensor", (lambda blk, st, sp: mm(
                    PSB[0][0:1, :], fcol[:, 0:1], scrA[:, blk],
                    start=st, stop=sp))(blk, c == 0, c == 7),
                    deps=[hx, h_fc] + ([hsum] if hsum else list(h_prev)))
            hsq = p.op("scalar", act(bsc, scrA, AF.Square), deps=[hx])
            hsqs = None
            for c in range(8):
                blk = slice(c * 512, c * 512 + 512)
                hsqs = p.op("tensor", (lambda blk, st, sp: mm(
                    PSB[1][0:1, :], onesb[:, 1:2], bsc[:, blk],
                    start=st, stop=sp))(blk, c == 0, c == 7),
                    deps=[hsq, h_o1] + ([hsqs] if hsqs else []))
            hstats = row_stats(hsum, hsqs)
            happ = []
            hpv = None
            for c in range(8):
                blk = slice(c * 512, c * 512 + 512)
                ha = p.op("vector", (lambda blk: tt(scrT[:, blk], scrA[:, blk],
                                                    PSB[3][:, :], ALU.mult))(blk),
                          deps=hstats + ([hpv] if hpv else []))
                hb_ = p.op("vector", (lambda blk: tt(scrT[:, blk], scrT[:, blk],
                                                     PSB[4][:, :], ALU.add))(blk),
                          deps=[ha])
                hpv = hb_
                happ.append(hb_)
            hsum = None
            for c in range(8):
                blk = slice(c * 512, c * 512 + 512)
                hsum = p.op("tensor", (lambda blk, st, sp: mm(
                    PSB[0][0:1, :], fcol[:, 0:1], scrT[:, blk],
                    start=st, stop=sp))(blk, c == 0, c == 7),
                    deps=[happ[c], hstats[0]] + ([hsum] if hsum else []))
            hsq = p.op("scalar", act(bsc, scrT, AF.Square), deps=happ)
            hsqs = None
            for c in range(8):
                blk = slice(c * 512, c * 512 + 512)
                hsqs = p.op("tensor", (lambda blk, st, sp: mm(
                    PSB[1][0:1, :], onesb[:, 1:2], bsc[:, blk],
                    start=st, stop=sp))(blk, c == 0, c == 7),
                    deps=[hsq] + ([hsqs] if hsqs else []))
            hstats = row_stats(hsum, hsqs)
            habs = None
            hq1dep = []
            for c in range(8):
                blk = slice(c * 512, c * 512 + 512)
                h1 = p.op("vector", (lambda blk: tt(scrU[:, blk], scrT[:, blk],
                                                    PSB[3][:, :], ALU.mult))(blk),
                          deps=hstats + hq1dep[-1:])
                h2 = p.op("vector", (lambda blk: tt(scrU[:, blk], scrU[:, blk],
                                                    PSB[4][:, :], ALU.add))(blk),
                          deps=[h1])
                h3 = p.op("vector", (lambda blk, c: ts(scrU[:, blk], scrU[:, blk],
                                                       cols[:, c:c + 1], ALU.mult,
                                                       cols[:, 8 + c:9 + c],
                                                       ALU.add))(blk, c),
                          deps=[h2])
                h4 = p.op("scalar", (lambda blk: act(scrV, scrU[:, blk], AF.Abs))(blk),
                          deps=[h3])
                if habs is None:
                    habs = p.op("vector", cp(scrM, scrV), deps=[h4])
                else:
                    habs = p.op("vector", tt(scrM, scrM, scrV, ALU.max),
                                deps=[h4, habs])
                hq1dep = [habs]
            hts = token_scale(124, invs1B, habs)
            hq = hts
            for c in range(2):
                blk = slice(c * 2048, c * 2048 + 2048)
                hq = [quant_wide(scrU[:, blk], xq[:, blk], 2048, hq + hts)]
            bar_q1 = p.barrier()
            h_ev = {6: bar_q1, 7: bar_q1}
            hgel = []
            for hc in range(32):
                bank = 6 + (hc % 2)
                hacc = None
                for c in range(8):
                    lhs = dqw1[:, c * 4096 + hc * 128: c * 4096 + hc * 128 + 128]
                    rhs = xq[:, c * 512:(c + 1) * 512]
                    hacc = p.op("tensor", (lambda lhs, rhs, bank, st, sp: mm(
                        PSB[bank][:, :], lhs, rhs, start=st, stop=sp))(
                            lhs, rhs, bank, c == 0, c == 7),
                        deps=([h_ev[bank], bar_q1] if hacc is None else [hacc]))
                he1 = p.op("vector", (lambda bank: tt(scrV, PSB[bank][:, :],
                                                      invs1B, ALU.mult))(bank),
                           deps=[hacc] + hgel[-1:])
                he2 = p.op("scalar", (lambda hc: act(
                    h1q[:, hc * 512:(hc + 1) * 512], scrV, AF.Gelu,
                    bias=cols[:, 16 + hc:17 + hc]))(hc), deps=[he1])
                h_ev[bank] = he1
                hgel.append(he2)
            hsum = None
            hsqs = None
            hsqp = None
            for hc in range(32):
                blk = slice(hc * 512, hc * 512 + 512)
                hsum = p.op("tensor", (lambda blk, st, sp: mm(
                    PSB[0][0:1, :], onesb[:, 2:3], h1q[:, blk],
                    start=st, stop=sp))(blk, hc == 0, hc == 31),
                    deps=[hgel[hc], h_o2] + ([hsum] if hsum else []))
                hsqp = p.op("scalar", (lambda blk: act(xq[:, 0:512], h1q[:, blk],
                                                       AF.Square))(blk),
                            deps=[hgel[hc]] + ([hsqs] if hsqs else []))
                hsqs = p.op("tensor", (lambda st, sp: mm(
                    PSB[1][0:1, :], onesb[:, 2:3], xq[:, 0:512],
                    start=st, stop=sp))(hc == 0, hc == 31),
                    deps=[hsqp])
            hstats = row_stats(hsum, hsqs)
            habs = None
            hq2dep = []
            for hc in range(32):
                blk = slice(hc * 512, hc * 512 + 512)
                h1 = p.op("vector", (lambda blk: tt(scrV, h1q[:, blk],
                                                    PSB[3][:, :], ALU.mult))(blk),
                          deps=hstats + hq2dep[-1:])
                h2 = p.op("vector", (lambda blk: tt(scrV, scrV,
                                                    PSB[4][:, :], ALU.add))(blk),
                          deps=[h1])
                h3 = p.op("vector", (lambda blk, hc: ts(scrV, scrV,
                                                        cols[:, 48 + hc:49 + hc],
                                                        ALU.mult,
                                                        cols[:, 80 + hc:81 + hc],
                                                        ALU.add))(blk, hc),
                          deps=[h2])
                h4 = p.op("scalar", act(scrW, scrV, AF.Abs), deps=[h3])
                if habs is None:
                    habs = p.op("vector", cp(scrM, scrW), deps=[h4])
                else:
                    habs = p.op("vector", tt(scrM, scrM, scrW, ALU.max),
                                deps=[h4, habs])
                h5_ = p.op("vector", (lambda blk: cp(h1q[:, blk], scrV))(blk),
                           deps=[h3, habs])
                hq2dep = [h5_, habs]
            hts2 = token_scale(125, invs2B, habs)
            hq = hts2 + hq2dep
            for hc4 in range(8):
                blk = slice(hc4 * 2048, hc4 * 2048 + 2048)
                hq = [quant_wide(h1q[:, blk], h1q[:, blk], 2048, hq + hts2)]
            bar_q2 = p.barrier()
            hw2 = bar_q2
            hmm2 = {d: None for d in range(8)}
            for hc in range(32):
                hd = p.dma("sync", (lambda hc: lambda e: e.dma_start(
                    out=scrU[:, 0:1024], in_=w2T[hc * 128:(hc + 1) * 128, :]))(hc),
                    deps=[hw2] if not isinstance(hw2, list) else hw2)
                hp_ = p.op("vector", ts(scrT[:, 0:1024], scrU[:, 0:1024],
                                        cols[:, 122:123], ALU.is_gt),
                           deps=[hd, h_t2])
                hn_ = p.op("vector", ts(scrT[:, 1024:2048], scrU[:, 0:1024],
                                        cols[:, 123:124], ALU.is_lt),
                           deps=[hd, h_t2n])
                hw2q = p.op("vector", tt(w2q, scrT[:, 0:1024], scrT[:, 1024:2048],
                                         ALU.subtract), deps=[hp_, hn_])
                hw2 = [hw2q]
                for dout in range(8):
                    lhs = w2q[:, dout * 128:(dout + 1) * 128]
                    rhs = h1q[:, hc * 512:(hc + 1) * 512]
                    hmm2[dout] = p.op("tensor", (lambda lhs, rhs, dout, st, sp: mm(
                        PSB[dout][:, :], lhs, rhs, start=st, stop=sp))(
                            lhs, rhs, dout, hc == 0, hc == 31),
                        deps=[hw2q, bar_q2] + ([hmm2[dout]] if hmm2[dout] else []))
                    hw2 = hw2 + [hmm2[dout]]
                hw2 = [hmm2[7]]
            hev = []
            for dout in range(8):
                blk = slice(dout * 512, dout * 512 + 512)
                he1 = p.op("vector", (lambda dout: tt(scrV, PSB[dout][:, :],
                                                      invs2B, ALU.mult))(dout),
                           deps=[hmm2[dout]] + hev[-1:])
                he2 = p.op("scalar", (lambda dout: act(scrW, scrV, AF.Identity,
                                                       bias=cols[:, 112 + dout:113 + dout]))(dout),
                           deps=[he1])
                he3 = p.op("vector", (lambda blk: tt(scrA[:, blk], scrW,
                                                     scrA[:, blk], ALU.add))(blk),
                           deps=[he2])
                hev.append(he3)
            ho = p.dma("gpsimd", (lambda Q: lambda e: e.dma_start(
                out=x2r3[:, :, slice(Q * 512, Q * 512 + 512)], in_=scrA3))(Q),
                deps=hev)
            h_prev = p.barrier()
    return p.barrier()



def make_B_inputs(inputs, x1):
    w1T = np.ascontiguousarray(inputs["L0_ffn_w1"].T.astype(np.float32))
    w2T = np.ascontiguousarray(inputs["L0_ffn_w2"].T.astype(np.float32))

    def colpack(v):
        v = np.asarray(v, dtype=np.float32)
        return np.ascontiguousarray(v.reshape(-1, 128).T)

    common = dict(
        w1T=w1T, w2T=w2T,
        g1c=colpack(inputs["L0_ffn_ln1_g"]), b1lnc=colpack(inputs["L0_ffn_ln1_b"]),
        b1c=colpack(inputs["L0_ffn_b1"]), g2c=colpack(inputs["L0_ffn_ln2_g"]),
        b2lnc=colpack(inputs["L0_ffn_ln2_b"]), b2c=colpack(inputs["L0_ffn_b2"]),
        onesf=np.ones((1, 128), dtype=np.float32),
        i128=np.eye(128, dtype=np.float32))
    in_maps = []
    for c in range(NCORES):
        b, half = c // 2, c % 2
        m = dict(common)
        m["x1T"] = np.ascontiguousarray(
            x1[b, half * HALF:(half + 1) * HALF].T.astype(np.float32))
        in_maps.append(m)
    return in_maps




def emit_C1(ctx, upto="full"):
    nc, p, bt, ft, PS, t = ctx["nc"], ctx["p"], ctx["bt"], ctx["ft"], ctx["PS"], ctx["t"]
    x2T = t["x2T"]
    x2oT = t["x2oT"]
    wqkvT = t["wqkvT"]
    onesf = t["onesf"]
    qd = t["qd"]
    kd = t["kd"]
    vd = t["vd"]
    koT = t["koT"]
    x2chunks = ctx.get("x2chunks")
    if True:
        ba = Arena(bt, GNB16)
        fa = Arena(ft, GNF32)

        wb = ba.alloc(128, 8 * 3072)
        xlnc = ba.alloc(128, 4096)
        sqb = ba.alloc(128, 4096)
        onesb = ba.alloc(128, 32)
        evb = ba.alloc(128, 4096)

        onesfT = fa.alloc(1, 128)
        fcol = fa.alloc(128, 32)
        rowsc = fa.alloc(128, 4096)
        scrA = fa.alloc(128, 4096)
        scrB = fa.alloc(128, 1024)

        PSB = [PS[:, k * 512:(k + 1) * 512] for k in range(8)]

        h_of = p.dma("sync", lambda e: e.dma_start(out=onesfT, in_=onesf))
        h_ones = p.op("vector", lambda e: e.memset(onesb[:, 0:1], 1.0))
        h_o1 = p.op("vector", lambda e: e.memset(onesb[:, 1:2], 1.0 / 1024.0),
                    deps=[h_ones])
        h_fc = p.op("vector", lambda e: e.memset(fcol[:, 0:1], 1.0 / 1024.0))

        dep = [h_of]
        h_w = None
        for c in range(8):
            hd = p.dma("sync", (lambda c: lambda e: e.dma_start(
                out=scrA[:, 0:3072], in_=wqkvT[c * 128:(c + 1) * 128, :]))(c),
                deps=dep)
            h_w = p.op("vector", (lambda c: cp(wb[:, c * 3072:(c + 1) * 3072],
                                               scrA[:, 0:3072]))(c), deps=[hd])
            dep = [h_w]
        bar_w = p.barrier()

        x2r3 = x2T.rearrange("(c p) i -> p c i", p=128)
        x2o3 = x2oT.rearrange("(c p) i -> p c i", p=128)
        scrA3 = scrA.rearrange("p (c i) -> p c i", i=512)
        qd3 = qd.rearrange("(c p) i -> p c i", p=128)
        kd3 = kd.rearrange("(c p) i -> p c i", p=128)
        ko3 = koT.rearrange("(c p) i -> p c i", p=128)

        def ln_chunk(src3, tok, deps):
            hx = p.dma("sync", (lambda tok: lambda e: e.dma_start(
                out=scrA3, in_=src3[:, :, tok]))(tok), deps=deps)
            hsq = p.op("scalar", act(sqb, scrA, AF.Square), deps=[hx])
            hsum = None
            for c in range(8):
                blk = slice(c * 512, c * 512 + 512)
                hsum = p.op("tensor", (lambda blk, st, sp: mm(
                    PSB[0][0:1, :], fcol[:, 0:1], scrA[:, blk],
                    start=st, stop=sp))(blk, c == 0, c == 7),
                    deps=[hx, h_fc] + ([hsum] if hsum else deps))
            hsqs = None
            for c in range(8):
                blk = slice(c * 512, c * 512 + 512)
                hsqs = p.op("tensor", (lambda blk, st, sp: mm(
                    PSB[1][0:1, :], onesb[:, 1:2], sqb[:, blk],
                    start=st, stop=sp))(blk, c == 0, c == 7),
                    deps=[hsq, h_o1] + ([hsqs] if hsqs else []))
            rm = rowsc[0:1, 0:512]
            rv = rowsc[0:1, 512:1024]
            rr = rowsc[0:1, 1024:1536]
            rmb = rowsc[0:1, 1536:2048]
            h1 = p.op("vector", cp(rm, PSB[0][0:1, :]), deps=[hsum] + deps)
            h2 = p.op("scalar", act(rv, rm, AF.Square), deps=[h1])
            h3 = p.op("vector", stt(rv, rv, -1.0, PSB[1][0:1, :], ALU.mult, ALU.add),
                      deps=[h2, hsqs])
            h3b = p.op("vector", ts(rv, rv, EPS, ALU.add), deps=[h3])
            h4 = p.op("scalar", act(rr, rv, AF.Ln), deps=[h3b])
            h5 = p.op("scalar", act(rr, rr, AF.Exp, scale=-0.5), deps=[h4])
            h6 = p.op("vector", stt(rmb, rm, -1.0, rr, ALU.mult, ALU.mult), deps=[h5])
            hb1 = p.op("tensor", mm(PSB[3][:, :], onesfT, rr), deps=[h5, h_of])
            hb2 = p.op("tensor", mm(PSB[4][:, :], onesfT, rmb), deps=[h6])
            happ = []
            hpv = None
            for c in range(8):
                blk = slice(c * 512, c * 512 + 512)
                ha = p.op("vector", (lambda blk: tt(scrB[:, 0:512], scrA[:, blk],
                                                    PSB[3][:, :], ALU.mult))(blk),
                          deps=[hb1, hx] + ([hpv] if hpv else []))
                hb_ = p.op("vector", (lambda blk: tt(xlnc[:, blk], scrB[:, 0:512],
                                                     PSB[4][:, :], ALU.add))(blk),
                          deps=[ha, hb2])
                hpv = hb_
                happ.append(hb_)
            return happ, [h1, h3, h6, hb1, hb2]

        def ln_chunk2(src_ap, deps):
            class _W:
                def __getitem__(self, idx):
                    return src_ap
            return ln_chunk(_W(), None, deps)

        h_prev = bar_w
        for ic8 in range(8):
            tok = slice(ic8 * 512, ic8 * 512 + 512)
            if x2chunks is not None:
                happ, hstats = ln_chunk2(x2chunks[ic8],
                                         h_prev if isinstance(h_prev, list) else [h_prev])
            else:
                happ, hstats = ln_chunk(x2r3, tok, h_prev if isinstance(h_prev, list) else [h_prev])
            hk = []
            hpse = [None] * 8
            for outc in range(8):
                hacc = None
                for c in range(8):
                    lhs = wb[:, c * 3072 + 1024 + outc * 128:
                             c * 3072 + 1024 + outc * 128 + 128]
                    rhs = xlnc[:, c * 512:(c + 1) * 512]
                    hacc = p.op("tensor", (lambda lhs, rhs, outc, st, sp: mm(
                        PSB[outc][:, :], lhs, rhs, start=st, stop=sp))(
                            lhs, rhs, outc, c == 0, c == 7),
                        deps=(happ + hstats + [bar_w] if hacc is None else [hacc]))
                he = p.op("vector", (lambda outc: cp(
                    evb[:, outc * 512:(outc + 1) * 512], PSB[outc][:, :]))(outc),
                    deps=[hacc])
                hk.append(he)
            hko = p.dma("gpsimd", (lambda tok: lambda e: e.dma_start(
                out=kd3[:, :, tok],
                in_=evb.rearrange("p (c i) -> p c i", i=512)))(tok), deps=hk)
            hv = []
            for jq in range(4):
                jc = ic8 * 4 + jq
                hacc1 = None
                hacc2 = None
                for c in range(8):
                    lhs = xlnc[:, c * 512 + jq * 128: c * 512 + jq * 128 + 128]
                    rhs1 = wb[:, c * 3072 + 2048: c * 3072 + 2048 + 512]
                    rhs2 = wb[:, c * 3072 + 2560: c * 3072 + 2560 + 512]
                    hacc1 = p.op("tensor", (lambda lhs, rhs1, st, sp: mm(
                        PSB[0][:, :], lhs, rhs1, start=st, stop=sp))(
                            lhs, rhs1, c == 0, c == 7),
                        deps=(happ + hk + hv if hacc1 is None else [hacc1]))
                    hacc2 = p.op("tensor", (lambda lhs, rhs2, st, sp: mm(
                        PSB[1][:, :], lhs, rhs2, start=st, stop=sp))(
                            lhs, rhs2, c == 0, c == 7),
                        deps=([hacc1] if c == 0 else [hacc2]))
                hev = p.op("vector", cp(evb[:, 0:512], PSB[0][:, :]),
                           deps=[hacc1, hacc2])
                hev2 = p.op("vector", cp(evb[:, 512:1024], PSB[1][:, :]),
                            deps=[hacc2, hev])
                hvo = p.dma("gpsimd", (lambda jc: lambda e: e.dma_start(
                    out=vd[jc * 128:(jc + 1) * 128, :], in_=evb[:, 0:1024]))(jc),
                    deps=[hev, hev2])
                hv = [hev, hev2, hvo]
            h_prev = happ + hstats + hk + hv + [hko]

        for oc in range(4):
            otok = slice(oc * 512, oc * 512 + 512)
            happ, hstats = ln_chunk(x2o3, otok, h_prev if isinstance(h_prev, list) else [h_prev])
            hk = []
            for outc in range(8):
                hacc = None
                for c in range(8):
                    lhs = wb[:, c * 3072 + outc * 128: c * 3072 + outc * 128 + 128]
                    rhs = xlnc[:, c * 512:(c + 1) * 512]
                    hacc = p.op("tensor", (lambda lhs, rhs, outc, st, sp: mm(
                        PSB[outc][:, :], lhs, rhs, start=st, stop=sp))(
                            lhs, rhs, outc, c == 0, c == 7),
                        deps=(happ + hstats + [bar_w] if hacc is None else [hacc]))
                he = p.op("vector", (lambda outc: cp(
                    evb[:, outc * 512:(outc + 1) * 512], PSB[outc][:, :]))(outc),
                    deps=[hacc])
                hk.append(he)
            hqo = p.dma("gpsimd", (lambda otok: lambda e: e.dma_start(
                out=qd3[:, :, otok],
                in_=evb.rearrange("p (c i) -> p c i", i=512)))(otok), deps=hk)
            hk2 = []
            for outc in range(8):
                hacc = None
                for c in range(8):
                    lhs = wb[:, c * 3072 + 1024 + outc * 128:
                             c * 3072 + 1024 + outc * 128 + 128]
                    rhs = xlnc[:, c * 512:(c + 1) * 512]
                    hacc = p.op("tensor", (lambda lhs, rhs, outc, st, sp: mm(
                        PSB[outc][:, :], lhs, rhs, start=st, stop=sp))(
                            lhs, rhs, outc, c == 0, c == 7),
                        deps=(happ + hstats + [bar_w, hqo] if hacc is None else [hacc]))
                he = p.op("vector", (lambda outc: cp(
                    evb[:, outc * 512:(outc + 1) * 512], PSB[outc][:, :]))(outc),
                    deps=[hacc, hqo])
                hk2.append(he)
            hko2 = p.dma("gpsimd", (lambda otok: lambda e: e.dma_start(
                out=ko3[:, :, otok],
                in_=evb.rearrange("p (c i) -> p c i", i=512)))(otok), deps=hk2)
            h_prev = happ + hstats + hk + hk2 + [hqo, hko2]

    return p.barrier()




def emit_C2(ctx, upto="full"):
    nc, p, bt, ft, PS, t = ctx["nc"], ctx["p"], ctx["bt"], ctx["ft"], ctx["PS"], ctx["t"]
    qdi = t["qd"]
    kdi = t["kd"]
    i128 = t["i128"]
    mrowP = t["mrowP"]
    if True:
        ba = Arena(bt, GNB16)
        fa = Arena(ft, GNF32)

        qT = ba.alloc(128, 8 * HALF)
        kT = ba.alloc(128, 8 * S)
        screxp = ba.alloc(128, 1024)

        i128f = fa.alloc(128, 128)
        dtile = fa.alloc(128, 128)
        dparts = fa.alloc(128, 128)
        denP = fa.alloc(128, 256)
        diagP = fa.alloc(128, 256)
        negc = fa.alloc(128, 32)

        h_i = p.dma("sync", lambda e: e.dma_start(out=i128f, in_=i128))
        h_negc = p.op("vector", lambda e: e.memset(negc[:, 0:1], -C_L1))
        qd3 = qdi.rearrange("(c p) i -> p c i", p=128)
        kd3 = kdi.rearrange("(c p) i -> p c i", p=128)
        h_lq = p.dma("sync", lambda e: e.dma_start(
            out=qT.rearrange("p (c i) -> p c i", i=HALF), in_=qd3))
        h_lk = p.dma("sync", lambda e: e.dma_start(
            out=kT.rearrange("p (c i) -> p c i", i=S), in_=kd3))
        bar_l = p.barrier()

        h_expfree = [bar_l, bar_l]
        h_dp_readers = [bar_l]
        for c in range(8):
            hA, hB = 2 * c, 2 * c + 1
            for ic in range(16):
                hexs = []
                h_dg = None
                for jp in range(4):
                    def emit_sp(c, ic, jp):
                        def f(e):
                            last = None
                            for jj in range(2):
                                j = jp * 2 + jj
                                for sb in (0, 64):
                                    last = e.matmul(
                                        PS[:, (sb // 64) * 2048 + (jp % 2) * 1024
                                           + jj * 512:
                                           (sb // 64) * 2048 + (jp % 2) * 1024
                                           + jj * 512 + 512],
                                        qT[sb:sb + 64,
                                           c * HALF + ic * 128: c * HALF + ic * 128 + 128],
                                        kT[sb:sb + 64, c * S + j * 512: c * S + j * 512 + 512],
                                        start=True, stop=True)
                            return last
                        return f
                    sdeps = [h_expfree[jp % 2], bar_l]
                    if h_dg is not None and jp == (ic // 8) + 2:
                        sdeps.append(h_dg)
                    hs = p.op("tensor", emit_sp(c, ic, jp), deps=sdeps, n=4)
                    hx = []
                    for side in range(2):
                        hx.append(p.op("scalar", (lambda side, jp, ic: act(
                            screxp, PS[:, side * 2048 + (jp % 2) * 1024:
                                       side * 2048 + (jp % 2) * 1024 + 1024],
                            AF.Exp, bias=negc[:, 0:1], scale=0.125,
                            accum=dparts[:, ic * 8 + side * 4 + jp:
                                         ic * 8 + side * 4 + jp + 1]))(side, jp, ic),
                            deps=[hs, h_negc] + h_dp_readers))
                    h_expfree[jp % 2] = hx
                    hexs.extend(hx)
                    if jp == ic // 8:
                        jd = ic // 4
                        off = (ic * 128) % 512
                        hdt = None
                        for side, hgl in ((0, hA), (1, hB)):
                            cb = side * 2048 + (jp % 2) * 1024 + (jd % 2) * 512 + off
                            ht1 = p.op("vector", (lambda cb: tt(
                                dtile, PS[:, cb:cb + 128], i128f, ALU.mult))(cb),
                                deps=[hs, h_i] + ([hdt] if hdt else []))
                            hdt = p.op("vector", (lambda hgl, ic: lambda e:
                                       e.reduce_sum(
                                           out=diagP[:, hgl * 16 + ic:
                                                     hgl * 16 + ic + 1],
                                           in_=dtile,
                                           axis=mybir.AxisListType.X))(hgl, ic),
                                       deps=[ht1])
                        h_dg = hdt
                hdrs = []
                for side, hgl in ((0, hA), (1, hB)):
                    hdrs.append(p.op("vector", (lambda hgl, ic, side: lambda e:
                                     e.reduce_sum(
                                         out=denP[:, hgl * 16 + ic:
                                                  hgl * 16 + ic + 1],
                                         in_=dparts[:, ic * 8 + side * 4:
                                                    ic * 8 + side * 4 + 4],
                                         axis=mybir.AxisListType.X))(hgl, ic, side),
                                     deps=hexs))
                h_dp_readers = hdrs
        bar_s = p.barrier()
        hr = p.op("vector", lambda e: e.reciprocal(out=denP, in_=denP),
                  deps=bar_s)
        he = p.op("scalar", act(diagP, diagP, AF.Exp, bias=negc[:, 0:1],
                                scale=0.125), deps=bar_s)
        hm = p.op("vector", tt(diagP, diagP, denP, ALU.mult), deps=[hr, he])
        p.dma("gpsimd", lambda e: e.dma_start(out=mrowP, in_=diagP),
              deps=[hm])
    return p.barrier()


def emit_C2_old(ctx, upto="full"):
    nc, p, bt, ft, PS, t = ctx["nc"], ctx["p"], ctx["bt"], ctx["ft"], ctx["PS"], ctx["t"]
    qdi = t["qd"]
    kdi = t["kd"]
    koi = t["koT"]
    onesf = t["onesf"]
    mrow = t["mrow"]
    if True:
        ba = Arena(bt, GNB16)
        fa = Arena(ft, GNF32)

        qT = ba.alloc(128, 8 * HALF)
        kT = ba.alloc(128, 8 * S)
        kos = ba.alloc(128, 2048)
        PTb = [ba.alloc(128, 1536), ba.alloc(128, 1536)]
        onesb = ba.alloc(128, 32)

        onesfT = fa.alloc(1, 128)
        negc = fa.alloc(128, 32)
        rowsc = fa.alloc(128, 4096)

        ps_s = [PS[:, 0:1536], PS[:, 1536:3072]]
        ps_r = PS[:, 3072:3584]
        ps_m = PS[:, 3584:4096]

        h_of = p.dma("sync", lambda e: e.dma_start(out=onesfT, in_=onesf))
        h_ones = p.op("vector", lambda e: e.memset(onesb[:, 0:1], 1.0))
        h_negc = p.op("vector", lambda e: e.memset(negc[:, 0:1], -C_L1))
        h_onef = p.op("vector", lambda e: e.memset(negc[:, 1:2], 1.0))

        qd3 = qdi.rearrange("(c p) i -> p c i", p=128)
        kd3 = kdi.rearrange("(c p) i -> p c i", p=128)
        h_lq = p.dma("sync", lambda e: e.dma_start(
            out=qT.rearrange("p (c i) -> p c i", i=HALF), in_=qd3))
        h_lk = p.dma("sync", lambda e: e.dma_start(
            out=kT.rearrange("p (c i) -> p c i", i=S), in_=kd3))
        bar_l = p.barrier()

        h_ptfree = [bar_l, bar_l]
        h_rs_free = bar_l
        h_kos = bar_l
        for h in range(16):
            hc_, hb_ = h // 2, (h % 2) * 64
            if hb_ == 0:
                h_kos = p.dma("sync", (lambda hc_: lambda e: e.dma_start(
                    out=kos, in_=koi[hc_ * 128:(hc_ + 1) * 128, :]))(hc_),
                    deps=[h_rs_free] if not isinstance(h_rs_free, list) else h_rs_free)
            hpr = p.op("vector", (lambda hc_, hb_: tt(
                rowsc[hb_:hb_ + 64, 0:2048],
                qT[hb_:hb_ + 64, hc_ * HALF:(hc_ + 1) * HALF],
                kos[hb_:hb_ + 64, :],
                ALU.mult))(hc_, hb_), deps=[bar_l, h_rs_free, h_kos])
            hdg = []
            for ic in range(4):
                hdg.append(p.op("tensor", (lambda hb_, ic: mm(
                    ps_m[0:1, 0:512], negc[hb_:hb_ + 64, 1:2],
                    rowsc[hb_:hb_ + 64, ic * 512:(ic + 1) * 512]))(hb_, ic),
                    deps=[hpr, h_onef] + ([hdg[-1]] if hdg else [])))
                hcpd = p.op("vector", (lambda ic: cp(
                    rowsc[0:1, 2048 + ic * 512: 2048 + ic * 512 + 512],
                    ps_m[0:1, 0:512]))(ic), deps=[hdg[-1]])
                hdg.append(hcpd)
            h_diag = hdg[-1]
            for ic in range(4):
                buf = 0
                h_sum = None
                for (j0, gn) in SGROUPS:
                    def emit_mm(buf, j0, gn, hc_, hb_, ic):
                        def f(e):
                            last = None
                            for t_ in range(gn):
                                jc = j0 + t_
                                lhs = kT[hb_:hb_ + 64,
                                         hc_ * S + jc * 128: hc_ * S + jc * 128 + 128]
                                rhs = qT[hb_:hb_ + 64,
                                         hc_ * HALF + ic * 512: hc_ * HALF + ic * 512 + 512]
                                last = e.matmul(ps_s[buf][:, t_ * 512:(t_ + 1) * 512],
                                                lhs, rhs, start=True, stop=True)
                            return last
                        return f
                    hmg = p.op("tensor", emit_mm(buf, j0, gn, hc_, hb_, ic),
                               deps=[h_ptfree[buf], bar_l], n=gn)
                    hex_ = p.op("scalar", (lambda buf, gn: act(
                        PTb[buf][:, 0:gn * 512], ps_s[buf][:, 0:gn * 512],
                        AF.Exp, bias=negc[:, 0:1], scale=0.125))(buf, gn),
                        deps=[hmg, h_negc])
                    def emit_sum(buf, j0, gn):
                        def f(e):
                            last = None
                            for t_ in range(gn):
                                jc = j0 + t_
                                last = e.matmul(ps_r[0:1, 0:512], onesb[:, 0:1],
                                                PTb[buf][:, t_ * 512:(t_ + 1) * 512],
                                                start=(jc == 0), stop=(jc == 31))
                            return last
                        return f
                    h_sum = p.op("tensor", emit_sum(buf, j0, gn),
                                 deps=[hex_, h_rs_free], n=gn)
                    h_ptfree[buf] = h_sum
                    buf ^= 1
                rsr = rowsc[0:1, 1024:1536]
                hcp = p.op("vector", cp(rsr, ps_r[0:1, 0:512]), deps=[h_sum])
                h_rs_free = hcp
                hl = p.op("scalar", act(rsr, rsr, AF.Ln), deps=[hcp])
                hr = p.op("scalar", act(rsr, rsr, AF.Exp, scale=-1.0), deps=[hl])
                hdexp = p.op("scalar", (lambda ic: act(
                    rowsc[0:1, 1536:2048], rowsc[0:1, 2048 + ic * 512: 2560 + ic * 512],
                    AF.Exp, bias=negc[0:1, 0:1], scale=0.125))(ic),
                    deps=[h_diag, hr])
                hm = p.op("vector", (lambda h, ic: tt(
                    rowsc[0:1, 1536:2048],
                    rowsc[0:1, 1536:2048], rsr, ALU.mult))(h, ic), deps=[hdexp])
                hout = p.dma("gpsimd", (lambda h, ic: lambda e: e.dma_start(
                    out=mrow[h:h + 1, ic * 512:(ic + 1) * 512],
                    in_=rowsc[0:1, 1536:2048]))(h, ic), deps=[hm])
                h_rs_free = [hcp, hout]
    return p.barrier()




def emit_D(ctx, upto="full"):
    nc, p, bt, ft, PS, t = ctx["nc"], ctx["p"], ctx["bt"], ctx["ft"], ctx["PS"], ctx["t"]
    x2oT = t["x2oT"]
    qdi = t["qd"]
    kdi = t["kd"]
    vdi = t["vd"]
    diagm = t["diagm"]
    diagmo = t["diagmo"]
    woutT = t["woutT"]
    boutc = t["boutc"]
    e65g = t["e65g"]
    i16b = t["i16b"]
    onesf = t["onesf"]
    i128 = t["i128"]
    x3T = t["x3T"]
    x3lnT = t["x3lnT"]
    amax = t["amax"]
    diag2 = ctx.get("diag2")
    if True:
        ba = Arena(bt, GNB16)
        fa = Arena(ft, GNF32)

        kg = ba.alloc(128, 2 * S)
        o_kg = ba.last_off
        qg = ba.alloc(128, 2 * HALF)
        Vg = ba.alloc(128, 32 * 260)
        numT = ba.alloc(128, 8 * HALF)
        PTb = [ba.alloc(128, 1536), ba.alloc(128, 1536)]
        keepb = ba.alloc(16, 4096)
        keepbo = ba.alloc(16, 2048)
        e65gT = ba.alloc(16, 1040)
        i16bT = ba.alloc(16, 16)
        onesb = ba.alloc(128, 32)

        onesfT = fa.alloc(1, 128)
        negc = fa.alloc(128, 32)
        boutT = fa.alloc(128, 8)
        fcol = fa.alloc(128, 32)
        i128T2 = fa.alloc(128, 128)
        scrC = fa.alloc(128, 4096)
        dg = fa.alloc(16, 4096)
        dg2 = fa.alloc(16, 4096)
        rowsc = fa.alloc(128, 4096)
        scrA = fa.alloc(128, 4096)
        scrB = fa.alloc(128, 1024)
        scrM = fa.alloc(128, 512)

        ps_s = [PS[:, 0:1536], PS[:, 1536:3072]]
        ps_av = PS[:, 3072:3584]
        ps_m = PS[:, 3584:4096]
        PSB = [PS[:, k * 512:(k + 1) * 512] for k in range(8)]

        h_of = p.dma("sync", lambda e: e.dma_start(out=onesfT, in_=onesf))
        h_e65 = p.dma("sync", lambda e: e.dma_start(out=e65gT, in_=e65g))
        h_i16 = p.dma("sync", lambda e: e.dma_start(out=i16bT, in_=i16b))
        h_bout = p.dma("sync", lambda e: e.dma_start(out=boutT, in_=boutc))
        if diag2 is None:
            h_dg = p.dma("sync", lambda e: e.dma_start(out=dg, in_=diagm))
        else:
            h_dga = p.dma("sync", lambda e: e.dma_start(out=dg[0:16, 0:2048],
                                                        in_=diag2[0]))
            h_dg = p.dma("sync", lambda e: e.dma_start(out=dg[0:16, 2048:4096],
                                                       in_=diag2[1]), deps=[h_dga])
        h_dgo = p.dma("sync", lambda e: e.dma_start(out=dg2[0:16, 0:2048], in_=diagmo))
        h_i1282 = p.dma("sync", lambda e: e.dma_start(out=i128T2, in_=i128))
        h_ones = p.op("vector", lambda e: e.memset(onesb[:, 0:1], 1.0))
        h_negc = p.op("vector", lambda e: e.memset(negc[:, 0:1], -C_L1))

        r0 = rowsc[0:16, 0:32]
        h1 = p.op("vector", lambda e: e.reduce_sum(out=r0[:, 0:1], in_=dg,
                                                   axis=mybir.AxisListType.X),
                  deps=[h_dg])
        h2 = p.op("scalar", act(scrA[0:16, 0:4096], dg, AF.Square), deps=[h_dg])
        h3 = p.op("vector", lambda e: e.reduce_sum(out=r0[:, 1:2],
                                                   in_=scrA[0:16, 0:4096],
                                                   axis=mybir.AxisListType.X),
                  deps=[h2])
        h4 = p.op("vector", ts(r0[:, 2:3], r0[:, 0:1], 1.0 / 4096.0, ALU.mult),
                  deps=[h1])
        h5 = p.op("scalar", act(r0[:, 3:4], r0[:, 2:3], AF.Square), deps=[h4])
        h6 = p.op("vector", ts(r0[:, 4:5], r0[:, 1:2], 1.0 / 4095.0, ALU.mult),
                  deps=[h3])
        h7 = p.op("vector", stt(r0[:, 5:6], r0[:, 3:4], -4096.0 / 4095.0,
                                r0[:, 4:5], ALU.mult, ALU.add), deps=[h5, h6])
        h8 = p.op("scalar", act(r0[:, 6:7], r0[:, 5:6], AF.Ln), deps=[h7])
        h9 = p.op("scalar", act(r0[:, 6:7], r0[:, 6:7], AF.Exp, scale=-0.5), deps=[h8])
        h10 = p.op("vector", ts(dg, dg, r0[:, 2:3], ALU.subtract), deps=[h9, h2])
        h11 = p.op("vector", ts(dg, dg, r0[:, 6:7], ALU.mult), deps=[h10])
        h12 = p.op("scalar", act(dg, dg, AF.Square), deps=[h11])
        h13 = p.op("vector", ts(dg, dg, 2.25, ALU.is_le), deps=[h12])
        h14 = p.op("vector", cp(keepb, dg), deps=[h13])
        h15 = p.op("vector", ts(dg2[0:16, 0:2048], dg2[0:16, 0:2048], r0[:, 2:3],
                                ALU.subtract), deps=[h_dgo, h9])
        h16 = p.op("vector", ts(dg2[0:16, 0:2048], dg2[0:16, 0:2048], r0[:, 6:7],
                                ALU.mult), deps=[h15])
        h17 = p.op("scalar", act(dg2[0:16, 0:2048], dg2[0:16, 0:2048], AF.Square),
                   deps=[h16])
        h18 = p.op("vector", ts(dg2[0:16, 0:2048], dg2[0:16, 0:2048], 2.25,
                                ALU.is_le), deps=[h17])
        h19 = p.op("vector", cp(keepbo, dg2[0:16, 0:2048]), deps=[h18])
        bar_keep = p.barrier()

        qd3 = qdi.rearrange("(c p) i -> p c i", p=128)
        kd3 = kdi.rearrange("(c p) i -> p c i", p=128)
        vd4 = vdi.rearrange("j (h e) -> j h e", e=64)

        ps_pair = [PS[:, 0:1024], PS[:, 1024:2048]]
        ps_avA = PS[:, 2048:2560]
        ps_avB = PS[:, 2560:3072]
        ps_nrm = PS[:, 3072:3584]
        ps_kx = PS[:, 3584:4096]
        PTb2 = [PTb[0][0:128, 0:1024], PTb[1][0:128, 0:1024]]
        denrow = rowsc[0:1, 0:4096]
        krow = dg[0:1, 0:4096]
        h_expbuf = [bar_keep, bar_keep]
        h_avbuf = [bar_keep, bar_keep]
        h_avA_free = [bar_keep]
        h_avB_free = [bar_keep]
        h_nrmA_prev = bar_keep
        h_nrmB_prev = bar_keep
        h_den_readers = [bar_keep]
        h_krow_readers = [bar_keep]
        prev_group = bar_keep
        for g in range(4):
            hkl = []
            for cch in range(2):
                hkl.append(p.dma("sync", (lambda g, cch: lambda e: e.dma_start(
                    out=kg[:, cch * S:(cch + 1) * S],
                    in_=kdi[g * 256 + cch * 128: g * 256 + (cch + 1) * 128, :]))(g, cch),
                    deps=prev_group))
                hkl.append(p.dma("sync", (lambda g, cch: lambda e: e.dma_start(
                    out=qg[:, cch * HALF:(cch + 1) * HALF],
                    in_=qdi[g * 256 + cch * 128: g * 256 + (cch + 1) * 128, :]))(g, cch),
                    deps=prev_group))
            hvm = []
            for jc in range(32):
                vg3 = Vg[:, jc * 260:(jc + 1) * 260].rearrange("p (h e) -> p h e", e=65)
                hvd = p.dma("sync", (lambda jc, g, vg3: lambda e: e.dma_start(
                    out=vg3[:, :, 0:64],
                    in_=vd4[jc * 128:(jc + 1) * 128, 4 * g:4 * g + 4, :]))(jc, g, vg3),
                    deps=prev_group)
                hvo = p.op("vector", (lambda vg3: lambda e: e.memset(
                    vg3[:, :, 64:65], 1.0))(vg3), deps=[hvd] + (hvm[-1:] if hvm else []))
                hke = p.op("tensor", (lambda jc, g: lambda e: e.matmul(
                    PS[:, 2048:2048 + 260][0:128, :] if False else PS[0:128, 2048:2308],
                    keepb[:, jc * 128:(jc + 1) * 128],
                    e65gT[:, g * 260:(g + 1) * 260], start=True, stop=True))(jc, g),
                    deps=[bar_keep, h_e65] + (hvm[-1:] if hvm else []))
                hmask = p.op("vector", (lambda jc: tt(
                    Vg[:, jc * 260:(jc + 1) * 260],
                    Vg[:, jc * 260:(jc + 1) * 260],
                    PS[0:128, 2048:2308], ALU.mult))(jc), deps=[hvo, hke])
                hvm.append(hmask)
            bar_g = p.barrier()

            for pp in range(2):
                cch = pp
                hhA, hhB = 2 * pp, 2 * pp + 1
                hA_, hB_ = 4 * g + hhA, 4 * g + hhB
                qc2 = 2 * g + pp
                hkextr = []
                for side, hgl in ((0, hA_), (1, hB_)):
                    for ic in range(4):
                        hkr = p.op("tensor", (lambda hgl, ic: mm(
                            ps_kx[0:1, 0:512], i16bT[:, hgl:hgl + 1],
                            keepbo[:, ic * 512:(ic + 1) * 512]))(hgl, ic),
                            deps=[h19, h_i16, bar_g] + h_krow_readers
                                 + hkextr[-1:])
                        hkc = p.op("vector", (lambda side, ic: cp(
                            krow[0:1, side * 2048 + ic * 512:
                                 side * 2048 + ic * 512 + 512],
                            ps_kx[0:1, 0:512]))(side, ic), deps=[hkr])
                        hkextr.append(hkc)
                pair_dens = []
                pair_copies = {}
                for ic in range(4):
                    isl = slice(cch * HALF + ic * 512, cch * HALF + ic * 512 + 512)
                    nsl = slice(qc2 * HALF + ic * 512, qc2 * HALF + ic * 512 + 512)
                    h_avA = h_avB = None
                    pend = None
                    for jc in range(32):
                        buf = jc % 2

                        def emit_spair(buf, jc, cch, isl):
                            def f(e):
                                e.matmul(ps_pair[buf][:, 0:512],
                                         kg[0:64, cch * S + jc * 128:
                                            cch * S + jc * 128 + 128],
                                         qg[0:64, isl], start=True, stop=True)
                                return e.matmul(
                                    ps_pair[buf][:, 512:1024],
                                    kg[64:128, cch * S + jc * 128:
                                       cch * S + jc * 128 + 128],
                                    qg[64:128, isl], start=True, stop=True)
                            return f
                        hs = p.op("tensor", emit_spair(buf, jc, cch, isl),
                                  deps=[h_expbuf[buf], bar_g], n=2)

                        if pend is not None:
                            pjc, pbuf, phex = pend

                            def emit_avs(pjc, pbuf, side, hh_s):
                                def f(e):
                                    return e.matmul(
                                        (ps_avA if side == 0 else ps_avB)[0:65, :],
                                        Vg[:, pjc * 260 + hh_s * 65:
                                           pjc * 260 + hh_s * 65 + 65],
                                        PTb2[pbuf][:, side * 512: side * 512 + 512],
                                        start=(pjc == 0), stop=(pjc == 31))
                                return f
                            h_avA = p.op("tensor", emit_avs(pjc, pbuf, 0, hhA),
                                         deps=([phex] + (h_avA_free if pjc == 0
                                                         else [h_avA])))
                            h_avB = p.op("tensor", emit_avs(pjc, pbuf, 1, hhB),
                                         deps=([phex] + (h_avB_free if pjc == 0
                                                         else [h_avB])))
                            h_avbuf[pbuf] = h_avB

                        hex_ = p.op("scalar", (lambda buf: act(
                            PTb2[buf][:, 0:1024], ps_pair[buf][:, 0:1024],
                            AF.Exp, bias=negc[:, 0:1], scale=0.125))(buf),
                            deps=[hs, h_avbuf[buf], h_negc])
                        h_expbuf[buf] = hex_
                        pend = (jc, buf, hex_)

                    pjc, pbuf, phex = pend

                    def emit_avs(pjc, pbuf, side, hh_s):
                        def f(e):
                            return e.matmul(
                                (ps_avA if side == 0 else ps_avB)[0:65, :],
                                Vg[:, pjc * 260 + hh_s * 65:
                                   pjc * 260 + hh_s * 65 + 65],
                                PTb2[pbuf][:, side * 512: side * 512 + 512],
                                start=(pjc == 0), stop=(pjc == 31))
                        return f
                    h_avA = p.op("tensor", emit_avs(pjc, pbuf, 0, hhA),
                                 deps=[phex, h_avA])
                    h_avB = p.op("tensor", emit_avs(pjc, pbuf, 1, hhB),
                                 deps=[phex, h_avB])
                    h_avbuf[pbuf] = h_avB

                    for side, (qb2, ps_av_s, h_av_s) in enumerate(
                            ((0, ps_avA, h_avA), (64, ps_avB, h_avB))):
                        hd1 = p.op("vector", (lambda side, ic, ps_av_s: cp(
                            denrow[0:1, side * 2048 + ic * 512:
                                   side * 2048 + ic * 512 + 512],
                            ps_av_s[64:65, :]))(side, ic, ps_av_s),
                            deps=[h_av_s] + h_den_readers)
                        pair_dens.append(hd1)
                        hcopy = p.op("vector", (lambda qb2, nsl, ps_av_s: cp(
                            numT[qb2:qb2 + 64, nsl], ps_av_s[0:64, :]))(
                                qb2, nsl, ps_av_s),
                            deps=[h_av_s])
                        pair_copies[(side, ic)] = hcopy
                        if side == 0:
                            h_avA_free = [hd1, hcopy]
                        else:
                            h_avB_free = [hd1, hcopy]
                hmx = p.op("vector", ts(denrow, denrow, 1e-30, ALU.max),
                           deps=pair_dens)
                hrec = p.op("vector",
                            lambda e: e.reciprocal(out=denrow, in_=denrow),
                            deps=[hmx])
                hkm = p.op("vector", tt(denrow, denrow, krow, ALU.mult),
                           deps=[hrec] + hkextr)
                h_krow_readers = [hkm]
                readers = []
                for side, qb2 in ((0, 0), (1, 64)):
                    hn_prev = h_nrmA_prev if side == 0 else h_nrmB_prev
                    for ic in range(4):
                        nsl = slice(qc2 * HALF + ic * 512,
                                    qc2 * HALF + ic * 512 + 512)
                        rsl = slice(side * 2048 + ic * 512,
                                    side * 2048 + ic * 512 + 512)
                        hf = p.op("tensor", (lambda qb2, rsl: mm(
                            ps_nrm[qb2:qb2 + 64, :], onesfT[0:1, 0:64],
                            denrow[0:1, rsl]))(qb2, rsl),
                            deps=[hkm, hn_prev, h_of])
                        hn = p.op("vector", (lambda qb2, nsl: tt(
                            numT[qb2:qb2 + 64, nsl], numT[qb2:qb2 + 64, nsl],
                            ps_nrm[qb2:qb2 + 64, :], ALU.mult))(qb2, nsl),
                            deps=[hf, pair_copies[(side, ic)]])
                        readers.append(hf)
                        hn_prev = hn
                    if side == 0:
                        h_nrmA_prev = hn_prev
                    else:
                        h_nrmB_prev = hn_prev
                h_den_readers = readers
            prev_group = p.barrier()
        bar_attn = prev_group

        wob = bt[0:128, o_kg:o_kg + 8 * 1024]
        sqb = bt[0:128, o_kg + 8192:o_kg + 8192 + 4096]
        dep = bar_attn
        h_wo = None
        for c in range(8):
            hd = p.dma("sync", (lambda c: lambda e: e.dma_start(
                out=scrB[:, 0:1024], in_=woutT[c * 128:(c + 1) * 128, :]))(c), deps=dep)
            h_wo = p.op("vector", (lambda c: cp(wob[:, c * 1024:(c + 1) * 1024],
                                                scrB[:, 0:1024]))(c), deps=[hd])
            dep = [h_wo]
        h_fc = p.op("vector", lambda e: e.memset(fcol[:, 0:1], 1.0 / 1024.0))
        h_o1 = p.op("vector", lambda e: e.memset(onesb[:, 1:2], 1.0 / 1024.0),
                    deps=[h_ones])
        x2o3 = x2oT.rearrange("(c p) i -> p c i", p=128)
        x3r3 = x3T.rearrange("(c p) i -> p c i", p=128)
        x3l3 = x3lnT.rearrange("(c p) i -> p c i", p=128)
        scrA3 = scrA.rearrange("p (c i) -> p c i", i=512)
        scrC3 = scrC.rearrange("p (c i) -> p c i", i=512)
        hout = []
        hra = [h_wo]
        h_psm_free = bar_attn
        h_amax = None
        for ic in range(4):
            otok = slice(ic * 512, ic * 512 + 512)
            hx = p.dma("sync", (lambda otok: lambda e: e.dma_start(
                out=scrA3, in_=x2o3[:, :, otok]))(otok), deps=hra)
            hnew = []
            for doc in range(8):
                hacc = None
                for c in range(8):
                    lhs = wob[:, c * 1024 + doc * 128: c * 1024 + doc * 128 + 128]
                    rhs = numT[:, c * HALF + ic * 512: c * HALF + ic * 512 + 512]
                    hacc = p.op("tensor", (lambda lhs, rhs, st, sp: mm(
                        ps_m[:, :], lhs, rhs, start=st, stop=sp))(
                            lhs, rhs, c == 0, c == 7),
                        deps=([h_wo, bar_attn, h_psm_free] if hacc is None else [hacc]))
                blk = slice(doc * 512, doc * 512 + 512)
                hb1_ = p.op("scalar", (lambda doc: act(scrB[:, 0:512], ps_m[:, :],
                                                       AF.Identity,
                                                       bias=boutT[:, doc:doc + 1]))(doc),
                            deps=[hacc, h_bout] + hnew[-1:])
                hr = p.op("vector", (lambda blk: tt(scrA[:, blk], scrB[:, 0:512],
                                                    scrA[:, blk], ALU.add))(blk),
                          deps=[hb1_, hx])
                h_psm_free = hr
                hnew.append(hr)
            ho = p.dma("gpsimd", (lambda ic: lambda e: e.dma_start(
                out=x3r3[:, :, slice(ic * 512, ic * 512 + 512)], in_=scrA3))(ic),
                deps=hnew)
            hsq = p.op("scalar", act(sqb, scrA, AF.Square), deps=hnew)
            hsum = None
            for c in range(8):
                blk = slice(c * 512, c * 512 + 512)
                hsum = p.op("tensor", (lambda blk, st, sp: mm(
                    PSB[0][0:1, :], fcol[:, 0:1], scrA[:, blk],
                    start=st, stop=sp))(blk, c == 0, c == 7),
                    deps=[h_fc, h_psm_free] + hnew + ([hsum] if hsum else []))
            hsqs = None
            for c in range(8):
                blk = slice(c * 512, c * 512 + 512)
                hsqs = p.op("tensor", (lambda blk, st, sp: mm(
                    PSB[1][0:1, :], onesb[:, 1:2], sqb[:, blk],
                    start=st, stop=sp))(blk, c == 0, c == 7),
                    deps=[hsq, h_o1] + ([hsqs] if hsqs else []))
            rm = rowsc[0:1, 0:512]
            rv = rowsc[0:1, 512:1024]
            rr = rowsc[0:1, 1024:1536]
            rmb = rowsc[0:1, 1536:2048]
            s1 = p.op("vector", cp(rm, PSB[0][0:1, :]), deps=[hsum] + hout[-1:])
            s2 = p.op("scalar", act(rv, rm, AF.Square), deps=[s1])
            s3 = p.op("vector", stt(rv, rv, -1.0, PSB[1][0:1, :], ALU.mult, ALU.add),
                      deps=[s2, hsqs])
            s3b = p.op("vector", ts(rv, rv, EPS, ALU.add), deps=[s3])
            s4 = p.op("scalar", act(rr, rv, AF.Ln), deps=[s3b])
            s5 = p.op("scalar", act(rr, rr, AF.Exp, scale=-0.5), deps=[s4])
            s6 = p.op("vector", stt(rmb, rm, -1.0, rr, ALU.mult, ALU.mult), deps=[s5])
            sb1 = p.op("tensor", mm(PSB[3][:, :], onesfT, rr), deps=[s5, h_of])
            sb2 = p.op("tensor", mm(PSB[4][:, :], onesfT, rmb), deps=[s6])
            hln = []
            hpv = None
            for c in range(8):
                blk = slice(c * 512, c * 512 + 512)
                la = p.op("vector", (lambda blk: tt(scrB[:, 0:512], scrA[:, blk],
                                                    PSB[3][:, :], ALU.mult))(blk),
                          deps=[sb1, ho] + ([hpv] if hpv else []))
                lb = p.op("vector", (lambda blk: tt(scrC[:, blk], scrB[:, 0:512],
                                                    PSB[4][:, :], ALU.add))(blk),
                          deps=[la, sb2])
                lc = p.op("scalar", (lambda blk: act(scrB[:, 512:1024],
                                                     scrC[:, blk], AF.Abs))(blk),
                          deps=[lb])
                if h_amax is None:
                    h_amax = p.op("vector", cp(scrM, scrB[:, 512:1024]), deps=[lc])
                else:
                    h_amax = p.op("vector", tt(scrM, scrM, scrB[:, 512:1024],
                                               ALU.max), deps=[lc, h_amax])
                hpv = h_amax
                hln.append(lb)
            ho2 = p.dma("gpsimd", (lambda ic: lambda e: e.dma_start(
                out=x3l3[:, :, slice(ic * 512, ic * 512 + 512)], in_=scrC3))(ic),
                deps=hln + [h_amax])
            hout = [ho, ho2]
            hra = hln + [ho, ho2, h_amax]
        a1 = p.op("vector", lambda e: e.reduce_max(out=scrM[:, 0:1], in_=scrM,
                                                   axis=mybir.AxisListType.X),
                  deps=[h_amax])
        a2 = p.op("vector", cp(scrB[0:64, 0:1], scrM[64:128, 0:1]), deps=[a1])
        a3 = p.op("vector", tt(scrM[0:64, 0:1], scrM[0:64, 0:1], scrB[0:64, 0:1],
                               ALU.max), deps=[a2])
        a4 = p.op("vector", cp(scrB[0:32, 0:1], scrM[32:64, 0:1]), deps=[a3])
        a5 = p.op("vector", tt(scrM[0:32, 0:1], scrM[0:32, 0:1], scrB[0:32, 0:1],
                               ALU.max), deps=[a4])
        a6 = p.op("tensor", lambda e: e.transpose(PS[0:1, 2048:2080],
                                                   scrM[0:32, 0:1],
                                                   i128T2[0:32, 0:32]),
                  deps=[a5, h_i1282])
        a7 = p.op("vector", cp(rowsc[0:1, 0:32], PS[0:1, 2048:2080]), deps=[a6])
        a8 = p.op("vector", lambda e: e.reduce_max(out=rowsc[0:1, 32:33],
                                                   in_=rowsc[0:1, 0:32],
                                                   axis=mybir.AxisListType.X),
                  deps=[a7])
        az = p.op("vector", lambda e: e.memset(rowsc[0:1, 33:48], 0.0), deps=[a8])
        ha = p.dma("gpsimd", lambda e: e.dma_start(out=amax, in_=rowsc[0:1, 32:48]),
                   deps=[a8, az])
    return p.barrier()




def emit_F(ctx, upto="full"):
    nc, p, bt, ft, PS, t = ctx["nc"], ctx["p"], ctx["bt"], ctx["ft"], ctx["PS"], ctx["t"]
    x3Ti = t["x3T"]
    x3l = t["x3lnT"]
    amax8 = t["amax8"]
    wT = t["wT"]
    biasc = t["biasc"]
    onesf = t["onesf"]
    i128 = t["i128"]
    outT = t["outT"]
    if True:
        ba = Arena(bt, GNB16)
        fa = Arena(ft, GNF32)

        wq8 = ba.alloc(128, 8 * 1024)
        xq8 = ba.alloc(128, 8 * 512)
        onesb = ba.alloc(128, 32)

        onesfT = fa.alloc(1, 128)
        i128T = fa.alloc(128, 128)
        biasT = fa.alloc(128, 32)
        gcol = fa.alloc(128, 32)
        rowsc = fa.alloc(128, 1024)
        wf = fa.alloc(128, 1024)
        scrA = fa.alloc(128, 4096)
        scrR = fa.alloc(128, 4096)
        scrV = fa.alloc(128, 4096)
        scrU = fa.alloc(128, 2048)
        scrW = fa.alloc(128, 2048)
        scrD = fa.alloc(128, 2048)

        PSB = [PS[:, k * 512:(k + 1) * 512] for k in range(8)]

        h_of = p.dma("sync", lambda e: e.dma_start(out=onesfT, in_=onesf))
        h_i128 = p.dma("sync", lambda e: e.dma_start(out=i128T, in_=i128))
        h_bias = p.dma("sync", lambda e: e.dma_start(out=biasT[:, 0:8], in_=biasc))
        h_am = None
        h_tc = p.op("vector", lambda e: e.memset(gcol[:, 2:3], 1e-38))
        h_am = p.dma("sync", lambda e: e.dma_start(out=rowsc[0:1, 100:116],
                                                   in_=amax8))
        bar0 = p.barrier()

        hx1 = p.op("vector", lambda e: e.reduce_max(out=rowsc[0:1, 8:9],
                                                    in_=rowsc[0:1, 0:16][0:1, 0:8]
                                                    if False else rowsc[0:1, 100:116],
                                                    axis=mybir.AxisListType.X),
                   deps=[h_am])
        hwa = None
        hwl = None
        for c in range(8):
            hwl = p.dma("sync", (lambda c: lambda e: e.dma_start(
                out=wf, in_=wT[c * 128:(c + 1) * 128, :]))(c),
                deps=[bar0 if hwa is None else hwa])
            ha = p.op("scalar", act(scrA[:, 0:1024], wf, AF.Abs), deps=[hwl])
            hr = p.op("vector", (lambda c: lambda e: e.reduce_max(
                out=scrV[:, c:c + 1], in_=scrA[:, 0:1024],
                axis=mybir.AxisListType.X))(c), deps=[ha])
            hwa = hr
        hw1 = p.op("vector", lambda e: e.reduce_max(out=scrV[:, 8:9],
                                                    in_=scrV[:, 0:8],
                                                    axis=mybir.AxisListType.X),
                   deps=[hwa])
        hw2 = p.op("vector", cp(scrW[0:64, 0:1], scrV[64:128, 8:9]), deps=[hw1])
        hw3 = p.op("vector", tt(scrV[0:64, 8:9], scrV[0:64, 8:9], scrW[0:64, 0:1],
                                ALU.max), deps=[hw2])
        hw4 = p.op("vector", cp(scrW[0:32, 0:1], scrV[32:64, 8:9]), deps=[hw3])
        hw5 = p.op("vector", tt(scrV[0:32, 8:9], scrV[0:32, 8:9], scrW[0:32, 0:1],
                                ALU.max), deps=[hw4])
        hw6 = p.op("tensor", lambda e: e.transpose(PS[0:1, 0:32], scrV[0:32, 8:9],
                                                   i128T[0:32, 0:32]),
                   deps=[hw5, h_i128])
        hw7 = p.op("vector", cp(rowsc[0:1, 16:48], PS[0:1, 0:32]), deps=[hw6])
        hw8 = p.op("vector", lambda e: e.reduce_max(out=rowsc[0:1, 9:10],
                                                    in_=rowsc[0:1, 16:48],
                                                    axis=mybir.AxisListType.X),
                   deps=[hw7])
        hs1 = p.op("vector", ts(rowsc[0:1, 10:12], rowsc[0:1, 8:10], 1e-12, ALU.max),
                   deps=[hx1, hw8])
        hs2 = p.op("vector", lambda e: e.reciprocal(out=rowsc[0:1, 12:14],
                                                    in_=rowsc[0:1, 10:12]),
                   deps=[hs1])
        hs3 = p.op("vector", ts(rowsc[0:1, 12:14], rowsc[0:1, 12:14], 448.0,
                                ALU.mult, 448.0, ALU.min), deps=[hs2])
        hs4 = p.op("vector", lambda e: e.reciprocal(out=rowsc[0:1, 14:16],
                                                    in_=rowsc[0:1, 12:14]),
                   deps=[hs3])
        hs5 = p.op("vector", tt(rowsc[0:1, 20:21], rowsc[0:1, 14:15],
                                rowsc[0:1, 15:16], ALU.mult), deps=[hs4])
        hb1 = p.op("tensor", mm(PS[0:128, 0:2], onesfT, rowsc[0:1, 12:14]),
                   deps=[hs5, h_of])
        hb2 = p.op("vector", cp(gcol[:, 0:2], PS[0:128, 0:2]), deps=[hb1])
        hb3 = p.op("tensor", mm(PS[0:128, 4:5], onesfT, rowsc[0:1, 20:21]),
                   deps=[hb2])
        hb4 = p.op("vector", cp(gcol[:, 4:5], PS[0:128, 4:5]), deps=[hb3])
        bar_s = p.barrier()

        def emit_fp8(src_ap, dst_ap, scale_col, width, deps, voff=0):
            v = scrV[:, voff:voff + width]
            u = scrU[:, 0:width]
            w_ = scrW[:, 0:width]
            d_ = scrD[:, 0:width]
            h1 = p.op("vector", (lambda: ts(v, src_ap,
                                            gcol[:, scale_col:scale_col + 1],
                                            ALU.mult))(), deps=deps)
            h2 = p.op("vector", ts(v, v, 448.0, ALU.min, -448.0, ALU.max), deps=[h1])
            h3 = p.op("scalar", act(u, v, AF.Square), deps=[h2])
            h4 = p.op("scalar", act(u, u, AF.Ln, bias=gcol[:, 2:3]), deps=[h3, h_tc])
            h5 = p.op("vector", ts(u, u, 0.5 / LN2, ALU.mult, 127.0, ALU.add),
                      deps=[h4])
            h6 = p.op("vector", cp(w_, u), deps=[h5])
            hc = frac_cascade(p, w_, d_, 7, 8, [h6])
            h7 = p.op("vector", tt(u, u, w_, ALU.subtract), deps=[hc])
            h8 = p.op("vector", ts(u, u, -130.0, ALU.add, -9.0, ALU.max), deps=[h7])
            h9 = p.op("scalar", act(w_, u, AF.Exp, scale=-LN2), deps=[h8])
            h10 = p.op("vector", tt(v, v, w_, ALU.mult), deps=[h9])
            h11 = p.op("vector", ts(v, v, 16.5, ALU.add), deps=[h10])
            h12 = p.op("vector", cp(w_, v), deps=[h11])
            hc2 = frac_cascade(p, w_, d_, 4, 5, [h12])
            h13 = p.op("vector", tt(v, v, w_, ALU.subtract), deps=[hc2])
            h14 = p.op("vector", ts(v, v, -16.0, ALU.add), deps=[h13])
            h15 = p.op("scalar", act(w_, u, AF.Exp, scale=LN2), deps=[h14])
            h16 = p.op("vector", (lambda: tt(dst_ap, v, w_, ALU.mult))(), deps=[h15])
            return h16

        def emit_bf16round(width, deps, voff=0):
            v = scrV[:, voff:voff + width]
            u = scrU[:, 0:width]
            w_ = scrW[:, 0:width]
            d_ = scrD[:, 0:width]
            h3 = p.op("scalar", act(u, v, AF.Square), deps=deps)
            h4 = p.op("scalar", act(u, u, AF.Ln, bias=gcol[:, 2:3]), deps=[h3, h_tc])
            h5 = p.op("vector", ts(u, u, 0.5 / LN2, ALU.mult, 127.0, ALU.add),
                      deps=[h4])
            h6 = p.op("vector", cp(w_, u), deps=[h5])
            hc = frac_cascade(p, w_, d_, 7, 8, [h6])
            h7 = p.op("vector", tt(u, u, w_, ALU.subtract), deps=[hc])
            h8 = p.op("vector", ts(u, u, -134.0, ALU.add, -60.0, ALU.max), deps=[h7])
            h9 = p.op("scalar", act(w_, u, AF.Exp, scale=-LN2), deps=[h8])
            h10 = p.op("vector", tt(v, v, w_, ALU.mult), deps=[h9])
            h11 = p.op("vector", ts(v, v, 256.5, ALU.add), deps=[h10])
            h12 = p.op("vector", cp(w_, v), deps=[h11])
            hc2 = frac_cascade(p, w_, d_, 8, 9, [h12])
            h13 = p.op("vector", tt(v, v, w_, ALU.subtract), deps=[hc2])
            h14 = p.op("vector", ts(v, v, -256.0, ALU.add), deps=[h13])
            h15 = p.op("scalar", act(w_, u, AF.Exp, scale=LN2), deps=[h14])
            h16 = p.op("vector", tt(v, v, w_, ALU.mult), deps=[h15])
            return h16

        hq = bar_s
        for c in range(8):
            hwl = p.dma("sync", (lambda c: lambda e: e.dma_start(
                out=wf, in_=wT[c * 128:(c + 1) * 128, :]))(c),
                deps=hq if isinstance(hq, list) else [hq])
            hq = [emit_fp8(wf, wq8[:, c * 1024:(c + 1) * 1024], 1, 1024, [hwl])]
        bar_wq = p.barrier()

        x3r3 = x3Ti.rearrange("(c p) i -> p c i", p=128)
        x3l3 = x3l.rearrange("(c p) i -> p c i", p=128)
        outr3 = outT.rearrange("(c p) i -> p c i", p=128)
        scrA3 = scrA.rearrange("p (c i) -> p c i", i=512)

        hprev = bar_wq
        for tc in range(4):
            tok = slice(tc * 512, tc * 512 + 512)
            hx = p.dma("sync", (lambda tok: lambda e: e.dma_start(
                out=scrA3, in_=x3l3[:, :, tok]))(tok), deps=hprev)
            hxr = p.dma("sync", (lambda tok: lambda e: e.dma_start(
                out=scrR.rearrange("p (c i) -> p c i", i=512),
                in_=x3r3[:, :, tok]))(tok), deps=hprev)
            hq8a = emit_fp8(scrA[:, 0:2048], xq8[:, 0:2048], 0, 2048, [hx])
            hq8 = emit_fp8(scrA[:, 2048:4096], xq8[:, 2048:4096], 0, 2048, [hq8a])
            hmm = {}
            for dout in range(8):
                hacc = None
                for c in range(8):
                    lhs = wq8[:, c * 1024 + dout * 128: c * 1024 + dout * 128 + 128]
                    rhs = xq8[:, c * 512:(c + 1) * 512]
                    hacc = p.op("tensor", (lambda lhs, rhs, dout, st, sp: mm(
                        PSB[dout][:, :], lhs, rhs, start=st, stop=sp))(
                            lhs, rhs, dout, c == 0, c == 7),
                        deps=([hq8, bar_wq] if hacc is None else [hacc]))
                hmm[dout] = hacc
            hev = [hq8]
            for dout in range(8):
                blk = slice(dout * 512, dout * 512 + 512)
                hev.append(p.op("vector", (lambda blk, dout: ts(
                    scrV[:, blk], PSB[dout][:, :], gcol[:, 4:5], ALU.mult,
                    biasT[:, dout:dout + 1], ALU.add))(blk, dout),
                    deps=[hmm[dout], hb4, h_bias] + hev[-1:]))
            hrd1 = emit_bf16round(2048, hev, voff=0)
            hrd = emit_bf16round(2048, [hrd1], voff=2048)
            hres = p.op("vector", tt(scrA, scrV, scrR, ALU.add), deps=[hrd, hxr])
            ho = p.dma("gpsimd", (lambda tok: lambda e: e.dma_start(
                out=outr3[:, :, tok], in_=scrA3))(tok), deps=[hres])
            hprev = [ho, hres]
    return p.barrier()




def _rope_permute_cols(pairs):
    perm = []
    for (a, b_) in pairs:
        perm.extend(range(a * 64, a * 64 + 32))
        perm.extend(range(b_ * 64, b_ * 64 + 32))
        perm.extend(range(a * 64 + 32, a * 64 + 64))
        perm.extend(range(b_ * 64 + 32, b_ * 64 + 64))
    return np.array(perm)


def _headperm_rows(pairs):
    perm = []
    for (a, b_) in pairs:
        perm.extend(range(a * 64, (a + 1) * 64))
        perm.extend(range(b_ * 64, (b_ + 1) * 64))
    return np.array(perm)


def make_A_inputs(inputs):
    x = np.ascontiguousarray(inputs["L0_attn_x"], dtype=np.float32)
    qperm = _rope_permute_cols(QPAIRS)
    kperm = _rope_permute_cols([(0, 1), (2, 3)])
    wq_p = np.ascontiguousarray(inputs["L0_attn_w_q"][:, qperm], dtype=np.float32)
    wk_p = np.ascontiguousarray(inputs["L0_attn_w_k"][:, kperm], dtype=np.float32)
    wv = np.ascontiguousarray(inputs["L0_attn_w_v"], dtype=np.float32)
    wo_p = np.ascontiguousarray(inputs["L0_attn_w_o"][_headperm_rows(QPAIRS), :],
                                dtype=np.float32)
    posr = np.arange(S, dtype=np.float32).reshape(1, S)
    didx = np.arange(32, dtype=np.float32).reshape(1, 32)
    onesf = np.ones((1, 128), dtype=np.float32)
    pr = np.arange(128)
    jrep = (np.arange(32)[:, None] == (pr[None, :] % 32)).astype(np.float32)
    sgn = np.where(pr < 64, -1.0, 1.0).astype(np.float32)
    jsgn = jrep * sgn[None, :]
    common = dict(wq=wq_p, wk=wk_p, wv=wv, wo=wo_p, posr=posr, didx=didx,
                  onesf=onesf, jrep=jrep, jsgn=jsgn)
    in_maps = []
    for c in range(NCORES):
        b, half = c // 2, c % 2
        m = dict(common)
        xT = np.ascontiguousarray(x[b].T)
        m["xbT"] = xT
        m["xoT"] = np.ascontiguousarray(xT[:, half * HALF:(half + 1) * HALF])
        m["poso"] = posr[:, half * HALF:(half + 1) * HALF].copy()
        in_maps.append(m)
    return in_maps


def run_launch_A(inputs, core_ids=None):
    in_maps = make_A_inputs(inputs)
    nc0 = build_launch_A()
    cids = core_ids or list(range(NCORES))
    r = run_bass_kernel_spmd(nc0, [in_maps[c] for c in cids], core_ids=cids)
    x1 = np.empty((B, S, D), dtype=np.float32)
    for i, c in enumerate(cids):
        b, half = c // 2, c % 2
        x1[b, half * HALF:(half + 1) * HALF] = r.results[i]["x1T"].T
    return x1


def make_C1_inputs(inputs, x2):
    wqkvT = np.ascontiguousarray(inputs["L1_attn_w_qkv"].T.astype(np.float32))
    common = dict(wqkvT=wqkvT, onesf=np.ones((1, 128), dtype=np.float32))
    in_maps = []
    for c in range(NCORES):
        b, half = c // 2, c % 2
        m = dict(common)
        xT = np.ascontiguousarray(x2[b].T.astype(np.float32))
        m["x2T"] = xT
        m["x2oT"] = np.ascontiguousarray(xT[:, half * HALF:(half + 1) * HALF])
        in_maps.append(m)
    return in_maps


def make_C2_inputs(qd, kd):
    in_maps = []
    for c in range(NCORES):
        b, half = c // 2, c % 2
        m = {}
        m["qd"] = qd[c]
        m["kd"] = np.ascontiguousarray(np.concatenate(
            [kd[c][:, half * HALF:(half + 1) * HALF],
             kd[c][:, (1 - half) * HALF:(2 - half) * HALF]], axis=1))
        m["i128"] = np.eye(128, dtype=np.float32)
        in_maps.append(m)
    return in_maps


def mrowP_to_mrow(mp):
    return np.ascontiguousarray(
        np.asarray(mp, dtype=np.float32).reshape(128, 16, 16)
        .transpose(1, 2, 0).reshape(16, HALF))


def _e65g():
    e = np.zeros((16, 1040), dtype=np.float32)
    for g in range(4):
        for j in range(4):
            h = 4 * g + j
            e[h, g * 260 + j * 65: g * 260 + j * 65 + 65] = 1.0
    import ml_dtypes
    return e.astype(ml_dtypes.bfloat16)


def make_D_inputs(inputs, x2, qd, kd, vd, mrow):
    import ml_dtypes
    woutT = np.ascontiguousarray(inputs["L1_attn_w_out"].T.astype(np.float32))
    bout = np.asarray(inputs["L1_attn_b_out"], dtype=np.float32)
    boutc = np.ascontiguousarray(bout.reshape(-1, 128).T)
    e65 = _e65g()
    i16b = np.eye(16, dtype=np.float32).astype(ml_dtypes.bfloat16)
    common = dict(woutT=woutT, boutc=boutc, e65g=e65, i16b=i16b,
                  onesf=np.ones((1, 128), dtype=np.float32),
                  i128=np.eye(128, dtype=np.float32))
    diag_full = {}
    for b in range(B):
        dm = np.empty((16, S), dtype=np.float32)
        dm[:, 0:HALF] = mrow[2 * b]
        dm[:, HALF:S] = mrow[2 * b + 1]
        diag_full[b] = dm
    in_maps = []
    for c in range(NCORES):
        b, half = c // 2, c % 2
        m = dict(common)
        xT = np.ascontiguousarray(x2[b].T.astype(np.float32))
        m["x2oT"] = np.ascontiguousarray(xT[:, half * HALF:(half + 1) * HALF])
        m["qd"] = qd[c]
        m["kd"] = kd[c]
        m["vd"] = vd[c]
        m["diagm"] = diag_full[b]
        m["diagmo"] = np.ascontiguousarray(
            diag_full[b][:, half * HALF:(half + 1) * HALF])
        in_maps.append(m)
    return in_maps


def make_F_inputs(inputs, x3T, x3lnT, amaxs):
    wTt = np.ascontiguousarray(inputs["L1_ffn_weight"].T.astype(np.float32))
    biasv = np.asarray(inputs["L1_ffn_bias"], dtype=np.float32)
    biasc = np.ascontiguousarray(biasv.reshape(-1, 128).T)
    a = np.zeros((1, 16), dtype=np.float32)
    a[0, :len(amaxs)] = np.array(amaxs, dtype=np.float32)
    amax8 = a
    common = dict(wT=wTt, biasc=biasc, amax8=amax8,
                  onesf=np.ones((1, 128), dtype=np.float32),
                  i128=np.eye(128, dtype=np.float32))
    in_maps = []
    for c in range(NCORES):
        m = dict(common)
        m["x3T"] = x3T[c]
        m["x3lnT"] = x3lnT[c]
        in_maps.append(m)
    return in_maps


def kernel(**inputs):
    cids = list(range(NCORES))
    in_A = make_A_inputs(inputs)
    rA = run_bass_kernel_spmd(build_launch_A(), in_A, core_ids=cids)
    x1 = np.empty((B, S, D), dtype=np.float32)
    for i, c in enumerate(cids):
        b, half = c // 2, c % 2
        x1[b, half * HALF:(half + 1) * HALF] = rA.results[i]["x1T"].T
    in_B = make_B_inputs(inputs, x1)
    rB = run_bass_kernel_spmd(build_launch_B(), in_B, core_ids=cids)
    x2 = np.empty((B, S, D), dtype=np.float32)
    for i, c in enumerate(cids):
        b, half = c // 2, c % 2
        x2[b, half * HALF:(half + 1) * HALF] = rB.results[i]["x2T"].T
    in_C1 = make_C1_inputs(inputs, x2)
    rC1 = run_bass_kernel_spmd(build_launch_C1(), in_C1, core_ids=cids)
    qd = {c: rC1.results[i]["qd"] for i, c in enumerate(cids)}
    kd = {c: rC1.results[i]["kd"] for i, c in enumerate(cids)}
    vd = {c: rC1.results[i]["vd"] for i, c in enumerate(cids)}
    in_C2 = make_C2_inputs(qd, kd)
    rC2 = run_bass_kernel_spmd(build_launch_C2(), in_C2, core_ids=cids)
    mrow = {c: mrowP_to_mrow(rC2.results[i]["mrowP"]) for i, c in enumerate(cids)}
    in_D = make_D_inputs(inputs, x2, qd, kd, vd, mrow)
    rD = run_bass_kernel_spmd(build_launch_D(), in_D, core_ids=cids)
    x3T = {c: rD.results[i]["x3T"] for i, c in enumerate(cids)}
    x3lnT = {c: rD.results[i]["x3lnT"] for i, c in enumerate(cids)}
    amaxs = [float(rD.results[i]["amax"][0, 0]) for i, c in enumerate(cids)]
    in_F = make_F_inputs(inputs, x3T, x3lnT, amaxs)
    rF = run_bass_kernel_spmd(build_launch_F(), in_F, core_ids=cids)
    out = np.empty((B, S, D), dtype=np.float32)
    for i, c in enumerate(cids):
        b, half = c // 2, c % 2
        out[b, half * HALF:(half + 1) * HALF] = rF.results[i]["outT"].T
    return out


# version 06
# speedup vs baseline: 1.6552x; 1.1037x over previous
import numpy as np
import concourse.bass as bass
import concourse.mybir as mybir
from concourse.bass_utils import run_bass_kernel_spmd
from concourse.alu_op_type import AluOpType as ALU

F32 = mybir.dt.float32
BF16 = mybir.dt.bfloat16
AF = mybir.ActivationFunctionType

NCORES = 8
B, S, D = 4, 4096, 1024
HALF = S // 2
LN10K = float(np.log(10000.0))
LN2 = float(np.log(2.0))
TWO_PI = 2 * np.pi
TWO_PI_HI = 6.28125
TWO_PI_LO = TWO_PI - TWO_PI_HI
C_L0 = 24.0
C_L1 = 10.0
EPS = 1e-5

ENGINES = ("sync", "vector", "scalar", "tensor", "gpsimd")

QPAIRS = [(0, 4), (1, 5), (2, 6), (3, 7), (8, 12), (9, 13), (10, 14), (11, 15)]
QCHUNK, QBASE = {}, {}
for _c, (_a, _b) in enumerate(QPAIRS):
    QCHUNK[_a], QBASE[_a] = _c, 0
    QCHUNK[_b], QBASE[_b] = _c, 64


class H:
    __slots__ = ("eng", "kind", "idx")

    def __init__(self, eng, kind, idx):
        self.eng, self.kind, self.idx = eng, kind, idx


class Prog:

    def __init__(self):
        self.ops = {e: [] for e in ENGINES}
        self.ccount = {e: 0 for e in ENGINES}
        self.dcount = {e: 0 for e in ENGINES}
        self.floor = {}

    def _waits_for(self, deps):
        w = {}

        def add(d):
            if d is None:
                return
            if isinstance(d, (list, tuple)):
                for x in d:
                    add(x)
            else:
                key = (d.eng, d.kind)
                if w.get(key, -1) < d.idx:
                    w[key] = d.idx
        add(deps)
        for key, val in self.floor.items():
            if w.get(key, -1) < val:
                w[key] = val
        return w

    def op(self, eng, emit, deps=(), n=1):
        w = self._waits_for(deps)
        if eng == "tensor":
            c = self.ccount["tensor"]
            if c + n > 16:
                key = ("tensor", "c")
                need = c + n - 16
                if w.get(key, -1) < need:
                    w[key] = need
        self.ops[eng].append((emit, w, "c", n))
        self.ccount[eng] += n
        return H(eng, "c", self.ccount[eng])

    def dma(self, eng, emit, deps=()):
        w = self._waits_for(deps)
        self.ops[eng].append((emit, w, "d", 1))
        self.dcount[eng] += 1
        return H(eng, "d", 16 * self.dcount[eng])

    def set_floor(self, deps):
        w = {}
        for d in deps:
            key = (d.eng, d.kind)
            if w.get(key, -1) < d.idx:
                w[key] = d.idx
        self.floor = w

    def barrier(self):
        out = []
        for e in ENGINES:
            if self.ccount[e]:
                out.append(H(e, "c", self.ccount[e]))
            if self.dcount[e]:
                out.append(H(e, "d", 16 * self.dcount[e]))
        return out

    def build(self, nc):
        stack, sems = [], {}
        for e in ENGINES:
            for k in ("c", "d"):
                g = nc.semaphore(f"s_{k}_{e}")
                sems[(e, k)] = g.__enter__()
                stack.append(g)
        bg = nc.Block()
        block = bg.__enter__()

        def make_body(engname):
            oplist = self.ops[engname]

            def body(engobj):
                waited = {}
                for emit, w, kind, n in oplist:
                    for key, val in sorted(w.items()):
                        if waited.get(key, -1) >= val:
                            continue
                        engobj.wait_ge(sems[key], val)
                        waited[key] = val
                    last = emit(engobj)
                    if kind == "c":
                        last.then_inc(sems[(engname, "c")], n)
                    else:
                        last.then_inc(sems[(engname, "d")], 16)
            return body

        for e in ENGINES:
            if self.ops[e]:
                getattr(block, e)(make_body(e))
        bg.__exit__(None, None, None)
        for g in reversed(stack):
            g.__exit__(None, None, None)


class Arena:
    def __init__(self, t, ncols):
        self.t, self.ncols, self.off, self.marks = t, ncols, 0, []

    def alloc(self, p, w, align=32):
        o = (self.off + align - 1) // align * align
        assert o + w <= self.ncols, f"arena overflow {o}+{w}>{self.ncols}"
        self.off = o + w
        self.last_off = o
        return self.t[0:p, o:o + w]

    def mark(self):
        self.marks.append(self.off)

    def release(self):
        self.off = self.marks.pop()



def ts(out, in0, s1, op0, s2=None, op1=None, accum=None):
    kw = {}
    if op1 is not None:
        kw["op1"] = op1
    if accum is not None:
        kw["accum_out"] = accum
    return lambda e: e.tensor_scalar(out=out, in0=in0, scalar1=s1, scalar2=s2,
                                     op0=op0, **kw)


def tt(out, a, bb, op):
    return lambda e: e.tensor_tensor(out=out, in0=a, in1=bb, op=op)


def stt(out, in0, scalar, in1, op0, op1):
    return lambda e: e.scalar_tensor_tensor(out=out, in0=in0, scalar=scalar,
                                            in1=in1, op0=op0, op1=op1)


def act(out, in_, func, bias=0.0, scale=1.0, accum=None):
    kw = {"accum_out": accum} if accum is not None else {}
    return lambda e: e.activation(out=out, in_=in_, func=func, bias=bias,
                                  scale=scale, **kw)


def cp(out, in_):
    return lambda e: e.tensor_copy(out=out, in_=in_)


def mm(out, lhsT, rhs, start=True, stop=True):
    return lambda e: e.matmul(out, lhsT, rhs, start=start, stop=stop)


def frac_cascade(p, u, d, topbit, nbits, deps, eng="vector"):
    h = deps
    for k in range(topbit, topbit - nbits, -1):
        c = float(2.0 ** k)
        h1 = p.op(eng, ts(d, u, c, ALU.is_ge, -c, ALU.mult), deps=h)
        h = [p.op(eng, tt(u, u, d, ALU.add), deps=[h1])]
    return h[0]


DEBUG_NGROUPS = 99
SGROUPS = []
_j = 0
while _j < 32:
    _n = min(3, 32 - _j)
    SGROUPS.append((_j, _n))
    _j += _n



GNB16, GNF32 = 54784, 23296

def _standalone(emit, ios, upto="full"):
    nc = bass.Bass("TRN2", target_bir_lowering=False)
    t = {}
    for (name, shape, dt_, kind) in ios:
        t[name] = nc.dram_tensor(name, shape, dt_, kind=kind).ap()
    with (
        nc.sbuf_tensor("AB16", [128, GNB16], BF16) as bt,
        nc.sbuf_tensor("AF32", [128, GNF32], F32) as ft,
        nc.sbuf_tensor("AI32", [128, 2048], mybir.dt.int32) as it,
        nc.psum_tensor("PS", [128, 4096], F32) as PS,
    ):
        p = Prog()
        ctx = dict(nc=nc, p=p, bt=bt, ft=ft, PS=PS, t=t, it=it)
        emit(ctx, upto)
        p.op("gpsimd", lambda e: e.nop(), deps=p.barrier())
        p.build(nc)
    return nc

A_IOS = [("xbT", [D, S], F32, "ExternalInput"), ("xoT", [D, HALF], F32, "ExternalInput"),
         ("wq", [D, 1024], F32, "ExternalInput"), ("wk", [D, 256], F32, "ExternalInput"),
         ("wv", [D, 256], F32, "ExternalInput"), ("wo", [D, 1024], F32, "ExternalInput"),
         ("posr", [1, S], F32, "ExternalInput"), ("poso", [1, HALF], F32, "ExternalInput"),
         ("didx", [1, 32], F32, "ExternalInput"),
         ("onesf", [1, 128], F32, "ExternalInput"), ("jrep", [32, 128], F32, "ExternalInput"),
         ("jsgn", [32, 128], F32, "ExternalInput"), ("x1T", [D, HALF], F32, "ExternalOutput")]

B_IOS = [("x1T", [D, HALF], F32, "ExternalInput"), ("w1T", [D, 4096], F32, "ExternalInput"),
         ("w2T", [4096, D], F32, "ExternalInput"), ("g1c", [128, 8], F32, "ExternalInput"),
         ("b1lnc", [128, 8], F32, "ExternalInput"), ("b1c", [128, 32], F32, "ExternalInput"),
         ("g2c", [128, 32], F32, "ExternalInput"), ("b2lnc", [128, 32], F32, "ExternalInput"),
         ("b2c", [128, 8], F32, "ExternalInput"), ("onesf", [1, 128], F32, "ExternalInput"),
         ("i128", [128, 128], F32, "ExternalInput"), ("x2T", [D, HALF], F32, "ExternalOutput")]

C1_IOS = [("x2T", [D, S], F32, "ExternalInput"), ("x2oT", [D, HALF], F32, "ExternalInput"),
          ("wqkvT", [D, 3072], F32, "ExternalInput"), ("onesf", [1, 128], F32, "ExternalInput"),
          ("qd", [D, HALF], BF16, "ExternalOutput"), ("kd", [D, S], BF16, "ExternalOutput"),
          ("vd", [S, D], BF16, "ExternalOutput")]

C2_IOS = [("qd", [D, HALF], BF16, "ExternalInput"), ("kd", [D, S], BF16, "ExternalInput"),
          ("i128", [128, 128], F32, "ExternalInput"),
          ("mrowP", [128, 256], F32, "ExternalOutput")]

D_IOS = [("x2oT", [D, HALF], F32, "ExternalInput"), ("qd", [D, HALF], BF16, "ExternalInput"),
         ("kd", [D, S], BF16, "ExternalInput"), ("vd", [S, D], BF16, "ExternalInput"),
         ("diagm", [16, S], F32, "ExternalInput"), ("diagmo", [16, HALF], F32, "ExternalInput"),
         ("woutT", [D, D], F32, "ExternalInput"), ("boutc", [128, 8], F32, "ExternalInput"),
         ("e65g", [16, 1040], BF16, "ExternalInput"), ("i16b", [16, 16], BF16, "ExternalInput"),
         ("onesf", [1, 128], F32, "ExternalInput"), ("i128", [128, 128], F32, "ExternalInput"),
         ("x3T", [D, HALF], F32, "ExternalOutput"), ("x3lnT", [D, HALF], F32, "ExternalOutput"),
         ("amax", [1, 16], F32, "ExternalOutput")]

F_IOS = [("x3T", [D, HALF], F32, "ExternalInput"), ("x3lnT", [D, HALF], F32, "ExternalInput"),
         ("amax8", [1, 16], F32, "ExternalInput"), ("wT", [D, D], F32, "ExternalInput"),
         ("biasc", [128, 8], F32, "ExternalInput"), ("onesf", [1, 128], F32, "ExternalInput"),
         ("i128", [128, 128], F32, "ExternalInput"), ("outT", [D, HALF], F32, "ExternalOutput")]


def build_launch_A(upto="full"):
    return _standalone(emit_A, A_IOS, upto)

def build_launch_B(upto="full"):
    return _standalone(emit_B, B_IOS, upto)

def build_launch_C1(upto="full"):
    return _standalone(emit_C1, C1_IOS, upto)

def build_launch_C2(upto="full"):
    return _standalone(emit_C2, C2_IOS, upto)

def build_launch_D(upto="full"):
    return _standalone(emit_D, D_IOS, upto)

def build_launch_F(upto="full"):
    return _standalone(emit_F, F_IOS, upto)


PAIRS_RG = [[0, 1], [2, 3], [4, 5], [6, 7]]
ALL_RG = [[0, 1, 2, 3, 4, 5, 6, 7]]


def build_fused():
    nc = bass.Bass("TRN2", target_bir_lowering=False)
    t = {}
    ext_inputs = [
        ("xbT", [D, S], F32), ("xoT", [D, HALF], F32), ("wq", [D, 1024], F32),
        ("wk", [D, 256], F32), ("wv", [D, 256], F32), ("wo", [D, 1024], F32),
        ("posr", [1, S], F32), ("poso", [1, HALF], F32), ("didx", [1, 32], F32),
        ("onesf", [1, 128], F32), ("jrep", [32, 128], F32), ("jsgn", [32, 128], F32),
        ("w1T", [D, 4096], F32), ("w2T", [4096, D], F32),
        ("g1c", [128, 8], F32), ("b1lnc", [128, 8], F32), ("b1c", [128, 32], F32),
        ("g2c", [128, 32], F32), ("b2lnc", [128, 32], F32), ("b2c", [128, 8], F32),
        ("i128", [128, 128], F32), ("wqkvT", [D, 3072], F32),
        ("woutT", [D, D], F32), ("boutc", [128, 8], F32),
        ("e65g", [16, 1040], BF16), ("i16b", [16, 16], BF16),
        ("wT", [D, D], F32), ("biasc", [128, 8], F32),
    ]
    for (name, shape, dt_) in ext_inputs:
        t[name] = nc.dram_tensor(name, shape, dt_, kind="ExternalInput").ap()
    outT = nc.dram_tensor("outT", [D, HALF], F32, kind="ExternalOutput").ap()
    x1d = nc.dram_tensor("x1d", [D, HALF], F32, kind="Internal").ap()
    x2own = nc.dram_tensor("x2own", [1, D * HALF], F32, kind="Internal").ap()
    x2g = nc.dram_tensor("x2g", [2, D * HALF], F32, kind="Internal").ap()
    qdD = nc.dram_tensor("qdD", [D, HALF], BF16, kind="Internal").ap()
    kdD = nc.dram_tensor("kdD", [D, S], BF16, kind="Internal").ap()
    koD = nc.dram_tensor("koD", [D, HALF], BF16, kind="Internal").ap()
    vdD = nc.dram_tensor("vdD", [S, D], BF16, kind="Internal").ap()
    mrowD = nc.dram_tensor("mrowD", [1, 16 * HALF], F32, kind="Internal").ap()
    mrowG = nc.dram_tensor("mrowG", [2, 16 * HALF], F32, kind="Internal").ap()
    x3d = nc.dram_tensor("x3d", [D, HALF], F32, kind="Internal").ap()
    x3lnd = nc.dram_tensor("x3lnd", [D, HALF], F32, kind="Internal").ap()
    amaxD = nc.dram_tensor("amaxD", [1, 16], F32, kind="Internal").ap()
    amaxG = nc.dram_tensor("amaxG", [1, 16], F32, kind="Internal").ap()

    with (
        nc.sbuf_tensor("AB16", [128, GNB16], BF16) as bt,
        nc.sbuf_tensor("AF32", [128, GNF32], F32) as ft,
        nc.psum_tensor("PS", [128, 4096], F32) as PS,
    ):
        p = Prog()

        def ctx_for(names_map, extra=None):
            tt_ = dict(t)
            tt_.update(names_map)
            c = dict(nc=nc, p=p, bt=bt, ft=ft, PS=PS, t=tt_)
            if extra:
                c.update(extra)
            return c

        emit_A(ctx_for({"x1T": x1d}))
        p.set_floor(p.barrier())
        emit_B(ctx_for({"x1T": x1d,
                        "x2T": x2own.rearrange("o (a b) -> (o a) b", b=HALF)}))
        p.set_floor(p.barrier())
        hag = p.dma("gpsimd", lambda e: e.collective_compute(
            "AllGather", ALU.bypass, PAIRS_RG, [x2own], [x2g]))
        p.set_floor(p.barrier())
        x2g4 = x2g.rearrange("r (c p i) -> r p c i", p=128, i=HALF)
        x2chunks = [x2g4[ic8 // 4, :, :, (ic8 % 4) * 512:(ic8 % 4) * 512 + 512]
                    for ic8 in range(8)]
        emit_C1(ctx_for({"x2T": x2g.rearrange("r (c i) -> (r c) i", i=HALF)[0:D, :],
                         "x2oT": x2own.rearrange("o (a b) -> (o a) b", b=HALF),
                         "qd": qdD, "kd": kdD, "vd": vdD, "koT": koD},
                        extra={"x2chunks": x2chunks}))
        p.set_floor(p.barrier())
        emit_C2(ctx_for({"qd": qdD, "kd": kdD, "koT": koD,
                         "mrow": mrowD.rearrange("o (h i) -> (o h) i", i=HALF)}))
        p.set_floor(p.barrier())
        hagm = p.dma("gpsimd", lambda e: e.collective_compute(
            "AllGather", ALU.bypass, PAIRS_RG, [mrowD], [mrowG]))
        p.set_floor(p.barrier())
        mg = mrowG.rearrange("r (h i) -> r h i", i=HALF)
        emit_D(ctx_for({"x2oT": x2own.rearrange("o (a b) -> (o a) b", b=HALF),
                        "qd": qdD, "kd": kdD, "vd": vdD,
                        "diagm": mrowG.rearrange("r (h i) -> (r h) i", i=HALF)[0:16, :],
                        "diagmo": mrowD.rearrange("o (h i) -> (o h) i", i=HALF),
                        "x3T": x3d, "x3lnT": x3lnd, "amax": amaxD},
                       extra={"diag2": [mg[0], mg[1]]}))
        p.set_floor(p.barrier())
        har = p.dma("gpsimd", lambda e: e.collective_compute(
            "AllReduce", ALU.max, ALL_RG, [amaxD], [amaxG]))
        p.set_floor(p.barrier())
        emit_F(ctx_for({"x3T": x3d, "x3lnT": x3lnd, "amax8": amaxG,
                        "outT": outT}))
        p.op("gpsimd", lambda e: e.nop(), deps=p.barrier())
        p.build(nc)
    return nc


def make_fused_inputs(inputs):
    in_A = make_A_inputs(inputs)
    x_dummy = np.zeros((B, S, D), dtype=np.float32)
    common_B = make_B_inputs(inputs, x_dummy)[0]
    wqkvT = np.ascontiguousarray(inputs["L1_attn_w_qkv"].T.astype(np.float32))
    woutT = np.ascontiguousarray(inputs["L1_attn_w_out"].T.astype(np.float32))
    bout = np.asarray(inputs["L1_attn_b_out"], dtype=np.float32)
    wTt = np.ascontiguousarray(inputs["L1_ffn_weight"].T.astype(np.float32))
    biasv = np.asarray(inputs["L1_ffn_bias"], dtype=np.float32)
    import ml_dtypes
    in_maps = []
    for c in range(NCORES):
        m = dict(in_A[c])
        for k in ("w1T", "w2T", "g1c", "b1lnc", "b1c", "g2c", "b2lnc", "b2c",
                  "i128"):
            m[k] = common_B[k]
        m["wqkvT"] = wqkvT
        m["woutT"] = woutT
        m["boutc"] = np.ascontiguousarray(bout.reshape(-1, 128).T)
        m["e65g"] = _e65g()
        m["i16b"] = np.eye(16, dtype=np.float32).astype(ml_dtypes.bfloat16)
        m["wT"] = wTt
        m["biasc"] = np.ascontiguousarray(biasv.reshape(-1, 128).T)
        in_maps.append(m)
    return in_maps


def kernel_fused(**inputs):
    cids = list(range(NCORES))
    in_maps = make_fused_inputs(inputs)
    r = run_bass_kernel_spmd(build_fused(), in_maps, core_ids=cids)
    out = np.empty((B, S, D), dtype=np.float32)
    for i, c in enumerate(cids):
        b, half = c // 2, c % 2
        out[b, half * HALF:(half + 1) * HALF] = r.results[i]["outT"].T
    return out



def emit_A(ctx, upto="full"):
    nc, p, bt, ft, PS, t = ctx["nc"], ctx["p"], ctx["bt"], ctx["ft"], ctx["PS"], ctx["t"]
    xbT = t["xbT"]
    xoT = t["xoT"]
    wq = t["wq"]
    wk = t["wk"]
    wv = t["wv"]
    wo = t["wo"]
    posr = t["posr"]
    poso = t["poso"]
    onesf = t["onesf"]
    jrep = t["jrep"]
    jsgn = t["jsgn"]
    x1T = t["x1T"]
    didx = t["didx"]
    if True:
        ba = Arena(bt, GNB16)
        fa = Arena(ft, GNF32)

        onesb = ba.alloc(128, 32)
        qT = ba.alloc(128, 8 * HALF)
        o_qT = ba.last_off
        kT = ba.alloc(128, 2 * S)
        Vp = ba.alloc(128, 32 * 260)
        sqb = ba.alloc(128, 4096)
        o_sqb = ba.last_off
        ba.mark()
        wqb = ba.alloc(128, 8 * 1024)
        wkb = ba.alloc(128, 8 * 256)
        wvb = ba.alloc(128, 8 * 256)
        xlnc = ba.alloc(128, 4096)

        onesfT = fa.alloc(1, 128)
        onescol = fa.alloc(128, 32)
        rowsc = fa.alloc(128, 4096)
        scrA = fa.alloc(128, 4096)
        scrB = fa.alloc(128, 1024)
        scrC = fa.alloc(128, 1024)
        fa.mark()
        CCk = fa.alloc(128, S)
        SSk = fa.alloc(128, S)
        CCq = fa.alloc(128, HALF)
        SSq = fa.alloc(128, HALF)
        jrA = fa.alloc(64, 128)
        o_jr = fa.last_off
        jsA = fa.alloc(64, 128)
        o_js = fa.last_off
        invfr = fa.alloc(1, 64)
        didxr = fa.alloc(1, 64)
        jrepT = ft[32:64, o_jr:o_jr + 128]
        jsgnT = ft[32:64, o_js:o_js + 128]

        frq = scrA[0:32, :]
        u32 = rowsc[0:32, :]
        srow = scrA[32:64, :]

        ps_s = [PS[:, 0:1536], PS[:, 1536:3072]]
        ps_av = PS[:, 3072:3584]
        ps_m = PS[:, 3584:4096]
        ps_b0 = PS[:, 0:512]
        ps_b1 = PS[:, 512:1024]
        ps_b3 = PS[:, 1536:2048]
        ps_b4 = PS[:, 2048:2560]

        h_ones = p.op("vector", lambda e: e.memset(onesb[:, 0:1], 1.0))
        h_ob1 = p.op("vector", lambda e: e.memset(onesb[:, 1:2], 1.0 / 1024.0),
                     deps=[h_ones])
        h_ocol = p.op("vector", lambda e: e.memset(onescol[:, 0:1], 1.0 / 1024.0))
        h_negc = p.op("vector", lambda e: e.memset(onescol[:, 1:2], -C_L0))
        h_onesf = p.dma("sync", lambda e: e.dma_start(out=onesfT, in_=onesf))
        h_j1 = p.dma("sync", lambda e: e.dma_start(out=jrepT, in_=jrep))
        h_j2 = p.dma("sync", lambda e: e.dma_start(out=jsgnT, in_=jsgn))

        h_di = p.dma("sync", lambda e: e.dma_start(out=didxr[0:1, 0:32], in_=didx))
        h_invf = p.op("scalar", act(invfr[0:1, 0:32], didxr[0:1, 0:32], AF.Exp,
                                    scale=-LN10K / 32.0), deps=[h_di])

        def build_tables2(pos_ap, n, CXd, SXd, dep0):
            hp = p.dma("sync", lambda e: e.dma_start(out=rowsc[0:1, 0:n], in_=pos_ap),
                       deps=dep0)
            hf = [hp, h_invf]
            for c4 in range(n // 512):
                sl = slice(c4 * 512, (c4 + 1) * 512)
                hm = p.op("tensor", (lambda sl: mm(ps_m[0:32, 0:512], invfr[0:1, 0:32],
                                                   rowsc[0:1, sl]))(sl), deps=hf)
                hf = [p.op("scalar", (lambda sl: act(frq[:, sl], ps_m[0:32, 0:512],
                                                     AF.Identity))(sl), deps=[hm])]
            h_frq = hf[0]

            def sin_rows(shift, deps):
                OFF = 1024.0
                fr = frq[:, 0:n]
                uu = u32[:, 0:n]
                dk = CXd[0:32, 0:n]
                h1 = p.op("vector", ts(uu, fr, 1.0 / TWO_PI, ALU.mult,
                                       shift / TWO_PI + 0.5 + OFF, ALU.add), deps=deps)
                hfr = frac_cascade(p, uu, dk, 10, 11, [h1])
                h2 = p.op("vector", ts(dk, fr, 1.0 / TWO_PI, ALU.mult,
                                       shift / TWO_PI + 0.5 + OFF, ALU.add), deps=[hfr])
                h3 = p.op("vector", tt(dk, dk, uu, ALU.subtract), deps=[h2])
                h4 = p.op("vector", ts(dk, dk, -OFF, ALU.add), deps=[h3])
                h5 = p.op("vector", stt(uu, dk, -TWO_PI_HI, fr, ALU.mult, ALU.add),
                          deps=[h4])
                h6 = p.op("vector", ts(uu, uu, float(shift), ALU.add), deps=[h5])
                h7 = p.op("vector", stt(uu, dk, -TWO_PI_LO, uu, ALU.mult, ALU.add),
                          deps=[h6])
                h8 = p.op("vector", ts(uu, uu, float(np.pi), ALU.min,
                                       float(-np.pi), ALU.max), deps=[h7])
                return p.op("scalar", act(srow[:, 0:n], uu, AF.Sin), deps=[h8])

            hs = sin_rows(0.0, [h_frq])
            hlast = hs
            for c4 in range(n // 512):
                sl = slice(c4 * 512, (c4 + 1) * 512)
                h1 = p.op("tensor", (lambda sl: mm(ps_m[:, 0:512], jsgnT,
                                                   srow[:, sl]))(sl),
                          deps=[hs, h_j2] if c4 == 0 else [hlast, hs])
                hlast = p.op("scalar", (lambda sl: act(SXd[:, sl], ps_m[:, 0:512],
                                                       AF.Identity))(sl), deps=[h1])
            hc = sin_rows(np.pi / 2, [hlast])
            hlast = hc
            for c4 in range(n // 512):
                sl = slice(c4 * 512, (c4 + 1) * 512)
                h1 = p.op("tensor", (lambda sl: mm(ps_m[:, 0:512], jrepT,
                                                   srow[:, sl]))(sl),
                          deps=[hc, h_j1] if c4 == 0 else [hlast, hc])
                hlast = p.op("scalar", (lambda sl: act(CXd[:, sl], ps_m[:, 0:512],
                                                       AF.Identity))(sl), deps=[h1])
            return hlast

        h_t1 = build_tables2(posr, S, CCk, SSk, [h_di])
        h_t2 = build_tables2(poso, HALF, CCq, SSq, [h_t1])

        def load_weight(dst, src, ncols, dep0):
            dep = dep0
            out = None
            for c in range(8):
                hd = p.dma("sync", (lambda c: lambda e: e.dma_start(
                    out=scrB[:, 0:ncols], in_=src[c * 128:(c + 1) * 128, :]))(c),
                    deps=dep)
                out = p.op("vector", (lambda c: cp(dst[:, c * ncols:(c + 1) * ncols],
                                                   scrB[:, 0:ncols]))(c), deps=[hd])
                dep = [out]
            return out

        h_wq = load_weight(wqb, wq, 1024, [h_t2])
        h_wk = load_weight(wkb, wk, 256, [h_wq])
        h_wv = load_weight(wvb, wv, 256, [h_wk])
        bar_w = p.barrier()

        xr3 = xbT.rearrange("(c p) i -> p c i", p=128)
        xo3 = xoT.rearrange("(c p) i -> p c i", p=128)
        scrA3 = scrA.rearrange("p (c i) -> p c i", i=512)

        def ln_chunk(src3, tok, deps):
            hx = p.dma("sync", (lambda tok: lambda e: e.dma_start(
                out=scrA3, in_=src3[:, :, tok]))(tok), deps=deps)
            hsq = p.op("scalar", act(sqb, scrA, AF.Square), deps=[hx])

            def emit_sum(dst, lhs_col, srct):
                def f(e):
                    last = None
                    for c in range(8):
                        last = e.matmul(dst, lhs_col,
                                        srct[:, c * 512:(c + 1) * 512],
                                        start=(c == 0), stop=(c == 7))
                    return last
                return f
            hsum = p.op("tensor", emit_sum(ps_b0[0:1, :], onescol[:, 0:1], scrA),
                        deps=[hx, h_ocol] + deps, n=8)
            hsqs = p.op("tensor", emit_sum(ps_b1[0:1, :], onesb[:, 1:2], sqb),
                        deps=[hsq, h_ob1] + deps, n=8)
            rm = rowsc[0:1, 0:512]
            rv = rowsc[0:1, 512:1024]
            rr = rowsc[0:1, 1024:1536]
            rmb = rowsc[0:1, 1536:2048]
            h1 = p.op("vector", cp(rm, ps_b0[0:1, :]), deps=[hsum] + deps)
            h2 = p.op("scalar", act(rv, rm, AF.Square), deps=[h1])
            h3 = p.op("vector", stt(rv, rv, -1.0, ps_b1[0:1, :], ALU.mult, ALU.add),
                      deps=[h2, hsqs])
            h3b = p.op("vector", ts(rv, rv, EPS, ALU.add), deps=[h3])
            h4 = p.op("scalar", act(rr, rv, AF.Ln), deps=[h3b])
            h5 = p.op("scalar", act(rr, rr, AF.Exp, scale=-0.5), deps=[h4])
            h6 = p.op("vector", stt(rmb, rm, -1.0, rr, ALU.mult, ALU.mult), deps=[h5])
            hb1 = p.op("tensor", mm(ps_b3[:, :], onesfT, rr), deps=[h5, h_onesf] + deps)
            hb2 = p.op("tensor", mm(ps_b4[:, :], onesfT, rmb), deps=[h6])
            happ = []
            hprev = None
            for c in range(8):
                blk = slice(c * 512, c * 512 + 512)
                ha = p.op("vector", (lambda blk: tt(scrB[:, 0:512], scrA[:, blk],
                                                    ps_b3[:, :], ALU.mult))(blk),
                          deps=[hb1, hx] + deps + ([hprev] if hprev else []))
                hb_ = p.op("vector", (lambda blk: tt(xlnc[:, blk], scrB[:, 0:512],
                                                     ps_b4[:, :], ALU.add))(blk),
                          deps=[ha, hb2])
                hprev = hb_
                happ.append(hb_)
            return happ, [h1, h3, h6, hb1, hb2]

        def rope_evac(dst_ap, tokslice, CX, SX, deps):
            hsw1 = p.op("vector", cp(scrC[0:64, 0:512], ps_m[64:128, :]), deps=deps)
            hsw2 = p.op("vector", cp(scrC[64:128, 0:512], ps_m[0:64, :]), deps=deps)
            ht1 = p.op("vector", (lambda: tt(scrB[:, 0:512], ps_m[:, :],
                                             CX[:, tokslice], ALU.mult))(),
                       deps=[hsw1, hsw2])
            ht2 = p.op("vector", (lambda: tt(scrC[:, 512:1024], scrC[:, 0:512],
                                             SX[:, tokslice], ALU.mult))(),
                       deps=[ht1])
            he = p.op("vector", tt(dst_ap, scrB[:, 0:512], scrC[:, 512:1024], ALU.add),
                      deps=[ht2])
            return he

        prev_tail = bar_w
        h_psm_free = bar_w
        h_psav_free = bar_w
        for ic8 in range(8):
            tok = slice(ic8 * 512, ic8 * 512 + 512)
            happ, hstats = ln_chunk(xr3, tok, [prev_tail] if not isinstance(prev_tail, list) else prev_tail)
            evacs = []
            for kc in range(2):
                def emit_kp(kc):
                    def f(e):
                        last = None
                        for c in range(8):
                            last = e.matmul(
                                ps_m[:, :],
                                wkb[:, c * 256 + kc * 128: c * 256 + kc * 128 + 128],
                                xlnc[:, c * 512:(c + 1) * 512],
                                start=(c == 0), stop=(c == 7))
                        return last
                    return f
                hacc = p.op("tensor", emit_kp(kc),
                            deps=happ + [h_psm_free, bar_w], n=8)
                dstc = kc * S + ic8 * 512
                he = rope_evac(kT[:, dstc:dstc + 512], tok, CCk, SSk, [hacc])
                h_psm_free = he
                evacs.append(he)
            for jq in range(4):
                jc = ic8 * 4 + jq

                def emit_vp(jq):
                    def f(e):
                        last = None
                        for c in range(8):
                            last = e.matmul(
                                ps_av[:, 0:256],
                                xlnc[:, c * 512 + jq * 128: c * 512 + jq * 128 + 128],
                                wvb[:, c * 256:(c + 1) * 256],
                                start=(c == 0), stop=(c == 7))
                        return last
                    return f
                hacc = p.op("tensor", emit_vp(jq),
                            deps=happ + [h_psav_free, bar_w], n=8)
                vdst = Vp[:, jc * 260: jc * 260 + 260].rearrange(
                    "p (h e) -> p h e", e=65)
                hvc = p.op("vector", (lambda vdst: cp(
                    vdst[:, :, 0:64],
                    ps_av[:, 0:256].rearrange("p (h e) -> p h e", e=64)))(vdst),
                    deps=[hacc])
                hvo = p.op("vector", (lambda vdst: lambda e: e.memset(
                    vdst[:, :, 64:65], 1.0))(vdst), deps=[hvc])
                h_psav_free = hvc
            prev_tail = happ + hstats + evacs + [h_psm_free, h_psav_free]

        for oc in range(4):
            otok = slice(oc * 512, oc * 512 + 512)
            happ, hstats = ln_chunk(xo3, otok, prev_tail)
            evacs = []
            for dqc in range(8):
                def emit_qp(dqc):
                    def f(e):
                        last = None
                        for c in range(8):
                            last = e.matmul(
                                ps_m[:, :],
                                wqb[:, c * 1024 + dqc * 128: c * 1024 + dqc * 128 + 128],
                                xlnc[:, c * 512:(c + 1) * 512],
                                start=(c == 0), stop=(c == 7))
                        return last
                    return f
                hacc = p.op("tensor", emit_qp(dqc),
                            deps=happ + [h_psm_free, bar_w], n=8)
                dstc = dqc * HALF + oc * 512
                he = rope_evac(qT[:, dstc:dstc + 512], otok, CCq, SSq, [hacc])
                h_psm_free = he
                evacs.append(he)
            prev_tail = happ + hstats + evacs + [h_psm_free]

        bar_proj = p.barrier()

        h_rep = bar_proj
        for (tile, width, nch) in ((qT, HALF, 8), (kT, S, 2)):
            for c in range(nch):
                sl = slice(c * width, (c + 1) * width)
                w_ = width
                h1 = p.op("vector", (lambda tile, sl, w_: cp(
                    sqb[0:32, 0:w_], tile[32:64, sl]))(tile, sl, w_), deps=h_rep)
                h2 = p.op("vector", (lambda tile, sl: cp(
                    tile[32:64, sl], tile[64:96, sl]))(tile, sl), deps=[h1])
                h3 = p.op("vector", (lambda tile, sl, w_: cp(
                    tile[64:96, sl], sqb[0:32, 0:w_]))(tile, sl, w_), deps=[h2])
                h_rep = [h3]
        bar_rep = p.barrier()

        PTb = [bt[0:128, o_sqb:o_sqb + 1024], bt[0:128, o_sqb + 1024:o_sqb + 2048]]
        ba.release()
        numT = ba.alloc(128, 8 * HALF)

        ps_pair = [PS[:, 0:1024], PS[:, 1024:2048]]
        ps_avA = PS[:, 2048:2560]
        ps_avB = PS[:, 2560:3072]
        ps_nrm = PS[:, 3072:3584]

        h_expbuf = [bar_rep, bar_rep]
        h_avbuf = [bar_rep, bar_rep]
        h_avA_free = [bar_rep]
        h_avB_free = [bar_rep]
        h_nrmA_prev = bar_rep
        h_nrmB_prev = bar_rep
        h_den_readers = [bar_rep]
        denrow = rowsc[0:1, 0:4096]
        attn_limit = 9999
        skip_norm = False
        skip_av = False
        skip_den = False
        if upto.startswith("attn") and len(upto) > 4:
            tail = upto[4:]
            while tail and tail[-1] in "nad":
                if tail[-1] == "n":
                    skip_norm = True
                elif tail[-1] == "a":
                    skip_av = True
                    skip_den = True
                    skip_norm = True
                elif tail[-1] == "d":
                    skip_den = True
                    skip_norm = True
                tail = tail[:-1]
            attn_limit = int(tail)
        attn_iter = 0
        for pr, (hA, hB) in enumerate(QPAIRS):
            qc = pr
            kvA, kvB = hA // 4, hB // 4
            kcA, kbA = kvA // 2, (kvA % 2) * 64
            kcB, kbB = kvB // 2, (kvB % 2) * 64
            assert kbA == 0 and kbB == 64
            pair_dens = []
            pair_copies = {}
            for ic in range(4):
                if attn_iter >= attn_limit:
                    continue
                attn_iter += 1
                isl = slice(qc * HALF + ic * 512, qc * HALF + ic * 512 + 512)
                h_avA = h_avB = None
                pend = None
                for jc in range(32):
                    buf = jc % 2

                    def emit_spair(buf, jc, kcA, kcB, isl):
                        def f(e):
                            e.matmul(ps_pair[buf][:, 0:512],
                                     kT[0:64, kcA * S + jc * 128: kcA * S + jc * 128 + 128],
                                     qT[0:64, isl], start=True, stop=True)
                            return e.matmul(
                                ps_pair[buf][:, 512:1024],
                                kT[64:128, kcB * S + jc * 128: kcB * S + jc * 128 + 128],
                                qT[64:128, isl], start=True, stop=True)
                        return f
                    hs = p.op("tensor", emit_spair(buf, jc, kcA, kcB, isl),
                              deps=[h_expbuf[buf], bar_rep], n=2)

                    if pend is not None and not skip_av:
                        pjc, pbuf, phex = pend

                        def emit_avs(pjc, pbuf, side):
                            kv = kvA if side == 0 else kvB
                            def f(e):
                                return e.matmul(
                                    (ps_avA if side == 0 else ps_avB)[0:65, :],
                                    Vp[:, pjc * 260 + kv * 65: pjc * 260 + kv * 65 + 65],
                                    PTb[pbuf][:, side * 512: side * 512 + 512],
                                    start=(pjc == 0), stop=(pjc == 31))
                            return f
                        h_avA = p.op("tensor", emit_avs(pjc, pbuf, 0),
                                     deps=([phex] + (h_avA_free if pjc == 0
                                                     else [h_avA])))
                        h_avB = p.op("tensor", emit_avs(pjc, pbuf, 1),
                                     deps=([phex] + (h_avB_free if pjc == 0
                                                     else [h_avB])))
                        h_avbuf[pbuf] = h_avB

                    hex_ = p.op("scalar", (lambda buf: act(
                        PTb[buf][:, 0:1024], ps_pair[buf][:, 0:1024],
                        AF.Exp, bias=onescol[:, 1:2], scale=0.125))(buf),
                        deps=[hs, h_avbuf[buf], h_negc])
                    h_expbuf[buf] = hex_
                    pend = (jc, buf, hex_)

                if pend is not None and not skip_av:
                    pjc, pbuf, phex = pend

                    def emit_avs(pjc, pbuf, side):
                        kv = kvA if side == 0 else kvB
                        def f(e):
                            return e.matmul(
                                (ps_avA if side == 0 else ps_avB)[0:65, :],
                                Vp[:, pjc * 260 + kv * 65: pjc * 260 + kv * 65 + 65],
                                PTb[pbuf][:, side * 512: side * 512 + 512],
                                start=(pjc == 0), stop=(pjc == 31))
                        return f
                    h_avA = p.op("tensor", emit_avs(pjc, pbuf, 0),
                                 deps=[phex, h_avA])
                    h_avB = p.op("tensor", emit_avs(pjc, pbuf, 1),
                                 deps=[phex, h_avB])
                    h_avbuf[pbuf] = h_avB
                if skip_av:
                    continue

                for side, (qb, ps_av_s, h_av_s) in enumerate(
                        ((0, ps_avA, h_avA), (64, ps_avB, h_avB))):
                    frees = []
                    if not skip_den:
                        hd1 = p.op("vector", (lambda side, ic, ps_av_s: cp(
                            denrow[0:1, side * 2048 + ic * 512:
                                   side * 2048 + ic * 512 + 512],
                            ps_av_s[64:65, :]))(side, ic, ps_av_s),
                            deps=[h_av_s] + h_den_readers)
                        pair_dens.append(hd1)
                        frees.append(hd1)
                    hcopy = p.op("vector", (lambda qb, ps_av_s: cp(
                        numT[qb:qb + 64, isl], ps_av_s[0:64, :]))(qb, ps_av_s),
                        deps=[h_av_s])
                    pair_copies[(side, ic)] = hcopy
                    frees.append(hcopy)
                    if side == 0:
                        h_avA_free = frees
                    else:
                        h_avB_free = frees
            if not (skip_av or skip_den or skip_norm) and pair_dens:
                hrec = p.op("vector",
                            lambda e: e.reciprocal(out=denrow, in_=denrow),
                            deps=pair_dens)
                readers = []
                for side, qb in ((0, 0), (1, 64)):
                    hn_prev = h_nrmA_prev if side == 0 else h_nrmB_prev
                    for ic in range(4):
                        if (side, ic) not in pair_copies:
                            continue
                        isl2 = slice(qc * HALF + ic * 512, qc * HALF + ic * 512 + 512)
                        rsl = slice(side * 2048 + ic * 512,
                                    side * 2048 + ic * 512 + 512)
                        hf = p.op("tensor", (lambda qb, rsl: mm(
                            ps_nrm[qb:qb + 64, :], onesfT[0:1, 0:64],
                            denrow[0:1, rsl]))(qb, rsl),
                            deps=[hrec, hn_prev])
                        hn = p.op("vector", (lambda qb, isl2: tt(
                            numT[qb:qb + 64, isl2], numT[qb:qb + 64, isl2],
                            ps_nrm[qb:qb + 64, :], ALU.mult))(qb, isl2),
                            deps=[hf, pair_copies.get((side, ic), bar_rep)])
                        readers.append(hf)
                        hn_prev = hn
                    if side == 0:
                        h_nrmA_prev = hn_prev
                    else:
                        h_nrmB_prev = hn_prev
                h_den_readers = readers
        bar_attn = p.barrier()

        wob = bt[0:128, o_qT:o_qT + 8 * 1024]
        dep = bar_attn
        h_wo = None
        for c in range(8):
            hd = p.dma("sync", (lambda c: lambda e: e.dma_start(
                out=scrB[:, 0:1024], in_=wo[c * 128:(c + 1) * 128, :]))(c), deps=dep)
            h_wo = p.op("vector", (lambda c: cp(wob[:, c * 1024:(c + 1) * 1024],
                                                scrB[:, 0:1024]))(c), deps=[hd])
            dep = [h_wo]
        x1r3 = x1T.rearrange("(c p) i -> p c i", p=128)
        scrB3 = scrB.rearrange("p (c i) -> p c i", i=512)
        hout = []
        hra = [h_wo]
        h_psm_free = bar_attn
        for ic in range(4):
            otok = slice(ic * 512, ic * 512 + 512)
            hx = p.dma("sync", (lambda otok: lambda e: e.dma_start(
                out=scrA3, in_=xo3[:, :, otok]))(otok), deps=hra)
            hnew = []
            for doc in range(8):
                def emit_op(doc, ic):
                    def f(e):
                        last = None
                        for c in range(8):
                            last = e.matmul(
                                ps_m[:, :],
                                wob[:, c * 1024 + doc * 128: c * 1024 + doc * 128 + 128],
                                numT[:, c * HALF + ic * 512: c * HALF + ic * 512 + 512],
                                start=(c == 0), stop=(c == 7))
                        return last
                    return f
                hacc = p.op("tensor", emit_op(doc, ic),
                            deps=[h_wo, bar_attn, h_psm_free], n=8)
                blk = slice(doc * 512, doc * 512 + 512)
                hr = p.op("vector", (lambda blk: tt(scrA[:, blk], ps_m[:, :],
                                                    scrA[:, blk], ALU.add))(blk),
                          deps=[hacc, hx])
                h_psm_free = hr
                hnew.append(hr)
            ho = p.dma("gpsimd", (lambda ic: lambda e: e.dma_start(
                out=x1r3[:, :, slice(ic * 512, ic * 512 + 512)], in_=scrA3))(ic),
                deps=hnew)
            hout.append(ho)
            hra = hnew + [ho]
        p.op("gpsimd", lambda e: e.nop(), deps=hout)

    return p.barrier()





def emit_B(ctx, upto="full"):
    nc, p, bt, ft, PS, t = ctx["nc"], ctx["p"], ctx["bt"], ctx["ft"], ctx["PS"], ctx["t"]
    x1T = t["x1T"]
    w1T = t["w1T"]
    w2T = t["w2T"]
    g1c = t["g1c"]
    b1lnc = t["b1lnc"]
    b1c = t["b1c"]
    g2c = t["g2c"]
    b2lnc = t["b2lnc"]
    b2c = t["b2c"]
    onesf = t["onesf"]
    i128 = t["i128"]
    x2T = t["x2T"]
    it = ctx["it"]
    if True:
        ba = Arena(bt, GNB16)
        fa = Arena(ft, GNF32)

        dqw1 = ba.alloc(128, 8 * 4096)
        h1q = ba.alloc(128, 32 * 512)
        o_h1q = ba.last_off
        xq = ba.alloc(128, 8 * 512)
        w2q = ba.alloc(128, 1024)
        onesb = ba.alloc(128, 32)
        bsc = bt[0:128, o_h1q:o_h1q + 4096]

        onesfT = fa.alloc(1, 128)
        i128T = fa.alloc(128, 128)
        fcol = fa.alloc(128, 32)
        cols = fa.alloc(128, 160)
        rowsc = fa.alloc(128, 4096)
        scrA = fa.alloc(128, 4096)
        scrT = fa.alloc(128, 4096)
        scrU = fa.alloc(128, 4096)
        scrV = fa.alloc(128, 512)
        scrW = fa.alloc(128, 512)
        scrD = fa.alloc(128, 512)
        scrM = fa.alloc(128, 512)
        invs1B = fa.alloc(128, 512)
        invs2B = fa.alloc(128, 512)

        PSB = [PS[:, k * 512:(k + 1) * 512] for k in range(8)]

        h_of = p.dma("sync", lambda e: e.dma_start(out=onesfT, in_=onesf))
        h_i128 = p.dma("sync", lambda e: e.dma_start(out=i128T, in_=i128))
        for (dst0, w_, srcw) in ((0, 8, g1c), (8, 8, b1lnc), (16, 32, b1c),
                                 (48, 32, g2c), (80, 32, b2lnc), (112, 8, b2c)):
            p.dma("sync", (lambda dst0, w_, srcw: lambda e: e.dma_start(
                out=cols[:, dst0:dst0 + w_], in_=srcw))(dst0, w_, srcw))
        h_ones = p.op("vector", lambda e: e.memset(onesb[:, 0:1], 1.0))
        h_o1 = p.op("vector", lambda e: e.memset(onesb[:, 1:2], 1.0 / 1024.0),
                    deps=[h_ones])
        h_o2 = p.op("vector", lambda e: e.memset(onesb[:, 2:3], 1.0 / 4096.0),
                    deps=[h_o1])
        h_fc = p.op("vector", lambda e: e.memset(fcol[:, 0:1], 1.0 / 1024.0))
        h_fc2 = p.op("vector", lambda e: e.memset(fcol[:, 1:2], 1.0 / 4096.0),
                     deps=[h_fc])
        bar0 = p.barrier()

        def wmean(srcw, nchunks, ncols, out_idx, dep0):
            ncc = ncols // 512
            assert ncc <= 8
            hacc = [None] * ncc
            dep = list(dep0)
            habs = None
            for c in range(nchunks):
                hd = p.dma("sync", (lambda c: lambda e: e.dma_start(
                    out=scrU[:, 0:ncols], in_=srcw[c * 128:(c + 1) * 128, :]))(c),
                    deps=dep + ([habs] if habs else []))
                habs = p.op("scalar", (lambda ncols: act(bsc[:, 0:ncols],
                                                         scrU[:, 0:ncols], AF.Abs))(ncols),
                            deps=[hd])
                dep = []
                for cc in range(ncc):
                    hacc[cc] = p.op("tensor", (lambda cc, c: mm(
                        PSB[cc % 4][0:1, (cc // 4) * 512 if False else 0:512][0:1, 0:512]
                        if False else PSB[cc % 4][0:1, :], onesb[:, 0:1],
                        bsc[:, cc * 512:cc * 512 + 512],
                        start=(c == 0), stop=(c == nchunks - 1)))(cc, c),
                        deps=[habs, h_ones] + ([hacc[cc]] if hacc[cc] else []))
            hred = None
            for cc in range(ncc):
                h1 = p.op("vector", (lambda cc: cp(rowsc[0:1, 0:512],
                                                   PSB[cc % 4][0:1, :]))(cc),
                          deps=[hacc[cc]] + ([hred] if hred else []))
                h2 = p.op("vector", (lambda cc: lambda e: e.reduce_sum(
                    out=rowsc[0:1, 1024 + cc:1025 + cc], in_=rowsc[0:1, 0:512],
                    axis=mybir.AxisListType.X))(cc), deps=[h1])
                hred = h2
            h3 = p.op("vector", (lambda ncc, out_idx: lambda e: e.reduce_sum(
                out=rowsc[0:1, 1040 + out_idx:1041 + out_idx],
                in_=rowsc[0:1, 1024:1024 + ncc],
                axis=mybir.AxisListType.X))(ncc, out_idx), deps=[hred])
            return h3

        def wmean_cols(srcw, nchunks, ncols, out_idx, dep0):
            if ncols <= 2048:
                return wmean(srcw, nchunks, ncols, out_idx, dep0)
            h1 = wmean_sub(srcw, nchunks, ncols, 0, ncols // 2, 14, dep0)
            h2 = wmean_sub(srcw, nchunks, ncols, ncols // 2, ncols // 2, 15, [h1])
            h3 = p.op("vector", (lambda out_idx: lambda e: e.reduce_sum(
                out=rowsc[0:1, 1040 + out_idx:1041 + out_idx],
                in_=rowsc[0:1, 1038:1040],
                axis=mybir.AxisListType.X))(out_idx), deps=[h2])
            return h3

        def wmean_sub(srcw, nchunks, ncols, c0, cw, tmp_idx, dep0):
            ncc = cw // 512
            hacc = [None] * ncc
            dep = list(dep0)
            habs = None
            for c in range(nchunks):
                hd = p.dma("sync", (lambda c, c0, cw: lambda e: e.dma_start(
                    out=scrU[:, 0:cw], in_=srcw[c * 128:(c + 1) * 128, c0:c0 + cw]))(c, c0, cw),
                    deps=dep + ([habs] if habs else []))
                habs = p.op("scalar", (lambda cw: act(bsc[:, 0:cw],
                                                      scrU[:, 0:cw], AF.Abs))(cw),
                            deps=[hd])
                dep = []
                for cc in range(ncc):
                    hacc[cc] = p.op("tensor", (lambda cc, c: mm(
                        PSB[cc][0:1, :], onesb[:, 0:1],
                        bsc[:, cc * 512:cc * 512 + 512],
                        start=(c == 0), stop=(c == nchunks - 1)))(cc, c),
                        deps=[habs, h_ones] + ([hacc[cc]] if hacc[cc] else []))
            hred = None
            for cc in range(ncc):
                h1 = p.op("vector", (lambda cc: cp(rowsc[0:1, 0:512],
                                                   PSB[cc][0:1, :]))(cc),
                          deps=[hacc[cc]] + ([hred] if hred else []))
                h2 = p.op("vector", (lambda cc: lambda e: e.reduce_sum(
                    out=rowsc[0:1, 1024 + cc:1025 + cc], in_=rowsc[0:1, 0:512],
                    axis=mybir.AxisListType.X))(cc), deps=[h1])
                hred = h2
            h3 = p.op("vector", (lambda ncc, tmp_idx: lambda e: e.reduce_sum(
                out=rowsc[0:1, 1024 + tmp_idx:1025 + tmp_idx],
                in_=rowsc[0:1, 1024:1024 + ncc],
                axis=mybir.AxisListType.X))(ncc, tmp_idx), deps=[hred])
            return h3

        h_w1m = wmean_cols(w1T, 8, 4096, 0, bar0)
        h_w2m = wmean(w2T, 32, 1024, 1, [h_w1m])
        h5 = p.op("vector", ts(rowsc[0:1, 1044:1045], rowsc[0:1, 1040:1041],
                               1.0 / (1024 * 4096), ALU.mult, 1e-5, ALU.max),
                  deps=[h_w2m])
        h6 = p.op("vector", ts(rowsc[0:1, 1045:1046], rowsc[0:1, 1041:1042],
                               1.0 / (4096 * 1024), ALU.mult, 1e-5, ALU.max),
                  deps=[h5])
        hb = p.op("tensor", mm(PSB[7][:, 0:2], onesfT, rowsc[0:1, 1044:1046]),
                  deps=[h6, h_of])
        hc1 = p.op("vector", cp(cols[:, 124:126], PSB[7][:, 0:2]), deps=[hb])
        h_t1 = p.op("vector", ts(cols[:, 120:121], cols[:, 124:125], 0.5, ALU.mult),
                    deps=[hc1])
        h_t1n = p.op("vector", ts(cols[:, 121:122], cols[:, 120:121], -1.0, ALU.mult),
                     deps=[h_t1])
        h_t2 = p.op("vector", ts(cols[:, 122:123], cols[:, 125:126], 0.5, ALU.mult),
                    deps=[h_t1n])
        h_t2n = p.op("vector", ts(cols[:, 123:124], cols[:, 122:123], -1.0, ALU.mult),
                     deps=[h_t2])
        bar_mean = p.barrier()

        dep = bar_mean
        for c in range(8):
            hd = p.dma("sync", (lambda c: lambda e: e.dma_start(
                out=scrU, in_=w1T[c * 128:(c + 1) * 128, :]))(c), deps=dep)
            hp_ = p.op("vector", ts(scrT, scrU, cols[:, 120:121], ALU.is_gt),
                       deps=[hd, h_t1])
            hn_ = p.op("vector", ts(rowsc, scrU, cols[:, 121:122], ALU.is_lt),
                       deps=[hd, h_t1n])
            hq = p.op("vector", (lambda c: tt(dqw1[:, c * 4096:(c + 1) * 4096],
                                              scrT, rowsc, ALU.subtract))(c),
                      deps=[hp_, hn_])
            dep = [hq, hp_, hn_]
        bar_w1 = p.barrier()

        prev_stats = [bar_w1]

        def row_stats(hsum, hsqs):
            nonlocal_box = prev_stats
            rm = rowsc[0:1, 0:512]
            rv = rowsc[0:1, 512:1024]
            rr = rowsc[0:1, 2048:2560]
            rmb = rowsc[0:1, 2560:3072]
            h1 = p.op("vector", cp(rm, PSB[0][0:1, :]),
                      deps=[hsum] + list(nonlocal_box))
            h2 = p.op("scalar", act(rv, rm, AF.Square), deps=[h1])
            h3 = p.op("vector", stt(rv, rv, -1.0, PSB[1][0:1, :], ALU.mult, ALU.add),
                      deps=[h2, hsqs])
            h3b = p.op("vector", ts(rv, rv, EPS, ALU.add), deps=[h3])
            h4 = p.op("scalar", act(rr, rv, AF.Ln), deps=[h3b])
            h5_ = p.op("scalar", act(rr, rr, AF.Exp, scale=-0.5), deps=[h4])
            h6_ = p.op("vector", stt(rmb, rm, -1.0, rr, ALU.mult, ALU.mult), deps=[h5_])
            hb1 = p.op("tensor", mm(PSB[3][:, :], onesfT, rr), deps=[h5_, h_of])
            hb2 = p.op("tensor", mm(PSB[4][:, :], onesfT, rmb), deps=[h6_])
            nonlocal_box.clear()
            nonlocal_box.extend([hb1, hb2, h6_])
            return [hb1, hb2]

        def token_scale(mean_col, invs_dstB, habs_last):
            hc1 = p.op("vector", cp(scrV[0:64, :], scrM[64:128, :]), deps=[habs_last])
            hm1 = p.op("vector", tt(scrM[0:64, :], scrM[0:64, :], scrV[0:64, :],
                                    ALU.max), deps=[hc1])
            hc2 = p.op("vector", cp(scrV[0:32, :], scrM[32:64, :]), deps=[hm1])
            hm2 = p.op("vector", tt(scrM[0:32, :], scrM[0:32, :], scrV[0:32, :],
                                    ALU.max), deps=[hc2])
            htr = []
            for g in range(4):
                h1 = p.op("tensor", (lambda g: lambda e: e.transpose(
                    PSB[5][0:128, g * 32:(g + 1) * 32],
                    scrM[0:32, g * 128:(g + 1) * 128],
                    i128T[0:32, 0:32]))(g),
                    deps=[hm2, h_i128] + ([htr[-1]] if htr else []))
                h2 = p.op("vector", (lambda g: lambda e: e.reduce_max(
                    out=scrD[:, g:g + 1], in_=PSB[5][0:128, g * 32:(g + 1) * 32],
                    axis=mybir.AxisListType.X))(g), deps=[h1])
                htr.append(h2)
            hcl = p.op("vector", ts(scrD[:, 0:4], scrD[:, 0:4], 1e-5, ALU.max),
                       deps=htr)
            hs = p.op("vector", lambda e: e.reciprocal(out=scrD[:, 4:8],
                                                       in_=scrD[:, 0:4]), deps=[hcl])
            hs2 = p.op("vector", ts(scrD[:, 4:8], scrD[:, 4:8], 127.0, ALU.mult),
                       deps=[hs])
            hi = p.op("vector", ts(scrD[:, 8:12], scrD[:, 0:4],
                                   cols[:, mean_col:mean_col + 1],
                                   ALU.mult, 1.0 / 127.0, ALU.mult), deps=[hcl])
            hrow = []
            for gi, srcc in enumerate((4, 5, 6, 7, 8, 9, 10, 11)):
                dstc = gi * 128
                h1 = p.op("tensor", (lambda srcc: lambda e: e.transpose(
                    PSB[5][0:1, 0:128], scrD[:, srcc:srcc + 1], i128T))(srcc),
                    deps=[hs2, hi, h_i128] + ([hrow[-1]] if hrow else []))
                h2 = p.op("vector", (lambda dstc: cp(
                    rowsc[0:1, 1024 + dstc:1024 + dstc + 128],
                    PSB[5][0:1, 0:128]))(dstc), deps=[h1])
                hrow.append(h2)
            hbs = p.op("tensor", mm(PSB[2][:, :], onesfT, rowsc[0:1, 1024:1536]),
                       deps=[hrow[-1], h_of])
            hbi = p.op("tensor", mm(PSB[5][0:128, :], onesfT, rowsc[0:1, 1536:2048]),
                       deps=[hrow[-1], hbs])
            hbc = p.op("scalar", (lambda: act(invs_dstB, PSB[5][0:128, :],
                                              AF.Identity))(), deps=[hbi])
            return [hbs, hbc]

        def quant_wide(src_ap, dst_ap, w, deps):
            r = w // 512
            fsc = scrT[:, 0:w]
            f3 = fsc.rearrange("p (r t) -> p r t", t=512)
            s3 = src_ap.rearrange("p (r t) -> p r t", t=512)
            sc3 = PSB[2][:, :].unsqueeze(1).broadcast_to((128, r, 512))
            h1 = p.op("vector", (lambda f3, s3, sc3: lambda e: e.tensor_tensor(
                out=f3, in0=s3, in1=sc3, op=ALU.mult))(f3, s3, sc3), deps=deps)
            h2 = p.op("vector", ts(fsc, fsc, 127.4999, ALU.min,
                                   -128.4999, ALU.max), deps=[h1])
            h3 = p.op("vector", (lambda fsc, w: lambda e: e.tensor_scalar_add(
                out=it[:, 0:w], in0=fsc, scalar1=0.0))(fsc, w), deps=[h2])
            h4 = p.op("vector", (lambda fsc, w: cp(fsc, it[:, 0:w]))(fsc, w),
                      deps=[h3])
            h5_ = p.op("vector", cp(dst_ap, fsc), deps=[h4])
            return h5_

        x1r3 = x1T.rearrange("(c p) i -> p c i", p=128)
        x2r3 = x2T.rearrange("(c p) i -> p c i", p=128)
        scrA3 = scrA.rearrange("p (c i) -> p c i", i=512)

        h_prev = bar_w1
        for Q in range(4):
            qtok = slice(Q * 512, Q * 512 + 512)
            hx = p.dma("sync", (lambda qtok: lambda e: e.dma_start(
                out=scrA3, in_=x1r3[:, :, qtok]))(qtok), deps=h_prev)
            def emit_sum8(dst, lhs_col, srct):
                def f(e):
                    last = None
                    for c in range(8):
                        last = e.matmul(dst, lhs_col,
                                        srct[:, c * 512:(c + 1) * 512],
                                        start=(c == 0), stop=(c == 7))
                    return last
                return f
            hsum = p.op("tensor", emit_sum8(PSB[0][0:1, :], fcol[:, 0:1], scrA),
                        deps=[hx, h_fc] + list(h_prev), n=8)
            hsq = p.op("scalar", act(bsc, scrA, AF.Square), deps=[hx])
            hsqs = p.op("tensor", emit_sum8(PSB[1][0:1, :], onesb[:, 1:2], bsc),
                        deps=[hsq, h_o1], n=8)
            hstats = row_stats(hsum, hsqs)
            happ = []
            hpv = None
            for c in range(8):
                blk = slice(c * 512, c * 512 + 512)
                ha = p.op("vector", (lambda blk: tt(scrT[:, blk], scrA[:, blk],
                                                    PSB[3][:, :], ALU.mult))(blk),
                          deps=hstats + ([hpv] if hpv else []))
                hb_ = p.op("vector", (lambda blk: tt(scrT[:, blk], scrT[:, blk],
                                                     PSB[4][:, :], ALU.add))(blk),
                          deps=[ha])
                hpv = hb_
                happ.append(hb_)
            hsum = p.op("tensor", emit_sum8(PSB[0][0:1, :], fcol[:, 0:1], scrT),
                        deps=happ + [hstats[0]], n=8)
            hsq = p.op("scalar", act(bsc, scrT, AF.Square), deps=happ)
            hsqs = p.op("tensor", emit_sum8(PSB[1][0:1, :], onesb[:, 1:2], bsc),
                        deps=[hsq], n=8)
            hstats = row_stats(hsum, hsqs)
            habs = None
            hq1dep = []
            for c in range(8):
                blk = slice(c * 512, c * 512 + 512)
                h1 = p.op("vector", (lambda blk: tt(scrU[:, blk], scrT[:, blk],
                                                    PSB[3][:, :], ALU.mult))(blk),
                          deps=hstats + hq1dep[-1:])
                h2 = p.op("vector", (lambda blk: tt(scrU[:, blk], scrU[:, blk],
                                                    PSB[4][:, :], ALU.add))(blk),
                          deps=[h1])
                h3 = p.op("vector", (lambda blk, c: ts(scrU[:, blk], scrU[:, blk],
                                                       cols[:, c:c + 1], ALU.mult,
                                                       cols[:, 8 + c:9 + c],
                                                       ALU.add))(blk, c),
                          deps=[h2])
                h4 = p.op("scalar", (lambda blk: act(scrV, scrU[:, blk], AF.Abs))(blk),
                          deps=[h3])
                if habs is None:
                    habs = p.op("vector", cp(scrM, scrV), deps=[h4])
                else:
                    habs = p.op("vector", tt(scrM, scrM, scrV, ALU.max),
                                deps=[h4, habs])
                hq1dep = [habs]
            hts = token_scale(124, invs1B, habs)
            hq = hts
            for c in range(2):
                blk = slice(c * 2048, c * 2048 + 2048)
                hq = [quant_wide(scrU[:, blk], xq[:, blk], 2048, hq + hts)]
            bar_q1 = p.barrier()
            h_ev = {6: bar_q1, 7: bar_q1}
            hgel = []
            for hc in range(32):
                bank = 6 + (hc % 2)

                def emit_mm1(hc, bank):
                    def f(e):
                        last = None
                        for c in range(8):
                            last = e.matmul(
                                PSB[bank][:, :],
                                dqw1[:, c * 4096 + hc * 128: c * 4096 + hc * 128 + 128],
                                xq[:, c * 512:(c + 1) * 512],
                                start=(c == 0), stop=(c == 7))
                        return last
                    return f
                hacc = p.op("tensor", emit_mm1(hc, bank),
                            deps=[h_ev[bank], bar_q1], n=8)
                he1 = p.op("vector", (lambda bank: tt(scrV, PSB[bank][:, :],
                                                      invs1B, ALU.mult))(bank),
                           deps=[hacc] + hgel[-1:])
                he2 = p.op("scalar", (lambda hc: act(
                    h1q[:, hc * 512:(hc + 1) * 512], scrV, AF.Gelu,
                    bias=cols[:, 16 + hc:17 + hc]))(hc), deps=[he1])
                h_ev[bank] = he1
                hgel.append(he2)
            hsum = None
            hsqs = None
            hsqp = None
            for hc in range(32):
                blk = slice(hc * 512, hc * 512 + 512)
                hsum = p.op("tensor", (lambda blk, st, sp: mm(
                    PSB[0][0:1, :], onesb[:, 2:3], h1q[:, blk],
                    start=st, stop=sp))(blk, hc == 0, hc == 31),
                    deps=[hgel[hc], h_o2] + ([hsum] if hsum else []))
                hsqp = p.op("scalar", (lambda blk: act(xq[:, 0:512], h1q[:, blk],
                                                       AF.Square))(blk),
                            deps=[hgel[hc]] + ([hsqs] if hsqs else []))
                hsqs = p.op("tensor", (lambda st, sp: mm(
                    PSB[1][0:1, :], onesb[:, 2:3], xq[:, 0:512],
                    start=st, stop=sp))(hc == 0, hc == 31),
                    deps=[hsqp])
            hstats = row_stats(hsum, hsqs)
            habs = None
            hq2dep = []
            for hc in range(32):
                blk = slice(hc * 512, hc * 512 + 512)
                h1 = p.op("vector", (lambda blk: tt(scrV, h1q[:, blk],
                                                    PSB[3][:, :], ALU.mult))(blk),
                          deps=hstats + hq2dep[-1:])
                h2 = p.op("vector", (lambda blk: tt(scrV, scrV,
                                                    PSB[4][:, :], ALU.add))(blk),
                          deps=[h1])
                h3 = p.op("vector", (lambda blk, hc: ts(scrV, scrV,
                                                        cols[:, 48 + hc:49 + hc],
                                                        ALU.mult,
                                                        cols[:, 80 + hc:81 + hc],
                                                        ALU.add))(blk, hc),
                          deps=[h2])
                h4 = p.op("scalar", act(scrW, scrV, AF.Abs), deps=[h3])
                if habs is None:
                    habs = p.op("vector", cp(scrM, scrW), deps=[h4])
                else:
                    habs = p.op("vector", tt(scrM, scrM, scrW, ALU.max),
                                deps=[h4, habs])
                h5_ = p.op("vector", (lambda blk: cp(h1q[:, blk], scrV))(blk),
                           deps=[h3, habs])
                hq2dep = [h5_, habs]
            hts2 = token_scale(125, invs2B, habs)
            hq = hts2 + hq2dep
            for hc4 in range(8):
                blk = slice(hc4 * 2048, hc4 * 2048 + 2048)
                hq = [quant_wide(h1q[:, blk], h1q[:, blk], 2048, hq + hts2)]
            bar_q2 = p.barrier()
            hw2 = bar_q2
            hmm2 = {d: None for d in range(8)}
            for hc in range(32):
                hd = p.dma("sync", (lambda hc: lambda e: e.dma_start(
                    out=scrU[:, 0:1024], in_=w2T[hc * 128:(hc + 1) * 128, :]))(hc),
                    deps=[hw2] if not isinstance(hw2, list) else hw2)
                hp_ = p.op("vector", ts(scrT[:, 0:1024], scrU[:, 0:1024],
                                        cols[:, 122:123], ALU.is_gt),
                           deps=[hd, h_t2])
                hn_ = p.op("vector", ts(scrT[:, 1024:2048], scrU[:, 0:1024],
                                        cols[:, 123:124], ALU.is_lt),
                           deps=[hd, h_t2n])
                hw2q = p.op("vector", tt(w2q, scrT[:, 0:1024], scrT[:, 1024:2048],
                                         ALU.subtract), deps=[hp_, hn_])
                def emit_mm2(hc):
                    def f(e):
                        last = None
                        for dout in range(8):
                            last = e.matmul(PSB[dout][:, :],
                                            w2q[:, dout * 128:(dout + 1) * 128],
                                            h1q[:, hc * 512:(hc + 1) * 512],
                                            start=(hc == 0), stop=(hc == 31))
                        return last
                    return f
                hall = p.op("tensor", emit_mm2(hc),
                            deps=[hw2q, bar_q2] + ([hmm2[7]] if hmm2[7] else []),
                            n=8)
                for dout in range(8):
                    hmm2[dout] = hall
                hw2 = [hall]
            hev = []
            for dout in range(8):
                blk = slice(dout * 512, dout * 512 + 512)
                he1 = p.op("vector", (lambda dout: tt(scrV, PSB[dout][:, :],
                                                      invs2B, ALU.mult))(dout),
                           deps=[hmm2[dout]] + hev[-1:])
                he2 = p.op("scalar", (lambda dout: act(scrW, scrV, AF.Identity,
                                                       bias=cols[:, 112 + dout:113 + dout]))(dout),
                           deps=[he1])
                he3 = p.op("vector", (lambda blk: tt(scrA[:, blk], scrW,
                                                     scrA[:, blk], ALU.add))(blk),
                           deps=[he2])
                hev.append(he3)
            ho = p.dma("gpsimd", (lambda Q: lambda e: e.dma_start(
                out=x2r3[:, :, slice(Q * 512, Q * 512 + 512)], in_=scrA3))(Q),
                deps=hev)
            h_prev = p.barrier()
    return p.barrier()



def make_B_inputs(inputs, x1):
    w1T = np.ascontiguousarray(inputs["L0_ffn_w1"].T.astype(np.float32))
    w2T = np.ascontiguousarray(inputs["L0_ffn_w2"].T.astype(np.float32))

    def colpack(v):
        v = np.asarray(v, dtype=np.float32)
        return np.ascontiguousarray(v.reshape(-1, 128).T)

    common = dict(
        w1T=w1T, w2T=w2T,
        g1c=colpack(inputs["L0_ffn_ln1_g"]), b1lnc=colpack(inputs["L0_ffn_ln1_b"]),
        b1c=colpack(inputs["L0_ffn_b1"]), g2c=colpack(inputs["L0_ffn_ln2_g"]),
        b2lnc=colpack(inputs["L0_ffn_ln2_b"]), b2c=colpack(inputs["L0_ffn_b2"]),
        onesf=np.ones((1, 128), dtype=np.float32),
        i128=np.eye(128, dtype=np.float32))
    in_maps = []
    for c in range(NCORES):
        b, half = c // 2, c % 2
        m = dict(common)
        m["x1T"] = np.ascontiguousarray(
            x1[b, half * HALF:(half + 1) * HALF].T.astype(np.float32))
        in_maps.append(m)
    return in_maps




def emit_C1(ctx, upto="full"):
    nc, p, bt, ft, PS, t = ctx["nc"], ctx["p"], ctx["bt"], ctx["ft"], ctx["PS"], ctx["t"]
    x2T = t["x2T"]
    x2oT = t["x2oT"]
    wqkvT = t["wqkvT"]
    onesf = t["onesf"]
    qd = t["qd"]
    kd = t["kd"]
    vd = t["vd"]
    x2chunks = ctx.get("x2chunks")
    if True:
        ba = Arena(bt, GNB16)
        fa = Arena(ft, GNF32)

        wb = ba.alloc(128, 8 * 3072)
        xlnc = ba.alloc(128, 4096)
        sqb = ba.alloc(128, 4096)
        onesb = ba.alloc(128, 32)
        evb = ba.alloc(128, 4096)

        onesfT = fa.alloc(1, 128)
        fcol = fa.alloc(128, 32)
        rowsc = fa.alloc(128, 4096)
        scrA = fa.alloc(128, 4096)
        scrB = fa.alloc(128, 1024)

        PSB = [PS[:, k * 512:(k + 1) * 512] for k in range(8)]

        h_of = p.dma("sync", lambda e: e.dma_start(out=onesfT, in_=onesf))
        h_ones = p.op("vector", lambda e: e.memset(onesb[:, 0:1], 1.0))
        h_o1 = p.op("vector", lambda e: e.memset(onesb[:, 1:2], 1.0 / 1024.0),
                    deps=[h_ones])
        h_fc = p.op("vector", lambda e: e.memset(fcol[:, 0:1], 1.0 / 1024.0))

        dep = [h_of]
        h_w = None
        for c in range(8):
            hd = p.dma("sync", (lambda c: lambda e: e.dma_start(
                out=scrA[:, 0:3072], in_=wqkvT[c * 128:(c + 1) * 128, :]))(c),
                deps=dep)
            h_w = p.op("vector", (lambda c: cp(wb[:, c * 3072:(c + 1) * 3072],
                                               scrA[:, 0:3072]))(c), deps=[hd])
            dep = [h_w]
        bar_w = p.barrier()

        x2r3 = x2T.rearrange("(c p) i -> p c i", p=128)
        x2o3 = x2oT.rearrange("(c p) i -> p c i", p=128)
        scrA3 = scrA.rearrange("p (c i) -> p c i", i=512)
        qd3 = qd.rearrange("(c p) i -> p c i", p=128)
        kd3 = kd.rearrange("(c p) i -> p c i", p=128)

        def ln_chunk(src3, tok, deps):
            hx = p.dma("sync", (lambda tok: lambda e: e.dma_start(
                out=scrA3, in_=src3[:, :, tok]))(tok), deps=deps)
            hsq = p.op("scalar", act(sqb, scrA, AF.Square), deps=[hx])

            def emit_sum(dst, lhs_col, srct):
                def f(e):
                    last = None
                    for c in range(8):
                        last = e.matmul(dst, lhs_col,
                                        srct[:, c * 512:(c + 1) * 512],
                                        start=(c == 0), stop=(c == 7))
                    return last
                return f
            hsum = p.op("tensor", emit_sum(PSB[0][0:1, :], fcol[:, 0:1], scrA),
                        deps=[hx, h_fc] + deps, n=8)
            hsqs = p.op("tensor", emit_sum(PSB[1][0:1, :], onesb[:, 1:2], sqb),
                        deps=[hsq, h_o1], n=8)
            rm = rowsc[0:1, 0:512]
            rv = rowsc[0:1, 512:1024]
            rr = rowsc[0:1, 1024:1536]
            rmb = rowsc[0:1, 1536:2048]
            h1 = p.op("vector", cp(rm, PSB[0][0:1, :]), deps=[hsum] + deps)
            h2 = p.op("scalar", act(rv, rm, AF.Square), deps=[h1])
            h3 = p.op("vector", stt(rv, rv, -1.0, PSB[1][0:1, :], ALU.mult, ALU.add),
                      deps=[h2, hsqs])
            h3b = p.op("vector", ts(rv, rv, EPS, ALU.add), deps=[h3])
            h4 = p.op("scalar", act(rr, rv, AF.Ln), deps=[h3b])
            h5 = p.op("scalar", act(rr, rr, AF.Exp, scale=-0.5), deps=[h4])
            h6 = p.op("vector", stt(rmb, rm, -1.0, rr, ALU.mult, ALU.mult), deps=[h5])
            hb1 = p.op("tensor", mm(PSB[3][:, :], onesfT, rr), deps=[h5, h_of])
            hb2 = p.op("tensor", mm(PSB[4][:, :], onesfT, rmb), deps=[h6])
            happ = []
            hpv = None
            for c in range(8):
                blk = slice(c * 512, c * 512 + 512)
                ha = p.op("vector", (lambda blk: tt(scrB[:, 0:512], scrA[:, blk],
                                                    PSB[3][:, :], ALU.mult))(blk),
                          deps=[hb1, hx] + ([hpv] if hpv else []))
                hb_ = p.op("vector", (lambda blk: tt(xlnc[:, blk], scrB[:, 0:512],
                                                     PSB[4][:, :], ALU.add))(blk),
                          deps=[ha, hb2])
                hpv = hb_
                happ.append(hb_)
            return happ, [h1, h3, h6, hb1, hb2]

        def ln_chunk2(src_ap, deps):
            class _W:
                def __getitem__(self, idx):
                    return src_ap
            return ln_chunk(_W(), None, deps)

        h_prev = bar_w
        for ic8 in range(8):
            tok = slice(ic8 * 512, ic8 * 512 + 512)
            if x2chunks is not None:
                happ, hstats = ln_chunk2(x2chunks[ic8],
                                         h_prev if isinstance(h_prev, list) else [h_prev])
            else:
                happ, hstats = ln_chunk(x2r3, tok, h_prev if isinstance(h_prev, list) else [h_prev])
            hk = []
            for outc in range(8):
                def emit_proj(outc, base):
                    def f(e):
                        last = None
                        for c in range(8):
                            last = e.matmul(
                                PSB[outc][:, :],
                                wb[:, c * 3072 + base + outc * 128:
                                   c * 3072 + base + outc * 128 + 128],
                                xlnc[:, c * 512:(c + 1) * 512],
                                start=(c == 0), stop=(c == 7))
                        return last
                    return f
                hacc = p.op("tensor", emit_proj(outc, 1024),
                            deps=happ + hstats + [bar_w] + hk[-1:], n=8)
                he = p.op("vector", (lambda outc: cp(
                    evb[:, outc * 512:(outc + 1) * 512], PSB[outc][:, :]))(outc),
                    deps=[hacc])
                hk.append(he)
            hko = p.dma("gpsimd", (lambda tok: lambda e: e.dma_start(
                out=kd3[:, :, tok],
                in_=evb.rearrange("p (c i) -> p c i", i=512)))(tok), deps=hk)
            hv = []
            for jq in range(4):
                jc = ic8 * 4 + jq

                def emit_vproj(jq):
                    def f(e):
                        last = None
                        for c in range(8):
                            lhs = xlnc[:, c * 512 + jq * 128: c * 512 + jq * 128 + 128]
                            e.matmul(PSB[0][:, :], lhs,
                                     wb[:, c * 3072 + 2048: c * 3072 + 2048 + 512],
                                     start=(c == 0), stop=(c == 7))
                            last = e.matmul(
                                PSB[1][:, :], lhs,
                                wb[:, c * 3072 + 2560: c * 3072 + 2560 + 512],
                                start=(c == 0), stop=(c == 7))
                        return last
                    return f
                hacc = p.op("tensor", emit_vproj(jq), deps=happ + hk + hv, n=16)
                hev = p.op("vector", cp(evb[:, 0:512], PSB[0][:, :]),
                           deps=[hacc, hko])
                hev2 = p.op("vector", cp(evb[:, 512:1024], PSB[1][:, :]),
                            deps=[hacc, hev])
                hvo = p.dma("gpsimd", (lambda jc: lambda e: e.dma_start(
                    out=vd[jc * 128:(jc + 1) * 128, :], in_=evb[:, 0:1024]))(jc),
                    deps=[hev, hev2])
                hv = [hev, hev2, hvo]
            h_prev = happ + hstats + hk + hv + [hko]

        for oc in range(4):
            otok = slice(oc * 512, oc * 512 + 512)
            happ, hstats = ln_chunk(x2o3, otok, h_prev if isinstance(h_prev, list) else [h_prev])
            hk = []
            for outc in range(8):
                def emit_proj(outc, base):
                    def f(e):
                        last = None
                        for c in range(8):
                            last = e.matmul(
                                PSB[outc][:, :],
                                wb[:, c * 3072 + base + outc * 128:
                                   c * 3072 + base + outc * 128 + 128],
                                xlnc[:, c * 512:(c + 1) * 512],
                                start=(c == 0), stop=(c == 7))
                        return last
                    return f
                hacc = p.op("tensor", emit_proj(outc, 0),
                            deps=happ + hstats + [bar_w] + hk[-1:], n=8)
                he = p.op("vector", (lambda outc: cp(
                    evb[:, outc * 512:(outc + 1) * 512], PSB[outc][:, :]))(outc),
                    deps=[hacc])
                hk.append(he)
            hqo = p.dma("gpsimd", (lambda otok: lambda e: e.dma_start(
                out=qd3[:, :, otok],
                in_=evb.rearrange("p (c i) -> p c i", i=512)))(otok), deps=hk)
            h_prev = happ + hstats + hk + [hqo]

    return p.barrier()




def emit_C2(ctx, upto="full"):
    nc, p, bt, ft, PS, t = ctx["nc"], ctx["p"], ctx["bt"], ctx["ft"], ctx["PS"], ctx["t"]
    qdi = t["qd"]
    kdi = t["kd"]
    i128 = t["i128"]
    mrowP = t["mrowP"]
    if True:
        ba = Arena(bt, GNB16)
        fa = Arena(ft, GNF32)

        qT = ba.alloc(128, 8 * HALF)
        kT = ba.alloc(128, 8 * S)
        screxp = ba.alloc(128, 1024)

        i128f = fa.alloc(128, 128)
        dtile = fa.alloc(128, 128)
        dparts = fa.alloc(128, 128)
        denP = fa.alloc(128, 256)
        diagP = fa.alloc(128, 256)
        negc = fa.alloc(128, 32)

        h_i = p.dma("sync", lambda e: e.dma_start(out=i128f, in_=i128))
        h_negc = p.op("vector", lambda e: e.memset(negc[:, 0:1], -C_L1))
        qd3 = qdi.rearrange("(c p) i -> p c i", p=128)
        kd3 = kdi.rearrange("(c p) i -> p c i", p=128)
        h_lq = p.dma("sync", lambda e: e.dma_start(
            out=qT.rearrange("p (c i) -> p c i", i=HALF), in_=qd3))
        h_lk = p.dma("sync", lambda e: e.dma_start(
            out=kT.rearrange("p (c i) -> p c i", i=S), in_=kd3))
        bar_l = p.barrier()

        h_expfree = [bar_l, bar_l]
        h_dp_readers = [bar_l]
        for c in range(8):
            hA, hB = 2 * c, 2 * c + 1
            for ic in range(16):
                hexs = []
                h_dg = None
                for jp in range(4):
                    def emit_sp(c, ic, jp):
                        def f(e):
                            last = None
                            for jj in range(2):
                                j = jp * 2 + jj
                                for sb in (0, 64):
                                    last = e.matmul(
                                        PS[:, (sb // 64) * 2048 + (jp % 2) * 1024
                                           + jj * 512:
                                           (sb // 64) * 2048 + (jp % 2) * 1024
                                           + jj * 512 + 512],
                                        qT[sb:sb + 64,
                                           c * HALF + ic * 128: c * HALF + ic * 128 + 128],
                                        kT[sb:sb + 64, c * S + j * 512: c * S + j * 512 + 512],
                                        start=True, stop=True)
                            return last
                        return f
                    sdeps = [h_expfree[jp % 2], bar_l]
                    if h_dg is not None and jp == (ic // 8) + 2:
                        sdeps.append(h_dg)
                    hs = p.op("tensor", emit_sp(c, ic, jp), deps=sdeps, n=4)
                    hx = []
                    for side in range(2):
                        hx.append(p.op("scalar", (lambda side, jp, ic: act(
                            screxp, PS[:, side * 2048 + (jp % 2) * 1024:
                                       side * 2048 + (jp % 2) * 1024 + 1024],
                            AF.Exp, bias=negc[:, 0:1], scale=0.125,
                            accum=dparts[:, ic * 8 + side * 4 + jp:
                                         ic * 8 + side * 4 + jp + 1]))(side, jp, ic),
                            deps=[hs, h_negc] + h_dp_readers))
                    h_expfree[jp % 2] = hx
                    hexs.extend(hx)
                    if jp == ic // 8:
                        jd = ic // 4
                        off = (ic * 128) % 512
                        hdt = None
                        for side, hgl in ((0, hA), (1, hB)):
                            cb = side * 2048 + (jp % 2) * 1024 + (jd % 2) * 512 + off
                            ht1 = p.op("vector", (lambda cb: tt(
                                dtile, PS[:, cb:cb + 128], i128f, ALU.mult))(cb),
                                deps=[hs, h_i] + ([hdt] if hdt else []))
                            hdt = p.op("vector", (lambda hgl, ic: lambda e:
                                       e.reduce_sum(
                                           out=diagP[:, hgl * 16 + ic:
                                                     hgl * 16 + ic + 1],
                                           in_=dtile,
                                           axis=mybir.AxisListType.X))(hgl, ic),
                                       deps=[ht1])
                        h_dg = hdt
                hdrs = []
                for side, hgl in ((0, hA), (1, hB)):
                    hdrs.append(p.op("vector", (lambda hgl, ic, side: lambda e:
                                     e.reduce_sum(
                                         out=denP[:, hgl * 16 + ic:
                                                  hgl * 16 + ic + 1],
                                         in_=dparts[:, ic * 8 + side * 4:
                                                    ic * 8 + side * 4 + 4],
                                         axis=mybir.AxisListType.X))(hgl, ic, side),
                                     deps=hexs))
                h_dp_readers = hdrs
        bar_s = p.barrier()
        hr = p.op("vector", lambda e: e.reciprocal(out=denP, in_=denP),
                  deps=bar_s)
        he = p.op("scalar", act(diagP, diagP, AF.Exp, bias=negc[:, 0:1],
                                scale=0.125), deps=bar_s)
        hm = p.op("vector", tt(diagP, diagP, denP, ALU.mult), deps=[hr, he])
        p.dma("gpsimd", lambda e: e.dma_start(out=mrowP, in_=diagP),
              deps=[hm])
    return p.barrier()


def emit_C2_old(ctx, upto="full"):
    nc, p, bt, ft, PS, t = ctx["nc"], ctx["p"], ctx["bt"], ctx["ft"], ctx["PS"], ctx["t"]
    qdi = t["qd"]
    kdi = t["kd"]
    koi = t["koT"]
    onesf = t["onesf"]
    mrow = t["mrow"]
    if True:
        ba = Arena(bt, GNB16)
        fa = Arena(ft, GNF32)

        qT = ba.alloc(128, 8 * HALF)
        kT = ba.alloc(128, 8 * S)
        kos = ba.alloc(128, 2048)
        PTb = [ba.alloc(128, 1536), ba.alloc(128, 1536)]
        onesb = ba.alloc(128, 32)

        onesfT = fa.alloc(1, 128)
        negc = fa.alloc(128, 32)
        rowsc = fa.alloc(128, 4096)

        ps_s = [PS[:, 0:1536], PS[:, 1536:3072]]
        ps_r = PS[:, 3072:3584]
        ps_m = PS[:, 3584:4096]

        h_of = p.dma("sync", lambda e: e.dma_start(out=onesfT, in_=onesf))
        h_ones = p.op("vector", lambda e: e.memset(onesb[:, 0:1], 1.0))
        h_negc = p.op("vector", lambda e: e.memset(negc[:, 0:1], -C_L1))
        h_onef = p.op("vector", lambda e: e.memset(negc[:, 1:2], 1.0))

        qd3 = qdi.rearrange("(c p) i -> p c i", p=128)
        kd3 = kdi.rearrange("(c p) i -> p c i", p=128)
        h_lq = p.dma("sync", lambda e: e.dma_start(
            out=qT.rearrange("p (c i) -> p c i", i=HALF), in_=qd3))
        h_lk = p.dma("sync", lambda e: e.dma_start(
            out=kT.rearrange("p (c i) -> p c i", i=S), in_=kd3))
        bar_l = p.barrier()

        h_ptfree = [bar_l, bar_l]
        h_rs_free = bar_l
        h_kos = bar_l
        for h in range(16):
            hc_, hb_ = h // 2, (h % 2) * 64
            if hb_ == 0:
                h_kos = p.dma("sync", (lambda hc_: lambda e: e.dma_start(
                    out=kos, in_=koi[hc_ * 128:(hc_ + 1) * 128, :]))(hc_),
                    deps=[h_rs_free] if not isinstance(h_rs_free, list) else h_rs_free)
            hpr = p.op("vector", (lambda hc_, hb_: tt(
                rowsc[hb_:hb_ + 64, 0:2048],
                qT[hb_:hb_ + 64, hc_ * HALF:(hc_ + 1) * HALF],
                kos[hb_:hb_ + 64, :],
                ALU.mult))(hc_, hb_), deps=[bar_l, h_rs_free, h_kos])
            hdg = []
            for ic in range(4):
                hdg.append(p.op("tensor", (lambda hb_, ic: mm(
                    ps_m[0:1, 0:512], negc[hb_:hb_ + 64, 1:2],
                    rowsc[hb_:hb_ + 64, ic * 512:(ic + 1) * 512]))(hb_, ic),
                    deps=[hpr, h_onef] + ([hdg[-1]] if hdg else [])))
                hcpd = p.op("vector", (lambda ic: cp(
                    rowsc[0:1, 2048 + ic * 512: 2048 + ic * 512 + 512],
                    ps_m[0:1, 0:512]))(ic), deps=[hdg[-1]])
                hdg.append(hcpd)
            h_diag = hdg[-1]
            for ic in range(4):
                buf = 0
                h_sum = None
                for (j0, gn) in SGROUPS:
                    def emit_mm(buf, j0, gn, hc_, hb_, ic):
                        def f(e):
                            last = None
                            for t_ in range(gn):
                                jc = j0 + t_
                                lhs = kT[hb_:hb_ + 64,
                                         hc_ * S + jc * 128: hc_ * S + jc * 128 + 128]
                                rhs = qT[hb_:hb_ + 64,
                                         hc_ * HALF + ic * 512: hc_ * HALF + ic * 512 + 512]
                                last = e.matmul(ps_s[buf][:, t_ * 512:(t_ + 1) * 512],
                                                lhs, rhs, start=True, stop=True)
                            return last
                        return f
                    hmg = p.op("tensor", emit_mm(buf, j0, gn, hc_, hb_, ic),
                               deps=[h_ptfree[buf], bar_l], n=gn)
                    hex_ = p.op("scalar", (lambda buf, gn: act(
                        PTb[buf][:, 0:gn * 512], ps_s[buf][:, 0:gn * 512],
                        AF.Exp, bias=negc[:, 0:1], scale=0.125))(buf, gn),
                        deps=[hmg, h_negc])
                    def emit_sum(buf, j0, gn):
                        def f(e):
                            last = None
                            for t_ in range(gn):
                                jc = j0 + t_
                                last = e.matmul(ps_r[0:1, 0:512], onesb[:, 0:1],
                                                PTb[buf][:, t_ * 512:(t_ + 1) * 512],
                                                start=(jc == 0), stop=(jc == 31))
                            return last
                        return f
                    h_sum = p.op("tensor", emit_sum(buf, j0, gn),
                                 deps=[hex_, h_rs_free], n=gn)
                    h_ptfree[buf] = h_sum
                    buf ^= 1
                rsr = rowsc[0:1, 1024:1536]
                hcp = p.op("vector", cp(rsr, ps_r[0:1, 0:512]), deps=[h_sum])
                h_rs_free = hcp
                hl = p.op("scalar", act(rsr, rsr, AF.Ln), deps=[hcp])
                hr = p.op("scalar", act(rsr, rsr, AF.Exp, scale=-1.0), deps=[hl])
                hdexp = p.op("scalar", (lambda ic: act(
                    rowsc[0:1, 1536:2048], rowsc[0:1, 2048 + ic * 512: 2560 + ic * 512],
                    AF.Exp, bias=negc[0:1, 0:1], scale=0.125))(ic),
                    deps=[h_diag, hr])
                hm = p.op("vector", (lambda h, ic: tt(
                    rowsc[0:1, 1536:2048],
                    rowsc[0:1, 1536:2048], rsr, ALU.mult))(h, ic), deps=[hdexp])
                hout = p.dma("gpsimd", (lambda h, ic: lambda e: e.dma_start(
                    out=mrow[h:h + 1, ic * 512:(ic + 1) * 512],
                    in_=rowsc[0:1, 1536:2048]))(h, ic), deps=[hm])
                h_rs_free = [hcp, hout]
    return p.barrier()




def emit_D(ctx, upto="full"):
    nc, p, bt, ft, PS, t = ctx["nc"], ctx["p"], ctx["bt"], ctx["ft"], ctx["PS"], ctx["t"]
    x2oT = t["x2oT"]
    qdi = t["qd"]
    kdi = t["kd"]
    vdi = t["vd"]
    diagm = t["diagm"]
    diagmo = t["diagmo"]
    woutT = t["woutT"]
    boutc = t["boutc"]
    e65g = t["e65g"]
    i16b = t["i16b"]
    onesf = t["onesf"]
    i128 = t["i128"]
    x3T = t["x3T"]
    x3lnT = t["x3lnT"]
    amax = t["amax"]
    diag2 = ctx.get("diag2")
    if True:
        ba = Arena(bt, GNB16)
        fa = Arena(ft, GNF32)

        kg = ba.alloc(128, 2 * S)
        o_kg = ba.last_off
        qg = ba.alloc(128, 2 * HALF)
        Vg = ba.alloc(128, 32 * 260)
        numT = ba.alloc(128, 8 * HALF)
        PTb = [ba.alloc(128, 1536), ba.alloc(128, 1536)]
        keepb = ba.alloc(16, 4096)
        keepbo = ba.alloc(16, 2048)
        e65gT = ba.alloc(16, 1040)
        i16bT = ba.alloc(16, 16)
        onesb = ba.alloc(128, 32)

        onesfT = fa.alloc(1, 128)
        negc = fa.alloc(128, 32)
        boutT = fa.alloc(128, 8)
        fcol = fa.alloc(128, 32)
        i128T2 = fa.alloc(128, 128)
        scrC = fa.alloc(128, 4096)
        dg = fa.alloc(16, 4096)
        dg2 = fa.alloc(16, 4096)
        rowsc = fa.alloc(128, 4096)
        scrA = fa.alloc(128, 4096)
        scrB = fa.alloc(128, 1024)
        scrM = fa.alloc(128, 512)

        ps_s = [PS[:, 0:1536], PS[:, 1536:3072]]
        ps_av = PS[:, 3072:3584]
        ps_m = PS[:, 3584:4096]
        PSB = [PS[:, k * 512:(k + 1) * 512] for k in range(8)]

        h_of = p.dma("sync", lambda e: e.dma_start(out=onesfT, in_=onesf))
        h_e65 = p.dma("sync", lambda e: e.dma_start(out=e65gT, in_=e65g))
        h_i16 = p.dma("sync", lambda e: e.dma_start(out=i16bT, in_=i16b))
        h_bout = p.dma("sync", lambda e: e.dma_start(out=boutT, in_=boutc))
        if diag2 is None:
            h_dg = p.dma("sync", lambda e: e.dma_start(out=dg, in_=diagm))
        else:
            h_dga = p.dma("sync", lambda e: e.dma_start(out=dg[0:16, 0:2048],
                                                        in_=diag2[0]))
            h_dg = p.dma("sync", lambda e: e.dma_start(out=dg[0:16, 2048:4096],
                                                       in_=diag2[1]), deps=[h_dga])
        h_dgo = p.dma("sync", lambda e: e.dma_start(out=dg2[0:16, 0:2048], in_=diagmo))
        h_i1282 = p.dma("sync", lambda e: e.dma_start(out=i128T2, in_=i128))
        h_ones = p.op("vector", lambda e: e.memset(onesb[:, 0:1], 1.0))
        h_negc = p.op("vector", lambda e: e.memset(negc[:, 0:1], -C_L1))

        r0 = rowsc[0:16, 0:32]
        h1 = p.op("vector", lambda e: e.reduce_sum(out=r0[:, 0:1], in_=dg,
                                                   axis=mybir.AxisListType.X),
                  deps=[h_dg])
        h2 = p.op("scalar", act(scrA[0:16, 0:4096], dg, AF.Square), deps=[h_dg])
        h3 = p.op("vector", lambda e: e.reduce_sum(out=r0[:, 1:2],
                                                   in_=scrA[0:16, 0:4096],
                                                   axis=mybir.AxisListType.X),
                  deps=[h2])
        h4 = p.op("vector", ts(r0[:, 2:3], r0[:, 0:1], 1.0 / 4096.0, ALU.mult),
                  deps=[h1])
        h5 = p.op("scalar", act(r0[:, 3:4], r0[:, 2:3], AF.Square), deps=[h4])
        h6 = p.op("vector", ts(r0[:, 4:5], r0[:, 1:2], 1.0 / 4095.0, ALU.mult),
                  deps=[h3])
        h7 = p.op("vector", stt(r0[:, 5:6], r0[:, 3:4], -4096.0 / 4095.0,
                                r0[:, 4:5], ALU.mult, ALU.add), deps=[h5, h6])
        h8 = p.op("scalar", act(r0[:, 6:7], r0[:, 5:6], AF.Ln), deps=[h7])
        h9 = p.op("scalar", act(r0[:, 6:7], r0[:, 6:7], AF.Exp, scale=-0.5), deps=[h8])
        h10 = p.op("vector", ts(dg, dg, r0[:, 2:3], ALU.subtract), deps=[h9, h2])
        h11 = p.op("vector", ts(dg, dg, r0[:, 6:7], ALU.mult), deps=[h10])
        h12 = p.op("scalar", act(dg, dg, AF.Square), deps=[h11])
        h13 = p.op("vector", ts(dg, dg, 2.25, ALU.is_le), deps=[h12])
        h14 = p.op("vector", cp(keepb, dg), deps=[h13])
        h15 = p.op("vector", ts(dg2[0:16, 0:2048], dg2[0:16, 0:2048], r0[:, 2:3],
                                ALU.subtract), deps=[h_dgo, h9])
        h16 = p.op("vector", ts(dg2[0:16, 0:2048], dg2[0:16, 0:2048], r0[:, 6:7],
                                ALU.mult), deps=[h15])
        h17 = p.op("scalar", act(dg2[0:16, 0:2048], dg2[0:16, 0:2048], AF.Square),
                   deps=[h16])
        h18 = p.op("vector", ts(dg2[0:16, 0:2048], dg2[0:16, 0:2048], 2.25,
                                ALU.is_le), deps=[h17])
        h19 = p.op("vector", cp(keepbo, dg2[0:16, 0:2048]), deps=[h18])
        bar_keep = p.barrier()

        qd3 = qdi.rearrange("(c p) i -> p c i", p=128)
        kd3 = kdi.rearrange("(c p) i -> p c i", p=128)
        vd4 = vdi.rearrange("j (h e) -> j h e", e=64)

        ps_pair = [PS[:, 0:1024], PS[:, 1024:2048]]
        ps_avA = PS[:, 2048:2560]
        ps_avB = PS[:, 2560:3072]
        ps_nrm = PS[:, 3072:3584]
        ps_kx = PS[:, 3584:4096]
        PTb2 = [PTb[0][0:128, 0:1024], PTb[1][0:128, 0:1024]]
        denrow = rowsc[0:1, 0:4096]
        krow = dg[0:1, 0:4096]
        h_expbuf = [bar_keep, bar_keep]
        h_avbuf = [bar_keep, bar_keep]
        h_avA_free = [bar_keep]
        h_avB_free = [bar_keep]
        h_nrmA_prev = bar_keep
        h_nrmB_prev = bar_keep
        h_den_readers = [bar_keep]
        h_krow_readers = [bar_keep]
        prev_group = bar_keep
        for g in range(4):
            hkl = []
            for cch in range(2):
                hkl.append(p.dma("sync", (lambda g, cch: lambda e: e.dma_start(
                    out=kg[:, cch * S:(cch + 1) * S],
                    in_=kdi[g * 256 + cch * 128: g * 256 + (cch + 1) * 128, :]))(g, cch),
                    deps=prev_group))
                hkl.append(p.dma("sync", (lambda g, cch: lambda e: e.dma_start(
                    out=qg[:, cch * HALF:(cch + 1) * HALF],
                    in_=qdi[g * 256 + cch * 128: g * 256 + (cch + 1) * 128, :]))(g, cch),
                    deps=prev_group))
            hvm = []
            for jc in range(32):
                vg3 = Vg[:, jc * 260:(jc + 1) * 260].rearrange("p (h e) -> p h e", e=65)
                hvd = p.dma("sync", (lambda jc, g, vg3: lambda e: e.dma_start(
                    out=vg3[:, :, 0:64],
                    in_=vd4[jc * 128:(jc + 1) * 128, 4 * g:4 * g + 4, :]))(jc, g, vg3),
                    deps=prev_group)
                hvo = p.op("vector", (lambda vg3: lambda e: e.memset(
                    vg3[:, :, 64:65], 1.0))(vg3), deps=[hvd] + (hvm[-1:] if hvm else []))
                hke = p.op("tensor", (lambda jc, g: lambda e: e.matmul(
                    PS[:, 2048:2048 + 260][0:128, :] if False else PS[0:128, 2048:2308],
                    keepb[:, jc * 128:(jc + 1) * 128],
                    e65gT[:, g * 260:(g + 1) * 260], start=True, stop=True))(jc, g),
                    deps=[bar_keep, h_e65] + (hvm[-1:] if hvm else []))
                hmask = p.op("vector", (lambda jc: tt(
                    Vg[:, jc * 260:(jc + 1) * 260],
                    Vg[:, jc * 260:(jc + 1) * 260],
                    PS[0:128, 2048:2308], ALU.mult))(jc), deps=[hvo, hke])
                hvm.append(hmask)
            bar_g = p.barrier()

            for pp in range(2):
                cch = pp
                hhA, hhB = 2 * pp, 2 * pp + 1
                hA_, hB_ = 4 * g + hhA, 4 * g + hhB
                qc2 = 2 * g + pp
                hkextr = []
                for side, hgl in ((0, hA_), (1, hB_)):
                    for ic in range(4):
                        hkr = p.op("tensor", (lambda hgl, ic: mm(
                            ps_kx[0:1, 0:512], i16bT[:, hgl:hgl + 1],
                            keepbo[:, ic * 512:(ic + 1) * 512]))(hgl, ic),
                            deps=[h19, h_i16, bar_g] + h_krow_readers
                                 + hkextr[-1:])
                        hkc = p.op("vector", (lambda side, ic: cp(
                            krow[0:1, side * 2048 + ic * 512:
                                 side * 2048 + ic * 512 + 512],
                            ps_kx[0:1, 0:512]))(side, ic), deps=[hkr])
                        hkextr.append(hkc)
                pair_dens = []
                pair_copies = {}
                for ic in range(4):
                    isl = slice(cch * HALF + ic * 512, cch * HALF + ic * 512 + 512)
                    nsl = slice(qc2 * HALF + ic * 512, qc2 * HALF + ic * 512 + 512)
                    h_avA = h_avB = None
                    pend = None
                    for jc in range(32):
                        buf = jc % 2

                        def emit_spair(buf, jc, cch, isl):
                            def f(e):
                                e.matmul(ps_pair[buf][:, 0:512],
                                         kg[0:64, cch * S + jc * 128:
                                            cch * S + jc * 128 + 128],
                                         qg[0:64, isl], start=True, stop=True)
                                return e.matmul(
                                    ps_pair[buf][:, 512:1024],
                                    kg[64:128, cch * S + jc * 128:
                                       cch * S + jc * 128 + 128],
                                    qg[64:128, isl], start=True, stop=True)
                            return f
                        hs = p.op("tensor", emit_spair(buf, jc, cch, isl),
                                  deps=[h_expbuf[buf], bar_g], n=2)

                        if pend is not None:
                            pjc, pbuf, phex = pend

                            def emit_avs(pjc, pbuf, side, hh_s):
                                def f(e):
                                    return e.matmul(
                                        (ps_avA if side == 0 else ps_avB)[0:65, :],
                                        Vg[:, pjc * 260 + hh_s * 65:
                                           pjc * 260 + hh_s * 65 + 65],
                                        PTb2[pbuf][:, side * 512: side * 512 + 512],
                                        start=(pjc == 0), stop=(pjc == 31))
                                return f
                            h_avA = p.op("tensor", emit_avs(pjc, pbuf, 0, hhA),
                                         deps=([phex] + (h_avA_free if pjc == 0
                                                         else [h_avA])))
                            h_avB = p.op("tensor", emit_avs(pjc, pbuf, 1, hhB),
                                         deps=([phex] + (h_avB_free if pjc == 0
                                                         else [h_avB])))
                            h_avbuf[pbuf] = h_avB

                        hex_ = p.op("scalar", (lambda buf: act(
                            PTb2[buf][:, 0:1024], ps_pair[buf][:, 0:1024],
                            AF.Exp, bias=negc[:, 0:1], scale=0.125))(buf),
                            deps=[hs, h_avbuf[buf], h_negc])
                        h_expbuf[buf] = hex_
                        pend = (jc, buf, hex_)

                    pjc, pbuf, phex = pend

                    def emit_avs(pjc, pbuf, side, hh_s):
                        def f(e):
                            return e.matmul(
                                (ps_avA if side == 0 else ps_avB)[0:65, :],
                                Vg[:, pjc * 260 + hh_s * 65:
                                   pjc * 260 + hh_s * 65 + 65],
                                PTb2[pbuf][:, side * 512: side * 512 + 512],
                                start=(pjc == 0), stop=(pjc == 31))
                        return f
                    h_avA = p.op("tensor", emit_avs(pjc, pbuf, 0, hhA),
                                 deps=[phex, h_avA])
                    h_avB = p.op("tensor", emit_avs(pjc, pbuf, 1, hhB),
                                 deps=[phex, h_avB])
                    h_avbuf[pbuf] = h_avB

                    for side, (qb2, ps_av_s, h_av_s) in enumerate(
                            ((0, ps_avA, h_avA), (64, ps_avB, h_avB))):
                        hd1 = p.op("vector", (lambda side, ic, ps_av_s: cp(
                            denrow[0:1, side * 2048 + ic * 512:
                                   side * 2048 + ic * 512 + 512],
                            ps_av_s[64:65, :]))(side, ic, ps_av_s),
                            deps=[h_av_s] + h_den_readers)
                        pair_dens.append(hd1)
                        hcopy = p.op("vector", (lambda qb2, nsl, ps_av_s: cp(
                            numT[qb2:qb2 + 64, nsl], ps_av_s[0:64, :]))(
                                qb2, nsl, ps_av_s),
                            deps=[h_av_s])
                        pair_copies[(side, ic)] = hcopy
                        if side == 0:
                            h_avA_free = [hd1, hcopy]
                        else:
                            h_avB_free = [hd1, hcopy]
                hmx = p.op("vector", ts(denrow, denrow, 1e-30, ALU.max),
                           deps=pair_dens)
                hrec = p.op("vector",
                            lambda e: e.reciprocal(out=denrow, in_=denrow),
                            deps=[hmx])
                hkm = p.op("vector", tt(denrow, denrow, krow, ALU.mult),
                           deps=[hrec] + hkextr)
                h_krow_readers = [hkm]
                readers = []
                for side, qb2 in ((0, 0), (1, 64)):
                    hn_prev = h_nrmA_prev if side == 0 else h_nrmB_prev
                    for ic in range(4):
                        nsl = slice(qc2 * HALF + ic * 512,
                                    qc2 * HALF + ic * 512 + 512)
                        rsl = slice(side * 2048 + ic * 512,
                                    side * 2048 + ic * 512 + 512)
                        hf = p.op("tensor", (lambda qb2, rsl: mm(
                            ps_nrm[qb2:qb2 + 64, :], onesfT[0:1, 0:64],
                            denrow[0:1, rsl]))(qb2, rsl),
                            deps=[hkm, hn_prev, h_of])
                        hn = p.op("vector", (lambda qb2, nsl: tt(
                            numT[qb2:qb2 + 64, nsl], numT[qb2:qb2 + 64, nsl],
                            ps_nrm[qb2:qb2 + 64, :], ALU.mult))(qb2, nsl),
                            deps=[hf, pair_copies[(side, ic)]])
                        readers.append(hf)
                        hn_prev = hn
                    if side == 0:
                        h_nrmA_prev = hn_prev
                    else:
                        h_nrmB_prev = hn_prev
                h_den_readers = readers
            prev_group = p.barrier()
        bar_attn = prev_group

        wob = bt[0:128, o_kg:o_kg + 8 * 1024]
        sqb = bt[0:128, o_kg + 8192:o_kg + 8192 + 4096]
        dep = bar_attn
        h_wo = None
        for c in range(8):
            hd = p.dma("sync", (lambda c: lambda e: e.dma_start(
                out=scrB[:, 0:1024], in_=woutT[c * 128:(c + 1) * 128, :]))(c), deps=dep)
            h_wo = p.op("vector", (lambda c: cp(wob[:, c * 1024:(c + 1) * 1024],
                                                scrB[:, 0:1024]))(c), deps=[hd])
            dep = [h_wo]
        h_fc = p.op("vector", lambda e: e.memset(fcol[:, 0:1], 1.0 / 1024.0))
        h_o1 = p.op("vector", lambda e: e.memset(onesb[:, 1:2], 1.0 / 1024.0),
                    deps=[h_ones])
        x2o3 = x2oT.rearrange("(c p) i -> p c i", p=128)
        x3r3 = x3T.rearrange("(c p) i -> p c i", p=128)
        x3l3 = x3lnT.rearrange("(c p) i -> p c i", p=128)
        scrA3 = scrA.rearrange("p (c i) -> p c i", i=512)
        scrC3 = scrC.rearrange("p (c i) -> p c i", i=512)
        hout = []
        hra = [h_wo]
        h_psm_free = bar_attn
        h_amax = None
        for ic in range(4):
            otok = slice(ic * 512, ic * 512 + 512)
            hx = p.dma("sync", (lambda otok: lambda e: e.dma_start(
                out=scrA3, in_=x2o3[:, :, otok]))(otok), deps=hra)
            hnew = []
            for doc in range(8):
                def emit_op(doc, ic):
                    def f(e):
                        last = None
                        for c in range(8):
                            last = e.matmul(
                                ps_m[:, :],
                                wob[:, c * 1024 + doc * 128: c * 1024 + doc * 128 + 128],
                                numT[:, c * HALF + ic * 512: c * HALF + ic * 512 + 512],
                                start=(c == 0), stop=(c == 7))
                        return last
                    return f
                hacc = p.op("tensor", emit_op(doc, ic),
                            deps=[h_wo, bar_attn, h_psm_free], n=8)
                blk = slice(doc * 512, doc * 512 + 512)
                hb1_ = p.op("scalar", (lambda doc: act(scrB[:, 0:512], ps_m[:, :],
                                                       AF.Identity,
                                                       bias=boutT[:, doc:doc + 1]))(doc),
                            deps=[hacc, h_bout] + hnew[-1:])
                hr = p.op("vector", (lambda blk: tt(scrA[:, blk], scrB[:, 0:512],
                                                    scrA[:, blk], ALU.add))(blk),
                          deps=[hb1_, hx])
                h_psm_free = hr
                hnew.append(hr)
            ho = p.dma("gpsimd", (lambda ic: lambda e: e.dma_start(
                out=x3r3[:, :, slice(ic * 512, ic * 512 + 512)], in_=scrA3))(ic),
                deps=hnew)
            hsq = p.op("scalar", act(sqb, scrA, AF.Square), deps=hnew)

            def emit_sum(dst, lhs_col, srct):
                def f(e):
                    last = None
                    for c in range(8):
                        last = e.matmul(dst, lhs_col,
                                        srct[:, c * 512:(c + 1) * 512],
                                        start=(c == 0), stop=(c == 7))
                    return last
                return f
            hsum = p.op("tensor", emit_sum(PSB[0][0:1, :], fcol[:, 0:1], scrA),
                        deps=[h_fc, h_psm_free] + hnew, n=8)
            hsqs = p.op("tensor", emit_sum(PSB[1][0:1, :], onesb[:, 1:2], sqb),
                        deps=[hsq, h_o1], n=8)
            rm = rowsc[0:1, 0:512]
            rv = rowsc[0:1, 512:1024]
            rr = rowsc[0:1, 1024:1536]
            rmb = rowsc[0:1, 1536:2048]
            s1 = p.op("vector", cp(rm, PSB[0][0:1, :]), deps=[hsum] + hout[-1:])
            s2 = p.op("scalar", act(rv, rm, AF.Square), deps=[s1])
            s3 = p.op("vector", stt(rv, rv, -1.0, PSB[1][0:1, :], ALU.mult, ALU.add),
                      deps=[s2, hsqs])
            s3b = p.op("vector", ts(rv, rv, EPS, ALU.add), deps=[s3])
            s4 = p.op("scalar", act(rr, rv, AF.Ln), deps=[s3b])
            s5 = p.op("scalar", act(rr, rr, AF.Exp, scale=-0.5), deps=[s4])
            s6 = p.op("vector", stt(rmb, rm, -1.0, rr, ALU.mult, ALU.mult), deps=[s5])
            sb1 = p.op("tensor", mm(PSB[3][:, :], onesfT, rr), deps=[s5, h_of])
            sb2 = p.op("tensor", mm(PSB[4][:, :], onesfT, rmb), deps=[s6])
            hln = []
            hpv = None
            for c in range(8):
                blk = slice(c * 512, c * 512 + 512)
                la = p.op("vector", (lambda blk: tt(scrB[:, 0:512], scrA[:, blk],
                                                    PSB[3][:, :], ALU.mult))(blk),
                          deps=[sb1, ho] + ([hpv] if hpv else []))
                lb = p.op("vector", (lambda blk: tt(scrC[:, blk], scrB[:, 0:512],
                                                    PSB[4][:, :], ALU.add))(blk),
                          deps=[la, sb2])
                lc = p.op("scalar", (lambda blk: act(scrB[:, 512:1024],
                                                     scrC[:, blk], AF.Abs))(blk),
                          deps=[lb])
                if h_amax is None:
                    h_amax = p.op("vector", cp(scrM, scrB[:, 512:1024]), deps=[lc])
                else:
                    h_amax = p.op("vector", tt(scrM, scrM, scrB[:, 512:1024],
                                               ALU.max), deps=[lc, h_amax])
                hpv = h_amax
                hln.append(lb)
            ho2 = p.dma("gpsimd", (lambda ic: lambda e: e.dma_start(
                out=x3l3[:, :, slice(ic * 512, ic * 512 + 512)], in_=scrC3))(ic),
                deps=hln + [h_amax])
            hout = [ho, ho2]
            hra = hln + [ho, ho2, h_amax]
        a1 = p.op("vector", lambda e: e.reduce_max(out=scrM[:, 0:1], in_=scrM,
                                                   axis=mybir.AxisListType.X),
                  deps=[h_amax])
        a2 = p.op("vector", cp(scrB[0:64, 0:1], scrM[64:128, 0:1]), deps=[a1])
        a3 = p.op("vector", tt(scrM[0:64, 0:1], scrM[0:64, 0:1], scrB[0:64, 0:1],
                               ALU.max), deps=[a2])
        a4 = p.op("vector", cp(scrB[0:32, 0:1], scrM[32:64, 0:1]), deps=[a3])
        a5 = p.op("vector", tt(scrM[0:32, 0:1], scrM[0:32, 0:1], scrB[0:32, 0:1],
                               ALU.max), deps=[a4])
        a6 = p.op("tensor", lambda e: e.transpose(PS[0:1, 2048:2080],
                                                   scrM[0:32, 0:1],
                                                   i128T2[0:32, 0:32]),
                  deps=[a5, h_i1282])
        a7 = p.op("vector", cp(rowsc[0:1, 0:32], PS[0:1, 2048:2080]), deps=[a6])
        a8 = p.op("vector", lambda e: e.reduce_max(out=rowsc[0:1, 32:33],
                                                   in_=rowsc[0:1, 0:32],
                                                   axis=mybir.AxisListType.X),
                  deps=[a7])
        az = p.op("vector", lambda e: e.memset(rowsc[0:1, 33:48], 0.0), deps=[a8])
        ha = p.dma("gpsimd", lambda e: e.dma_start(out=amax, in_=rowsc[0:1, 32:48]),
                   deps=[a8, az])
    return p.barrier()




def emit_F(ctx, upto="full"):
    nc, p, bt, ft, PS, t = ctx["nc"], ctx["p"], ctx["bt"], ctx["ft"], ctx["PS"], ctx["t"]
    x3Ti = t["x3T"]
    x3l = t["x3lnT"]
    amax8 = t["amax8"]
    wT = t["wT"]
    biasc = t["biasc"]
    onesf = t["onesf"]
    i128 = t["i128"]
    outT = t["outT"]
    if True:
        ba = Arena(bt, GNB16)
        fa = Arena(ft, GNF32)

        wq8 = ba.alloc(128, 8 * 1024)
        xq8 = ba.alloc(128, 8 * 512)
        onesb = ba.alloc(128, 32)

        onesfT = fa.alloc(1, 128)
        i128T = fa.alloc(128, 128)
        biasT = fa.alloc(128, 32)
        gcol = fa.alloc(128, 32)
        rowsc = fa.alloc(128, 1024)
        wf = fa.alloc(128, 1024)
        scrA = fa.alloc(128, 4096)
        scrR = fa.alloc(128, 4096)
        scrV = fa.alloc(128, 4096)
        scrU = fa.alloc(128, 2048)
        scrW = fa.alloc(128, 2048)
        scrD = fa.alloc(128, 2048)

        PSB = [PS[:, k * 512:(k + 1) * 512] for k in range(8)]

        h_of = p.dma("sync", lambda e: e.dma_start(out=onesfT, in_=onesf))
        h_i128 = p.dma("sync", lambda e: e.dma_start(out=i128T, in_=i128))
        h_bias = p.dma("sync", lambda e: e.dma_start(out=biasT[:, 0:8], in_=biasc))
        h_am = None
        h_tc = p.op("vector", lambda e: e.memset(gcol[:, 2:3], 1e-38))
        h_am = p.dma("sync", lambda e: e.dma_start(out=rowsc[0:1, 100:116],
                                                   in_=amax8))
        bar0 = p.barrier()

        hx1 = p.op("vector", lambda e: e.reduce_max(out=rowsc[0:1, 8:9],
                                                    in_=rowsc[0:1, 0:16][0:1, 0:8]
                                                    if False else rowsc[0:1, 100:116],
                                                    axis=mybir.AxisListType.X),
                   deps=[h_am])
        hwa = None
        hwl = None
        for c in range(8):
            hwl = p.dma("sync", (lambda c: lambda e: e.dma_start(
                out=wf, in_=wT[c * 128:(c + 1) * 128, :]))(c),
                deps=[bar0 if hwa is None else hwa])
            ha = p.op("scalar", act(scrA[:, 0:1024], wf, AF.Abs), deps=[hwl])
            hr = p.op("vector", (lambda c: lambda e: e.reduce_max(
                out=scrV[:, c:c + 1], in_=scrA[:, 0:1024],
                axis=mybir.AxisListType.X))(c), deps=[ha])
            hwa = hr
        hw1 = p.op("vector", lambda e: e.reduce_max(out=scrV[:, 8:9],
                                                    in_=scrV[:, 0:8],
                                                    axis=mybir.AxisListType.X),
                   deps=[hwa])
        hw2 = p.op("vector", cp(scrW[0:64, 0:1], scrV[64:128, 8:9]), deps=[hw1])
        hw3 = p.op("vector", tt(scrV[0:64, 8:9], scrV[0:64, 8:9], scrW[0:64, 0:1],
                                ALU.max), deps=[hw2])
        hw4 = p.op("vector", cp(scrW[0:32, 0:1], scrV[32:64, 8:9]), deps=[hw3])
        hw5 = p.op("vector", tt(scrV[0:32, 8:9], scrV[0:32, 8:9], scrW[0:32, 0:1],
                                ALU.max), deps=[hw4])
        hw6 = p.op("tensor", lambda e: e.transpose(PS[0:1, 0:32], scrV[0:32, 8:9],
                                                   i128T[0:32, 0:32]),
                   deps=[hw5, h_i128])
        hw7 = p.op("vector", cp(rowsc[0:1, 16:48], PS[0:1, 0:32]), deps=[hw6])
        hw8 = p.op("vector", lambda e: e.reduce_max(out=rowsc[0:1, 9:10],
                                                    in_=rowsc[0:1, 16:48],
                                                    axis=mybir.AxisListType.X),
                   deps=[hw7])
        hs1 = p.op("vector", ts(rowsc[0:1, 10:12], rowsc[0:1, 8:10], 1e-12, ALU.max),
                   deps=[hx1, hw8])
        hs2 = p.op("vector", lambda e: e.reciprocal(out=rowsc[0:1, 12:14],
                                                    in_=rowsc[0:1, 10:12]),
                   deps=[hs1])
        hs3 = p.op("vector", ts(rowsc[0:1, 12:14], rowsc[0:1, 12:14], 448.0,
                                ALU.mult, 448.0, ALU.min), deps=[hs2])
        hs4 = p.op("vector", lambda e: e.reciprocal(out=rowsc[0:1, 14:16],
                                                    in_=rowsc[0:1, 12:14]),
                   deps=[hs3])
        hs5 = p.op("vector", tt(rowsc[0:1, 20:21], rowsc[0:1, 14:15],
                                rowsc[0:1, 15:16], ALU.mult), deps=[hs4])
        hb1 = p.op("tensor", mm(PS[0:128, 0:2], onesfT, rowsc[0:1, 12:14]),
                   deps=[hs5, h_of])
        hb2 = p.op("vector", cp(gcol[:, 0:2], PS[0:128, 0:2]), deps=[hb1])
        hb3 = p.op("tensor", mm(PS[0:128, 4:5], onesfT, rowsc[0:1, 20:21]),
                   deps=[hb2])
        hb4 = p.op("vector", cp(gcol[:, 4:5], PS[0:128, 4:5]), deps=[hb3])
        bar_s = p.barrier()

        def emit_fp8(src_ap, dst_ap, scale_col, width, deps, voff=0):
            v = scrV[:, voff:voff + width]
            u = scrU[:, 0:width]
            w_ = scrW[:, 0:width]
            d_ = scrD[:, 0:width]
            h1 = p.op("vector", (lambda: ts(v, src_ap,
                                            gcol[:, scale_col:scale_col + 1],
                                            ALU.mult))(), deps=deps)
            h2 = p.op("vector", ts(v, v, 448.0, ALU.min, -448.0, ALU.max), deps=[h1])
            h3 = p.op("scalar", act(u, v, AF.Square), deps=[h2])
            h4 = p.op("scalar", act(u, u, AF.Ln, bias=gcol[:, 2:3]), deps=[h3, h_tc])
            h5 = p.op("vector", ts(u, u, 0.5 / LN2, ALU.mult, 127.0, ALU.add),
                      deps=[h4])
            h6 = p.op("vector", cp(w_, u), deps=[h5])
            hc = frac_cascade(p, w_, d_, 7, 8, [h6])
            h7 = p.op("vector", tt(u, u, w_, ALU.subtract), deps=[hc])
            h8 = p.op("vector", ts(u, u, -130.0, ALU.add, -9.0, ALU.max), deps=[h7])
            h9 = p.op("scalar", act(w_, u, AF.Exp, scale=-LN2), deps=[h8])
            h10 = p.op("vector", tt(v, v, w_, ALU.mult), deps=[h9])
            h11 = p.op("vector", ts(v, v, 16.5, ALU.add), deps=[h10])
            h12 = p.op("vector", cp(w_, v), deps=[h11])
            hc2 = frac_cascade(p, w_, d_, 4, 5, [h12])
            h13 = p.op("vector", tt(v, v, w_, ALU.subtract), deps=[hc2])
            h14 = p.op("vector", ts(v, v, -16.0, ALU.add), deps=[h13])
            h15 = p.op("scalar", act(w_, u, AF.Exp, scale=LN2), deps=[h14])
            h16 = p.op("vector", (lambda: tt(dst_ap, v, w_, ALU.mult))(), deps=[h15])
            return h16

        def emit_bf16round(width, deps, voff=0):
            v = scrV[:, voff:voff + width]
            u = scrU[:, 0:width]
            w_ = scrW[:, 0:width]
            d_ = scrD[:, 0:width]
            h3 = p.op("scalar", act(u, v, AF.Square), deps=deps)
            h4 = p.op("scalar", act(u, u, AF.Ln, bias=gcol[:, 2:3]), deps=[h3, h_tc])
            h5 = p.op("vector", ts(u, u, 0.5 / LN2, ALU.mult, 127.0, ALU.add),
                      deps=[h4])
            h6 = p.op("vector", cp(w_, u), deps=[h5])
            hc = frac_cascade(p, w_, d_, 7, 8, [h6])
            h7 = p.op("vector", tt(u, u, w_, ALU.subtract), deps=[hc])
            h8 = p.op("vector", ts(u, u, -134.0, ALU.add, -60.0, ALU.max), deps=[h7])
            h9 = p.op("scalar", act(w_, u, AF.Exp, scale=-LN2), deps=[h8])
            h10 = p.op("vector", tt(v, v, w_, ALU.mult), deps=[h9])
            h11 = p.op("vector", ts(v, v, 256.5, ALU.add), deps=[h10])
            h12 = p.op("vector", cp(w_, v), deps=[h11])
            hc2 = frac_cascade(p, w_, d_, 8, 9, [h12])
            h13 = p.op("vector", tt(v, v, w_, ALU.subtract), deps=[hc2])
            h14 = p.op("vector", ts(v, v, -256.0, ALU.add), deps=[h13])
            h15 = p.op("scalar", act(w_, u, AF.Exp, scale=LN2), deps=[h14])
            h16 = p.op("vector", tt(v, v, w_, ALU.mult), deps=[h15])
            return h16

        hq = bar_s
        for c in range(8):
            hwl = p.dma("sync", (lambda c: lambda e: e.dma_start(
                out=wf, in_=wT[c * 128:(c + 1) * 128, :]))(c),
                deps=hq if isinstance(hq, list) else [hq])
            hq = [emit_fp8(wf, wq8[:, c * 1024:(c + 1) * 1024], 1, 1024, [hwl])]
        bar_wq = p.barrier()

        x3r3 = x3Ti.rearrange("(c p) i -> p c i", p=128)
        x3l3 = x3l.rearrange("(c p) i -> p c i", p=128)
        outr3 = outT.rearrange("(c p) i -> p c i", p=128)
        scrA3 = scrA.rearrange("p (c i) -> p c i", i=512)

        hprev = bar_wq
        for tc in range(4):
            tok = slice(tc * 512, tc * 512 + 512)
            hx = p.dma("sync", (lambda tok: lambda e: e.dma_start(
                out=scrA3, in_=x3l3[:, :, tok]))(tok), deps=hprev)
            hxr = p.dma("sync", (lambda tok: lambda e: e.dma_start(
                out=scrR.rearrange("p (c i) -> p c i", i=512),
                in_=x3r3[:, :, tok]))(tok), deps=hprev)
            hq8a = emit_fp8(scrA[:, 0:2048], xq8[:, 0:2048], 0, 2048, [hx])
            hq8 = emit_fp8(scrA[:, 2048:4096], xq8[:, 2048:4096], 0, 2048, [hq8a])
            hmm = {}
            for dout in range(8):
                hacc = None
                for c in range(8):
                    lhs = wq8[:, c * 1024 + dout * 128: c * 1024 + dout * 128 + 128]
                    rhs = xq8[:, c * 512:(c + 1) * 512]
                    hacc = p.op("tensor", (lambda lhs, rhs, dout, st, sp: mm(
                        PSB[dout][:, :], lhs, rhs, start=st, stop=sp))(
                            lhs, rhs, dout, c == 0, c == 7),
                        deps=([hq8, bar_wq] if hacc is None else [hacc]))
                hmm[dout] = hacc
            hev = [hq8]
            for dout in range(8):
                blk = slice(dout * 512, dout * 512 + 512)
                hev.append(p.op("vector", (lambda blk, dout: ts(
                    scrV[:, blk], PSB[dout][:, :], gcol[:, 4:5], ALU.mult,
                    biasT[:, dout:dout + 1], ALU.add))(blk, dout),
                    deps=[hmm[dout], hb4, h_bias] + hev[-1:]))
            hrd1 = emit_bf16round(2048, hev, voff=0)
            hrd = emit_bf16round(2048, [hrd1], voff=2048)
            hres = p.op("vector", tt(scrA, scrV, scrR, ALU.add), deps=[hrd, hxr])
            ho = p.dma("gpsimd", (lambda tok: lambda e: e.dma_start(
                out=outr3[:, :, tok], in_=scrA3))(tok), deps=[hres])
            hprev = [ho, hres]
    return p.barrier()




def _rope_permute_cols(pairs):
    perm = []
    for (a, b_) in pairs:
        perm.extend(range(a * 64, a * 64 + 32))
        perm.extend(range(b_ * 64, b_ * 64 + 32))
        perm.extend(range(a * 64 + 32, a * 64 + 64))
        perm.extend(range(b_ * 64 + 32, b_ * 64 + 64))
    return np.array(perm)


def _headperm_rows(pairs):
    perm = []
    for (a, b_) in pairs:
        perm.extend(range(a * 64, (a + 1) * 64))
        perm.extend(range(b_ * 64, (b_ + 1) * 64))
    return np.array(perm)


def make_A_inputs(inputs):
    x = np.ascontiguousarray(inputs["L0_attn_x"], dtype=np.float32)
    qperm = _rope_permute_cols(QPAIRS)
    kperm = _rope_permute_cols([(0, 1), (2, 3)])
    wq_p = np.ascontiguousarray(inputs["L0_attn_w_q"][:, qperm], dtype=np.float32)
    wk_p = np.ascontiguousarray(inputs["L0_attn_w_k"][:, kperm], dtype=np.float32)
    wv = np.ascontiguousarray(inputs["L0_attn_w_v"], dtype=np.float32)
    wo_p = np.ascontiguousarray(inputs["L0_attn_w_o"][_headperm_rows(QPAIRS), :],
                                dtype=np.float32)
    posr = np.arange(S, dtype=np.float32).reshape(1, S)
    didx = np.arange(32, dtype=np.float32).reshape(1, 32)
    onesf = np.ones((1, 128), dtype=np.float32)
    pr = np.arange(128)
    jrep = (np.arange(32)[:, None] == (pr[None, :] % 32)).astype(np.float32)
    sgn = np.where(pr < 64, -1.0, 1.0).astype(np.float32)
    jsgn = jrep * sgn[None, :]
    common = dict(wq=wq_p, wk=wk_p, wv=wv, wo=wo_p, posr=posr, didx=didx,
                  onesf=onesf, jrep=jrep, jsgn=jsgn)
    in_maps = []
    for c in range(NCORES):
        b, half = c // 2, c % 2
        m = dict(common)
        xT = np.ascontiguousarray(x[b].T)
        m["xbT"] = xT
        m["xoT"] = np.ascontiguousarray(xT[:, half * HALF:(half + 1) * HALF])
        m["poso"] = posr[:, half * HALF:(half + 1) * HALF].copy()
        in_maps.append(m)
    return in_maps


def run_launch_A(inputs, core_ids=None):
    in_maps = make_A_inputs(inputs)
    nc0 = build_launch_A()
    cids = core_ids or list(range(NCORES))
    r = run_bass_kernel_spmd(nc0, [in_maps[c] for c in cids], core_ids=cids)
    x1 = np.empty((B, S, D), dtype=np.float32)
    for i, c in enumerate(cids):
        b, half = c // 2, c % 2
        x1[b, half * HALF:(half + 1) * HALF] = r.results[i]["x1T"].T
    return x1


def make_C1_inputs(inputs, x2):
    wqkvT = np.ascontiguousarray(inputs["L1_attn_w_qkv"].T.astype(np.float32))
    common = dict(wqkvT=wqkvT, onesf=np.ones((1, 128), dtype=np.float32))
    in_maps = []
    for c in range(NCORES):
        b, half = c // 2, c % 2
        m = dict(common)
        xT = np.ascontiguousarray(x2[b].T.astype(np.float32))
        m["x2T"] = xT
        m["x2oT"] = np.ascontiguousarray(xT[:, half * HALF:(half + 1) * HALF])
        in_maps.append(m)
    return in_maps


def make_C2_inputs(qd, kd):
    in_maps = []
    for c in range(NCORES):
        b, half = c // 2, c % 2
        m = {}
        m["qd"] = qd[c]
        m["kd"] = np.ascontiguousarray(np.concatenate(
            [kd[c][:, half * HALF:(half + 1) * HALF],
             kd[c][:, (1 - half) * HALF:(2 - half) * HALF]], axis=1))
        m["i128"] = np.eye(128, dtype=np.float32)
        in_maps.append(m)
    return in_maps


def mrowP_to_mrow(mp):
    return np.ascontiguousarray(
        np.asarray(mp, dtype=np.float32).reshape(128, 16, 16)
        .transpose(1, 2, 0).reshape(16, HALF))


def _e65g():
    e = np.zeros((16, 1040), dtype=np.float32)
    for g in range(4):
        for j in range(4):
            h = 4 * g + j
            e[h, g * 260 + j * 65: g * 260 + j * 65 + 65] = 1.0
    import ml_dtypes
    return e.astype(ml_dtypes.bfloat16)


def make_D_inputs(inputs, x2, qd, kd, vd, mrow):
    import ml_dtypes
    woutT = np.ascontiguousarray(inputs["L1_attn_w_out"].T.astype(np.float32))
    bout = np.asarray(inputs["L1_attn_b_out"], dtype=np.float32)
    boutc = np.ascontiguousarray(bout.reshape(-1, 128).T)
    e65 = _e65g()
    i16b = np.eye(16, dtype=np.float32).astype(ml_dtypes.bfloat16)
    common = dict(woutT=woutT, boutc=boutc, e65g=e65, i16b=i16b,
                  onesf=np.ones((1, 128), dtype=np.float32),
                  i128=np.eye(128, dtype=np.float32))
    diag_full = {}
    for b in range(B):
        dm = np.empty((16, S), dtype=np.float32)
        dm[:, 0:HALF] = mrow[2 * b]
        dm[:, HALF:S] = mrow[2 * b + 1]
        diag_full[b] = dm
    in_maps = []
    for c in range(NCORES):
        b, half = c // 2, c % 2
        m = dict(common)
        xT = np.ascontiguousarray(x2[b].T.astype(np.float32))
        m["x2oT"] = np.ascontiguousarray(xT[:, half * HALF:(half + 1) * HALF])
        m["qd"] = qd[c]
        m["kd"] = kd[c]
        m["vd"] = vd[c]
        m["diagm"] = diag_full[b]
        m["diagmo"] = np.ascontiguousarray(
            diag_full[b][:, half * HALF:(half + 1) * HALF])
        in_maps.append(m)
    return in_maps


def make_F_inputs(inputs, x3T, x3lnT, amaxs):
    wTt = np.ascontiguousarray(inputs["L1_ffn_weight"].T.astype(np.float32))
    biasv = np.asarray(inputs["L1_ffn_bias"], dtype=np.float32)
    biasc = np.ascontiguousarray(biasv.reshape(-1, 128).T)
    a = np.zeros((1, 16), dtype=np.float32)
    a[0, :len(amaxs)] = np.array(amaxs, dtype=np.float32)
    amax8 = a
    common = dict(wT=wTt, biasc=biasc, amax8=amax8,
                  onesf=np.ones((1, 128), dtype=np.float32),
                  i128=np.eye(128, dtype=np.float32))
    in_maps = []
    for c in range(NCORES):
        m = dict(common)
        m["x3T"] = x3T[c]
        m["x3lnT"] = x3lnT[c]
        in_maps.append(m)
    return in_maps


def kernel(**inputs):
    cids = list(range(NCORES))
    in_A = make_A_inputs(inputs)
    rA = run_bass_kernel_spmd(build_launch_A(), in_A, core_ids=cids)
    x1 = np.empty((B, S, D), dtype=np.float32)
    for i, c in enumerate(cids):
        b, half = c // 2, c % 2
        x1[b, half * HALF:(half + 1) * HALF] = rA.results[i]["x1T"].T
    in_B = make_B_inputs(inputs, x1)
    rB = run_bass_kernel_spmd(build_launch_B(), in_B, core_ids=cids)
    x2 = np.empty((B, S, D), dtype=np.float32)
    for i, c in enumerate(cids):
        b, half = c // 2, c % 2
        x2[b, half * HALF:(half + 1) * HALF] = rB.results[i]["x2T"].T
    in_C1 = make_C1_inputs(inputs, x2)
    rC1 = run_bass_kernel_spmd(build_launch_C1(), in_C1, core_ids=cids)
    qd = {c: rC1.results[i]["qd"] for i, c in enumerate(cids)}
    kd = {c: rC1.results[i]["kd"] for i, c in enumerate(cids)}
    vd = {c: rC1.results[i]["vd"] for i, c in enumerate(cids)}
    in_C2 = make_C2_inputs(qd, kd)
    rC2 = run_bass_kernel_spmd(build_launch_C2(), in_C2, core_ids=cids)
    mrow = {c: mrowP_to_mrow(rC2.results[i]["mrowP"]) for i, c in enumerate(cids)}
    in_D = make_D_inputs(inputs, x2, qd, kd, vd, mrow)
    rD = run_bass_kernel_spmd(build_launch_D(), in_D, core_ids=cids)
    x3T = {c: rD.results[i]["x3T"] for i, c in enumerate(cids)}
    x3lnT = {c: rD.results[i]["x3lnT"] for i, c in enumerate(cids)}
    amaxs = [float(rD.results[i]["amax"][0, 0]) for i, c in enumerate(cids)]
    in_F = make_F_inputs(inputs, x3T, x3lnT, amaxs)
    rF = run_bass_kernel_spmd(build_launch_F(), in_F, core_ids=cids)
    out = np.empty((B, S, D), dtype=np.float32)
    for i, c in enumerate(cids):
        b, half = c // 2, c % 2
        out[b, half * HALF:(half + 1) * HALF] = rF.results[i]["outT"].T
    return out


# version 07 (final)
# speedup vs baseline: 1.6613x; 1.0037x over previous
import numpy as np
import concourse.bass as bass
import concourse.mybir as mybir
from concourse.bass_utils import run_bass_kernel_spmd
from concourse.alu_op_type import AluOpType as ALU

F32 = mybir.dt.float32
BF16 = mybir.dt.bfloat16
AF = mybir.ActivationFunctionType

NCORES = 8
B, S, D = 4, 4096, 1024
HALF = S // 2
LN10K = float(np.log(10000.0))
LN2 = float(np.log(2.0))
TWO_PI = 2 * np.pi
TWO_PI_HI = 6.28125
TWO_PI_LO = TWO_PI - TWO_PI_HI
C_L0 = 24.0
C_L1 = 10.0
EPS = 1e-5

ENGINES = ("sync", "vector", "scalar", "tensor", "gpsimd")

QPAIRS = [(0, 4), (1, 5), (2, 6), (3, 7), (8, 12), (9, 13), (10, 14), (11, 15)]
QCHUNK, QBASE = {}, {}
for _c, (_a, _b) in enumerate(QPAIRS):
    QCHUNK[_a], QBASE[_a] = _c, 0
    QCHUNK[_b], QBASE[_b] = _c, 64


class H:
    __slots__ = ("eng", "kind", "idx")

    def __init__(self, eng, kind, idx):
        self.eng, self.kind, self.idx = eng, kind, idx


class Prog:

    def __init__(self):
        self.ops = {e: [] for e in ENGINES}
        self.ccount = {e: 0 for e in ENGINES}
        self.dcount = {e: 0 for e in ENGINES}
        self.floor = {}

    def _waits_for(self, deps):
        w = {}

        def add(d):
            if d is None:
                return
            if isinstance(d, (list, tuple)):
                for x in d:
                    add(x)
            else:
                key = (d.eng, d.kind)
                if w.get(key, -1) < d.idx:
                    w[key] = d.idx
        add(deps)
        for key, val in self.floor.items():
            if w.get(key, -1) < val:
                w[key] = val
        return w

    def op(self, eng, emit, deps=(), n=1):
        w = self._waits_for(deps)
        if eng == "tensor":
            c = self.ccount["tensor"]
            if c + n > 16:
                key = ("tensor", "c")
                need = c + n - 16
                if w.get(key, -1) < need:
                    w[key] = need
        self.ops[eng].append((emit, w, "c", n))
        self.ccount[eng] += n
        return H(eng, "c", self.ccount[eng])

    def dma(self, eng, emit, deps=()):
        w = self._waits_for(deps)
        self.ops[eng].append((emit, w, "d", 1))
        self.dcount[eng] += 1
        return H(eng, "d", 16 * self.dcount[eng])

    def set_floor(self, deps):
        w = {}
        for d in deps:
            key = (d.eng, d.kind)
            if w.get(key, -1) < d.idx:
                w[key] = d.idx
        self.floor = w

    def barrier(self):
        out = []
        for e in ENGINES:
            if self.ccount[e]:
                out.append(H(e, "c", self.ccount[e]))
            if self.dcount[e]:
                out.append(H(e, "d", 16 * self.dcount[e]))
        return out

    def build(self, nc):
        stack, sems = [], {}
        for e in ENGINES:
            for k in ("c", "d"):
                g = nc.semaphore(f"s_{k}_{e}")
                sems[(e, k)] = g.__enter__()
                stack.append(g)
        bg = nc.Block()
        block = bg.__enter__()

        def make_body(engname):
            oplist = self.ops[engname]

            def body(engobj):
                waited = {}
                for emit, w, kind, n in oplist:
                    for key, val in sorted(w.items()):
                        if waited.get(key, -1) >= val:
                            continue
                        engobj.wait_ge(sems[key], val)
                        waited[key] = val
                    last = emit(engobj)
                    if kind == "c":
                        last.then_inc(sems[(engname, "c")], n)
                    else:
                        last.then_inc(sems[(engname, "d")], 16)
            return body

        for e in ENGINES:
            if self.ops[e]:
                getattr(block, e)(make_body(e))
        bg.__exit__(None, None, None)
        for g in reversed(stack):
            g.__exit__(None, None, None)


class Arena:
    def __init__(self, t, ncols):
        self.t, self.ncols, self.off, self.marks = t, ncols, 0, []

    def alloc(self, p, w, align=32):
        o = (self.off + align - 1) // align * align
        assert o + w <= self.ncols, f"arena overflow {o}+{w}>{self.ncols}"
        self.off = o + w
        self.last_off = o
        return self.t[0:p, o:o + w]

    def mark(self):
        self.marks.append(self.off)

    def release(self):
        self.off = self.marks.pop()



def ts(out, in0, s1, op0, s2=None, op1=None, accum=None):
    kw = {}
    if op1 is not None:
        kw["op1"] = op1
    if accum is not None:
        kw["accum_out"] = accum
    return lambda e: e.tensor_scalar(out=out, in0=in0, scalar1=s1, scalar2=s2,
                                     op0=op0, **kw)


def tt(out, a, bb, op):
    return lambda e: e.tensor_tensor(out=out, in0=a, in1=bb, op=op)


def stt(out, in0, scalar, in1, op0, op1):
    return lambda e: e.scalar_tensor_tensor(out=out, in0=in0, scalar=scalar,
                                            in1=in1, op0=op0, op1=op1)


def act(out, in_, func, bias=0.0, scale=1.0, accum=None):
    kw = {"accum_out": accum} if accum is not None else {}
    return lambda e: e.activation(out=out, in_=in_, func=func, bias=bias,
                                  scale=scale, **kw)


def cp(out, in_):
    return lambda e: e.tensor_copy(out=out, in_=in_)


def mm(out, lhsT, rhs, start=True, stop=True):
    return lambda e: e.matmul(out, lhsT, rhs, start=start, stop=stop)


def frac_cascade(p, u, d, topbit, nbits, deps, eng="vector"):
    h = deps
    for k in range(topbit, topbit - nbits, -1):
        c = float(2.0 ** k)
        h1 = p.op(eng, ts(d, u, c, ALU.is_ge, -c, ALU.mult), deps=h)
        h = [p.op(eng, tt(u, u, d, ALU.add), deps=[h1])]
    return h[0]


DEBUG_NGROUPS = 99
SGROUPS = []
_j = 0
while _j < 32:
    _n = min(3, 32 - _j)
    SGROUPS.append((_j, _n))
    _j += _n



GNB16, GNF32 = 54784, 23296

def _standalone(emit, ios, upto="full"):
    nc = bass.Bass("TRN2", target_bir_lowering=False)
    t = {}
    for (name, shape, dt_, kind) in ios:
        t[name] = nc.dram_tensor(name, shape, dt_, kind=kind).ap()
    with (
        nc.sbuf_tensor("AB16", [128, GNB16], BF16) as bt,
        nc.sbuf_tensor("AF32", [128, GNF32], F32) as ft,
        nc.sbuf_tensor("AI32", [128, 2048], mybir.dt.int32) as it,
        nc.psum_tensor("PS", [128, 4096], F32) as PS,
    ):
        p = Prog()
        ctx = dict(nc=nc, p=p, bt=bt, ft=ft, PS=PS, t=t, it=it)
        emit(ctx, upto)
        p.op("gpsimd", lambda e: e.nop(), deps=p.barrier())
        p.build(nc)
    return nc

A_IOS = [("xbT", [D, S], F32, "ExternalInput"), ("xoT", [D, HALF], F32, "ExternalInput"),
         ("wq", [D, 1024], F32, "ExternalInput"), ("wk", [D, 256], F32, "ExternalInput"),
         ("wv", [D, 256], F32, "ExternalInput"), ("wo", [D, 1024], F32, "ExternalInput"),
         ("posr", [1, S], F32, "ExternalInput"), ("poso", [1, HALF], F32, "ExternalInput"),
         ("didx", [1, 32], F32, "ExternalInput"),
         ("onesf", [1, 128], F32, "ExternalInput"), ("jrep", [32, 128], F32, "ExternalInput"),
         ("jsgn", [32, 128], F32, "ExternalInput"), ("x1T", [D, HALF], F32, "ExternalOutput")]

B_IOS = [("x1T", [D, HALF], F32, "ExternalInput"), ("w1T", [D, 4096], F32, "ExternalInput"),
         ("w2T", [4096, D], F32, "ExternalInput"), ("g1c", [128, 8], F32, "ExternalInput"),
         ("b1lnc", [128, 8], F32, "ExternalInput"), ("b1c", [128, 32], F32, "ExternalInput"),
         ("g2c", [128, 32], F32, "ExternalInput"), ("b2lnc", [128, 32], F32, "ExternalInput"),
         ("b2c", [128, 8], F32, "ExternalInput"), ("onesf", [1, 128], F32, "ExternalInput"),
         ("i128", [128, 128], F32, "ExternalInput"), ("x2T", [D, HALF], F32, "ExternalOutput")]

C1_IOS = [("x2T", [D, S], F32, "ExternalInput"), ("x2oT", [D, HALF], F32, "ExternalInput"),
          ("wqkvT", [D, 3072], F32, "ExternalInput"), ("onesf", [1, 128], F32, "ExternalInput"),
          ("qd", [D, HALF], BF16, "ExternalOutput"), ("kd", [D, S], BF16, "ExternalOutput"),
          ("vd", [S, D], BF16, "ExternalOutput")]

C2_IOS = [("qd", [D, HALF], BF16, "ExternalInput"), ("kd", [D, S], BF16, "ExternalInput"),
          ("i128", [128, 128], F32, "ExternalInput"),
          ("mrowP", [128, 256], F32, "ExternalOutput")]

D_IOS = [("x2oT", [D, HALF], F32, "ExternalInput"), ("qd", [D, HALF], BF16, "ExternalInput"),
         ("kd", [D, S], BF16, "ExternalInput"), ("vd", [S, D], BF16, "ExternalInput"),
         ("diagm", [16, S], F32, "ExternalInput"), ("diagmo", [16, HALF], F32, "ExternalInput"),
         ("woutT", [D, D], F32, "ExternalInput"), ("boutc", [128, 8], F32, "ExternalInput"),
         ("e65g", [16, 1040], BF16, "ExternalInput"), ("i16b", [16, 16], BF16, "ExternalInput"),
         ("onesf", [1, 128], F32, "ExternalInput"), ("i128", [128, 128], F32, "ExternalInput"),
         ("x3T", [D, HALF], F32, "ExternalOutput"), ("x3lnT", [D, HALF], F32, "ExternalOutput"),
         ("amax", [1, 16], F32, "ExternalOutput")]

F_IOS = [("x3T", [D, HALF], F32, "ExternalInput"), ("x3lnT", [D, HALF], F32, "ExternalInput"),
         ("amax8", [1, 16], F32, "ExternalInput"), ("wT", [D, D], F32, "ExternalInput"),
         ("biasc", [128, 8], F32, "ExternalInput"), ("onesf", [1, 128], F32, "ExternalInput"),
         ("i128", [128, 128], F32, "ExternalInput"), ("outT", [D, HALF], F32, "ExternalOutput")]


def build_launch_A(upto="full"):
    return _standalone(emit_A, A_IOS, upto)

def build_launch_AB(upto="full"):
    nc = bass.Bass("TRN2", target_bir_lowering=False)
    t = {}
    seen = set()
    for (name, shape, dt_, kind) in A_IOS + B_IOS:
        if name in seen:
            continue
        seen.add(name)
        k = "Internal" if name == "x1T" else kind
        t[name] = nc.dram_tensor(name, shape, dt_, kind=k).ap()
    with (
        nc.sbuf_tensor("AB16", [128, GNB16], BF16) as bt,
        nc.sbuf_tensor("AF32", [128, GNF32], F32) as ft,
        nc.sbuf_tensor("AI32", [128, 2048], mybir.dt.int32) as it,
        nc.psum_tensor("PS", [128, 4096], F32) as PS,
    ):
        p = Prog()
        ctx = dict(nc=nc, p=p, bt=bt, ft=ft, PS=PS, t=t, it=it)
        barA = emit_A(ctx, upto)
        p.set_floor(barA)
        emit_B(ctx, upto)
        p.op("gpsimd", lambda e: e.nop(), deps=p.barrier())
        p.build(nc)
    return nc

def build_launch_B(upto="full"):
    return _standalone(emit_B, B_IOS, upto)

def build_launch_C1(upto="full"):
    return _standalone(emit_C1, C1_IOS, upto)

def build_launch_C2(upto="full"):
    return _standalone(emit_C2, C2_IOS, upto)

def build_launch_D(upto="full"):
    return _standalone(emit_D, D_IOS, upto)

def build_launch_F(upto="full"):
    return _standalone(emit_F, F_IOS, upto)


PAIRS_RG = [[0, 1], [2, 3], [4, 5], [6, 7]]
ALL_RG = [[0, 1, 2, 3, 4, 5, 6, 7]]


def build_fused():
    nc = bass.Bass("TRN2", target_bir_lowering=False)
    t = {}
    ext_inputs = [
        ("xbT", [D, S], F32), ("xoT", [D, HALF], F32), ("wq", [D, 1024], F32),
        ("wk", [D, 256], F32), ("wv", [D, 256], F32), ("wo", [D, 1024], F32),
        ("posr", [1, S], F32), ("poso", [1, HALF], F32), ("didx", [1, 32], F32),
        ("onesf", [1, 128], F32), ("jrep", [32, 128], F32), ("jsgn", [32, 128], F32),
        ("w1T", [D, 4096], F32), ("w2T", [4096, D], F32),
        ("g1c", [128, 8], F32), ("b1lnc", [128, 8], F32), ("b1c", [128, 32], F32),
        ("g2c", [128, 32], F32), ("b2lnc", [128, 32], F32), ("b2c", [128, 8], F32),
        ("i128", [128, 128], F32), ("wqkvT", [D, 3072], F32),
        ("woutT", [D, D], F32), ("boutc", [128, 8], F32),
        ("e65g", [16, 1040], BF16), ("i16b", [16, 16], BF16),
        ("wT", [D, D], F32), ("biasc", [128, 8], F32),
    ]
    for (name, shape, dt_) in ext_inputs:
        t[name] = nc.dram_tensor(name, shape, dt_, kind="ExternalInput").ap()
    outT = nc.dram_tensor("outT", [D, HALF], F32, kind="ExternalOutput").ap()
    x1d = nc.dram_tensor("x1d", [D, HALF], F32, kind="Internal").ap()
    x2own = nc.dram_tensor("x2own", [1, D * HALF], F32, kind="Internal").ap()
    x2g = nc.dram_tensor("x2g", [2, D * HALF], F32, kind="Internal").ap()
    qdD = nc.dram_tensor("qdD", [D, HALF], BF16, kind="Internal").ap()
    kdD = nc.dram_tensor("kdD", [D, S], BF16, kind="Internal").ap()
    koD = nc.dram_tensor("koD", [D, HALF], BF16, kind="Internal").ap()
    vdD = nc.dram_tensor("vdD", [S, D], BF16, kind="Internal").ap()
    mrowD = nc.dram_tensor("mrowD", [1, 16 * HALF], F32, kind="Internal").ap()
    mrowG = nc.dram_tensor("mrowG", [2, 16 * HALF], F32, kind="Internal").ap()
    x3d = nc.dram_tensor("x3d", [D, HALF], F32, kind="Internal").ap()
    x3lnd = nc.dram_tensor("x3lnd", [D, HALF], F32, kind="Internal").ap()
    amaxD = nc.dram_tensor("amaxD", [1, 16], F32, kind="Internal").ap()
    amaxG = nc.dram_tensor("amaxG", [1, 16], F32, kind="Internal").ap()

    with (
        nc.sbuf_tensor("AB16", [128, GNB16], BF16) as bt,
        nc.sbuf_tensor("AF32", [128, GNF32], F32) as ft,
        nc.psum_tensor("PS", [128, 4096], F32) as PS,
    ):
        p = Prog()

        def ctx_for(names_map, extra=None):
            tt_ = dict(t)
            tt_.update(names_map)
            c = dict(nc=nc, p=p, bt=bt, ft=ft, PS=PS, t=tt_)
            if extra:
                c.update(extra)
            return c

        emit_A(ctx_for({"x1T": x1d}))
        p.set_floor(p.barrier())
        emit_B(ctx_for({"x1T": x1d,
                        "x2T": x2own.rearrange("o (a b) -> (o a) b", b=HALF)}))
        p.set_floor(p.barrier())
        hag = p.dma("gpsimd", lambda e: e.collective_compute(
            "AllGather", ALU.bypass, PAIRS_RG, [x2own], [x2g]))
        p.set_floor(p.barrier())
        x2g4 = x2g.rearrange("r (c p i) -> r p c i", p=128, i=HALF)
        x2chunks = [x2g4[ic8 // 4, :, :, (ic8 % 4) * 512:(ic8 % 4) * 512 + 512]
                    for ic8 in range(8)]
        emit_C1(ctx_for({"x2T": x2g.rearrange("r (c i) -> (r c) i", i=HALF)[0:D, :],
                         "x2oT": x2own.rearrange("o (a b) -> (o a) b", b=HALF),
                         "qd": qdD, "kd": kdD, "vd": vdD, "koT": koD},
                        extra={"x2chunks": x2chunks}))
        p.set_floor(p.barrier())
        emit_C2(ctx_for({"qd": qdD, "kd": kdD, "koT": koD,
                         "mrow": mrowD.rearrange("o (h i) -> (o h) i", i=HALF)}))
        p.set_floor(p.barrier())
        hagm = p.dma("gpsimd", lambda e: e.collective_compute(
            "AllGather", ALU.bypass, PAIRS_RG, [mrowD], [mrowG]))
        p.set_floor(p.barrier())
        mg = mrowG.rearrange("r (h i) -> r h i", i=HALF)
        emit_D(ctx_for({"x2oT": x2own.rearrange("o (a b) -> (o a) b", b=HALF),
                        "qd": qdD, "kd": kdD, "vd": vdD,
                        "diagm": mrowG.rearrange("r (h i) -> (r h) i", i=HALF)[0:16, :],
                        "diagmo": mrowD.rearrange("o (h i) -> (o h) i", i=HALF),
                        "x3T": x3d, "x3lnT": x3lnd, "amax": amaxD},
                       extra={"diag2": [mg[0], mg[1]]}))
        p.set_floor(p.barrier())
        har = p.dma("gpsimd", lambda e: e.collective_compute(
            "AllReduce", ALU.max, ALL_RG, [amaxD], [amaxG]))
        p.set_floor(p.barrier())
        emit_F(ctx_for({"x3T": x3d, "x3lnT": x3lnd, "amax8": amaxG,
                        "outT": outT}))
        p.op("gpsimd", lambda e: e.nop(), deps=p.barrier())
        p.build(nc)
    return nc


def make_fused_inputs(inputs):
    in_A = make_A_inputs(inputs)
    x_dummy = np.zeros((B, S, D), dtype=np.float32)
    common_B = make_B_inputs(inputs, x_dummy)[0]
    wqkvT = np.ascontiguousarray(inputs["L1_attn_w_qkv"].T.astype(np.float32))
    woutT = np.ascontiguousarray(inputs["L1_attn_w_out"].T.astype(np.float32))
    bout = np.asarray(inputs["L1_attn_b_out"], dtype=np.float32)
    wTt = np.ascontiguousarray(inputs["L1_ffn_weight"].T.astype(np.float32))
    biasv = np.asarray(inputs["L1_ffn_bias"], dtype=np.float32)
    import ml_dtypes
    in_maps = []
    for c in range(NCORES):
        m = dict(in_A[c])
        for k in ("w1T", "w2T", "g1c", "b1lnc", "b1c", "g2c", "b2lnc", "b2c",
                  "i128"):
            m[k] = common_B[k]
        m["wqkvT"] = wqkvT
        m["woutT"] = woutT
        m["boutc"] = np.ascontiguousarray(bout.reshape(-1, 128).T)
        m["e65g"] = _e65g()
        m["i16b"] = np.eye(16, dtype=np.float32).astype(ml_dtypes.bfloat16)
        m["wT"] = wTt
        m["biasc"] = np.ascontiguousarray(biasv.reshape(-1, 128).T)
        in_maps.append(m)
    return in_maps


def kernel_fused(**inputs):
    cids = list(range(NCORES))
    in_maps = make_fused_inputs(inputs)
    r = run_bass_kernel_spmd(build_fused(), in_maps, core_ids=cids)
    out = np.empty((B, S, D), dtype=np.float32)
    for i, c in enumerate(cids):
        b, half = c // 2, c % 2
        out[b, half * HALF:(half + 1) * HALF] = r.results[i]["outT"].T
    return out



def emit_A(ctx, upto="full"):
    nc, p, bt, ft, PS, t = ctx["nc"], ctx["p"], ctx["bt"], ctx["ft"], ctx["PS"], ctx["t"]
    xbT = t["xbT"]
    xoT = t["xoT"]
    wq = t["wq"]
    wk = t["wk"]
    wv = t["wv"]
    wo = t["wo"]
    posr = t["posr"]
    poso = t["poso"]
    onesf = t["onesf"]
    jrep = t["jrep"]
    jsgn = t["jsgn"]
    x1T = t["x1T"]
    didx = t["didx"]
    if True:
        ba = Arena(bt, GNB16)
        fa = Arena(ft, GNF32)

        onesb = ba.alloc(128, 32)
        qT = ba.alloc(128, 8 * HALF)
        o_qT = ba.last_off
        kT = ba.alloc(128, 2 * S)
        Vp = ba.alloc(128, 32 * 260)
        sqb = ba.alloc(128, 4096)
        o_sqb = ba.last_off
        ba.mark()
        wqb = ba.alloc(128, 8 * 1024)
        wkb = ba.alloc(128, 8 * 256)
        wvb = ba.alloc(128, 8 * 256)
        xlnc = ba.alloc(128, 4096)

        onesfT = fa.alloc(1, 128)
        onescol = fa.alloc(128, 32)
        rowsc = fa.alloc(128, 4096)
        scrA = fa.alloc(128, 4096)
        scrB = fa.alloc(128, 1024)
        scrC = fa.alloc(128, 1024)
        fa.mark()
        CCk = fa.alloc(128, S)
        SSk = fa.alloc(128, S)
        CCq = fa.alloc(128, HALF)
        SSq = fa.alloc(128, HALF)
        jrA = fa.alloc(64, 128)
        o_jr = fa.last_off
        jsA = fa.alloc(64, 128)
        o_js = fa.last_off
        invfr = fa.alloc(1, 64)
        didxr = fa.alloc(1, 64)
        jrepT = ft[32:64, o_jr:o_jr + 128]
        jsgnT = ft[32:64, o_js:o_js + 128]

        frq = scrA[0:32, :]
        u32 = rowsc[0:32, :]
        srow = scrA[32:64, :]

        ps_s = [PS[:, 0:1536], PS[:, 1536:3072]]
        ps_av = PS[:, 3072:3584]
        ps_m = PS[:, 3584:4096]
        ps_b0 = PS[:, 0:512]
        ps_b1 = PS[:, 512:1024]
        ps_b3 = PS[:, 1536:2048]
        ps_b4 = PS[:, 2048:2560]

        h_ones = p.op("vector", lambda e: e.memset(onesb[:, 0:1], 1.0))
        h_ob1 = p.op("vector", lambda e: e.memset(onesb[:, 1:2], 1.0 / 1024.0),
                     deps=[h_ones])
        h_ocol = p.op("vector", lambda e: e.memset(onescol[:, 0:1], 1.0 / 1024.0))
        h_negc = p.op("vector", lambda e: e.memset(onescol[:, 1:2], -C_L0))
        h_onesf = p.dma("sync", lambda e: e.dma_start(out=onesfT, in_=onesf))
        h_j1 = p.dma("sync", lambda e: e.dma_start(out=jrepT, in_=jrep))
        h_j2 = p.dma("sync", lambda e: e.dma_start(out=jsgnT, in_=jsgn))

        h_di = p.dma("sync", lambda e: e.dma_start(out=didxr[0:1, 0:32], in_=didx))
        h_invf = p.op("scalar", act(invfr[0:1, 0:32], didxr[0:1, 0:32], AF.Exp,
                                    scale=-LN10K / 32.0), deps=[h_di])

        def build_tables2(pos_ap, n, CXd, SXd, dep0):
            hp = p.dma("sync", lambda e: e.dma_start(out=rowsc[0:1, 0:n], in_=pos_ap),
                       deps=dep0)
            hf = [hp, h_invf]
            for c4 in range(n // 512):
                sl = slice(c4 * 512, (c4 + 1) * 512)
                hm = p.op("tensor", (lambda sl: mm(ps_m[0:32, 0:512], invfr[0:1, 0:32],
                                                   rowsc[0:1, sl]))(sl), deps=hf)
                hf = [p.op("scalar", (lambda sl: act(frq[:, sl], ps_m[0:32, 0:512],
                                                     AF.Identity))(sl), deps=[hm])]
            h_frq = hf[0]

            def sin_rows(shift, deps):
                OFF = 1024.0
                fr = frq[:, 0:n]
                uu = u32[:, 0:n]
                dk = CXd[0:32, 0:n]
                h1 = p.op("vector", ts(uu, fr, 1.0 / TWO_PI, ALU.mult,
                                       shift / TWO_PI + 0.5 + OFF, ALU.add), deps=deps)
                hfr = frac_cascade(p, uu, dk, 10, 11, [h1])
                h2 = p.op("vector", ts(dk, fr, 1.0 / TWO_PI, ALU.mult,
                                       shift / TWO_PI + 0.5 + OFF, ALU.add), deps=[hfr])
                h3 = p.op("vector", tt(dk, dk, uu, ALU.subtract), deps=[h2])
                h4 = p.op("vector", ts(dk, dk, -OFF, ALU.add), deps=[h3])
                h5 = p.op("vector", stt(uu, dk, -TWO_PI_HI, fr, ALU.mult, ALU.add),
                          deps=[h4])
                h6 = p.op("vector", ts(uu, uu, float(shift), ALU.add), deps=[h5])
                h7 = p.op("vector", stt(uu, dk, -TWO_PI_LO, uu, ALU.mult, ALU.add),
                          deps=[h6])
                h8 = p.op("vector", ts(uu, uu, float(np.pi), ALU.min,
                                       float(-np.pi), ALU.max), deps=[h7])
                return p.op("scalar", act(srow[:, 0:n], uu, AF.Sin), deps=[h8])

            hs = sin_rows(0.0, [h_frq])
            hlast = hs
            for c4 in range(n // 512):
                sl = slice(c4 * 512, (c4 + 1) * 512)
                h1 = p.op("tensor", (lambda sl: mm(ps_m[:, 0:512], jsgnT,
                                                   srow[:, sl]))(sl),
                          deps=[hs, h_j2] if c4 == 0 else [hlast, hs])
                hlast = p.op("scalar", (lambda sl: act(SXd[:, sl], ps_m[:, 0:512],
                                                       AF.Identity))(sl), deps=[h1])
            hc = sin_rows(np.pi / 2, [hlast])
            hlast = hc
            for c4 in range(n // 512):
                sl = slice(c4 * 512, (c4 + 1) * 512)
                h1 = p.op("tensor", (lambda sl: mm(ps_m[:, 0:512], jrepT,
                                                   srow[:, sl]))(sl),
                          deps=[hc, h_j1] if c4 == 0 else [hlast, hc])
                hlast = p.op("scalar", (lambda sl: act(CXd[:, sl], ps_m[:, 0:512],
                                                       AF.Identity))(sl), deps=[h1])
            return hlast

        h_t1 = build_tables2(posr, S, CCk, SSk, [h_di])
        h_t2 = build_tables2(poso, HALF, CCq, SSq, [h_t1])

        def load_weight(dst, src, ncols, dep0):
            dep = dep0
            out = None
            for c in range(8):
                hd = p.dma("sync", (lambda c: lambda e: e.dma_start(
                    out=scrB[:, 0:ncols], in_=src[c * 128:(c + 1) * 128, :]))(c),
                    deps=dep)
                out = p.op("vector", (lambda c: cp(dst[:, c * ncols:(c + 1) * ncols],
                                                   scrB[:, 0:ncols]))(c), deps=[hd])
                dep = [out]
            return out

        h_wq = load_weight(wqb, wq, 1024, [h_t2])
        h_wk = load_weight(wkb, wk, 256, [h_wq])
        h_wv = load_weight(wvb, wv, 256, [h_wk])
        bar_w = p.barrier()

        xr3 = xbT.rearrange("(c p) i -> p c i", p=128)
        xo3 = xoT.rearrange("(c p) i -> p c i", p=128)
        scrA3 = scrA.rearrange("p (c i) -> p c i", i=512)

        def ln_chunk(src3, tok, deps):
            hx = p.dma("sync", (lambda tok: lambda e: e.dma_start(
                out=scrA3, in_=src3[:, :, tok]))(tok), deps=deps)
            hsq = p.op("scalar", act(sqb, scrA, AF.Square), deps=[hx])

            def emit_sum(dst, lhs_col, srct):
                def f(e):
                    last = None
                    for c in range(8):
                        last = e.matmul(dst, lhs_col,
                                        srct[:, c * 512:(c + 1) * 512],
                                        start=(c == 0), stop=(c == 7))
                    return last
                return f
            hsum = p.op("tensor", emit_sum(ps_b0[0:1, :], onescol[:, 0:1], scrA),
                        deps=[hx, h_ocol] + deps, n=8)
            hsqs = p.op("tensor", emit_sum(ps_b1[0:1, :], onesb[:, 1:2], sqb),
                        deps=[hsq, h_ob1] + deps, n=8)
            rm = rowsc[0:1, 0:512]
            rv = rowsc[0:1, 512:1024]
            rr = rowsc[0:1, 1024:1536]
            rmb = rowsc[0:1, 1536:2048]
            h1 = p.op("vector", cp(rm, ps_b0[0:1, :]), deps=[hsum] + deps)
            h2 = p.op("scalar", act(rv, rm, AF.Square), deps=[h1])
            h3 = p.op("vector", stt(rv, rv, -1.0, ps_b1[0:1, :], ALU.mult, ALU.add),
                      deps=[h2, hsqs])
            h3b = p.op("vector", ts(rv, rv, EPS, ALU.add), deps=[h3])
            h4 = p.op("scalar", act(rr, rv, AF.Ln), deps=[h3b])
            h5 = p.op("scalar", act(rr, rr, AF.Exp, scale=-0.5), deps=[h4])
            h6 = p.op("vector", stt(rmb, rm, -1.0, rr, ALU.mult, ALU.mult), deps=[h5])
            hb1 = p.op("tensor", mm(ps_b3[:, :], onesfT, rr), deps=[h5, h_onesf] + deps)
            hb2 = p.op("tensor", mm(ps_b4[:, :], onesfT, rmb), deps=[h6])
            happ = []
            hprev = None
            for c in range(8):
                blk = slice(c * 512, c * 512 + 512)
                ha = p.op("vector", (lambda blk: tt(scrB[:, 0:512], scrA[:, blk],
                                                    ps_b3[:, :], ALU.mult))(blk),
                          deps=[hb1, hx] + deps + ([hprev] if hprev else []))
                hb_ = p.op("vector", (lambda blk: tt(xlnc[:, blk], scrB[:, 0:512],
                                                     ps_b4[:, :], ALU.add))(blk),
                          deps=[ha, hb2])
                hprev = hb_
                happ.append(hb_)
            return happ, [h1, h3, h6, hb1, hb2]

        def rope_evac(dst_ap, tokslice, CX, SX, deps):
            hsw1 = p.op("vector", cp(scrC[0:64, 0:512], ps_m[64:128, :]), deps=deps)
            hsw2 = p.op("vector", cp(scrC[64:128, 0:512], ps_m[0:64, :]), deps=deps)
            ht1 = p.op("vector", (lambda: tt(scrB[:, 0:512], ps_m[:, :],
                                             CX[:, tokslice], ALU.mult))(),
                       deps=[hsw1, hsw2])
            ht2 = p.op("vector", (lambda: tt(scrC[:, 512:1024], scrC[:, 0:512],
                                             SX[:, tokslice], ALU.mult))(),
                       deps=[ht1])
            he = p.op("vector", tt(dst_ap, scrB[:, 0:512], scrC[:, 512:1024], ALU.add),
                      deps=[ht2])
            return he

        prev_tail = bar_w
        h_psm_free = bar_w
        h_psav_free = bar_w
        for ic8 in range(8):
            tok = slice(ic8 * 512, ic8 * 512 + 512)
            happ, hstats = ln_chunk(xr3, tok, [prev_tail] if not isinstance(prev_tail, list) else prev_tail)
            evacs = []
            for kc in range(2):
                def emit_kp(kc):
                    def f(e):
                        last = None
                        for c in range(8):
                            last = e.matmul(
                                ps_m[:, :],
                                wkb[:, c * 256 + kc * 128: c * 256 + kc * 128 + 128],
                                xlnc[:, c * 512:(c + 1) * 512],
                                start=(c == 0), stop=(c == 7))
                        return last
                    return f
                hacc = p.op("tensor", emit_kp(kc),
                            deps=happ + [h_psm_free, bar_w], n=8)
                dstc = kc * S + ic8 * 512
                he = rope_evac(kT[:, dstc:dstc + 512], tok, CCk, SSk, [hacc])
                h_psm_free = he
                evacs.append(he)
            for jq in range(4):
                jc = ic8 * 4 + jq

                def emit_vp(jq):
                    def f(e):
                        last = None
                        for c in range(8):
                            last = e.matmul(
                                ps_av[:, 0:256],
                                xlnc[:, c * 512 + jq * 128: c * 512 + jq * 128 + 128],
                                wvb[:, c * 256:(c + 1) * 256],
                                start=(c == 0), stop=(c == 7))
                        return last
                    return f
                hacc = p.op("tensor", emit_vp(jq),
                            deps=happ + [h_psav_free, bar_w], n=8)
                vdst = Vp[:, jc * 260: jc * 260 + 260].rearrange(
                    "p (h e) -> p h e", e=65)
                hvc = p.op("vector", (lambda vdst: cp(
                    vdst[:, :, 0:64],
                    ps_av[:, 0:256].rearrange("p (h e) -> p h e", e=64)))(vdst),
                    deps=[hacc])
                hvo = p.op("vector", (lambda vdst: lambda e: e.memset(
                    vdst[:, :, 64:65], 1.0))(vdst), deps=[hvc])
                h_psav_free = hvc
            prev_tail = happ + hstats + evacs + [h_psm_free, h_psav_free]

        for oc in range(4):
            otok = slice(oc * 512, oc * 512 + 512)
            happ, hstats = ln_chunk(xo3, otok, prev_tail)
            evacs = []
            for dqc in range(8):
                def emit_qp(dqc):
                    def f(e):
                        last = None
                        for c in range(8):
                            last = e.matmul(
                                ps_m[:, :],
                                wqb[:, c * 1024 + dqc * 128: c * 1024 + dqc * 128 + 128],
                                xlnc[:, c * 512:(c + 1) * 512],
                                start=(c == 0), stop=(c == 7))
                        return last
                    return f
                hacc = p.op("tensor", emit_qp(dqc),
                            deps=happ + [h_psm_free, bar_w], n=8)
                dstc = dqc * HALF + oc * 512
                he = rope_evac(qT[:, dstc:dstc + 512], otok, CCq, SSq, [hacc])
                h_psm_free = he
                evacs.append(he)
            prev_tail = happ + hstats + evacs + [h_psm_free]

        bar_proj = p.barrier()

        h_rep = bar_proj
        for (tile, width, nch) in ((qT, HALF, 8), (kT, S, 2)):
            for c in range(nch):
                sl = slice(c * width, (c + 1) * width)
                w_ = width
                h1 = p.op("vector", (lambda tile, sl, w_: cp(
                    sqb[0:32, 0:w_], tile[32:64, sl]))(tile, sl, w_), deps=h_rep)
                h2 = p.op("vector", (lambda tile, sl: cp(
                    tile[32:64, sl], tile[64:96, sl]))(tile, sl), deps=[h1])
                h3 = p.op("vector", (lambda tile, sl, w_: cp(
                    tile[64:96, sl], sqb[0:32, 0:w_]))(tile, sl, w_), deps=[h2])
                h_rep = [h3]
        bar_rep = p.barrier()

        PTb = [bt[0:128, o_sqb:o_sqb + 1024], bt[0:128, o_sqb + 1024:o_sqb + 2048]]
        ba.release()
        numT = ba.alloc(128, 8 * HALF)

        ps_pair = [PS[:, 0:1024], PS[:, 1024:2048]]
        ps_avA = PS[:, 2048:2560]
        ps_avB = PS[:, 2560:3072]
        ps_nrm = PS[:, 3072:3584]

        h_expbuf = [bar_rep, bar_rep]
        h_avbuf = [bar_rep, bar_rep]
        h_avA_free = [bar_rep]
        h_avB_free = [bar_rep]
        h_nrmA_prev = bar_rep
        h_nrmB_prev = bar_rep
        h_den_readers = [bar_rep]
        denrow = rowsc[0:1, 0:4096]
        attn_limit = 9999
        skip_norm = False
        skip_av = False
        skip_den = False
        if upto.startswith("attn") and len(upto) > 4:
            tail = upto[4:]
            while tail and tail[-1] in "nad":
                if tail[-1] == "n":
                    skip_norm = True
                elif tail[-1] == "a":
                    skip_av = True
                    skip_den = True
                    skip_norm = True
                elif tail[-1] == "d":
                    skip_den = True
                    skip_norm = True
                tail = tail[:-1]
            attn_limit = int(tail)
        attn_iter = 0
        for pr, (hA, hB) in enumerate(QPAIRS):
            qc = pr
            kvA, kvB = hA // 4, hB // 4
            kcA, kbA = kvA // 2, (kvA % 2) * 64
            kcB, kbB = kvB // 2, (kvB % 2) * 64
            assert kbA == 0 and kbB == 64
            pair_dens = []
            pair_copies = {}
            for ic in range(4):
                if attn_iter >= attn_limit:
                    continue
                attn_iter += 1
                isl = slice(qc * HALF + ic * 512, qc * HALF + ic * 512 + 512)
                h_avA = h_avB = None
                pend = None
                for jc in range(32):
                    buf = jc % 2

                    def emit_spair(buf, jc, kcA, kcB, isl):
                        def f(e):
                            e.matmul(ps_pair[buf][:, 0:512],
                                     kT[0:64, kcA * S + jc * 128: kcA * S + jc * 128 + 128],
                                     qT[0:64, isl], start=True, stop=True)
                            return e.matmul(
                                ps_pair[buf][:, 512:1024],
                                kT[64:128, kcB * S + jc * 128: kcB * S + jc * 128 + 128],
                                qT[64:128, isl], start=True, stop=True)
                        return f
                    hs = p.op("tensor", emit_spair(buf, jc, kcA, kcB, isl),
                              deps=[h_expbuf[buf], bar_rep], n=2)

                    if pend is not None and not skip_av:
                        pjc, pbuf, phex = pend

                        def emit_avs(pjc, pbuf, side):
                            kv = kvA if side == 0 else kvB
                            def f(e):
                                return e.matmul(
                                    (ps_avA if side == 0 else ps_avB)[0:65, :],
                                    Vp[:, pjc * 260 + kv * 65: pjc * 260 + kv * 65 + 65],
                                    PTb[pbuf][:, side * 512: side * 512 + 512],
                                    start=(pjc == 0), stop=(pjc == 31))
                            return f
                        h_avA = p.op("tensor", emit_avs(pjc, pbuf, 0),
                                     deps=([phex] + (h_avA_free if pjc == 0
                                                     else [h_avA])))
                        h_avB = p.op("tensor", emit_avs(pjc, pbuf, 1),
                                     deps=([phex] + (h_avB_free if pjc == 0
                                                     else [h_avB])))
                        h_avbuf[pbuf] = h_avB

                    hex_ = p.op("scalar", (lambda buf: act(
                        PTb[buf][:, 0:1024], ps_pair[buf][:, 0:1024],
                        AF.Exp, bias=onescol[:, 1:2], scale=0.125))(buf),
                        deps=[hs, h_avbuf[buf], h_negc])
                    h_expbuf[buf] = hex_
                    pend = (jc, buf, hex_)

                if pend is not None and not skip_av:
                    pjc, pbuf, phex = pend

                    def emit_avs(pjc, pbuf, side):
                        kv = kvA if side == 0 else kvB
                        def f(e):
                            return e.matmul(
                                (ps_avA if side == 0 else ps_avB)[0:65, :],
                                Vp[:, pjc * 260 + kv * 65: pjc * 260 + kv * 65 + 65],
                                PTb[pbuf][:, side * 512: side * 512 + 512],
                                start=(pjc == 0), stop=(pjc == 31))
                        return f
                    h_avA = p.op("tensor", emit_avs(pjc, pbuf, 0),
                                 deps=[phex, h_avA])
                    h_avB = p.op("tensor", emit_avs(pjc, pbuf, 1),
                                 deps=[phex, h_avB])
                    h_avbuf[pbuf] = h_avB
                if skip_av:
                    continue

                for side, (qb, ps_av_s, h_av_s) in enumerate(
                        ((0, ps_avA, h_avA), (64, ps_avB, h_avB))):
                    frees = []
                    if not skip_den:
                        hd1 = p.op("vector", (lambda side, ic, ps_av_s: cp(
                            denrow[0:1, side * 2048 + ic * 512:
                                   side * 2048 + ic * 512 + 512],
                            ps_av_s[64:65, :]))(side, ic, ps_av_s),
                            deps=[h_av_s] + h_den_readers)
                        pair_dens.append(hd1)
                        frees.append(hd1)
                    hcopy = p.op("vector", (lambda qb, ps_av_s: cp(
                        numT[qb:qb + 64, isl], ps_av_s[0:64, :]))(qb, ps_av_s),
                        deps=[h_av_s])
                    pair_copies[(side, ic)] = hcopy
                    frees.append(hcopy)
                    if side == 0:
                        h_avA_free = frees
                    else:
                        h_avB_free = frees
            if not (skip_av or skip_den or skip_norm) and pair_dens:
                hrec = p.op("vector",
                            lambda e: e.reciprocal(out=denrow, in_=denrow),
                            deps=pair_dens)
                readers = []
                for side, qb in ((0, 0), (1, 64)):
                    hn_prev = h_nrmA_prev if side == 0 else h_nrmB_prev
                    for ic in range(4):
                        if (side, ic) not in pair_copies:
                            continue
                        isl2 = slice(qc * HALF + ic * 512, qc * HALF + ic * 512 + 512)
                        rsl = slice(side * 2048 + ic * 512,
                                    side * 2048 + ic * 512 + 512)
                        hf = p.op("tensor", (lambda qb, rsl: mm(
                            ps_nrm[qb:qb + 64, :], onesfT[0:1, 0:64],
                            denrow[0:1, rsl]))(qb, rsl),
                            deps=[hrec, hn_prev])
                        hn = p.op("vector", (lambda qb, isl2: tt(
                            numT[qb:qb + 64, isl2], numT[qb:qb + 64, isl2],
                            ps_nrm[qb:qb + 64, :], ALU.mult))(qb, isl2),
                            deps=[hf, pair_copies.get((side, ic), bar_rep)])
                        readers.append(hf)
                        hn_prev = hn
                    if side == 0:
                        h_nrmA_prev = hn_prev
                    else:
                        h_nrmB_prev = hn_prev
                h_den_readers = readers
        bar_attn = p.barrier()

        wob = bt[0:128, o_qT:o_qT + 8 * 1024]
        dep = bar_attn
        h_wo = None
        for c in range(8):
            hd = p.dma("sync", (lambda c: lambda e: e.dma_start(
                out=scrB[:, 0:1024], in_=wo[c * 128:(c + 1) * 128, :]))(c), deps=dep)
            h_wo = p.op("vector", (lambda c: cp(wob[:, c * 1024:(c + 1) * 1024],
                                                scrB[:, 0:1024]))(c), deps=[hd])
            dep = [h_wo]
        x1r3 = x1T.rearrange("(c p) i -> p c i", p=128)
        scrB3 = scrB.rearrange("p (c i) -> p c i", i=512)
        hout = []
        hra = [h_wo]
        h_psm_free = bar_attn
        for ic in range(4):
            otok = slice(ic * 512, ic * 512 + 512)
            hx = p.dma("sync", (lambda otok: lambda e: e.dma_start(
                out=scrA3, in_=xo3[:, :, otok]))(otok), deps=hra)
            hnew = []
            for doc in range(8):
                def emit_op(doc, ic):
                    def f(e):
                        last = None
                        for c in range(8):
                            last = e.matmul(
                                ps_m[:, :],
                                wob[:, c * 1024 + doc * 128: c * 1024 + doc * 128 + 128],
                                numT[:, c * HALF + ic * 512: c * HALF + ic * 512 + 512],
                                start=(c == 0), stop=(c == 7))
                        return last
                    return f
                hacc = p.op("tensor", emit_op(doc, ic),
                            deps=[h_wo, bar_attn, h_psm_free], n=8)
                blk = slice(doc * 512, doc * 512 + 512)
                hr = p.op("vector", (lambda blk: tt(scrA[:, blk], ps_m[:, :],
                                                    scrA[:, blk], ALU.add))(blk),
                          deps=[hacc, hx])
                h_psm_free = hr
                hnew.append(hr)
            ho = p.dma("gpsimd", (lambda ic: lambda e: e.dma_start(
                out=x1r3[:, :, slice(ic * 512, ic * 512 + 512)], in_=scrA3))(ic),
                deps=hnew)
            hout.append(ho)
            hra = hnew + [ho]
        p.op("gpsimd", lambda e: e.nop(), deps=hout)

    return p.barrier()





def emit_B(ctx, upto="full"):
    nc, p, bt, ft, PS, t = ctx["nc"], ctx["p"], ctx["bt"], ctx["ft"], ctx["PS"], ctx["t"]
    x1T = t["x1T"]
    w1T = t["w1T"]
    w2T = t["w2T"]
    g1c = t["g1c"]
    b1lnc = t["b1lnc"]
    b1c = t["b1c"]
    g2c = t["g2c"]
    b2lnc = t["b2lnc"]
    b2c = t["b2c"]
    onesf = t["onesf"]
    i128 = t["i128"]
    x2T = t["x2T"]
    it = ctx["it"]
    if True:
        ba = Arena(bt, GNB16)
        fa = Arena(ft, GNF32)

        dqw1 = ba.alloc(128, 8 * 4096)
        h1q = ba.alloc(128, 32 * 512)
        o_h1q = ba.last_off
        xq = ba.alloc(128, 8 * 512)
        w2q = ba.alloc(128, 1024)
        onesb = ba.alloc(128, 32)
        bsc = bt[0:128, o_h1q:o_h1q + 4096]

        onesfT = fa.alloc(1, 128)
        i128T = fa.alloc(128, 128)
        fcol = fa.alloc(128, 32)
        cols = fa.alloc(128, 160)
        rowsc = fa.alloc(128, 4096)
        scrA = fa.alloc(128, 4096)
        scrT = fa.alloc(128, 4096)
        scrU = fa.alloc(128, 4096)
        scrV = fa.alloc(128, 512)
        scrW = fa.alloc(128, 512)
        scrD = fa.alloc(128, 512)
        scrM = fa.alloc(128, 512)
        invs1B = fa.alloc(128, 512)
        invs2B = fa.alloc(128, 512)

        PSB = [PS[:, k * 512:(k + 1) * 512] for k in range(8)]

        h_of = p.dma("sync", lambda e: e.dma_start(out=onesfT, in_=onesf))
        h_i128 = p.dma("sync", lambda e: e.dma_start(out=i128T, in_=i128))
        for (dst0, w_, srcw) in ((0, 8, g1c), (8, 8, b1lnc), (16, 32, b1c),
                                 (48, 32, g2c), (80, 32, b2lnc), (112, 8, b2c)):
            p.dma("sync", (lambda dst0, w_, srcw: lambda e: e.dma_start(
                out=cols[:, dst0:dst0 + w_], in_=srcw))(dst0, w_, srcw))
        h_ones = p.op("vector", lambda e: e.memset(onesb[:, 0:1], 1.0))
        h_o1 = p.op("vector", lambda e: e.memset(onesb[:, 1:2], 1.0 / 1024.0),
                    deps=[h_ones])
        h_o2 = p.op("vector", lambda e: e.memset(onesb[:, 2:3], 1.0 / 4096.0),
                    deps=[h_o1])
        h_fc = p.op("vector", lambda e: e.memset(fcol[:, 0:1], 1.0 / 1024.0))
        h_fc2 = p.op("vector", lambda e: e.memset(fcol[:, 1:2], 1.0 / 4096.0),
                     deps=[h_fc])
        bar0 = p.barrier()

        def wmean(srcw, nchunks, ncols, out_idx, dep0):
            ncc = ncols // 512
            assert ncc <= 8
            hacc = [None] * ncc
            dep = list(dep0)
            habs = None
            for c in range(nchunks):
                hd = p.dma("sync", (lambda c: lambda e: e.dma_start(
                    out=scrU[:, 0:ncols], in_=srcw[c * 128:(c + 1) * 128, :]))(c),
                    deps=dep + ([habs] if habs else []))
                habs = p.op("scalar", (lambda ncols: act(bsc[:, 0:ncols],
                                                         scrU[:, 0:ncols], AF.Abs))(ncols),
                            deps=[hd])
                dep = []
                for cc in range(ncc):
                    hacc[cc] = p.op("tensor", (lambda cc, c: mm(
                        PSB[cc % 4][0:1, (cc // 4) * 512 if False else 0:512][0:1, 0:512]
                        if False else PSB[cc % 4][0:1, :], onesb[:, 0:1],
                        bsc[:, cc * 512:cc * 512 + 512],
                        start=(c == 0), stop=(c == nchunks - 1)))(cc, c),
                        deps=[habs, h_ones] + ([hacc[cc]] if hacc[cc] else []))
            hred = None
            for cc in range(ncc):
                h1 = p.op("vector", (lambda cc: cp(rowsc[0:1, 0:512],
                                                   PSB[cc % 4][0:1, :]))(cc),
                          deps=[hacc[cc]] + ([hred] if hred else []))
                h2 = p.op("vector", (lambda cc: lambda e: e.reduce_sum(
                    out=rowsc[0:1, 1024 + cc:1025 + cc], in_=rowsc[0:1, 0:512],
                    axis=mybir.AxisListType.X))(cc), deps=[h1])
                hred = h2
            h3 = p.op("vector", (lambda ncc, out_idx: lambda e: e.reduce_sum(
                out=rowsc[0:1, 1040 + out_idx:1041 + out_idx],
                in_=rowsc[0:1, 1024:1024 + ncc],
                axis=mybir.AxisListType.X))(ncc, out_idx), deps=[hred])
            return h3

        def wmean_cols(srcw, nchunks, ncols, out_idx, dep0):
            if ncols <= 2048:
                return wmean(srcw, nchunks, ncols, out_idx, dep0)
            h1 = wmean_sub(srcw, nchunks, ncols, 0, ncols // 2, 14, dep0)
            h2 = wmean_sub(srcw, nchunks, ncols, ncols // 2, ncols // 2, 15, [h1])
            h3 = p.op("vector", (lambda out_idx: lambda e: e.reduce_sum(
                out=rowsc[0:1, 1040 + out_idx:1041 + out_idx],
                in_=rowsc[0:1, 1038:1040],
                axis=mybir.AxisListType.X))(out_idx), deps=[h2])
            return h3

        def wmean_sub(srcw, nchunks, ncols, c0, cw, tmp_idx, dep0):
            ncc = cw // 512
            hacc = [None] * ncc
            dep = list(dep0)
            habs = None
            for c in range(nchunks):
                hd = p.dma("sync", (lambda c, c0, cw: lambda e: e.dma_start(
                    out=scrU[:, 0:cw], in_=srcw[c * 128:(c + 1) * 128, c0:c0 + cw]))(c, c0, cw),
                    deps=dep + ([habs] if habs else []))
                habs = p.op("scalar", (lambda cw: act(bsc[:, 0:cw],
                                                      scrU[:, 0:cw], AF.Abs))(cw),
                            deps=[hd])
                dep = []
                for cc in range(ncc):
                    hacc[cc] = p.op("tensor", (lambda cc, c: mm(
                        PSB[cc][0:1, :], onesb[:, 0:1],
                        bsc[:, cc * 512:cc * 512 + 512],
                        start=(c == 0), stop=(c == nchunks - 1)))(cc, c),
                        deps=[habs, h_ones] + ([hacc[cc]] if hacc[cc] else []))
            hred = None
            for cc in range(ncc):
                h1 = p.op("vector", (lambda cc: cp(rowsc[0:1, 0:512],
                                                   PSB[cc][0:1, :]))(cc),
                          deps=[hacc[cc]] + ([hred] if hred else []))
                h2 = p.op("vector", (lambda cc: lambda e: e.reduce_sum(
                    out=rowsc[0:1, 1024 + cc:1025 + cc], in_=rowsc[0:1, 0:512],
                    axis=mybir.AxisListType.X))(cc), deps=[h1])
                hred = h2
            h3 = p.op("vector", (lambda ncc, tmp_idx: lambda e: e.reduce_sum(
                out=rowsc[0:1, 1024 + tmp_idx:1025 + tmp_idx],
                in_=rowsc[0:1, 1024:1024 + ncc],
                axis=mybir.AxisListType.X))(ncc, tmp_idx), deps=[hred])
            return h3

        h_w1m = wmean_cols(w1T, 8, 4096, 0, bar0)
        h_w2m = wmean(w2T, 32, 1024, 1, [h_w1m])
        h5 = p.op("vector", ts(rowsc[0:1, 1044:1045], rowsc[0:1, 1040:1041],
                               1.0 / (1024 * 4096), ALU.mult, 1e-5, ALU.max),
                  deps=[h_w2m])
        h6 = p.op("vector", ts(rowsc[0:1, 1045:1046], rowsc[0:1, 1041:1042],
                               1.0 / (4096 * 1024), ALU.mult, 1e-5, ALU.max),
                  deps=[h5])
        hb = p.op("tensor", mm(PSB[7][:, 0:2], onesfT, rowsc[0:1, 1044:1046]),
                  deps=[h6, h_of])
        hc1 = p.op("vector", cp(cols[:, 124:126], PSB[7][:, 0:2]), deps=[hb])
        h_t1 = p.op("vector", ts(cols[:, 120:121], cols[:, 124:125], 0.5, ALU.mult),
                    deps=[hc1])
        h_t1n = p.op("vector", ts(cols[:, 121:122], cols[:, 120:121], -1.0, ALU.mult),
                     deps=[h_t1])
        h_t2 = p.op("vector", ts(cols[:, 122:123], cols[:, 125:126], 0.5, ALU.mult),
                    deps=[h_t1n])
        h_t2n = p.op("vector", ts(cols[:, 123:124], cols[:, 122:123], -1.0, ALU.mult),
                     deps=[h_t2])
        bar_mean = p.barrier()

        dep = bar_mean
        for c in range(8):
            hd = p.dma("sync", (lambda c: lambda e: e.dma_start(
                out=scrU, in_=w1T[c * 128:(c + 1) * 128, :]))(c), deps=dep)
            hp_ = p.op("vector", ts(scrT, scrU, cols[:, 120:121], ALU.is_gt),
                       deps=[hd, h_t1])
            hn_ = p.op("vector", ts(rowsc, scrU, cols[:, 121:122], ALU.is_lt),
                       deps=[hd, h_t1n])
            hq = p.op("vector", (lambda c: tt(dqw1[:, c * 4096:(c + 1) * 4096],
                                              scrT, rowsc, ALU.subtract))(c),
                      deps=[hp_, hn_])
            dep = [hq, hp_, hn_]
        bar_w1 = p.barrier()

        prev_stats = [bar_w1]

        def row_stats(hsum, hsqs):
            nonlocal_box = prev_stats
            rm = rowsc[0:1, 0:512]
            rv = rowsc[0:1, 512:1024]
            rr = rowsc[0:1, 2048:2560]
            rmb = rowsc[0:1, 2560:3072]
            h1 = p.op("vector", cp(rm, PSB[0][0:1, :]),
                      deps=[hsum] + list(nonlocal_box))
            h2 = p.op("scalar", act(rv, rm, AF.Square), deps=[h1])
            h3 = p.op("vector", stt(rv, rv, -1.0, PSB[1][0:1, :], ALU.mult, ALU.add),
                      deps=[h2, hsqs])
            h3b = p.op("vector", ts(rv, rv, EPS, ALU.add), deps=[h3])
            h4 = p.op("scalar", act(rr, rv, AF.Ln), deps=[h3b])
            h5_ = p.op("scalar", act(rr, rr, AF.Exp, scale=-0.5), deps=[h4])
            h6_ = p.op("vector", stt(rmb, rm, -1.0, rr, ALU.mult, ALU.mult), deps=[h5_])
            hb1 = p.op("tensor", mm(PSB[3][:, :], onesfT, rr), deps=[h5_, h_of])
            hb2 = p.op("tensor", mm(PSB[4][:, :], onesfT, rmb), deps=[h6_])
            nonlocal_box.clear()
            nonlocal_box.extend([hb1, hb2, h6_])
            return [hb1, hb2]

        def token_scale(mean_col, invs_dstB, habs_last):
            hc1 = p.op("vector", cp(scrV[0:64, :], scrM[64:128, :]), deps=[habs_last])
            hm1 = p.op("vector", tt(scrM[0:64, :], scrM[0:64, :], scrV[0:64, :],
                                    ALU.max), deps=[hc1])
            hc2 = p.op("vector", cp(scrV[0:32, :], scrM[32:64, :]), deps=[hm1])
            hm2 = p.op("vector", tt(scrM[0:32, :], scrM[0:32, :], scrV[0:32, :],
                                    ALU.max), deps=[hc2])
            htr = []
            for g in range(4):
                h1 = p.op("tensor", (lambda g: lambda e: e.transpose(
                    PSB[5][0:128, g * 32:(g + 1) * 32],
                    scrM[0:32, g * 128:(g + 1) * 128],
                    i128T[0:32, 0:32]))(g),
                    deps=[hm2, h_i128] + ([htr[-1]] if htr else []))
                h2 = p.op("vector", (lambda g: lambda e: e.reduce_max(
                    out=scrD[:, g:g + 1], in_=PSB[5][0:128, g * 32:(g + 1) * 32],
                    axis=mybir.AxisListType.X))(g), deps=[h1])
                htr.append(h2)
            hcl = p.op("vector", ts(scrD[:, 0:4], scrD[:, 0:4], 1e-5, ALU.max),
                       deps=htr)
            hs = p.op("vector", lambda e: e.reciprocal(out=scrD[:, 4:8],
                                                       in_=scrD[:, 0:4]), deps=[hcl])
            hs2 = p.op("vector", ts(scrD[:, 4:8], scrD[:, 4:8], 127.0, ALU.mult),
                       deps=[hs])
            hi = p.op("vector", ts(scrD[:, 8:12], scrD[:, 0:4],
                                   cols[:, mean_col:mean_col + 1],
                                   ALU.mult, 1.0 / 127.0, ALU.mult), deps=[hcl])
            hrow = []
            for gi, srcc in enumerate((4, 5, 6, 7, 8, 9, 10, 11)):
                dstc = gi * 128
                h1 = p.op("tensor", (lambda srcc: lambda e: e.transpose(
                    PSB[5][0:1, 0:128], scrD[:, srcc:srcc + 1], i128T))(srcc),
                    deps=[hs2, hi, h_i128] + ([hrow[-1]] if hrow else []))
                h2 = p.op("vector", (lambda dstc: cp(
                    rowsc[0:1, 1024 + dstc:1024 + dstc + 128],
                    PSB[5][0:1, 0:128]))(dstc), deps=[h1])
                hrow.append(h2)
            hbs = p.op("tensor", mm(PSB[2][:, :], onesfT, rowsc[0:1, 1024:1536]),
                       deps=[hrow[-1], h_of])
            hbi = p.op("tensor", mm(PSB[5][0:128, :], onesfT, rowsc[0:1, 1536:2048]),
                       deps=[hrow[-1], hbs])
            hbc = p.op("scalar", (lambda: act(invs_dstB, PSB[5][0:128, :],
                                              AF.Identity))(), deps=[hbi])
            return [hbs, hbc]

        def quant_wide(src_ap, dst_ap, w, deps):
            r = w // 512
            fsc = scrT[:, 0:w]
            f3 = fsc.rearrange("p (r t) -> p r t", t=512)
            s3 = src_ap.rearrange("p (r t) -> p r t", t=512)
            sc3 = PSB[2][:, :].unsqueeze(1).broadcast_to((128, r, 512))
            h1 = p.op("vector", (lambda f3, s3, sc3: lambda e: e.tensor_tensor(
                out=f3, in0=s3, in1=sc3, op=ALU.mult))(f3, s3, sc3), deps=deps)
            h2 = p.op("vector", ts(fsc, fsc, 127.4999, ALU.min,
                                   -128.4999, ALU.max), deps=[h1])
            h3 = p.op("vector", (lambda fsc, w: lambda e: e.tensor_scalar_add(
                out=it[:, 0:w], in0=fsc, scalar1=0.0))(fsc, w), deps=[h2])
            h4 = p.op("vector", (lambda fsc, w: cp(fsc, it[:, 0:w]))(fsc, w),
                      deps=[h3])
            h5_ = p.op("vector", cp(dst_ap, fsc), deps=[h4])
            return h5_

        x1r3 = x1T.rearrange("(c p) i -> p c i", p=128)
        x2r3 = x2T.rearrange("(c p) i -> p c i", p=128)
        scrA3 = scrA.rearrange("p (c i) -> p c i", i=512)

        h_prev = bar_w1
        for Q in range(4):
            qtok = slice(Q * 512, Q * 512 + 512)
            hx = p.dma("sync", (lambda qtok: lambda e: e.dma_start(
                out=scrA3, in_=x1r3[:, :, qtok]))(qtok), deps=h_prev)
            def emit_sum8(dst, lhs_col, srct):
                def f(e):
                    last = None
                    for c in range(8):
                        last = e.matmul(dst, lhs_col,
                                        srct[:, c * 512:(c + 1) * 512],
                                        start=(c == 0), stop=(c == 7))
                    return last
                return f
            hsum = p.op("tensor", emit_sum8(PSB[0][0:1, :], fcol[:, 0:1], scrA),
                        deps=[hx, h_fc] + list(h_prev), n=8)
            hsq = p.op("scalar", act(bsc, scrA, AF.Square), deps=[hx])
            hsqs = p.op("tensor", emit_sum8(PSB[1][0:1, :], onesb[:, 1:2], bsc),
                        deps=[hsq, h_o1], n=8)
            hstats = row_stats(hsum, hsqs)
            happ = []
            hpv = None
            for c in range(8):
                blk = slice(c * 512, c * 512 + 512)
                ha = p.op("vector", (lambda blk: tt(scrT[:, blk], scrA[:, blk],
                                                    PSB[3][:, :], ALU.mult))(blk),
                          deps=hstats + ([hpv] if hpv else []))
                hb_ = p.op("vector", (lambda blk: tt(scrT[:, blk], scrT[:, blk],
                                                     PSB[4][:, :], ALU.add))(blk),
                          deps=[ha])
                hpv = hb_
                happ.append(hb_)
            hsum = p.op("tensor", emit_sum8(PSB[0][0:1, :], fcol[:, 0:1], scrT),
                        deps=happ + [hstats[0]], n=8)
            hsq = p.op("scalar", act(bsc, scrT, AF.Square), deps=happ)
            hsqs = p.op("tensor", emit_sum8(PSB[1][0:1, :], onesb[:, 1:2], bsc),
                        deps=[hsq], n=8)
            hstats = row_stats(hsum, hsqs)
            habs = None
            hq1dep = []
            for c in range(8):
                blk = slice(c * 512, c * 512 + 512)
                h1 = p.op("vector", (lambda blk: tt(scrU[:, blk], scrT[:, blk],
                                                    PSB[3][:, :], ALU.mult))(blk),
                          deps=hstats + hq1dep[-1:])
                h2 = p.op("vector", (lambda blk: tt(scrU[:, blk], scrU[:, blk],
                                                    PSB[4][:, :], ALU.add))(blk),
                          deps=[h1])
                h3 = p.op("vector", (lambda blk, c: ts(scrU[:, blk], scrU[:, blk],
                                                       cols[:, c:c + 1], ALU.mult,
                                                       cols[:, 8 + c:9 + c],
                                                       ALU.add))(blk, c),
                          deps=[h2])
                h4 = p.op("scalar", (lambda blk: act(scrV, scrU[:, blk], AF.Abs))(blk),
                          deps=[h3])
                if habs is None:
                    habs = p.op("vector", cp(scrM, scrV), deps=[h4])
                else:
                    habs = p.op("vector", tt(scrM, scrM, scrV, ALU.max),
                                deps=[h4, habs])
                hq1dep = [habs]
            hts = token_scale(124, invs1B, habs)
            hq = hts
            for c in range(2):
                blk = slice(c * 2048, c * 2048 + 2048)
                hq = [quant_wide(scrU[:, blk], xq[:, blk], 2048, hq + hts)]
            bar_q1 = p.barrier()
            h_ev = {6: bar_q1, 7: bar_q1}
            hgel = []
            for hc in range(32):
                bank = 6 + (hc % 2)

                def emit_mm1(hc, bank):
                    def f(e):
                        last = None
                        for c in range(8):
                            last = e.matmul(
                                PSB[bank][:, :],
                                dqw1[:, c * 4096 + hc * 128: c * 4096 + hc * 128 + 128],
                                xq[:, c * 512:(c + 1) * 512],
                                start=(c == 0), stop=(c == 7))
                        return last
                    return f
                hacc = p.op("tensor", emit_mm1(hc, bank),
                            deps=[h_ev[bank], bar_q1], n=8)
                he1 = p.op("vector", (lambda bank: tt(scrV, PSB[bank][:, :],
                                                      invs1B, ALU.mult))(bank),
                           deps=[hacc] + hgel[-1:])
                he2 = p.op("scalar", (lambda hc: act(
                    h1q[:, hc * 512:(hc + 1) * 512], scrV, AF.Gelu,
                    bias=cols[:, 16 + hc:17 + hc]))(hc), deps=[he1])
                h_ev[bank] = he1
                hgel.append(he2)
            hsum = None
            hsqs = None
            hsqp = None
            for hc in range(32):
                blk = slice(hc * 512, hc * 512 + 512)
                hsum = p.op("tensor", (lambda blk, st, sp: mm(
                    PSB[0][0:1, :], onesb[:, 2:3], h1q[:, blk],
                    start=st, stop=sp))(blk, hc == 0, hc == 31),
                    deps=[hgel[hc], h_o2] + ([hsum] if hsum else []))
                hsqp = p.op("scalar", (lambda blk: act(xq[:, 0:512], h1q[:, blk],
                                                       AF.Square))(blk),
                            deps=[hgel[hc]] + ([hsqs] if hsqs else []))
                hsqs = p.op("tensor", (lambda st, sp: mm(
                    PSB[1][0:1, :], onesb[:, 2:3], xq[:, 0:512],
                    start=st, stop=sp))(hc == 0, hc == 31),
                    deps=[hsqp])
            hstats = row_stats(hsum, hsqs)
            habs = None
            hq2dep = []
            for hc in range(32):
                blk = slice(hc * 512, hc * 512 + 512)
                h1 = p.op("vector", (lambda blk: tt(scrV, h1q[:, blk],
                                                    PSB[3][:, :], ALU.mult))(blk),
                          deps=hstats + hq2dep[-1:])
                h2 = p.op("vector", (lambda blk: tt(scrV, scrV,
                                                    PSB[4][:, :], ALU.add))(blk),
                          deps=[h1])
                h3 = p.op("vector", (lambda blk, hc: ts(scrV, scrV,
                                                        cols[:, 48 + hc:49 + hc],
                                                        ALU.mult,
                                                        cols[:, 80 + hc:81 + hc],
                                                        ALU.add))(blk, hc),
                          deps=[h2])
                h4 = p.op("scalar", act(scrW, scrV, AF.Abs), deps=[h3])
                if habs is None:
                    habs = p.op("vector", cp(scrM, scrW), deps=[h4])
                else:
                    habs = p.op("vector", tt(scrM, scrM, scrW, ALU.max),
                                deps=[h4, habs])
                h5_ = p.op("vector", (lambda blk: cp(h1q[:, blk], scrV))(blk),
                           deps=[h3, habs])
                hq2dep = [h5_, habs]
            hts2 = token_scale(125, invs2B, habs)
            hq = hts2 + hq2dep
            for hc4 in range(8):
                blk = slice(hc4 * 2048, hc4 * 2048 + 2048)
                hq = [quant_wide(h1q[:, blk], h1q[:, blk], 2048, hq + hts2)]
            bar_q2 = p.barrier()
            hw2 = bar_q2
            hmm2 = {d: None for d in range(8)}
            for hc in range(32):
                hd = p.dma("sync", (lambda hc: lambda e: e.dma_start(
                    out=scrU[:, 0:1024], in_=w2T[hc * 128:(hc + 1) * 128, :]))(hc),
                    deps=[hw2] if not isinstance(hw2, list) else hw2)
                hp_ = p.op("vector", ts(scrT[:, 0:1024], scrU[:, 0:1024],
                                        cols[:, 122:123], ALU.is_gt),
                           deps=[hd, h_t2])
                hn_ = p.op("vector", ts(scrT[:, 1024:2048], scrU[:, 0:1024],
                                        cols[:, 123:124], ALU.is_lt),
                           deps=[hd, h_t2n])
                hw2q = p.op("vector", tt(w2q, scrT[:, 0:1024], scrT[:, 1024:2048],
                                         ALU.subtract), deps=[hp_, hn_])
                def emit_mm2(hc):
                    def f(e):
                        last = None
                        for dout in range(8):
                            last = e.matmul(PSB[dout][:, :],
                                            w2q[:, dout * 128:(dout + 1) * 128],
                                            h1q[:, hc * 512:(hc + 1) * 512],
                                            start=(hc == 0), stop=(hc == 31))
                        return last
                    return f
                hall = p.op("tensor", emit_mm2(hc),
                            deps=[hw2q, bar_q2] + ([hmm2[7]] if hmm2[7] else []),
                            n=8)
                for dout in range(8):
                    hmm2[dout] = hall
                hw2 = [hall]
            hev = []
            for dout in range(8):
                blk = slice(dout * 512, dout * 512 + 512)
                he1 = p.op("vector", (lambda dout: tt(scrV, PSB[dout][:, :],
                                                      invs2B, ALU.mult))(dout),
                           deps=[hmm2[dout]] + hev[-1:])
                he2 = p.op("scalar", (lambda dout: act(scrW, scrV, AF.Identity,
                                                       bias=cols[:, 112 + dout:113 + dout]))(dout),
                           deps=[he1])
                he3 = p.op("vector", (lambda blk: tt(scrA[:, blk], scrW,
                                                     scrA[:, blk], ALU.add))(blk),
                           deps=[he2])
                hev.append(he3)
            ho = p.dma("gpsimd", (lambda Q: lambda e: e.dma_start(
                out=x2r3[:, :, slice(Q * 512, Q * 512 + 512)], in_=scrA3))(Q),
                deps=hev)
            h_prev = p.barrier()
    return p.barrier()



def make_B_inputs(inputs, x1):
    w1T = np.ascontiguousarray(inputs["L0_ffn_w1"].T.astype(np.float32))
    w2T = np.ascontiguousarray(inputs["L0_ffn_w2"].T.astype(np.float32))

    def colpack(v):
        v = np.asarray(v, dtype=np.float32)
        return np.ascontiguousarray(v.reshape(-1, 128).T)

    common = dict(
        w1T=w1T, w2T=w2T,
        g1c=colpack(inputs["L0_ffn_ln1_g"]), b1lnc=colpack(inputs["L0_ffn_ln1_b"]),
        b1c=colpack(inputs["L0_ffn_b1"]), g2c=colpack(inputs["L0_ffn_ln2_g"]),
        b2lnc=colpack(inputs["L0_ffn_ln2_b"]), b2c=colpack(inputs["L0_ffn_b2"]),
        onesf=np.ones((1, 128), dtype=np.float32),
        i128=np.eye(128, dtype=np.float32))
    in_maps = []
    for c in range(NCORES):
        b, half = c // 2, c % 2
        m = dict(common)
        m["x1T"] = np.ascontiguousarray(
            x1[b, half * HALF:(half + 1) * HALF].T.astype(np.float32))
        in_maps.append(m)
    return in_maps




def emit_C1(ctx, upto="full"):
    nc, p, bt, ft, PS, t = ctx["nc"], ctx["p"], ctx["bt"], ctx["ft"], ctx["PS"], ctx["t"]
    x2T = t["x2T"]
    x2oT = t["x2oT"]
    wqkvT = t["wqkvT"]
    onesf = t["onesf"]
    qd = t["qd"]
    kd = t["kd"]
    vd = t["vd"]
    x2chunks = ctx.get("x2chunks")
    if True:
        ba = Arena(bt, GNB16)
        fa = Arena(ft, GNF32)

        wb = ba.alloc(128, 8 * 3072)
        xlnc = ba.alloc(128, 4096)
        sqb = ba.alloc(128, 4096)
        onesb = ba.alloc(128, 32)
        evb = ba.alloc(128, 4096)

        onesfT = fa.alloc(1, 128)
        fcol = fa.alloc(128, 32)
        rowsc = fa.alloc(128, 4096)
        scrA = fa.alloc(128, 4096)
        scrB = fa.alloc(128, 1024)

        PSB = [PS[:, k * 512:(k + 1) * 512] for k in range(8)]

        h_of = p.dma("sync", lambda e: e.dma_start(out=onesfT, in_=onesf))
        h_ones = p.op("vector", lambda e: e.memset(onesb[:, 0:1], 1.0))
        h_o1 = p.op("vector", lambda e: e.memset(onesb[:, 1:2], 1.0 / 1024.0),
                    deps=[h_ones])
        h_fc = p.op("vector", lambda e: e.memset(fcol[:, 0:1], 1.0 / 1024.0))

        dep = [h_of]
        h_w = None
        for c in range(8):
            hd = p.dma("sync", (lambda c: lambda e: e.dma_start(
                out=scrA[:, 0:3072], in_=wqkvT[c * 128:(c + 1) * 128, :]))(c),
                deps=dep)
            h_w = p.op("vector", (lambda c: cp(wb[:, c * 3072:(c + 1) * 3072],
                                               scrA[:, 0:3072]))(c), deps=[hd])
            dep = [h_w]
        bar_w = p.barrier()

        x2r3 = x2T.rearrange("(c p) i -> p c i", p=128)
        x2o3 = x2oT.rearrange("(c p) i -> p c i", p=128)
        scrA3 = scrA.rearrange("p (c i) -> p c i", i=512)
        qd3 = qd.rearrange("(c p) i -> p c i", p=128)
        kd3 = kd.rearrange("(c p) i -> p c i", p=128)

        def ln_chunk(src3, tok, deps):
            hx = p.dma("sync", (lambda tok: lambda e: e.dma_start(
                out=scrA3, in_=src3[:, :, tok]))(tok), deps=deps)
            hsq = p.op("scalar", act(sqb, scrA, AF.Square), deps=[hx])

            def emit_sum(dst, lhs_col, srct):
                def f(e):
                    last = None
                    for c in range(8):
                        last = e.matmul(dst, lhs_col,
                                        srct[:, c * 512:(c + 1) * 512],
                                        start=(c == 0), stop=(c == 7))
                    return last
                return f
            hsum = p.op("tensor", emit_sum(PSB[0][0:1, :], fcol[:, 0:1], scrA),
                        deps=[hx, h_fc] + deps, n=8)
            hsqs = p.op("tensor", emit_sum(PSB[1][0:1, :], onesb[:, 1:2], sqb),
                        deps=[hsq, h_o1], n=8)
            rm = rowsc[0:1, 0:512]
            rv = rowsc[0:1, 512:1024]
            rr = rowsc[0:1, 1024:1536]
            rmb = rowsc[0:1, 1536:2048]
            h1 = p.op("vector", cp(rm, PSB[0][0:1, :]), deps=[hsum] + deps)
            h2 = p.op("scalar", act(rv, rm, AF.Square), deps=[h1])
            h3 = p.op("vector", stt(rv, rv, -1.0, PSB[1][0:1, :], ALU.mult, ALU.add),
                      deps=[h2, hsqs])
            h3b = p.op("vector", ts(rv, rv, EPS, ALU.add), deps=[h3])
            h4 = p.op("scalar", act(rr, rv, AF.Ln), deps=[h3b])
            h5 = p.op("scalar", act(rr, rr, AF.Exp, scale=-0.5), deps=[h4])
            h6 = p.op("vector", stt(rmb, rm, -1.0, rr, ALU.mult, ALU.mult), deps=[h5])
            hb1 = p.op("tensor", mm(PSB[3][:, :], onesfT, rr), deps=[h5, h_of])
            hb2 = p.op("tensor", mm(PSB[4][:, :], onesfT, rmb), deps=[h6])
            happ = []
            hpv = None
            for c in range(8):
                blk = slice(c * 512, c * 512 + 512)
                ha = p.op("vector", (lambda blk: tt(scrB[:, 0:512], scrA[:, blk],
                                                    PSB[3][:, :], ALU.mult))(blk),
                          deps=[hb1, hx] + ([hpv] if hpv else []))
                hb_ = p.op("vector", (lambda blk: tt(xlnc[:, blk], scrB[:, 0:512],
                                                     PSB[4][:, :], ALU.add))(blk),
                          deps=[ha, hb2])
                hpv = hb_
                happ.append(hb_)
            return happ, [h1, h3, h6, hb1, hb2]

        def ln_chunk2(src_ap, deps):
            class _W:
                def __getitem__(self, idx):
                    return src_ap
            return ln_chunk(_W(), None, deps)

        h_prev = bar_w
        for ic8 in range(8):
            tok = slice(ic8 * 512, ic8 * 512 + 512)
            if x2chunks is not None:
                happ, hstats = ln_chunk2(x2chunks[ic8],
                                         h_prev if isinstance(h_prev, list) else [h_prev])
            else:
                happ, hstats = ln_chunk(x2r3, tok, h_prev if isinstance(h_prev, list) else [h_prev])
            hk = []
            for outc in range(8):
                def emit_proj(outc, base):
                    def f(e):
                        last = None
                        for c in range(8):
                            last = e.matmul(
                                PSB[outc][:, :],
                                wb[:, c * 3072 + base + outc * 128:
                                   c * 3072 + base + outc * 128 + 128],
                                xlnc[:, c * 512:(c + 1) * 512],
                                start=(c == 0), stop=(c == 7))
                        return last
                    return f
                hacc = p.op("tensor", emit_proj(outc, 1024),
                            deps=happ + hstats + [bar_w] + hk[-1:], n=8)
                he = p.op("vector", (lambda outc: cp(
                    evb[:, outc * 512:(outc + 1) * 512], PSB[outc][:, :]))(outc),
                    deps=[hacc])
                hk.append(he)
            hko = p.dma("gpsimd", (lambda tok: lambda e: e.dma_start(
                out=kd3[:, :, tok],
                in_=evb.rearrange("p (c i) -> p c i", i=512)))(tok), deps=hk)
            hv = []
            for jq in range(4):
                jc = ic8 * 4 + jq

                def emit_vproj(jq):
                    def f(e):
                        last = None
                        for c in range(8):
                            lhs = xlnc[:, c * 512 + jq * 128: c * 512 + jq * 128 + 128]
                            e.matmul(PSB[0][:, :], lhs,
                                     wb[:, c * 3072 + 2048: c * 3072 + 2048 + 512],
                                     start=(c == 0), stop=(c == 7))
                            last = e.matmul(
                                PSB[1][:, :], lhs,
                                wb[:, c * 3072 + 2560: c * 3072 + 2560 + 512],
                                start=(c == 0), stop=(c == 7))
                        return last
                    return f
                hacc = p.op("tensor", emit_vproj(jq), deps=happ + hk + hv, n=16)
                hev = p.op("vector", cp(evb[:, 0:512], PSB[0][:, :]),
                           deps=[hacc, hko])
                hev2 = p.op("vector", cp(evb[:, 512:1024], PSB[1][:, :]),
                            deps=[hacc, hev])
                hvo = p.dma("gpsimd", (lambda jc: lambda e: e.dma_start(
                    out=vd[jc * 128:(jc + 1) * 128, :], in_=evb[:, 0:1024]))(jc),
                    deps=[hev, hev2])
                hv = [hev, hev2, hvo]
            h_prev = happ + hstats + hk + hv + [hko]

        for oc in range(4):
            otok = slice(oc * 512, oc * 512 + 512)
            happ, hstats = ln_chunk(x2o3, otok, h_prev if isinstance(h_prev, list) else [h_prev])
            hk = []
            for outc in range(8):
                def emit_proj(outc, base):
                    def f(e):
                        last = None
                        for c in range(8):
                            last = e.matmul(
                                PSB[outc][:, :],
                                wb[:, c * 3072 + base + outc * 128:
                                   c * 3072 + base + outc * 128 + 128],
                                xlnc[:, c * 512:(c + 1) * 512],
                                start=(c == 0), stop=(c == 7))
                        return last
                    return f
                hacc = p.op("tensor", emit_proj(outc, 0),
                            deps=happ + hstats + [bar_w] + hk[-1:], n=8)
                he = p.op("vector", (lambda outc: cp(
                    evb[:, outc * 512:(outc + 1) * 512], PSB[outc][:, :]))(outc),
                    deps=[hacc])
                hk.append(he)
            hqo = p.dma("gpsimd", (lambda otok: lambda e: e.dma_start(
                out=qd3[:, :, otok],
                in_=evb.rearrange("p (c i) -> p c i", i=512)))(otok), deps=hk)
            h_prev = happ + hstats + hk + [hqo]

    return p.barrier()




def emit_C2(ctx, upto="full"):
    nc, p, bt, ft, PS, t = ctx["nc"], ctx["p"], ctx["bt"], ctx["ft"], ctx["PS"], ctx["t"]
    qdi = t["qd"]
    kdi = t["kd"]
    i128 = t["i128"]
    mrowP = t["mrowP"]
    if True:
        ba = Arena(bt, GNB16)
        fa = Arena(ft, GNF32)

        qT = ba.alloc(128, 8 * HALF)
        kT = ba.alloc(128, 8 * S)
        screxp = ba.alloc(128, 1024)

        i128f = fa.alloc(128, 128)
        dtile = fa.alloc(128, 128)
        dparts = fa.alloc(128, 128)
        denP = fa.alloc(128, 256)
        diagP = fa.alloc(128, 256)
        negc = fa.alloc(128, 32)

        h_i = p.dma("sync", lambda e: e.dma_start(out=i128f, in_=i128))
        h_negc = p.op("vector", lambda e: e.memset(negc[:, 0:1], -C_L1))
        qd3 = qdi.rearrange("(c p) i -> p c i", p=128)
        kd3 = kdi.rearrange("(c p) i -> p c i", p=128)
        h_lq = p.dma("sync", lambda e: e.dma_start(
            out=qT.rearrange("p (c i) -> p c i", i=HALF), in_=qd3))
        h_lk = p.dma("sync", lambda e: e.dma_start(
            out=kT.rearrange("p (c i) -> p c i", i=S), in_=kd3))
        bar_l = p.barrier()

        h_expfree = [bar_l, bar_l]
        h_dp_readers = [bar_l]
        for c in range(8):
            hA, hB = 2 * c, 2 * c + 1
            for ic in range(16):
                hexs = []
                h_dg = None
                for jp in range(4):
                    def emit_sp(c, ic, jp):
                        def f(e):
                            last = None
                            for jj in range(2):
                                j = jp * 2 + jj
                                for sb in (0, 64):
                                    last = e.matmul(
                                        PS[:, (sb // 64) * 2048 + (jp % 2) * 1024
                                           + jj * 512:
                                           (sb // 64) * 2048 + (jp % 2) * 1024
                                           + jj * 512 + 512],
                                        qT[sb:sb + 64,
                                           c * HALF + ic * 128: c * HALF + ic * 128 + 128],
                                        kT[sb:sb + 64, c * S + j * 512: c * S + j * 512 + 512],
                                        start=True, stop=True)
                            return last
                        return f
                    sdeps = [h_expfree[jp % 2], bar_l]
                    if h_dg is not None and jp == (ic // 8) + 2:
                        sdeps.append(h_dg)
                    hs = p.op("tensor", emit_sp(c, ic, jp), deps=sdeps, n=4)
                    hx = []
                    for side in range(2):
                        hx.append(p.op("scalar", (lambda side, jp, ic: act(
                            screxp, PS[:, side * 2048 + (jp % 2) * 1024:
                                       side * 2048 + (jp % 2) * 1024 + 1024],
                            AF.Exp, bias=negc[:, 0:1], scale=0.125,
                            accum=dparts[:, ic * 8 + side * 4 + jp:
                                         ic * 8 + side * 4 + jp + 1]))(side, jp, ic),
                            deps=[hs, h_negc] + h_dp_readers))
                    h_expfree[jp % 2] = hx
                    hexs.extend(hx)
                    if jp == ic // 8:
                        jd = ic // 4
                        off = (ic * 128) % 512
                        hdt = None
                        for side, hgl in ((0, hA), (1, hB)):
                            cb = side * 2048 + (jp % 2) * 1024 + (jd % 2) * 512 + off
                            ht1 = p.op("vector", (lambda cb: tt(
                                dtile, PS[:, cb:cb + 128], i128f, ALU.mult))(cb),
                                deps=[hs, h_i] + ([hdt] if hdt else []))
                            hdt = p.op("vector", (lambda hgl, ic: lambda e:
                                       e.reduce_sum(
                                           out=diagP[:, hgl * 16 + ic:
                                                     hgl * 16 + ic + 1],
                                           in_=dtile,
                                           axis=mybir.AxisListType.X))(hgl, ic),
                                       deps=[ht1])
                        h_dg = hdt
                hdrs = []
                for side, hgl in ((0, hA), (1, hB)):
                    hdrs.append(p.op("vector", (lambda hgl, ic, side: lambda e:
                                     e.reduce_sum(
                                         out=denP[:, hgl * 16 + ic:
                                                  hgl * 16 + ic + 1],
                                         in_=dparts[:, ic * 8 + side * 4:
                                                    ic * 8 + side * 4 + 4],
                                         axis=mybir.AxisListType.X))(hgl, ic, side),
                                     deps=hexs))
                h_dp_readers = hdrs
        bar_s = p.barrier()
        hr = p.op("vector", lambda e: e.reciprocal(out=denP, in_=denP),
                  deps=bar_s)
        he = p.op("scalar", act(diagP, diagP, AF.Exp, bias=negc[:, 0:1],
                                scale=0.125), deps=bar_s)
        hm = p.op("vector", tt(diagP, diagP, denP, ALU.mult), deps=[hr, he])
        p.dma("gpsimd", lambda e: e.dma_start(out=mrowP, in_=diagP),
              deps=[hm])
    return p.barrier()


def emit_C2_old(ctx, upto="full"):
    nc, p, bt, ft, PS, t = ctx["nc"], ctx["p"], ctx["bt"], ctx["ft"], ctx["PS"], ctx["t"]
    qdi = t["qd"]
    kdi = t["kd"]
    koi = t["koT"]
    onesf = t["onesf"]
    mrow = t["mrow"]
    if True:
        ba = Arena(bt, GNB16)
        fa = Arena(ft, GNF32)

        qT = ba.alloc(128, 8 * HALF)
        kT = ba.alloc(128, 8 * S)
        kos = ba.alloc(128, 2048)
        PTb = [ba.alloc(128, 1536), ba.alloc(128, 1536)]
        onesb = ba.alloc(128, 32)

        onesfT = fa.alloc(1, 128)
        negc = fa.alloc(128, 32)
        rowsc = fa.alloc(128, 4096)

        ps_s = [PS[:, 0:1536], PS[:, 1536:3072]]
        ps_r = PS[:, 3072:3584]
        ps_m = PS[:, 3584:4096]

        h_of = p.dma("sync", lambda e: e.dma_start(out=onesfT, in_=onesf))
        h_ones = p.op("vector", lambda e: e.memset(onesb[:, 0:1], 1.0))
        h_negc = p.op("vector", lambda e: e.memset(negc[:, 0:1], -C_L1))
        h_onef = p.op("vector", lambda e: e.memset(negc[:, 1:2], 1.0))

        qd3 = qdi.rearrange("(c p) i -> p c i", p=128)
        kd3 = kdi.rearrange("(c p) i -> p c i", p=128)
        h_lq = p.dma("sync", lambda e: e.dma_start(
            out=qT.rearrange("p (c i) -> p c i", i=HALF), in_=qd3))
        h_lk = p.dma("sync", lambda e: e.dma_start(
            out=kT.rearrange("p (c i) -> p c i", i=S), in_=kd3))
        bar_l = p.barrier()

        h_ptfree = [bar_l, bar_l]
        h_rs_free = bar_l
        h_kos = bar_l
        for h in range(16):
            hc_, hb_ = h // 2, (h % 2) * 64
            if hb_ == 0:
                h_kos = p.dma("sync", (lambda hc_: lambda e: e.dma_start(
                    out=kos, in_=koi[hc_ * 128:(hc_ + 1) * 128, :]))(hc_),
                    deps=[h_rs_free] if not isinstance(h_rs_free, list) else h_rs_free)
            hpr = p.op("vector", (lambda hc_, hb_: tt(
                rowsc[hb_:hb_ + 64, 0:2048],
                qT[hb_:hb_ + 64, hc_ * HALF:(hc_ + 1) * HALF],
                kos[hb_:hb_ + 64, :],
                ALU.mult))(hc_, hb_), deps=[bar_l, h_rs_free, h_kos])
            hdg = []
            for ic in range(4):
                hdg.append(p.op("tensor", (lambda hb_, ic: mm(
                    ps_m[0:1, 0:512], negc[hb_:hb_ + 64, 1:2],
                    rowsc[hb_:hb_ + 64, ic * 512:(ic + 1) * 512]))(hb_, ic),
                    deps=[hpr, h_onef] + ([hdg[-1]] if hdg else [])))
                hcpd = p.op("vector", (lambda ic: cp(
                    rowsc[0:1, 2048 + ic * 512: 2048 + ic * 512 + 512],
                    ps_m[0:1, 0:512]))(ic), deps=[hdg[-1]])
                hdg.append(hcpd)
            h_diag = hdg[-1]
            for ic in range(4):
                buf = 0
                h_sum = None
                for (j0, gn) in SGROUPS:
                    def emit_mm(buf, j0, gn, hc_, hb_, ic):
                        def f(e):
                            last = None
                            for t_ in range(gn):
                                jc = j0 + t_
                                lhs = kT[hb_:hb_ + 64,
                                         hc_ * S + jc * 128: hc_ * S + jc * 128 + 128]
                                rhs = qT[hb_:hb_ + 64,
                                         hc_ * HALF + ic * 512: hc_ * HALF + ic * 512 + 512]
                                last = e.matmul(ps_s[buf][:, t_ * 512:(t_ + 1) * 512],
                                                lhs, rhs, start=True, stop=True)
                            return last
                        return f
                    hmg = p.op("tensor", emit_mm(buf, j0, gn, hc_, hb_, ic),
                               deps=[h_ptfree[buf], bar_l], n=gn)
                    hex_ = p.op("scalar", (lambda buf, gn: act(
                        PTb[buf][:, 0:gn * 512], ps_s[buf][:, 0:gn * 512],
                        AF.Exp, bias=negc[:, 0:1], scale=0.125))(buf, gn),
                        deps=[hmg, h_negc])
                    def emit_sum(buf, j0, gn):
                        def f(e):
                            last = None
                            for t_ in range(gn):
                                jc = j0 + t_
                                last = e.matmul(ps_r[0:1, 0:512], onesb[:, 0:1],
                                                PTb[buf][:, t_ * 512:(t_ + 1) * 512],
                                                start=(jc == 0), stop=(jc == 31))
                            return last
                        return f
                    h_sum = p.op("tensor", emit_sum(buf, j0, gn),
                                 deps=[hex_, h_rs_free], n=gn)
                    h_ptfree[buf] = h_sum
                    buf ^= 1
                rsr = rowsc[0:1, 1024:1536]
                hcp = p.op("vector", cp(rsr, ps_r[0:1, 0:512]), deps=[h_sum])
                h_rs_free = hcp
                hl = p.op("scalar", act(rsr, rsr, AF.Ln), deps=[hcp])
                hr = p.op("scalar", act(rsr, rsr, AF.Exp, scale=-1.0), deps=[hl])
                hdexp = p.op("scalar", (lambda ic: act(
                    rowsc[0:1, 1536:2048], rowsc[0:1, 2048 + ic * 512: 2560 + ic * 512],
                    AF.Exp, bias=negc[0:1, 0:1], scale=0.125))(ic),
                    deps=[h_diag, hr])
                hm = p.op("vector", (lambda h, ic: tt(
                    rowsc[0:1, 1536:2048],
                    rowsc[0:1, 1536:2048], rsr, ALU.mult))(h, ic), deps=[hdexp])
                hout = p.dma("gpsimd", (lambda h, ic: lambda e: e.dma_start(
                    out=mrow[h:h + 1, ic * 512:(ic + 1) * 512],
                    in_=rowsc[0:1, 1536:2048]))(h, ic), deps=[hm])
                h_rs_free = [hcp, hout]
    return p.barrier()




def emit_D(ctx, upto="full"):
    nc, p, bt, ft, PS, t = ctx["nc"], ctx["p"], ctx["bt"], ctx["ft"], ctx["PS"], ctx["t"]
    x2oT = t["x2oT"]
    qdi = t["qd"]
    kdi = t["kd"]
    vdi = t["vd"]
    diagm = t["diagm"]
    diagmo = t["diagmo"]
    woutT = t["woutT"]
    boutc = t["boutc"]
    e65g = t["e65g"]
    i16b = t["i16b"]
    onesf = t["onesf"]
    i128 = t["i128"]
    x3T = t["x3T"]
    x3lnT = t["x3lnT"]
    amax = t["amax"]
    diag2 = ctx.get("diag2")
    if True:
        ba = Arena(bt, GNB16)
        fa = Arena(ft, GNF32)

        kg = ba.alloc(128, 2 * S)
        o_kg = ba.last_off
        qg = ba.alloc(128, 2 * HALF)
        Vg = ba.alloc(128, 32 * 260)
        numT = ba.alloc(128, 8 * HALF)
        PTb = [ba.alloc(128, 1536), ba.alloc(128, 1536)]
        keepb = ba.alloc(16, 4096)
        keepbo = ba.alloc(16, 2048)
        e65gT = ba.alloc(16, 1040)
        i16bT = ba.alloc(16, 16)
        onesb = ba.alloc(128, 32)

        onesfT = fa.alloc(1, 128)
        negc = fa.alloc(128, 32)
        boutT = fa.alloc(128, 8)
        fcol = fa.alloc(128, 32)
        i128T2 = fa.alloc(128, 128)
        scrC = fa.alloc(128, 4096)
        dg = fa.alloc(16, 4096)
        dg2 = fa.alloc(16, 4096)
        rowsc = fa.alloc(128, 4096)
        scrA = fa.alloc(128, 4096)
        scrB = fa.alloc(128, 1024)
        scrM = fa.alloc(128, 512)

        ps_s = [PS[:, 0:1536], PS[:, 1536:3072]]
        ps_av = PS[:, 3072:3584]
        ps_m = PS[:, 3584:4096]
        PSB = [PS[:, k * 512:(k + 1) * 512] for k in range(8)]

        h_of = p.dma("sync", lambda e: e.dma_start(out=onesfT, in_=onesf))
        h_e65 = p.dma("sync", lambda e: e.dma_start(out=e65gT, in_=e65g))
        h_i16 = p.dma("sync", lambda e: e.dma_start(out=i16bT, in_=i16b))
        h_bout = p.dma("sync", lambda e: e.dma_start(out=boutT, in_=boutc))
        if diag2 is None:
            h_dg = p.dma("sync", lambda e: e.dma_start(out=dg, in_=diagm))
        else:
            h_dga = p.dma("sync", lambda e: e.dma_start(out=dg[0:16, 0:2048],
                                                        in_=diag2[0]))
            h_dg = p.dma("sync", lambda e: e.dma_start(out=dg[0:16, 2048:4096],
                                                       in_=diag2[1]), deps=[h_dga])
        h_dgo = p.dma("sync", lambda e: e.dma_start(out=dg2[0:16, 0:2048], in_=diagmo))
        h_i1282 = p.dma("sync", lambda e: e.dma_start(out=i128T2, in_=i128))
        h_ones = p.op("vector", lambda e: e.memset(onesb[:, 0:1], 1.0))
        h_negc = p.op("vector", lambda e: e.memset(negc[:, 0:1], -C_L1))

        r0 = rowsc[0:16, 0:32]
        h1 = p.op("vector", lambda e: e.reduce_sum(out=r0[:, 0:1], in_=dg,
                                                   axis=mybir.AxisListType.X),
                  deps=[h_dg])
        h2 = p.op("scalar", act(scrA[0:16, 0:4096], dg, AF.Square), deps=[h_dg])
        h3 = p.op("vector", lambda e: e.reduce_sum(out=r0[:, 1:2],
                                                   in_=scrA[0:16, 0:4096],
                                                   axis=mybir.AxisListType.X),
                  deps=[h2])
        h4 = p.op("vector", ts(r0[:, 2:3], r0[:, 0:1], 1.0 / 4096.0, ALU.mult),
                  deps=[h1])
        h5 = p.op("scalar", act(r0[:, 3:4], r0[:, 2:3], AF.Square), deps=[h4])
        h6 = p.op("vector", ts(r0[:, 4:5], r0[:, 1:2], 1.0 / 4095.0, ALU.mult),
                  deps=[h3])
        h7 = p.op("vector", stt(r0[:, 5:6], r0[:, 3:4], -4096.0 / 4095.0,
                                r0[:, 4:5], ALU.mult, ALU.add), deps=[h5, h6])
        h8 = p.op("scalar", act(r0[:, 6:7], r0[:, 5:6], AF.Ln), deps=[h7])
        h9 = p.op("scalar", act(r0[:, 6:7], r0[:, 6:7], AF.Exp, scale=-0.5), deps=[h8])
        h10 = p.op("vector", ts(dg, dg, r0[:, 2:3], ALU.subtract), deps=[h9, h2])
        h11 = p.op("vector", ts(dg, dg, r0[:, 6:7], ALU.mult), deps=[h10])
        h12 = p.op("scalar", act(dg, dg, AF.Square), deps=[h11])
        h13 = p.op("vector", ts(dg, dg, 2.25, ALU.is_le), deps=[h12])
        h14 = p.op("vector", cp(keepb, dg), deps=[h13])
        h15 = p.op("vector", ts(dg2[0:16, 0:2048], dg2[0:16, 0:2048], r0[:, 2:3],
                                ALU.subtract), deps=[h_dgo, h9])
        h16 = p.op("vector", ts(dg2[0:16, 0:2048], dg2[0:16, 0:2048], r0[:, 6:7],
                                ALU.mult), deps=[h15])
        h17 = p.op("scalar", act(dg2[0:16, 0:2048], dg2[0:16, 0:2048], AF.Square),
                   deps=[h16])
        h18 = p.op("vector", ts(dg2[0:16, 0:2048], dg2[0:16, 0:2048], 2.25,
                                ALU.is_le), deps=[h17])
        h19 = p.op("vector", cp(keepbo, dg2[0:16, 0:2048]), deps=[h18])
        bar_keep = p.barrier()

        qd3 = qdi.rearrange("(c p) i -> p c i", p=128)
        kd3 = kdi.rearrange("(c p) i -> p c i", p=128)
        vd4 = vdi.rearrange("j (h e) -> j h e", e=64)

        ps_pair = [PS[:, 0:1024], PS[:, 1024:2048]]
        ps_avA = PS[:, 2048:2560]
        ps_avB = PS[:, 2560:3072]
        ps_nrm = PS[:, 3072:3584]
        ps_kx = PS[:, 3584:4096]
        PTb2 = [PTb[0][0:128, 0:1024], PTb[1][0:128, 0:1024]]
        denrow = rowsc[0:1, 0:4096]
        krow = dg[0:1, 0:4096]
        h_expbuf = [bar_keep, bar_keep]
        h_avbuf = [bar_keep, bar_keep]
        h_avA_free = [bar_keep]
        h_avB_free = [bar_keep]
        h_nrmA_prev = bar_keep
        h_nrmB_prev = bar_keep
        h_den_readers = [bar_keep]
        h_krow_readers = [bar_keep]
        prev_group = bar_keep
        for g in range(4):
            hkl = []
            for cch in range(2):
                hkl.append(p.dma("sync", (lambda g, cch: lambda e: e.dma_start(
                    out=kg[:, cch * S:(cch + 1) * S],
                    in_=kdi[g * 256 + cch * 128: g * 256 + (cch + 1) * 128, :]))(g, cch),
                    deps=prev_group))
                hkl.append(p.dma("sync", (lambda g, cch: lambda e: e.dma_start(
                    out=qg[:, cch * HALF:(cch + 1) * HALF],
                    in_=qdi[g * 256 + cch * 128: g * 256 + (cch + 1) * 128, :]))(g, cch),
                    deps=prev_group))
            hvm = []
            for jc in range(32):
                vg3 = Vg[:, jc * 260:(jc + 1) * 260].rearrange("p (h e) -> p h e", e=65)
                hvd = p.dma("sync", (lambda jc, g, vg3: lambda e: e.dma_start(
                    out=vg3[:, :, 0:64],
                    in_=vd4[jc * 128:(jc + 1) * 128, 4 * g:4 * g + 4, :]))(jc, g, vg3),
                    deps=prev_group)
                hvo = p.op("vector", (lambda vg3: lambda e: e.memset(
                    vg3[:, :, 64:65], 1.0))(vg3), deps=[hvd] + (hvm[-1:] if hvm else []))
                hke = p.op("tensor", (lambda jc, g: lambda e: e.matmul(
                    PS[:, 2048:2048 + 260][0:128, :] if False else PS[0:128, 2048:2308],
                    keepb[:, jc * 128:(jc + 1) * 128],
                    e65gT[:, g * 260:(g + 1) * 260], start=True, stop=True))(jc, g),
                    deps=[bar_keep, h_e65] + (hvm[-1:] if hvm else []))
                hmask = p.op("vector", (lambda jc: tt(
                    Vg[:, jc * 260:(jc + 1) * 260],
                    Vg[:, jc * 260:(jc + 1) * 260],
                    PS[0:128, 2048:2308], ALU.mult))(jc), deps=[hvo, hke])
                hvm.append(hmask)
            bar_g = p.barrier()

            for pp in range(2):
                cch = pp
                hhA, hhB = 2 * pp, 2 * pp + 1
                hA_, hB_ = 4 * g + hhA, 4 * g + hhB
                qc2 = 2 * g + pp
                hkextr = []
                for side, hgl in ((0, hA_), (1, hB_)):
                    for ic in range(4):
                        hkr = p.op("tensor", (lambda hgl, ic: mm(
                            ps_kx[0:1, 0:512], i16bT[:, hgl:hgl + 1],
                            keepbo[:, ic * 512:(ic + 1) * 512]))(hgl, ic),
                            deps=[h19, h_i16, bar_g] + h_krow_readers
                                 + hkextr[-1:])
                        hkc = p.op("vector", (lambda side, ic: cp(
                            krow[0:1, side * 2048 + ic * 512:
                                 side * 2048 + ic * 512 + 512],
                            ps_kx[0:1, 0:512]))(side, ic), deps=[hkr])
                        hkextr.append(hkc)
                pair_dens = []
                pair_copies = {}
                for ic in range(4):
                    isl = slice(cch * HALF + ic * 512, cch * HALF + ic * 512 + 512)
                    nsl = slice(qc2 * HALF + ic * 512, qc2 * HALF + ic * 512 + 512)
                    h_avA = h_avB = None
                    pend = None
                    for jc in range(32):
                        buf = jc % 2

                        def emit_spair(buf, jc, cch, isl):
                            def f(e):
                                e.matmul(ps_pair[buf][:, 0:512],
                                         kg[0:64, cch * S + jc * 128:
                                            cch * S + jc * 128 + 128],
                                         qg[0:64, isl], start=True, stop=True)
                                return e.matmul(
                                    ps_pair[buf][:, 512:1024],
                                    kg[64:128, cch * S + jc * 128:
                                       cch * S + jc * 128 + 128],
                                    qg[64:128, isl], start=True, stop=True)
                            return f
                        hs = p.op("tensor", emit_spair(buf, jc, cch, isl),
                                  deps=[h_expbuf[buf], bar_g], n=2)

                        if pend is not None:
                            pjc, pbuf, phex = pend

                            def emit_avs(pjc, pbuf, side, hh_s):
                                def f(e):
                                    return e.matmul(
                                        (ps_avA if side == 0 else ps_avB)[0:65, :],
                                        Vg[:, pjc * 260 + hh_s * 65:
                                           pjc * 260 + hh_s * 65 + 65],
                                        PTb2[pbuf][:, side * 512: side * 512 + 512],
                                        start=(pjc == 0), stop=(pjc == 31))
                                return f
                            h_avA = p.op("tensor", emit_avs(pjc, pbuf, 0, hhA),
                                         deps=([phex] + (h_avA_free if pjc == 0
                                                         else [h_avA])))
                            h_avB = p.op("tensor", emit_avs(pjc, pbuf, 1, hhB),
                                         deps=([phex] + (h_avB_free if pjc == 0
                                                         else [h_avB])))
                            h_avbuf[pbuf] = h_avB

                        hex_ = p.op("scalar", (lambda buf: act(
                            PTb2[buf][:, 0:1024], ps_pair[buf][:, 0:1024],
                            AF.Exp, bias=negc[:, 0:1], scale=0.125))(buf),
                            deps=[hs, h_avbuf[buf], h_negc])
                        h_expbuf[buf] = hex_
                        pend = (jc, buf, hex_)

                    pjc, pbuf, phex = pend

                    def emit_avs(pjc, pbuf, side, hh_s):
                        def f(e):
                            return e.matmul(
                                (ps_avA if side == 0 else ps_avB)[0:65, :],
                                Vg[:, pjc * 260 + hh_s * 65:
                                   pjc * 260 + hh_s * 65 + 65],
                                PTb2[pbuf][:, side * 512: side * 512 + 512],
                                start=(pjc == 0), stop=(pjc == 31))
                        return f
                    h_avA = p.op("tensor", emit_avs(pjc, pbuf, 0, hhA),
                                 deps=[phex, h_avA])
                    h_avB = p.op("tensor", emit_avs(pjc, pbuf, 1, hhB),
                                 deps=[phex, h_avB])
                    h_avbuf[pbuf] = h_avB

                    for side, (qb2, ps_av_s, h_av_s) in enumerate(
                            ((0, ps_avA, h_avA), (64, ps_avB, h_avB))):
                        hd1 = p.op("vector", (lambda side, ic, ps_av_s: cp(
                            denrow[0:1, side * 2048 + ic * 512:
                                   side * 2048 + ic * 512 + 512],
                            ps_av_s[64:65, :]))(side, ic, ps_av_s),
                            deps=[h_av_s] + h_den_readers)
                        pair_dens.append(hd1)
                        hcopy = p.op("vector", (lambda qb2, nsl, ps_av_s: cp(
                            numT[qb2:qb2 + 64, nsl], ps_av_s[0:64, :]))(
                                qb2, nsl, ps_av_s),
                            deps=[h_av_s])
                        pair_copies[(side, ic)] = hcopy
                        if side == 0:
                            h_avA_free = [hd1, hcopy]
                        else:
                            h_avB_free = [hd1, hcopy]
                hmx = p.op("vector", ts(denrow, denrow, 1e-30, ALU.max),
                           deps=pair_dens)
                hrec = p.op("vector",
                            lambda e: e.reciprocal(out=denrow, in_=denrow),
                            deps=[hmx])
                hkm = p.op("vector", tt(denrow, denrow, krow, ALU.mult),
                           deps=[hrec] + hkextr)
                h_krow_readers = [hkm]
                readers = []
                for side, qb2 in ((0, 0), (1, 64)):
                    hn_prev = h_nrmA_prev if side == 0 else h_nrmB_prev
                    for ic in range(4):
                        nsl = slice(qc2 * HALF + ic * 512,
                                    qc2 * HALF + ic * 512 + 512)
                        rsl = slice(side * 2048 + ic * 512,
                                    side * 2048 + ic * 512 + 512)
                        hf = p.op("tensor", (lambda qb2, rsl: mm(
                            ps_nrm[qb2:qb2 + 64, :], onesfT[0:1, 0:64],
                            denrow[0:1, rsl]))(qb2, rsl),
                            deps=[hkm, hn_prev, h_of])
                        hn = p.op("vector", (lambda qb2, nsl: tt(
                            numT[qb2:qb2 + 64, nsl], numT[qb2:qb2 + 64, nsl],
                            ps_nrm[qb2:qb2 + 64, :], ALU.mult))(qb2, nsl),
                            deps=[hf, pair_copies[(side, ic)]])
                        readers.append(hf)
                        hn_prev = hn
                    if side == 0:
                        h_nrmA_prev = hn_prev
                    else:
                        h_nrmB_prev = hn_prev
                h_den_readers = readers
            prev_group = p.barrier()
        bar_attn = prev_group

        wob = bt[0:128, o_kg:o_kg + 8 * 1024]
        sqb = bt[0:128, o_kg + 8192:o_kg + 8192 + 4096]
        dep = bar_attn
        h_wo = None
        for c in range(8):
            hd = p.dma("sync", (lambda c: lambda e: e.dma_start(
                out=scrB[:, 0:1024], in_=woutT[c * 128:(c + 1) * 128, :]))(c), deps=dep)
            h_wo = p.op("vector", (lambda c: cp(wob[:, c * 1024:(c + 1) * 1024],
                                                scrB[:, 0:1024]))(c), deps=[hd])
            dep = [h_wo]
        h_fc = p.op("vector", lambda e: e.memset(fcol[:, 0:1], 1.0 / 1024.0))
        h_o1 = p.op("vector", lambda e: e.memset(onesb[:, 1:2], 1.0 / 1024.0),
                    deps=[h_ones])
        x2o3 = x2oT.rearrange("(c p) i -> p c i", p=128)
        x3r3 = x3T.rearrange("(c p) i -> p c i", p=128)
        x3l3 = x3lnT.rearrange("(c p) i -> p c i", p=128)
        scrA3 = scrA.rearrange("p (c i) -> p c i", i=512)
        scrC3 = scrC.rearrange("p (c i) -> p c i", i=512)
        hout = []
        hra = [h_wo]
        h_psm_free = bar_attn
        h_amax = None
        for ic in range(4):
            otok = slice(ic * 512, ic * 512 + 512)
            hx = p.dma("sync", (lambda otok: lambda e: e.dma_start(
                out=scrA3, in_=x2o3[:, :, otok]))(otok), deps=hra)
            hnew = []
            for doc in range(8):
                def emit_op(doc, ic):
                    def f(e):
                        last = None
                        for c in range(8):
                            last = e.matmul(
                                ps_m[:, :],
                                wob[:, c * 1024 + doc * 128: c * 1024 + doc * 128 + 128],
                                numT[:, c * HALF + ic * 512: c * HALF + ic * 512 + 512],
                                start=(c == 0), stop=(c == 7))
                        return last
                    return f
                hacc = p.op("tensor", emit_op(doc, ic),
                            deps=[h_wo, bar_attn, h_psm_free], n=8)
                blk = slice(doc * 512, doc * 512 + 512)
                hb1_ = p.op("scalar", (lambda doc: act(scrB[:, 0:512], ps_m[:, :],
                                                       AF.Identity,
                                                       bias=boutT[:, doc:doc + 1]))(doc),
                            deps=[hacc, h_bout] + hnew[-1:])
                hr = p.op("vector", (lambda blk: tt(scrA[:, blk], scrB[:, 0:512],
                                                    scrA[:, blk], ALU.add))(blk),
                          deps=[hb1_, hx])
                h_psm_free = hr
                hnew.append(hr)
            ho = p.dma("gpsimd", (lambda ic: lambda e: e.dma_start(
                out=x3r3[:, :, slice(ic * 512, ic * 512 + 512)], in_=scrA3))(ic),
                deps=hnew)
            hsq = p.op("scalar", act(sqb, scrA, AF.Square), deps=hnew)

            def emit_sum(dst, lhs_col, srct):
                def f(e):
                    last = None
                    for c in range(8):
                        last = e.matmul(dst, lhs_col,
                                        srct[:, c * 512:(c + 1) * 512],
                                        start=(c == 0), stop=(c == 7))
                    return last
                return f
            hsum = p.op("tensor", emit_sum(PSB[0][0:1, :], fcol[:, 0:1], scrA),
                        deps=[h_fc, h_psm_free] + hnew, n=8)
            hsqs = p.op("tensor", emit_sum(PSB[1][0:1, :], onesb[:, 1:2], sqb),
                        deps=[hsq, h_o1], n=8)
            rm = rowsc[0:1, 0:512]
            rv = rowsc[0:1, 512:1024]
            rr = rowsc[0:1, 1024:1536]
            rmb = rowsc[0:1, 1536:2048]
            s1 = p.op("vector", cp(rm, PSB[0][0:1, :]), deps=[hsum] + hout[-1:])
            s2 = p.op("scalar", act(rv, rm, AF.Square), deps=[s1])
            s3 = p.op("vector", stt(rv, rv, -1.0, PSB[1][0:1, :], ALU.mult, ALU.add),
                      deps=[s2, hsqs])
            s3b = p.op("vector", ts(rv, rv, EPS, ALU.add), deps=[s3])
            s4 = p.op("scalar", act(rr, rv, AF.Ln), deps=[s3b])
            s5 = p.op("scalar", act(rr, rr, AF.Exp, scale=-0.5), deps=[s4])
            s6 = p.op("vector", stt(rmb, rm, -1.0, rr, ALU.mult, ALU.mult), deps=[s5])
            sb1 = p.op("tensor", mm(PSB[3][:, :], onesfT, rr), deps=[s5, h_of])
            sb2 = p.op("tensor", mm(PSB[4][:, :], onesfT, rmb), deps=[s6])
            hln = []
            hpv = None
            for c in range(8):
                blk = slice(c * 512, c * 512 + 512)
                la = p.op("vector", (lambda blk: tt(scrB[:, 0:512], scrA[:, blk],
                                                    PSB[3][:, :], ALU.mult))(blk),
                          deps=[sb1, ho] + ([hpv] if hpv else []))
                lb = p.op("vector", (lambda blk: tt(scrC[:, blk], scrB[:, 0:512],
                                                    PSB[4][:, :], ALU.add))(blk),
                          deps=[la, sb2])
                lc = p.op("scalar", (lambda blk: act(scrB[:, 512:1024],
                                                     scrC[:, blk], AF.Abs))(blk),
                          deps=[lb])
                if h_amax is None:
                    h_amax = p.op("vector", cp(scrM, scrB[:, 512:1024]), deps=[lc])
                else:
                    h_amax = p.op("vector", tt(scrM, scrM, scrB[:, 512:1024],
                                               ALU.max), deps=[lc, h_amax])
                hpv = h_amax
                hln.append(lb)
            ho2 = p.dma("gpsimd", (lambda ic: lambda e: e.dma_start(
                out=x3l3[:, :, slice(ic * 512, ic * 512 + 512)], in_=scrC3))(ic),
                deps=hln + [h_amax])
            hout = [ho, ho2]
            hra = hln + [ho, ho2, h_amax]
        a1 = p.op("vector", lambda e: e.reduce_max(out=scrM[:, 0:1], in_=scrM,
                                                   axis=mybir.AxisListType.X),
                  deps=[h_amax])
        a2 = p.op("vector", cp(scrB[0:64, 0:1], scrM[64:128, 0:1]), deps=[a1])
        a3 = p.op("vector", tt(scrM[0:64, 0:1], scrM[0:64, 0:1], scrB[0:64, 0:1],
                               ALU.max), deps=[a2])
        a4 = p.op("vector", cp(scrB[0:32, 0:1], scrM[32:64, 0:1]), deps=[a3])
        a5 = p.op("vector", tt(scrM[0:32, 0:1], scrM[0:32, 0:1], scrB[0:32, 0:1],
                               ALU.max), deps=[a4])
        a6 = p.op("tensor", lambda e: e.transpose(PS[0:1, 2048:2080],
                                                   scrM[0:32, 0:1],
                                                   i128T2[0:32, 0:32]),
                  deps=[a5, h_i1282])
        a7 = p.op("vector", cp(rowsc[0:1, 0:32], PS[0:1, 2048:2080]), deps=[a6])
        a8 = p.op("vector", lambda e: e.reduce_max(out=rowsc[0:1, 32:33],
                                                   in_=rowsc[0:1, 0:32],
                                                   axis=mybir.AxisListType.X),
                  deps=[a7])
        az = p.op("vector", lambda e: e.memset(rowsc[0:1, 33:48], 0.0), deps=[a8])
        ha = p.dma("gpsimd", lambda e: e.dma_start(out=amax, in_=rowsc[0:1, 32:48]),
                   deps=[a8, az])
    return p.barrier()




def emit_F(ctx, upto="full"):
    nc, p, bt, ft, PS, t = ctx["nc"], ctx["p"], ctx["bt"], ctx["ft"], ctx["PS"], ctx["t"]
    x3Ti = t["x3T"]
    x3l = t["x3lnT"]
    amax8 = t["amax8"]
    wT = t["wT"]
    biasc = t["biasc"]
    onesf = t["onesf"]
    i128 = t["i128"]
    outT = t["outT"]
    if True:
        ba = Arena(bt, GNB16)
        fa = Arena(ft, GNF32)

        wq8 = ba.alloc(128, 8 * 1024)
        xq8 = ba.alloc(128, 8 * 512)
        onesb = ba.alloc(128, 32)

        onesfT = fa.alloc(1, 128)
        i128T = fa.alloc(128, 128)
        biasT = fa.alloc(128, 32)
        gcol = fa.alloc(128, 32)
        rowsc = fa.alloc(128, 1024)
        wf = fa.alloc(128, 1024)
        scrA = fa.alloc(128, 4096)
        scrR = fa.alloc(128, 4096)
        scrV = fa.alloc(128, 4096)
        scrU = fa.alloc(128, 2048)
        scrW = fa.alloc(128, 2048)
        scrD = fa.alloc(128, 2048)

        PSB = [PS[:, k * 512:(k + 1) * 512] for k in range(8)]

        h_of = p.dma("sync", lambda e: e.dma_start(out=onesfT, in_=onesf))
        h_i128 = p.dma("sync", lambda e: e.dma_start(out=i128T, in_=i128))
        h_bias = p.dma("sync", lambda e: e.dma_start(out=biasT[:, 0:8], in_=biasc))
        h_am = None
        h_tc = p.op("vector", lambda e: e.memset(gcol[:, 2:3], 1e-38))
        h_am = p.dma("sync", lambda e: e.dma_start(out=rowsc[0:1, 100:116],
                                                   in_=amax8))
        bar0 = p.barrier()

        hx1 = p.op("vector", lambda e: e.reduce_max(out=rowsc[0:1, 8:9],
                                                    in_=rowsc[0:1, 0:16][0:1, 0:8]
                                                    if False else rowsc[0:1, 100:116],
                                                    axis=mybir.AxisListType.X),
                   deps=[h_am])
        hwa = None
        hwl = None
        for c in range(8):
            hwl = p.dma("sync", (lambda c: lambda e: e.dma_start(
                out=wf, in_=wT[c * 128:(c + 1) * 128, :]))(c),
                deps=[bar0 if hwa is None else hwa])
            ha = p.op("scalar", act(scrA[:, 0:1024], wf, AF.Abs), deps=[hwl])
            hr = p.op("vector", (lambda c: lambda e: e.reduce_max(
                out=scrV[:, c:c + 1], in_=scrA[:, 0:1024],
                axis=mybir.AxisListType.X))(c), deps=[ha])
            hwa = hr
        hw1 = p.op("vector", lambda e: e.reduce_max(out=scrV[:, 8:9],
                                                    in_=scrV[:, 0:8],
                                                    axis=mybir.AxisListType.X),
                   deps=[hwa])
        hw2 = p.op("vector", cp(scrW[0:64, 0:1], scrV[64:128, 8:9]), deps=[hw1])
        hw3 = p.op("vector", tt(scrV[0:64, 8:9], scrV[0:64, 8:9], scrW[0:64, 0:1],
                                ALU.max), deps=[hw2])
        hw4 = p.op("vector", cp(scrW[0:32, 0:1], scrV[32:64, 8:9]), deps=[hw3])
        hw5 = p.op("vector", tt(scrV[0:32, 8:9], scrV[0:32, 8:9], scrW[0:32, 0:1],
                                ALU.max), deps=[hw4])
        hw6 = p.op("tensor", lambda e: e.transpose(PS[0:1, 0:32], scrV[0:32, 8:9],
                                                   i128T[0:32, 0:32]),
                   deps=[hw5, h_i128])
        hw7 = p.op("vector", cp(rowsc[0:1, 16:48], PS[0:1, 0:32]), deps=[hw6])
        hw8 = p.op("vector", lambda e: e.reduce_max(out=rowsc[0:1, 9:10],
                                                    in_=rowsc[0:1, 16:48],
                                                    axis=mybir.AxisListType.X),
                   deps=[hw7])
        hs1 = p.op("vector", ts(rowsc[0:1, 10:12], rowsc[0:1, 8:10], 1e-12, ALU.max),
                   deps=[hx1, hw8])
        hs2 = p.op("vector", lambda e: e.reciprocal(out=rowsc[0:1, 12:14],
                                                    in_=rowsc[0:1, 10:12]),
                   deps=[hs1])
        hs3 = p.op("vector", ts(rowsc[0:1, 12:14], rowsc[0:1, 12:14], 448.0,
                                ALU.mult, 448.0, ALU.min), deps=[hs2])
        hs4 = p.op("vector", lambda e: e.reciprocal(out=rowsc[0:1, 14:16],
                                                    in_=rowsc[0:1, 12:14]),
                   deps=[hs3])
        hs5 = p.op("vector", tt(rowsc[0:1, 20:21], rowsc[0:1, 14:15],
                                rowsc[0:1, 15:16], ALU.mult), deps=[hs4])
        hb1 = p.op("tensor", mm(PS[0:128, 0:2], onesfT, rowsc[0:1, 12:14]),
                   deps=[hs5, h_of])
        hb2 = p.op("vector", cp(gcol[:, 0:2], PS[0:128, 0:2]), deps=[hb1])
        hb3 = p.op("tensor", mm(PS[0:128, 4:5], onesfT, rowsc[0:1, 20:21]),
                   deps=[hb2])
        hb4 = p.op("vector", cp(gcol[:, 4:5], PS[0:128, 4:5]), deps=[hb3])
        bar_s = p.barrier()

        def emit_fp8(src_ap, dst_ap, scale_col, width, deps, voff=0):
            v = scrV[:, voff:voff + width]
            u = scrU[:, 0:width]
            w_ = scrW[:, 0:width]
            d_ = scrD[:, 0:width]
            h1 = p.op("vector", (lambda: ts(v, src_ap,
                                            gcol[:, scale_col:scale_col + 1],
                                            ALU.mult))(), deps=deps)
            h2 = p.op("vector", ts(v, v, 448.0, ALU.min, -448.0, ALU.max), deps=[h1])
            h3 = p.op("scalar", act(u, v, AF.Square), deps=[h2])
            h4 = p.op("scalar", act(u, u, AF.Ln, bias=gcol[:, 2:3]), deps=[h3, h_tc])
            h5 = p.op("vector", ts(u, u, 0.5 / LN2, ALU.mult, 127.0, ALU.add),
                      deps=[h4])
            h6 = p.op("vector", cp(w_, u), deps=[h5])
            hc = frac_cascade(p, w_, d_, 7, 8, [h6])
            h7 = p.op("vector", tt(u, u, w_, ALU.subtract), deps=[hc])
            h8 = p.op("vector", ts(u, u, -130.0, ALU.add, -9.0, ALU.max), deps=[h7])
            h9 = p.op("scalar", act(w_, u, AF.Exp, scale=-LN2), deps=[h8])
            h10 = p.op("vector", tt(v, v, w_, ALU.mult), deps=[h9])
            h11 = p.op("vector", ts(v, v, 16.5, ALU.add), deps=[h10])
            h12 = p.op("vector", cp(w_, v), deps=[h11])
            hc2 = frac_cascade(p, w_, d_, 4, 5, [h12])
            h13 = p.op("vector", tt(v, v, w_, ALU.subtract), deps=[hc2])
            h14 = p.op("vector", ts(v, v, -16.0, ALU.add), deps=[h13])
            h15 = p.op("scalar", act(w_, u, AF.Exp, scale=LN2), deps=[h14])
            h16 = p.op("vector", (lambda: tt(dst_ap, v, w_, ALU.mult))(), deps=[h15])
            return h16

        def emit_bf16round(width, deps, voff=0):
            v = scrV[:, voff:voff + width]
            u = scrU[:, 0:width]
            w_ = scrW[:, 0:width]
            d_ = scrD[:, 0:width]
            h3 = p.op("scalar", act(u, v, AF.Square), deps=deps)
            h4 = p.op("scalar", act(u, u, AF.Ln, bias=gcol[:, 2:3]), deps=[h3, h_tc])
            h5 = p.op("vector", ts(u, u, 0.5 / LN2, ALU.mult, 127.0, ALU.add),
                      deps=[h4])
            h6 = p.op("vector", cp(w_, u), deps=[h5])
            hc = frac_cascade(p, w_, d_, 7, 8, [h6])
            h7 = p.op("vector", tt(u, u, w_, ALU.subtract), deps=[hc])
            h8 = p.op("vector", ts(u, u, -134.0, ALU.add, -60.0, ALU.max), deps=[h7])
            h9 = p.op("scalar", act(w_, u, AF.Exp, scale=-LN2), deps=[h8])
            h10 = p.op("vector", tt(v, v, w_, ALU.mult), deps=[h9])
            h11 = p.op("vector", ts(v, v, 256.5, ALU.add), deps=[h10])
            h12 = p.op("vector", cp(w_, v), deps=[h11])
            hc2 = frac_cascade(p, w_, d_, 8, 9, [h12])
            h13 = p.op("vector", tt(v, v, w_, ALU.subtract), deps=[hc2])
            h14 = p.op("vector", ts(v, v, -256.0, ALU.add), deps=[h13])
            h15 = p.op("scalar", act(w_, u, AF.Exp, scale=LN2), deps=[h14])
            h16 = p.op("vector", tt(v, v, w_, ALU.mult), deps=[h15])
            return h16

        hq = bar_s
        for c in range(8):
            hwl = p.dma("sync", (lambda c: lambda e: e.dma_start(
                out=wf, in_=wT[c * 128:(c + 1) * 128, :]))(c),
                deps=hq if isinstance(hq, list) else [hq])
            hq = [emit_fp8(wf, wq8[:, c * 1024:(c + 1) * 1024], 1, 1024, [hwl])]
        bar_wq = p.barrier()

        x3r3 = x3Ti.rearrange("(c p) i -> p c i", p=128)
        x3l3 = x3l.rearrange("(c p) i -> p c i", p=128)
        outr3 = outT.rearrange("(c p) i -> p c i", p=128)
        scrA3 = scrA.rearrange("p (c i) -> p c i", i=512)

        hprev = bar_wq
        for tc in range(4):
            tok = slice(tc * 512, tc * 512 + 512)
            hx = p.dma("sync", (lambda tok: lambda e: e.dma_start(
                out=scrA3, in_=x3l3[:, :, tok]))(tok), deps=hprev)
            hxr = p.dma("sync", (lambda tok: lambda e: e.dma_start(
                out=scrR.rearrange("p (c i) -> p c i", i=512),
                in_=x3r3[:, :, tok]))(tok), deps=hprev)
            hq8a = emit_fp8(scrA[:, 0:2048], xq8[:, 0:2048], 0, 2048, [hx])
            hq8 = emit_fp8(scrA[:, 2048:4096], xq8[:, 2048:4096], 0, 2048, [hq8a])
            hmm = {}
            for dout in range(8):
                hacc = None
                for c in range(8):
                    lhs = wq8[:, c * 1024 + dout * 128: c * 1024 + dout * 128 + 128]
                    rhs = xq8[:, c * 512:(c + 1) * 512]
                    hacc = p.op("tensor", (lambda lhs, rhs, dout, st, sp: mm(
                        PSB[dout][:, :], lhs, rhs, start=st, stop=sp))(
                            lhs, rhs, dout, c == 0, c == 7),
                        deps=([hq8, bar_wq] if hacc is None else [hacc]))
                hmm[dout] = hacc
            hev = [hq8]
            for dout in range(8):
                blk = slice(dout * 512, dout * 512 + 512)
                hev.append(p.op("vector", (lambda blk, dout: ts(
                    scrV[:, blk], PSB[dout][:, :], gcol[:, 4:5], ALU.mult,
                    biasT[:, dout:dout + 1], ALU.add))(blk, dout),
                    deps=[hmm[dout], hb4, h_bias] + hev[-1:]))
            hrd1 = emit_bf16round(2048, hev, voff=0)
            hrd = emit_bf16round(2048, [hrd1], voff=2048)
            hres = p.op("vector", tt(scrA, scrV, scrR, ALU.add), deps=[hrd, hxr])
            ho = p.dma("gpsimd", (lambda tok: lambda e: e.dma_start(
                out=outr3[:, :, tok], in_=scrA3))(tok), deps=[hres])
            hprev = [ho, hres]
    return p.barrier()




def _rope_permute_cols(pairs):
    perm = []
    for (a, b_) in pairs:
        perm.extend(range(a * 64, a * 64 + 32))
        perm.extend(range(b_ * 64, b_ * 64 + 32))
        perm.extend(range(a * 64 + 32, a * 64 + 64))
        perm.extend(range(b_ * 64 + 32, b_ * 64 + 64))
    return np.array(perm)


def _headperm_rows(pairs):
    perm = []
    for (a, b_) in pairs:
        perm.extend(range(a * 64, (a + 1) * 64))
        perm.extend(range(b_ * 64, (b_ + 1) * 64))
    return np.array(perm)


def make_A_inputs(inputs):
    x = np.ascontiguousarray(inputs["L0_attn_x"], dtype=np.float32)
    qperm = _rope_permute_cols(QPAIRS)
    kperm = _rope_permute_cols([(0, 1), (2, 3)])
    wq_p = np.ascontiguousarray(inputs["L0_attn_w_q"][:, qperm], dtype=np.float32)
    wk_p = np.ascontiguousarray(inputs["L0_attn_w_k"][:, kperm], dtype=np.float32)
    wv = np.ascontiguousarray(inputs["L0_attn_w_v"], dtype=np.float32)
    wo_p = np.ascontiguousarray(inputs["L0_attn_w_o"][_headperm_rows(QPAIRS), :],
                                dtype=np.float32)
    posr = np.arange(S, dtype=np.float32).reshape(1, S)
    didx = np.arange(32, dtype=np.float32).reshape(1, 32)
    onesf = np.ones((1, 128), dtype=np.float32)
    pr = np.arange(128)
    jrep = (np.arange(32)[:, None] == (pr[None, :] % 32)).astype(np.float32)
    sgn = np.where(pr < 64, -1.0, 1.0).astype(np.float32)
    jsgn = jrep * sgn[None, :]
    common = dict(wq=wq_p, wk=wk_p, wv=wv, wo=wo_p, posr=posr, didx=didx,
                  onesf=onesf, jrep=jrep, jsgn=jsgn)
    in_maps = []
    for c in range(NCORES):
        b, half = c // 2, c % 2
        m = dict(common)
        xT = np.ascontiguousarray(x[b].T)
        m["xbT"] = xT
        m["xoT"] = np.ascontiguousarray(xT[:, half * HALF:(half + 1) * HALF])
        m["poso"] = posr[:, half * HALF:(half + 1) * HALF].copy()
        in_maps.append(m)
    return in_maps


def run_launch_A(inputs, core_ids=None):
    in_maps = make_A_inputs(inputs)
    nc0 = build_launch_A()
    cids = core_ids or list(range(NCORES))
    r = run_bass_kernel_spmd(nc0, [in_maps[c] for c in cids], core_ids=cids)
    x1 = np.empty((B, S, D), dtype=np.float32)
    for i, c in enumerate(cids):
        b, half = c // 2, c % 2
        x1[b, half * HALF:(half + 1) * HALF] = r.results[i]["x1T"].T
    return x1


def make_C1_inputs(inputs, x2):
    wqkvT = np.ascontiguousarray(inputs["L1_attn_w_qkv"].T.astype(np.float32))
    common = dict(wqkvT=wqkvT, onesf=np.ones((1, 128), dtype=np.float32))
    in_maps = []
    for c in range(NCORES):
        b, half = c // 2, c % 2
        m = dict(common)
        xT = np.ascontiguousarray(x2[b].T.astype(np.float32))
        m["x2T"] = xT
        m["x2oT"] = np.ascontiguousarray(xT[:, half * HALF:(half + 1) * HALF])
        in_maps.append(m)
    return in_maps


def make_C2_inputs(qd, kd):
    in_maps = []
    for c in range(NCORES):
        b, half = c // 2, c % 2
        m = {}
        m["qd"] = qd[c]
        m["kd"] = np.ascontiguousarray(np.concatenate(
            [kd[c][:, half * HALF:(half + 1) * HALF],
             kd[c][:, (1 - half) * HALF:(2 - half) * HALF]], axis=1))
        m["i128"] = np.eye(128, dtype=np.float32)
        in_maps.append(m)
    return in_maps


def mrowP_to_mrow(mp):
    return np.ascontiguousarray(
        np.asarray(mp, dtype=np.float32).reshape(128, 16, 16)
        .transpose(1, 2, 0).reshape(16, HALF))


def _e65g():
    e = np.zeros((16, 1040), dtype=np.float32)
    for g in range(4):
        for j in range(4):
            h = 4 * g + j
            e[h, g * 260 + j * 65: g * 260 + j * 65 + 65] = 1.0
    import ml_dtypes
    return e.astype(ml_dtypes.bfloat16)


def make_D_inputs(inputs, x2, qd, kd, vd, mrow):
    import ml_dtypes
    woutT = np.ascontiguousarray(inputs["L1_attn_w_out"].T.astype(np.float32))
    bout = np.asarray(inputs["L1_attn_b_out"], dtype=np.float32)
    boutc = np.ascontiguousarray(bout.reshape(-1, 128).T)
    e65 = _e65g()
    i16b = np.eye(16, dtype=np.float32).astype(ml_dtypes.bfloat16)
    common = dict(woutT=woutT, boutc=boutc, e65g=e65, i16b=i16b,
                  onesf=np.ones((1, 128), dtype=np.float32),
                  i128=np.eye(128, dtype=np.float32))
    diag_full = {}
    for b in range(B):
        dm = np.empty((16, S), dtype=np.float32)
        dm[:, 0:HALF] = mrow[2 * b]
        dm[:, HALF:S] = mrow[2 * b + 1]
        diag_full[b] = dm
    in_maps = []
    for c in range(NCORES):
        b, half = c // 2, c % 2
        m = dict(common)
        xT = np.ascontiguousarray(x2[b].T.astype(np.float32))
        m["x2oT"] = np.ascontiguousarray(xT[:, half * HALF:(half + 1) * HALF])
        m["qd"] = qd[c]
        m["kd"] = kd[c]
        m["vd"] = vd[c]
        m["diagm"] = diag_full[b]
        m["diagmo"] = np.ascontiguousarray(
            diag_full[b][:, half * HALF:(half + 1) * HALF])
        in_maps.append(m)
    return in_maps


def make_F_inputs(inputs, x3T, x3lnT, amaxs):
    wTt = np.ascontiguousarray(inputs["L1_ffn_weight"].T.astype(np.float32))
    biasv = np.asarray(inputs["L1_ffn_bias"], dtype=np.float32)
    biasc = np.ascontiguousarray(biasv.reshape(-1, 128).T)
    a = np.zeros((1, 16), dtype=np.float32)
    a[0, :len(amaxs)] = np.array(amaxs, dtype=np.float32)
    amax8 = a
    common = dict(wT=wTt, biasc=biasc, amax8=amax8,
                  onesf=np.ones((1, 128), dtype=np.float32),
                  i128=np.eye(128, dtype=np.float32))
    in_maps = []
    for c in range(NCORES):
        m = dict(common)
        m["x3T"] = x3T[c]
        m["x3lnT"] = x3lnT[c]
        in_maps.append(m)
    return in_maps


def kernel(**inputs):
    cids = list(range(NCORES))
    in_A = make_A_inputs(inputs)
    in_Bc = make_B_inputs(inputs, np.zeros((B, S, D), np.float32))
    in_AB = []
    for c in cids:
        m = dict(in_A[c])
        for k, v in in_Bc[c].items():
            if k != "x1T":
                m.setdefault(k, v)
        in_AB.append(m)
    rB = run_bass_kernel_spmd(build_launch_AB(), in_AB, core_ids=cids)
    x2 = np.empty((B, S, D), dtype=np.float32)
    for i, c in enumerate(cids):
        b, half = c // 2, c % 2
        x2[b, half * HALF:(half + 1) * HALF] = rB.results[i]["x2T"].T
    in_C1 = make_C1_inputs(inputs, x2)
    rC1 = run_bass_kernel_spmd(build_launch_C1(), in_C1, core_ids=cids)
    qd = {c: rC1.results[i]["qd"] for i, c in enumerate(cids)}
    kd = {c: rC1.results[i]["kd"] for i, c in enumerate(cids)}
    vd = {c: rC1.results[i]["vd"] for i, c in enumerate(cids)}
    in_C2 = make_C2_inputs(qd, kd)
    rC2 = run_bass_kernel_spmd(build_launch_C2(), in_C2, core_ids=cids)
    mrow = {c: mrowP_to_mrow(rC2.results[i]["mrowP"]) for i, c in enumerate(cids)}
    in_D = make_D_inputs(inputs, x2, qd, kd, vd, mrow)
    rD = run_bass_kernel_spmd(build_launch_D(), in_D, core_ids=cids)
    x3T = {c: rD.results[i]["x3T"] for i, c in enumerate(cids)}
    x3lnT = {c: rD.results[i]["x3lnT"] for i, c in enumerate(cids)}
    amaxs = [float(rD.results[i]["amax"][0, 0]) for i, c in enumerate(cids)]
    in_F = make_F_inputs(inputs, x3T, x3lnT, amaxs)
    rF = run_bass_kernel_spmd(build_launch_F(), in_F, core_ids=cids)
    out = np.empty((B, S, D), dtype=np.float32)
    for i, c in enumerate(cids):
        b, half = c // 2, c % 2
        out[b, half * HALF:(half + 1) * HALF] = rF.results[i]["outT"].T
    return out
